# version 11
# baseline (speedup 1.0000x reference)
import contextlib
import numpy as np
import concourse.bass as bass
import concourse.mybir as mybir
from concourse.bass_utils import run_bass_kernel_spmd

F32 = mybir.dt.float32
BF16 = mybir.dt.bfloat16
AF = mybir.ActivationFunctionType
ALU = mybir.AluOpType
AX = mybir.AxisListType.X

D = 1024
H = 16
FF = 256
DFF = 4096
NC = 8
CH = 128
NKC = D // CH
NMF = DFF // CH

EPS_LN = 1e-5
EPS_ATTN = 1e-8
LN_EPS_FFN = 1e-3
EPS_FFN = 1e-5
QB = 128.0
CLIP = QB - EPS_FFN

TT = 512
TCH = TT // CH


class Plan:

    ENGS = ("sync", "tensor", "vector", "scalar", "gpsimd")

    NLANES = {"sync": 8, "scalar": 6, "gpsimd": 4}

    def __init__(self, state, drop=(), psum_keys=()):
        self.state = state
        self.drop = tuple(drop)
        self.psum_keys = set(psum_keys)
        self.ops = []
        self.last_w = {}
        self.readers = {}

    def add(self, eng, emit, reads=(), writes=(), kind="compute", tag="",
            late=False):
        if any(tag.startswith(p) for p in self.drop):
            return None
        idx = len(self.ops)
        deps = set()
        for b in reads:
            w = self.last_w.get(b)
            if w is not None:
                deps.add(w)
            if b in self.psum_keys:
                for r in self.readers.get(b, ()):
                    if self.ops[r]["eng"] != eng:
                        deps.add(r)
        for b in writes:
            w = self.last_w.get(b)
            if w is not None:
                deps.add(w)
            for r in self.readers.get(b, ()):
                deps.add(r)
        for b in reads:
            self.readers.setdefault(b, []).append(idx)
        for b in writes:
            self.last_w[b] = idx
            self.readers[b] = []
        deps.discard(idx)
        lane = None
        if kind == "dma":
            lc = self.state.setdefault("lanecnt", {})
            lc[eng] = lc.get(eng, 0) + 1
            lane = lc[eng] % self.NLANES[eng]
        self.ops.append(dict(eng=eng, emit=emit, kind=kind, deps=deps, tag=tag,
                             late=late, inc=(kind != "compute"), incv=None,
                             waits=None, lane=lane, idx=idx))
        return idx

    def dma(self, eng, emit, reads=(), writes=(), tag=""):
        return self.add(eng, emit, reads, writes, kind="dma", tag=tag)

    def _sname(self, op):
        if op["kind"] == "dma":
            return "q_%s%d" % (op["eng"], op["lane"])
        if op["kind"] == "cc":
            return "cc"
        return "e_" + op["eng"]

    def _nowait(self, dep, op):
        if dep.get("late"):
            return False
        if dep["eng"] != op["eng"]:
            return False
        if dep["kind"] == "dma" or dep["kind"] == "cc":
            return dep["kind"] == op["kind"]
        return dep["eng"] == "tensor"

    FULL_WAITS = False

    def resolve(self):
        counts = self.state["counts"]
        ops = self.ops
        for op in ops:
            for d in op["deps"]:
                dep = ops[d]
                if not self._nowait(dep, op):
                    dep["inc"] = True
        for op in ops:
            if op["inc"]:
                sn = self._sname(op)
                counts[sn] = counts.get(sn, 0) + (16 if op["kind"] == "dma" else 1)
                op["incv"] = counts[sn]
        clocks = self.state["clocks"]
        opclock = [None] * len(ops)
        for op in ops:
            ec = clocks.setdefault(op["eng"], {})
            waits = []
            for d in sorted(op["deps"]):
                dep = ops[d]
                if self._nowait(dep, op):
                    continue
                sn, v = self._sname(dep), dep["incv"]
                assert v is not None, (op["tag"], dep["tag"])
                if ec.get(sn, 0) >= v and not Plan.FULL_WAITS:
                    continue
                if ec.get(sn, 0) >= v:
                    ec[sn] = max(ec.get(sn, 0), v)
                    waits.append((sn, v))
                    continue
                waits.append((sn, v))
                ec[sn] = v
                dc = opclock[d]
                if dc:
                    for s2, v2 in dc.items():
                        if ec.get(s2, 0) < v2:
                            ec[s2] = v2
            op["waits"] = waits
            opclock[op["idx"]] = dict(ec)

    def emit(self, nc, sems):
        self.resolve()
        counts = self.state["counts"]
        qsems = ["q_sync%d" % i for i in range(8)] + \
                ["q_scalar%d" % i for i in range(6)] + \
                ["q_gpsimd%d" % i for i in range(4)] + ["cc"]
        drain = [(sn, counts[sn]) for sn in qsems if counts.get(sn)]
        with nc.Block() as blk:
            for eng_name in self.ENGS:
                ops = [o for o in self.ops if o["eng"] == eng_name]
                if not ops and eng_name != "vector":
                    continue

                def make(ops, eng_name=eng_name):
                    def body(eng):
                        for op in ops:
                            for sn, v in op["waits"]:
                                eng.wait_ge(sems[sn], v)
                            inst = op["emit"](eng)
                            if op["inc"]:
                                assert inst is not None, op["tag"]
                                inst.then_inc(sems[self._sname(op)],
                                              16 if op["kind"] == "dma" else 1)
                        if eng_name == "vector":
                            ec = self.state["clocks"].setdefault("vector", {})
                            for sn, v in drain:
                                if ec.get(sn, 0) < v:
                                    eng.wait_ge(sems[sn], v)
                                    ec[sn] = v
                    return body
                getattr(blk, eng_name)(make(ops))


def build_nc(NT=8192, race=False, passes=(1, 2, 3), drop=(), debug=False):
    NTL = NT // TT
    NCH = NT // CH

    nc = bass.Bass("TRN2", target_bir_lowering=False,
                   detect_race_conditions=race)
    dt = nc.dram_tensor
    x_d = dt("x", [NT, D], F32, kind="ExternalInput").ap()
    wqkvT_d = dt("wqkvT", [D, 3 * D], F32, kind="ExternalInput").ap()
    projP_d = dt("projP", [H // 2, CH, 2 * FF], F32, kind="ExternalInput").ap()
    projQ_d = dt("projQ", [CH, 2 * H * CH], F32, kind="ExternalInput").ap()
    woutT_d = dt("woutT", [D, D], F32, kind="ExternalInput").ap()
    w1T_d = dt("w1T", [D, DFF], F32, kind="ExternalInput").ap()
    w2T_d = dt("w2T", [DFF, D], F32, kind="ExternalInput").ap()
    fw_d = dt("fw", [D, D], F32, kind="ExternalInput").ap()
    vecs_d = dt("vecs", [CH, 64], F32, kind="ExternalInput").ap()
    b1c_d = dt("b1c", [CH, NMF], F32, kind="ExternalInput").ap()
    iden_d = dt("iden", [CH, CH], BF16, kind="ExternalInput").ap()
    onesr_d = dt("onesr", [1, CH], F32, kind="ExternalInput").ap()
    onesc_d = dt("onescb", [CH, 1], BF16, kind="ExternalInput").ap()
    out_d = dt("out", [NT, D], F32, kind="ExternalOutput").ap()

    x1n_s = dt("x1n_s", [D, NT], BF16, kind="Internal").ap()
    part_s = dt("part_s", [NT, D], F32, kind="Internal").ap()
    w1s_s = dt("w1s_s", [NMF, CH, D], BF16, kind="Internal").ap()
    w2s_s = dt("w2s_s", [DFF, D], BF16, kind="Internal").ap()
    h2row_s = dt("h2row_s", [1, NT], F32, kind="Internal").ap()
    cc_in = dt("cc_in", [1, 1], F32, kind="Internal").ap()
    cc_out = dt("cc_out", [1, 1], F32, kind="Internal", addr_space="Shared").ap()

    dbg = {}
    if debug:
        dbg["scal"] = dt("dbg_scal", [1, 16], F32, kind="ExternalOutput").ap()
        dbg["gsc"] = dt("dbg_gsc", [CH, 8], F32, kind="ExternalOutput").ap()
        dbg["kvl"] = dt("dbg_kvl", [CH, 2 * H * 66], BF16, kind="ExternalOutput").ap()
        dbg["x1n"] = dt("dbg_x1n", [D, NT], BF16, kind="ExternalOutput").ap()
        dbg["part"] = dt("dbg_part", [NT, D], F32, kind="ExternalOutput").ap()
        dbg["h2row"] = dt("dbg_h2row", [1, NT], F32, kind="ExternalOutput").ap()
        dbg["stH"] = dt("dbg_stH", [CH, NT // CH], F32, kind="ExternalOutput").ap()
        dbg["stR"] = dt("dbg_stR", [CH, NT // CH], F32, kind="ExternalOutput").ap()
        dbg["stM"] = dt("dbg_stM", [CH, NT // CH], F32, kind="ExternalOutput").ap()
        dbg["mwq"] = dt("dbg_mwq", [CH, NKC * D], BF16, kind="ExternalOutput").ap()
    state = dict(counts={}, clocks={})
    es = contextlib.ExitStack()
    sems = {}
    for sn in (["e_tensor", "e_vector", "e_scalar", "e_gpsimd", "cc"]
               + ["q_sync%d" % i for i in range(8)]
               + ["q_scalar%d" % i for i in range(6)]
               + ["q_gpsimd%d" % i for i in range(4)]):
        sems[sn] = es.enter_context(nc.semaphore(sn))
    sb = lambda n, s, d_: es.enter_context(nc.sbuf_tensor(n, s, d_))

    iden = sb("idn", [CH, CH], BF16)
    onesr = sb("onr", [1, CH], F32)
    onesc = sb("onc", [CH, 1], BF16)
    vecs = sb("vcs", [CH, 64], F32)
    b1c = sb("b1cs", [CH, NMF], F32)
    statsR = sb("stR", [CH, NCH], F32)
    statsMUR = sb("stM", [CH, NCH], F32)
    statsH2 = sb("stH", [CH, NCH], F32)
    kvl = sb("kvl", [CH, 2 * H * 66], BF16)
    mwq = sb("mwq", [CH, NKC * D], BF16)
    gsc = sb("gsc", [CH, 8], F32)
    lngyc = sb("lgy", [CH, NKC], F32)
    lnbyc = sb("lby", [CH, NKC], F32)
    grow = sb("grw", [1, CH], F32)
    scal = sb("scl", [1, 16], F32)
    gacc = sb("gac", [CH, 1], F32)
    srow = sb("srw", [1, CH], F32)

    B_OUT = vecs[:, 0:8]
    GAM = vecs[:, 8:16]
    BET = vecs[:, 16:24]
    B2 = vecs[:, 24:32]
    LNG = vecs[:, 32:40]
    LNB = vecs[:, 40:48]
    EPS1 = vecs[:, 48:49]
    EPSF = vecs[:, 49:50]
    EPS1r = vecs[0:1, 48:49]

    def dch(ap2d, c, w=CH):
        return ap2d[:, c * w:(c + 1) * w]

    class _Skip(Exception):
        pass

    def _guard(n):
        if n not in passes:
            raise _Skip()

    with contextlib.ExitStack() as p1:
      try:
        _guard(1)
        sb1 = lambda n, s, d_: p1.enter_context(nc.sbuf_tensor(n, s, d_))
        ps1 = lambda n, s, d_: p1.enter_context(nc.psum_tensor(n, s, d_))
        kT = sb1("kT", [CH, NKC * D], BF16)
        vT = sb1("vT", [CH, NKC * D], BF16)
        lngb = sb1("lgb", [CH, D], BF16)
        lnbb = sb1("lbb", [CH, D], BF16)
        kvacc = sb1("kva", [65, H * FF], F32)
        projBD = sb1("pbd", [CH, (H // 2) * 2 * FF], BF16)
        wst = sb1("wst", [CH, DFF], F32)
        wst2 = sb1("ws2", [CH, DFF], BF16)
        rstg = sb1("rsg", [CH, D], F32)
        xin = sb1("xin", [CH, 2 * D], F32)
        xntm = sb1("xnt", [CH, 2 * D], BF16)
        xnfm = sb1("xnf", [CH, NKC * TT], BF16)
        kfm = sb1("kfm", [CH, NKC * TT], BF16)
        vext = sb1("vex", [CH, TCH * H * 66], BF16)
        kftm = sb1("kft", [CH, TCH * H * FF], BF16)
        ebuf = sb1("ebf", [CH, 2 * 2 * FF], BF16)
        tbuf = sb1("tbf", [CH, 2 * 2 * FF], BF16)
        gscr = sb1("gs1", [CH, D], BF16)
        gscr2 = sb1("gs2", [CH, D], BF16)
        sqd = sb1("sqd", [CH, D], F32)
        sred = sb1("srd", [CH, 8], F32)
        stS = sb1("stS", [CH, 16], F32)
        PT = ps1("PT", [CH, 2 * D], BF16)
        PF = [ps1(f"PF{i}", [CH, 512], F32) for i in range(6)]
        P = Plan(state, drop,
                 psum_keys=[f"PF{i}" for i in range(6)] + [("PT", 0), ("PT", 1)])

        for name, dstv, srcv in (("iden", iden, iden_d), ("onesr", onesr, onesr_d),
                                 ("onesc", onesc, onesc_d), ("vecs", vecs, vecs_d),
                                 ("b1c", b1c, b1c_d)):
            P.dma("sync", lambda e, a=dstv, b=srcv: e.dma_start(out=a[:], in_=b),
                  writes=[name], tag="ld_" + name)
        lngbf_d = dt("lngbf", [CH, D], F32, kind="ExternalInput").ap()
        lnbbf_d = dt("lnbbf", [CH, D], F32, kind="ExternalInput").ap()
        P.dma("sync", lambda e: e.dma_start(out=wst[:, 0:D], in_=lngbf_d),
              writes=[("wst", 0)], tag="ld_lngf")
        P.dma("sync", lambda e: e.dma_start(out=wst[:, D:2 * D], in_=lnbbf_d),
              writes=[("wst", 1)], tag="ld_lnbf")
        P.add("vector", lambda e: e.tensor_copy(lngb[:], wst[:, 0:D]),
              reads=[("wst", 0)], writes=["lngb"], tag="c_lngb")
        P.add("vector", lambda e: e.tensor_copy(lnbb[:], wst[:, D:2 * D]),
              reads=[("wst", 1)], writes=["lnbb"], tag="c_lnbb")
        P.add("vector", lambda e: e.memset(gacc[:], 0.0), writes=["gacc"], tag="ms_g")
        P.add("vector", lambda e: e.memset(kvl[:], 0.0), writes=["kvl"], tag="ms_kvl")
        P.add("vector", lambda e: e.memset(vext[:], 1.0),
              writes=[("vext", c, vg) for c in range(TCH) for vg in range(2)],
              tag="ms_vx")

        def ldw(dkey, dsl, src, eng="vector"):
            P.dma("sync", lambda e, src=src: e.dma_start(out=rstg[:, 0:src.shape[1]], in_=src),
                  writes=["rstg"], tag="ld" + str(dkey))
            P.add(eng, lambda e, dsl=dsl, w=src.shape[1]: e.tensor_copy(dsl, rstg[:, 0:w]),
                  reads=["rstg"], writes=[dkey], tag="cast" + str(dkey))

        for kc in range(NKC):
            ldw(("kT", kc), dch(kT, kc, D), wqkvT_d[kc * CH:(kc + 1) * CH, D:2 * D])
        for kc in range(NKC):
            ldw(("vT", kc), dch(vT, kc, D), wqkvT_d[kc * CH:(kc + 1) * CH, 2 * D:3 * D])
        for pr in range(H // 2):
            P.dma("sync", lambda e, pr=pr: e.dma_start(
                out=wst[:, 0:2 * FF], in_=projP_d[pr]),
                writes=[("wst", 0)], tag=f"ldpj_{pr}")
            P.add("vector", lambda e, pr=pr: e.tensor_copy(
                dch(projBD, pr, 2 * FF), wst[:, 0:2 * FF]),
                reads=[("wst", 0)], writes=[("projBD", pr)], tag=f"cpj_{pr}")

        def emit_prep():
            fw_r = fw_d.rearrange("(a p) u -> p a u", p=CH)
            for hf in range(2):
                P.dma("scalar", lambda e, hf=hf: e.dma_start(
                    out=wst[:].rearrange("p (a u) -> p a u", u=D),
                    in_=fw_r[:, hf * 4:(hf + 1) * 4, :]),
                    writes=[("wst", q) for q in range(4)], tag=f"ld_fw{hf}")
                P.add("vector", lambda e, hf=hf: e.reduce_sum(
                    sred[:, hf:hf + 1], wst[:], axis=AX, apply_absolute_value=True),
                    reads=[("wst", q) for q in range(4)], writes=[("sred", hf)],
                    tag=f"asum{hf}", late=True)
            P.add("vector", lambda e: e.tensor_add(sred[:, 2:3], sred[:, 0:1], sred[:, 1:2]),
                  reads=[("sred", 0), ("sred", 1)], writes=[("sred", 2)], tag="asum2")
            P.dma("gpsimd", lambda e: e.dma_start(out=srow[:], in_=sred[:, 2:3], allow_slow_non_contiguous=True),
                  reads=[("sred", 2)], writes=["srow"], tag="fold_b")
            P.add("vector", lambda e: e.reduce_sum(scal[:, 0:1], srow[:], axis=AX),
                  reads=["srow"], writes=[("scal", 0)], tag="bsum", late=True)
            P.add("vector", lambda e: e.tensor_scalar(
                scal[:, 1:2], scal[:, 0:1], 0.5 / (D * D), 0.5 * EPS_FFN, ALU.mult, ALU.add),
                reads=[("scal", 0)], writes=[("scal", 1)], tag="hb")
            P.add("vector", lambda e: e.tensor_scalar_mul(scal[:, 2:3], scal[:, 1:2], -1.0),
                  reads=[("scal", 1)], writes=[("scal", 2)], tag="nhb")
            P.add("tensor", lambda e: e.matmul(PF[0][:, 0:2], onesr[:], scal[:, 1:3],
                                               start=True, stop=True),
                  reads=[("scal", 1), ("scal", 2), "onesr"], writes=["PF0"], tag="bc_hb")
            P.add("vector", lambda e: e.tensor_copy(gsc[:, 2:4], PF[0][:, 0:2]),
                  reads=["PF0"], writes=[("gsc", 2)], tag="ev_hb")
            for hf in range(2):
                P.dma("scalar", lambda e, hf=hf: e.dma_start(
                    out=wst[:].rearrange("p (a u) -> p a u", u=D),
                    in_=fw_r[:, hf * 4:(hf + 1) * 4, :]),
                    writes=[("wst", q) for q in range(4)], tag=f"ld_fwb{hf}")
                P.add("vector", lambda e: e.tensor_single_scalar(
                    wst2[:], wst[:], gsc[:, 2:3], ALU.is_gt),
                    reads=[("wst", q) for q in range(4)] + [("gsc", 2)],
                    writes=[("wst2", q) for q in range(4)], tag=f"gt{hf}")
                P.add("vector", lambda e, hf=hf: e.scalar_tensor_tensor(
                    mwq[:, hf * DFF:(hf + 1) * DFF], wst[:], gsc[:, 3:4], wst2[:],
                    ALU.is_lt, ALU.subtract),
                    reads=[("wst", q) for q in range(4)]
                    + [("wst2", q) for q in range(4)] + [("gsc", 2)],
                    writes=[("mwq", hf)], tag=f"mwq{hf}")

            for kc in range(NMF):
                q = kc % 4
                ssl = wst[:, q * D:(q + 1) * D]
                s2 = wst2[:, q * D:(q + 1) * D]
                P.dma("scalar", lambda e, kc=kc, ssl=ssl: e.dma_start(
                    out=ssl, in_=w2T_d[kc * CH:(kc + 1) * CH, :]),
                    writes=[("wst", q)], tag=f"ldw2_{kc}")
                P.add("gpsimd", lambda e, ssl=ssl, s2=s2: e.tensor_copy(s2, ssl),
                      reads=[("wst", q)], writes=[("wst2", q)], tag=f"cw2_{kc}")
                P.dma("gpsimd", lambda e, kc=kc, s2=s2: e.dma_start(
                    out=w2s_s[kc * CH:(kc + 1) * CH, :], in_=s2),
                    reads=[("wst2", q)], writes=["w2s"], tag=f"sw2_{kc}")
            for kc in range(NKC):
                P.dma("scalar", lambda e, kc=kc: e.dma_start(
                    out=wst[:], in_=w1T_d[kc * CH:(kc + 1) * CH, :]),
                    writes=[("wst", q) for q in range(4)], tag=f"ldw1_{kc}")
                P.add("gpsimd", lambda e: e.tensor_copy(wst2[:], wst[:]),
                      reads=[("wst", q) for q in range(4)],
                      writes=[("wst2", q) for q in range(4)], tag=f"cw1_{kc}")
                P.dma("gpsimd", lambda e, kc=kc: e.dma_start(
                    out=w1s_s[:, :, kc * CH:(kc + 1) * CH].rearrange("m p c -> p m c"),
                    in_=wst2[:].rearrange("p (m c) -> p m c", c=CH)),
                    reads=[("wst2", q) for q in range(4)], writes=["w1s"], tag=f"sw1_{kc}")


        for t in range(NTL):
            if t == min(2, NTL - 1):
                emit_prep()
            tc0 = t * TCH
            for c in range(TCH):
                cc = t * TCH + c
                pk = c % 2
                xs = dch(xin, pk, D)
                P.dma("sync", lambda e, cc=cc, xs=xs: e.dma_start(
                    out=xs, in_=x_d[cc * CH:(cc + 1) * CH, :]),
                    writes=[("xin", pk)], tag=f"ldx_{cc}")
                P.add("vector", lambda e, xs=xs, c=c: e.reduce_sum(
                    sred[:, c:c + 1], xs, axis=AX),
                    reads=[("xin", pk)], writes=[("sred", c)], tag=f"sum_{cc}",
                    late=True)
                P.add("vector", lambda e, xs=xs, c=c: e.scalar_tensor_tensor(
                    sqd[:], xs, 1.0, xs, ALU.mult, ALU.mult,
                    accum_out=sred[:, 4 + c:5 + c]),
                    reads=[("xin", pk)], writes=[("sred", 4 + c), "sqd"],
                    tag=f"sq_{cc}", late=True)
            SR = lambda a, b: stS[:, a:b]
            P.add("vector", lambda e: e.tensor_scalar_mul(
                SR(0, 4), sred[:, 0:4], 1.0 / D),
                reads=[("sred", c2) for c2 in range(4)], writes=[("stS", 0)],
                tag=f"bmu_{t}")
            P.add("vector", lambda e: e.tensor_scalar_mul(
                SR(4, 8), sred[:, 4:8], 1.0 / D),
                reads=[("sred", 4 + c2) for c2 in range(4)], writes=[("stS", 1)],
                tag=f"be2_{t}")
            P.add("vector", lambda e: e.tensor_mul(SR(8, 12), SR(0, 4), SR(0, 4)),
                  reads=[("stS", 0)], writes=[("stS", 2)], tag=f"bm2_{t}")
            P.add("vector", lambda e: e.tensor_sub(SR(12, 16), SR(4, 8), SR(8, 12)),
                  reads=[("stS", 1), ("stS", 2)], writes=[("stS", 3)], tag=f"bvr_{t}")
            P.add("scalar", lambda e: e.activation(
                SR(8, 12), SR(12, 16), AF.Ln, bias=EPS1),
                reads=[("stS", 3)], writes=[("stS", 2)], tag=f"bln_{t}")
            P.add("scalar", lambda e, tc0=tc0: e.activation(
                statsR[:, tc0:tc0 + 4], SR(8, 12), AF.Exp, scale=-0.5),
                reads=[("stS", 2)],
                writes=[("stR", tc0 + c2) for c2 in range(4)], tag=f"br_{t}")
            P.add("vector", lambda e, tc0=tc0: e.scalar_tensor_tensor(
                statsMUR[:, tc0:tc0 + 4], SR(0, 4), -1.0, statsR[:, tc0:tc0 + 4],
                ALU.mult, ALU.mult),
                reads=[("stS", 0)] + [("stR", tc0 + c2) for c2 in range(4)],
                writes=[("stM", tc0 + c2) for c2 in range(4)], tag=f"bmr_{t}")
            P.add("vector", lambda e, tc0=tc0: e.tensor_mul(
                SR(0, 4), statsR[:, tc0:tc0 + 4], statsR[:, tc0:tc0 + 4]),
                reads=[("stR", tc0 + c2) for c2 in range(4)], writes=[("stS", 0)],
                tag=f"brr_{t}")
            P.add("vector", lambda e: e.tensor_mul(SR(4, 8), SR(0, 4), SR(12, 16)),
                  reads=[("stS", 0), ("stS", 3)], writes=[("stS", 1)], tag=f"bv2_{t}")
            P.add("scalar", lambda e: e.activation(
                SR(8, 12), SR(4, 8), AF.Ln, bias=EPSF),
                reads=[("stS", 1)], writes=[("stS", 2)], tag=f"bl2_{t}")
            P.add("scalar", lambda e, tc0=tc0: e.activation(
                statsH2[:, tc0:tc0 + 4], SR(8, 12), AF.Exp, scale=-0.5),
                reads=[("stS", 2)],
                writes=[("stH", tc0 + c2) for c2 in range(4)], tag=f"bh2_{t}")
            for c in range(TCH):
                cc = t * TCH + c
                pk = c % 2
                xs = dch(xin, pk, D)
                P.dma("sync", lambda e, cc=cc, xs=xs: e.dma_start(
                    out=xs, in_=x_d[cc * CH:(cc + 1) * CH, :]),
                    writes=[("xin", pk)], tag=f"ldx2_{cc}")
                P.add("scalar", lambda e, xs=xs, pk=pk, cc=cc: e.activation(
                    dch(xntm, pk, D), xs, AF.Identity,
                    bias=statsMUR[:, cc:cc + 1], scale=statsR[:, cc:cc + 1]),
                    reads=[("xin", pk), ("stM", cc), ("stR", cc)],
                    writes=[("xntm", pk)], tag=f"xn_{cc}")
                P.add("vector", lambda e, pk=pk, cc=cc: e.scalar_tensor_tensor(
                    gscr[:], dch(xntm, pk, D), statsH2[:, cc:cc + 1], lngb[:],
                    ALU.mult, ALU.mult),
                    reads=[("xntm", pk), ("stH", cc), "lngb"], writes=["gscr"],
                    tag=f"g1_{cc}")
                P.add("gpsimd", lambda e: e.tensor_add(gscr2[:], gscr[:], lnbb[:]),
                      reads=["gscr", "lnbb"], writes=["gscr2"], tag=f"g2_{cc}")
                P.add("vector", lambda e: e.reduce_max(
                    sred[:, 0:1], gscr2[:], axis=AX, apply_absolute_value=True),
                    reads=["gscr2"], writes=[("sred", 0)], tag=f"g3_{cc}",
                    late=True)
                P.add("vector", lambda e: e.tensor_max(gacc[:], gacc[:], sred[:, 0:1]),
                      reads=[("sred", 0), "gacc"], writes=["gacc"], tag=f"g4_{cc}")
                for kc in range(NKC):
                    P.add("tensor", lambda e, kc=kc, pk=pk: e.transpose(
                        PT[:, pk * D + kc * CH:pk * D + (kc + 1) * CH],
                        xntm[:, pk * D + kc * CH:pk * D + (kc + 1) * CH], iden[:]),
                        reads=[("xntm", pk), "iden"], writes=[("PT", pk)],
                        tag=f"xT_{cc}_{kc}")
                P.add("vector", lambda e, c=c, pk=pk: e.tensor_copy(
                    xnfm[:].rearrange("p (k tt) -> p k tt", tt=TT)
                    [:, :, c * CH:(c + 1) * CH],
                    PT[:, pk * D:(pk + 1) * D].rearrange("p (k w) -> p k w", w=CH)),
                    reads=[("PT", pk)], writes=[("xnfm", c)], tag=f"xTe_{cc}")
            P.dma("gpsimd", lambda e, t=t: e.dma_start(
                out=h2row_s[:, t * TT:(t + 1) * TT].rearrange(
                    "o (c p) -> (o p) c", p=CH),
                in_=statsH2[:, t * TCH:(t + 1) * TCH],
                allow_slow_non_contiguous=True),
                reads=[("stH", t * TCH + c2) for c2 in range(TCH)],
                writes=["h2row"], tag=f"h2r_{t}")

            for mc in range(NKC):
                bank = mc % 2
                def mm_k(e, mc=mc, bank=bank):
                    last = None
                    for kc in range(NKC):
                        last = e.matmul(PF[bank][:],
                                        kT[:, kc * D + mc * CH:kc * D + (mc + 1) * CH],
                                        dch(xnfm, kc, TT),
                                        start=(kc == 0), stop=(kc == NKC - 1))
                    return last
                P.add("tensor", mm_k,
                      reads=[("kT", kc2) for kc2 in range(NKC)]
                      + [("xnfm", c2) for c2 in range(TCH)],
                      writes=[f"PF{bank}"], tag=f"k_{t}_{mc}")
                P.add("scalar", lambda e, mc=mc, bank=bank: e.activation(
                    dch(kfm, mc, TT), PF[bank][:], AF.Copy),
                    reads=[f"PF{bank}"], writes=[("kfm", mc)], tag=f"ke_{t}_{mc}")

            for c in range(TCH):
                for vg in range(2):
                    bank = 2 + vg
                    def mm_v(e, c=c, vg=vg, bank=bank):
                        last = None
                        for kc in range(NKC):
                            last = e.matmul(
                                PF[bank][:],
                                xnfm[:, kc * TT + c * CH:kc * TT + (c + 1) * CH],
                                vT[:, kc * D + vg * 512:kc * D + (vg + 1) * 512],
                                start=(kc == 0), stop=(kc == NKC - 1))
                        return last
                    P.add("tensor", mm_v,
                          reads=[("vT", kc2) for kc2 in range(NKC)] + [("xnfm", c)],
                          writes=[f"PF{bank}"], tag=f"v_{t}_{c}_{vg}")
                    def ev_v(e, c=c, vg=vg, bank=bank):
                        dst = vext[:].rearrange("p (cc h s) -> p cc h s", h=H, s=66)[
                            :, c, vg * 8:(vg + 1) * 8, 0:64]
                        return e.tensor_copy(dst, PF[bank][:].rearrange(
                            "p (h w) -> p h w", w=64))
                    P.add("vector", ev_v, reads=[f"PF{bank}"],
                          writes=[("vext", c, vg)], tag=f"ve_{t}_{c}_{vg}")
            for c in range(TCH):
                for pr in range(H // 2):
                    bank = (c * (H // 2) + pr) % 4
                    pkk = pr % 2
                    P.add("tensor", lambda e, c=c, pr=pr, bank=bank: e.matmul(
                        PF[bank][:],
                        kfm[:, pr * TT + c * CH:pr * TT + (c + 1) * CH],
                        dch(projBD, pr, 2 * FF),
                        start=True, stop=True),
                        reads=[("kfm", pr), ("projBD", pr)],
                        writes=[f"PF{bank}"], tag=f"kf_{t}_{c}_{pr}")
                    eb = dch(ebuf, pkk, 2 * FF)
                    tb = dch(tbuf, pkk, 2 * FF)
                    P.add("scalar", lambda e, bank=bank, eb=eb: e.activation(
                        eb, PF[bank][:], AF.Exp),
                        reads=[f"PF{bank}"], writes=[("eb", pkk)],
                        tag=f"kfE_{t}_{c}_{pr}")
                    P.add("vector", lambda e, bank=bank, tb=tb: e.tensor_scalar(
                        tb, PF[bank][:], 1.0, 1.0, ALU.add, ALU.max),
                        reads=[f"PF{bank}"], writes=[("tb", pkk)],
                        tag=f"kfT_{t}_{c}_{pr}")
                    P.add("vector", lambda e, c=c, pr=pr, eb=eb, tb=tb: e.tensor_tensor(
                        kftm[:, (c * H + pr * 2) * FF:(c * H + pr * 2 + 2) * FF],
                        eb, tb, ALU.min),
                        reads=[("eb", pkk), ("tb", pkk)],
                        writes=[("kftm", c, pr)], tag=f"kfM_{t}_{c}_{pr}")
            for pr in range(H // 2):
                bank = 4 + (pr % 2)
                def mm_kv(e, pr=pr, bank=bank):
                    last = None
                    for c in range(TCH):
                        for hh in range(2):
                            h0 = pr * 2 + hh
                            last = e.matmul(
                                PF[bank][0:65, hh * FF:(hh + 1) * FF],
                                vext[:, (c * H + h0) * 66:(c * H + h0) * 66 + 65],
                                kftm[:, (c * H + h0) * FF:(c * H + h0 + 1) * FF],
                                start=(c == 0 and hh == 0),
                                stop=(c == TCH - 1 and hh == 1))
                    return last
                P.add("tensor", mm_kv,
                      reads=[("vext", c2, vg) for c2 in range(TCH) for vg in range(2)]
                      + [("kftm", c2, pr) for c2 in range(TCH)],
                      writes=[f"PF{bank}"], tag=f"kv_{t}_{pr}")
                if t == 0:
                    P.add("vector", lambda e, pr=pr, bank=bank: e.tensor_copy(
                        kvacc[:, pr * 2 * FF:(pr * 2 + 2) * FF], PF[bank][0:65, :]),
                        reads=[f"PF{bank}"], writes=[("kva", pr)], tag=f"kve_{t}_{pr}")
                else:
                    P.add("vector", lambda e, pr=pr, bank=bank: e.tensor_add(
                        kvacc[:, pr * 2 * FF:(pr * 2 + 2) * FF],
                        kvacc[:, pr * 2 * FF:(pr * 2 + 2) * FF], PF[bank][0:65, :]),
                        reads=[f"PF{bank}", ("kva", pr)], writes=[("kva", pr)],
                        tag=f"kve_{t}_{pr}")

        def ptoff(s):
            return (s % 2) * 1024 + (s // 2) * 66
        P.add("vector", lambda e: e.tensor_copy(wst2[0:65, 0:H * FF], kvacc[:]),
              reads=[("kva", pr) for pr in range(H // 2)],
              writes=[("wst2", q) for q in range(4)], tag="kv_cast")
        for h0 in range(H):
            for fc in range(2):
                sl = h0 * 2 + fc
                hf = sl % 2
                P.add("tensor", lambda e, h0=h0, fc=fc, sl=sl: e.transpose(
                    PT[:, ptoff(sl % 16):ptoff(sl % 16) + 65],
                    wst2[0:65, h0 * FF + fc * CH:h0 * FF + (fc + 1) * CH],
                    iden[0:65, 0:65]),
                    reads=[("wst2", q) for q in range(4)] + ["iden"],
                    writes=[("PT", hf)], tag=f"kvT_{h0}_{fc}")
                P.add("vector", lambda e, sl=sl: e.tensor_copy(
                    kvl[:, sl * 66:sl * 66 + 65],
                    PT[:, ptoff(sl % 16):ptoff(sl % 16) + 65]),
                    reads=[("PT", hf)], writes=["kvl"], tag=f"kvle_{sl}")
        build_nc._stash = dict(sred=sred, wst=wst)
        P.emit(nc, sems)
      except _Skip:
        pass

    with contextlib.ExitStack() as p2:
      try:
        _guard(2)
        sb2 = lambda n, s, d_: p2.enter_context(nc.sbuf_tensor(n, s, d_))
        ps2 = lambda n, s, d_: p2.enter_context(nc.psum_tensor(n, s, d_))
        qT = sb2("qT", [CH, NKC * D], BF16)
        projQ = sb2("pjq", [CH, 2 * H * CH], BF16)
        woT = sb2("woT", [CH, NKC * D], BF16)
        wst = sb2("wsb", [CH, 2 * D], F32)
        xin = sb2("xnb", [CH, 2 * D], F32)
        xpin = xin
        xntm = sb2("xtb", [CH, 2 * D], BF16)
        xnfm = sb2("xfb", [CH, 2 * NKC * TT], BF16)
        qfm = sb2("qfm", [CH, NKC * TT], BF16)
        shA = qfm
        qf = sb2("qf", [CH, 2 * H * TT], BF16)
        ebf2 = sb2("eb2", [CH, 2 * TT], BF16)
        tbf2 = sb2("tb2", [CH, 2 * TT], BF16)
        attnr = sb2("atr", [CH, 2 * D], BF16)
        z16 = sb2("z16", [CH, 16], F32)
        zr16 = sb2("zrs", [CH, 16], F32)
        attnfm = sb2("atf", [CH, NKC * TT], BF16)
        x1b = sb2("x1b", [CH, NKC * TT], BF16)
        x1sq = shA
        x1n = sb2("x1m", [CH, NKC * TT], BF16)
        rows = sb2("rws", [1, 4 * TT], F32)
        bc1 = sb2("bc1", [CH, TT], F32)
        bc2 = sb2("bc2", [CH, TT], F32)
        xq1 = sb2("xq1", [CH, TT], F32)
        xq2 = sb2("xq2", [CH, TT], F32)
        xqb = sb2("xqb", [CH, NKC * TT], BF16)
        ybf = shA
        xpart = sb2("xpt", [CH, 2 * D], F32)
        h2r = sb2("h2r", [1, TT], F32)
        PT2 = ps2("PT2", [CH, 2 * D], BF16)
        PT2b = ps2("PT2b", [CH, D], BF16)
        PF = [ps2(f"QF{i}", [CH, 512], F32) for i in range(5)]
        P = Plan(state, drop,
                 psum_keys=[f"QF{i}" for i in range(5)]
                 + [("PT2", 0), ("PT2", 1), "PT2b"])

        def row(i):
            return rows[:, i * TT:(i + 1) * TT]

        for pq in range(4):
            q = pq % 2
            ssl = wst[:, q * D:(q + 1) * D]
            P.dma("sync", lambda e, pq=pq, ssl=ssl: e.dma_start(
                out=ssl, in_=projQ_d[:, pq * D:(pq + 1) * D]),
                writes=[("wst", q)], tag=f"ldpq_{pq}")
            P.add("vector", lambda e, pq=pq, ssl=ssl: e.tensor_copy(
                projQ[:, pq * D:(pq + 1) * D], ssl),
                reads=[("wst", q)], writes=["projQ"], tag=f"cpq_{pq}")
        for kc in range(NKC):
            q = kc % 2
            ssl = wst[:, q * D:(q + 1) * D]
            P.dma("sync", lambda e, kc=kc, ssl=ssl: e.dma_start(
                out=ssl, in_=wqkvT_d[kc * CH:(kc + 1) * CH, 0:D]),
                writes=[("wst", q)], tag=f"ldq_{kc}")
            P.add("vector", lambda e, kc=kc, ssl=ssl: e.tensor_copy(
                dch(qT, kc, D), ssl),
                reads=[("wst", q)], writes=[("qT", kc)], tag=f"cq_{kc}")
        for kc in range(NKC):
            q = kc % 2
            ssl = wst[:, q * D:(q + 1) * D]
            P.dma("sync", lambda e, kc=kc, ssl=ssl: e.dma_start(
                out=ssl, in_=woutT_d[kc * CH:(kc + 1) * CH, :]),
                writes=[("wst", q)], tag=f"ldo_{kc}")
            P.add("vector", lambda e, kc=kc, ssl=ssl: e.tensor_copy(
                dch(woT, kc, D), ssl),
                reads=[("wst", q)], writes=[("woT", kc)], tag=f"co_{kc}")

        P.dma("gpsimd", lambda e: e.dma_start(out=srow[:], in_=gacc[:], allow_slow_non_contiguous=True),
              reads=["gacc"], writes=["srow"], tag="fold_g")
        P.add("vector", lambda e: e.reduce_max(scal[:, 3:4], srow[:], axis=AX),
              reads=["srow"], writes=[("scal", 3)], tag="gloc", late=True)
        P.dma("gpsimd", lambda e: e.dma_start(out=cc_in, in_=scal[:, 3:4]),
              reads=[("scal", 3)], writes=["cc_in"], tag="st_cc")
        P.add("gpsimd", lambda e: e.collective_compute(
            "AllReduce", ALU.max, replica_groups=[list(range(NC))],
            ins=[cc_in], outs=[cc_out]),
            reads=["cc_in"], writes=["cc_out"], kind="cc", tag="ar_gamma")
        P.dma("gpsimd", lambda e: e.dma_start(out=scal[:, 4:5], in_=cc_out),
              reads=["cc_out"], writes=[("scal", 4)], tag="ld_cc")
        P.add("vector", lambda e: e.reciprocal(scal[:, 5:6], scal[:, 4:5]),
              reads=[("scal", 4)], writes=[("scal", 5)], tag="rg0", late=True)
        P.add("vector", lambda e: e.tensor_scalar_mul(scal[:, 6:7], scal[:, 5:6], QB),
              reads=[("scal", 5)], writes=[("scal", 6)], tag="rgam")
        P.add("vector", lambda e: e.tensor_mul(scal[:, 7:8], scal[:, 4:5], scal[:, 0:1]),
              reads=[("scal", 4), ("scal", 0)], writes=[("scal", 7)], tag="gb")
        P.add("vector", lambda e: e.tensor_scalar_mul(
            scal[:, 8:9], scal[:, 7:8], -1.0 / (QB * D * D)),
            reads=[("scal", 7)], writes=[("scal", 8)], tag="nys")
        P.add("tensor", lambda e: e.matmul(PF[0][:, 0:4], onesr[:], scal[:, 5:9],
                                           start=True, stop=True),
              reads=[("scal", 6), ("scal", 8), "onesr"], writes=["QF0"], tag="bc_g")
        P.add("vector", lambda e: e.tensor_copy(gsc[:, 0:1], PF[0][:, 1:2]),
              reads=["QF0"], writes=[("gsc", 0)], tag="ev_rg")
        P.add("vector", lambda e: e.tensor_copy(gsc[:, 1:2], PF[0][:, 3:4]),
              reads=["QF0"], writes=[("gsc", 1)], tag="ev_ny")
        P.add("vector", lambda e: e.tensor_single_scalar(
            grow[:], onesr[:], scal[:, 6:7], ALU.mult),
            reads=[("scal", 6), "onesr"], writes=["grow"], tag="grow")
        P.add("vector", lambda e: e.tensor_copy(lngyc[:], LNG),
              reads=["vecs"], writes=["lngyc"], tag="lngyc")
        P.add("vector", lambda e: e.tensor_single_scalar(
            lnbyc[:], LNB, gsc[:, 0:1], ALU.mult),
            reads=["vecs", ("gsc", 0)], writes=["lnbyc"], tag="lnbyc")

        def ln1_tail(t):
                def mm_s1(e):
                    last = None
                    for kc in range(NKC):
                        last = e.matmul(PF[4][0:1, :], onesc[:], dch(x1b, kc, TT),
                                        start=(kc == 0), stop=(kc == NKC - 1))
                    return last
                P.add("tensor", mm_s1,
                      reads=[("x1b", kc2) for kc2 in range(NKC)],
                      writes=["QF4"], tag=f"s1_{t}")
                P.add("vector", lambda e: e.tensor_scalar_mul(
                    row(0), PF[4][0:1, :], 1.0 / D),
                    reads=["QF4"], writes=[("rw", 0)], tag=f"s1e_{t}")
                def mm_s2(e):
                    last = None
                    for kc in range(NKC):
                        last = e.matmul(PF[4][0:1, :], onesc[:], dch(x1sq, kc, TT),
                                        start=(kc == 0), stop=(kc == NKC - 1))
                    return last
                P.add("tensor", mm_s2,
                      reads=[("shA", kc2) for kc2 in range(NKC)],
                      writes=["QF4"], tag=f"s2_{t}")
                P.add("vector", lambda e: e.tensor_scalar_mul(
                    row(1), PF[4][0:1, :], 1.0 / D),
                    reads=["QF4"], writes=[("rw", 1)], tag=f"s2e_{t}")
                P.add("vector", lambda e: e.tensor_mul(row(3), row(0), row(0)),
                      reads=[("rw", 0)], writes=[("rw", 3)], tag=f"v2_{t}")
                P.add("vector", lambda e: e.tensor_sub(row(2), row(1), row(3)),
                      reads=[("rw", 1), ("rw", 3)], writes=[("rw", 2)], tag=f"v3_{t}")
                P.add("scalar", lambda e: e.activation(
                    row(3), row(2), AF.Ln, bias=EPS1r),
                    reads=[("rw", 2)], writes=[("rw", 3)], tag=f"v4_{t}")
                P.add("scalar", lambda e: e.activation(
                    row(2), row(3), AF.Exp, scale=-0.5),
                    reads=[("rw", 3)], writes=[("rw", 2)], tag=f"r1_{t}")
                P.add("vector", lambda e: e.scalar_tensor_tensor(
                    row(3), row(0), -1.0, row(2), ALU.mult, ALU.mult),
                    reads=[("rw", 0), ("rw", 2)], writes=[("rw", 3)], tag=f"q1_{t}")
                P.add("tensor", lambda e: e.matmul(PF[4][:], onesr[:], row(2),
                                                   start=True, stop=True),
                      reads=[("rw", 2)], writes=["QF4"], tag=f"b1_{t}")
                P.add("vector", lambda e: e.tensor_copy(bc1[:], PF[4][:]),
                      reads=["QF4"], writes=["bc1"], tag=f"b1e_{t}")
                P.add("tensor", lambda e: e.matmul(PF[4][:], onesr[:], row(3),
                                                   start=True, stop=True),
                      reads=[("rw", 3)], writes=["QF4"], tag=f"b2_{t}")
                P.add("vector", lambda e: e.tensor_copy(bc2[:], PF[4][:]),
                      reads=["QF4"], writes=["bc2"], tag=f"b2e_{t}")
                for mc in range(NKC):
                    P.add("gpsimd", lambda e, mc=mc: e.tensor_mul(
                        xq2[:], dch(x1b, mc, TT), bc1[:]),
                        reads=[("x1b", mc), "bc1"], writes=["xq2"], tag=f"n1_{t}_{mc}")
                    P.add("gpsimd", lambda e: e.tensor_add(xq2[:], xq2[:], bc2[:]),
                          reads=["xq2", "bc2"], writes=["xq2"], tag=f"n2_{t}_{mc}")
                    P.add("gpsimd", lambda e, mc=mc: e.tensor_scalar(
                        dch(x1n, mc, TT), xq2[:], GAM[:, mc:mc + 1], BET[:, mc:mc + 1],
                        ALU.mult, ALU.add),
                        reads=["xq2"], writes=[("x1n", mc)], tag=f"n3_{t}_{mc}")
                    P.dma("gpsimd", lambda e, mc=mc, t=t: e.dma_start(
                        out=x1n_s[mc * CH:(mc + 1) * CH, t * TT:(t + 1) * TT],
                        in_=dch(x1n, mc, TT)),
                        reads=[("x1n", mc)], writes=[("x1ns", t, mc)], tag=f"nS_{t}_{mc}")

        for t in range(NTL):
            xo = (t % 2) * NKC * TT
            xnk = t % 2

            def xch(kc, xo=None, t=t):
                o = (t % 2) * NKC * TT
                return xnfm[:, o + kc * TT:o + (kc + 1) * TT]
            for c in range(TCH):
                cc = t * TCH + c
                pk = c % 2
                xs = dch(xin, pk, D)
                P.dma("sync", lambda e, cc=cc, xs=xs: e.dma_start(
                    out=xs, in_=x_d[cc * CH:(cc + 1) * CH, :]),
                    writes=[("xin", pk)], tag=f"p2x_{cc}")
                P.add("scalar", lambda e, xs=xs, pk=pk, cc=cc: e.activation(
                    dch(xntm, pk, D), xs, AF.Identity,
                    bias=statsMUR[:, cc:cc + 1], scale=statsR[:, cc:cc + 1]),
                    reads=[("xin", pk)], writes=[("xntm", pk)], tag=f"p2n_{cc}")
                for kc in range(NKC):
                    P.add("tensor", lambda e, kc=kc, pk=pk: e.transpose(
                        PT2b[:, kc * CH:(kc + 1) * CH],
                        xntm[:, pk * D + kc * CH:pk * D + (kc + 1) * CH], iden[:]),
                        reads=[("xntm", pk)], writes=["PT2b"],
                        tag=f"p2T_{cc}_{kc}")
                P.add("vector", lambda e, c=c, pk=pk, xo=xo: e.tensor_copy(
                    xnfm[:, xo:xo + NKC * TT].rearrange("p (k tt) -> p k tt", tt=TT)
                    [:, :, c * CH:(c + 1) * CH],
                    PT2b[:].rearrange("p (k w) -> p k w", w=CH)),
                    reads=["PT2b"], writes=[("xnfm", xnk, c)], tag=f"p2e_{cc}")
            if t > 0:
                ln1_tail(t - 1)
            P.dma("sync", lambda e, t=t: e.dma_start(
                out=h2r[:], in_=h2row_s[:, t * TT:(t + 1) * TT]),
                writes=["h2r"], tag=f"fh_{t}")
            P.add("tensor", lambda e: e.matmul(PF[4][:], grow[:], h2r[:],
                                               start=True, stop=True),
                  reads=["h2r"], writes=["QF4"], tag=f"fb_{t}")
            for mc in range(NKC):
                P.add("vector", lambda e, mc=mc, xch=xch: e.scalar_tensor_tensor(
                    xq1[:], xch(mc), lngyc[:, mc:mc + 1], PF[4][:],
                    ALU.mult, ALU.mult),
                    reads=[("xnfm", xnk, c2) for c2 in range(TCH)] + ["QF4"],
                    writes=["xq1"], tag=f"f1_{t}_{mc}")
                P.add("gpsimd", lambda e, mc=mc: e.tensor_scalar(
                    xq1[:], xq1[:], lnbyc[:, mc:mc + 1], CLIP, ALU.add, ALU.min),
                    reads=["xq1"], writes=["xq1"], tag=f"f2_{t}_{mc}")
                P.add("gpsimd", lambda e, mc=mc: e.tensor_scalar_max(
                    dch(xqb, mc, TT), xq1[:], -CLIP),
                    reads=["xq1"], writes=[("xqb", mc)], tag=f"f3_{t}_{mc}")
            for mc in range(NKC):
                bank = mc % 2
                def mm_q(e, mc=mc, bank=bank, xch=xch):
                    last = None
                    for kc in range(NKC):
                        last = e.matmul(PF[bank][:],
                                        qT[:, kc * D + mc * CH:kc * D + (mc + 1) * CH],
                                        xch(kc),
                                        start=(kc == 0), stop=(kc == NKC - 1))
                    return last
                P.add("tensor", mm_q,
                      reads=[("qT", kc2) for kc2 in range(NKC)]
                      + [("xnfm", xnk, c2) for c2 in range(TCH)],
                      writes=[f"QF{bank}"], tag=f"q_{t}_{mc}")
                P.add("scalar", lambda e, mc=mc, bank=bank: e.activation(
                    dch(qfm, mc, TT), PF[bank][:], AF.Copy),
                    reads=[f"QF{bank}"], writes=[("shA", mc)], tag=f"qe_{t}_{mc}")
            for fc in range(2):
                for h0 in range(H):
                    pr, hh = h0 // 2, h0 % 2
                    base = hh * 64
                    sl = h0 * 2 + fc
                    bank = (fc * H + h0) % 4
                    P.add("tensor",
                          lambda e, pr=pr, sl=sl, bank=bank: e.matmul(
                              PF[bank][:],
                              dch(projQ, sl, CH),
                              dch(qfm, pr, TT),
                              start=True, stop=True),
                          reads=[("shA", pr), "projQ"], writes=[f"QF{bank}"],
                          tag=f"qf_{t}_{h0}_{fc}")
                    pe2 = sl % 2
                    P.add("scalar", lambda e, bank=bank, pe2=pe2: e.activation(
                        dch(ebf2, pe2, TT), PF[bank][:], AF.Exp),
                        reads=[f"QF{bank}"], writes=[("ebf2", pe2)],
                        tag=f"qfE_{t}_{h0}_{fc}")
                    P.add("vector", lambda e, bank=bank, pe2=pe2: e.tensor_scalar(
                        dch(tbf2, pe2, TT), PF[bank][:], 1.0, 1.0, ALU.add, ALU.max),
                        reads=[f"QF{bank}"], writes=[("tbf2", pe2)],
                        tag=f"qfT_{t}_{h0}_{fc}")
                    P.add("vector", lambda e, sl=sl, pe2=pe2: e.tensor_tensor(
                        qf[:, sl * TT:(sl + 1) * TT], dch(ebf2, pe2, TT),
                        dch(tbf2, pe2, TT), ALU.min),
                        reads=[("ebf2", pe2), ("tbf2", pe2)], writes=[("qf", sl)],
                        tag=f"qfM_{t}_{h0}_{fc}")
            for c in range(TCH):
                pk = c % 2
                for bl in range(4):
                    bank = 2 + bl % 2
                    def mm_a(e, c=c, bl=bl, bank=bank):
                        last = None
                        for s in range(4):
                            h0 = bl * 4 + s
                            for fc in range(2):
                                sl = h0 * 2 + fc
                                last = e.matmul(
                                    PF[bank][:, s * CH:s * CH + 65],
                                    qf[:, sl * TT + c * CH:sl * TT + (c + 1) * CH],
                                    kvl[:, sl * 66:sl * 66 + 65],
                                    start=(fc == 0), stop=(fc == 1))
                        return last
                    P.add("tensor", mm_a,
                          reads=[("qf", (bl * 4 + s) * 2 + fc)
                                 for s in range(4) for fc in range(2)],
                          writes=[f"QF{bank}"], tag=f"a_{t}_{c}_{bl}")
                    P.add("scalar", lambda e, bl=bl, bank=bank, pk=pk: e.activation(
                        attnr[:, pk * D + bl * 4 * 64:pk * D + (bl + 1) * 4 * 64]
                        .rearrange("p (s w) -> p s w", w=64),
                        PF[bank][:].rearrange("p (s w) -> p s w", w=CH)[:, :, 0:64],
                        AF.Copy),
                        reads=[f"QF{bank}"], writes=[("atr", pk)],
                        tag=f"ae_{t}_{c}_{bl}")
                    P.add("vector", lambda e, bl=bl, bank=bank: e.tensor_scalar_add(
                        z16[:, bl * 4:(bl + 1) * 4].rearrange(
                            "p (s o) -> p s o", o=1),
                        PF[bank][:].rearrange("p (s w) -> p s w", w=CH)[:, :, 64:65],
                        EPS_ATTN),
                        reads=[f"QF{bank}"], writes=[("z16", bl)],
                        tag=f"az_{t}_{c}_{bl}")
                P.add("vector", lambda e: e.reciprocal(zr16[:], z16[:]),
                      reads=[("z16", bl2) for bl2 in range(4)], writes=["zr16"],
                      tag=f"ar_{t}_{c}", late=True)
                P.add("vector", lambda e, pk=pk: e.tensor_tensor(
                    dch(attnr, pk, D).rearrange("p (h w) -> p h w", w=64),
                    dch(attnr, pk, D).rearrange("p (h w) -> p h w", w=64),
                    zr16[:].rearrange("p (h o) -> p h o", o=1).broadcast_to(
                        (CH, H, 64)), ALU.mult),
                    reads=["zr16", ("atr", pk)], writes=[("atr", pk)],
                    tag=f"as_{t}_{c}")
                for kc in range(NKC):
                    P.add("tensor", lambda e, kc=kc, pk=pk: e.transpose(
                        PT2[:, pk * D + kc * CH:pk * D + (kc + 1) * CH],
                        attnr[:, pk * D + kc * CH:pk * D + (kc + 1) * CH], iden[:]),
                        reads=[("atr", pk)], writes=[("PT2", pk)],
                        tag=f"aT_{t}_{c}_{kc}")
                P.add("vector", lambda e, c=c, pk=pk: e.tensor_copy(
                    attnfm[:].rearrange("p (k tt) -> p k tt", tt=TT)
                    [:, :, c * CH:(c + 1) * CH],
                    PT2[:, pk * D:(pk + 1) * D].rearrange("p (k w) -> p k w", w=CH)),
                    reads=[("PT2", pk)], writes=[("atf", c)], tag=f"aTe_{t}_{c}")
            for mc in range(NKC):
                bank = mc % 2
                def mm_y(e, mc=mc, bank=bank):
                    last = None
                    for kc in range(NKC):
                        last = e.matmul(PF[bank][:],
                                        mwq[:, kc * D + mc * CH:kc * D + (mc + 1) * CH],
                                        dch(xqb, kc, TT),
                                        start=(kc == 0), stop=(kc == NKC - 1))
                    return last
                P.add("tensor", mm_y,
                      reads=[("xqb", kc2) for kc2 in range(NKC)],
                      writes=[f"QF{bank}"], tag=f"y_{t}_{mc}")
                P.add("scalar", lambda e, mc=mc, bank=bank: e.activation(
                    dch(ybf, mc, TT), PF[bank][:], AF.Copy),
                    reads=[f"QF{bank}"], writes=[("shA", mc)], tag=f"ye_{t}_{mc}")
            for c in range(TCH):
                cc = t * TCH + c
                pk = c % 2
                for kc in range(NKC):
                    P.add("tensor", lambda e, kc=kc, pk=pk, c=c: e.transpose(
                        PT2[:, pk * D + kc * CH:pk * D + (kc + 1) * CH],
                        ybf[:, kc * TT + c * CH:kc * TT + (c + 1) * CH], iden[:]),
                        reads=[("shA", kc)], writes=[("PT2", pk)],
                        tag=f"yT_{t}_{c}_{kc}")
                xs = dch(xpin, pk, D)
                P.dma("sync", lambda e, cc=cc, xs=xs: e.dma_start(
                    out=xs, in_=x_d[cc * CH:(cc + 1) * CH, :]),
                    writes=[("xin", pk)], tag=f"px_{cc}")
                P.add("vector", lambda e, pk=pk: e.scalar_tensor_tensor(
                    dch(xpart, pk, D), PT2[:, pk * D:(pk + 1) * D], gsc[:, 1:2],
                    dch(xpin, pk, D), ALU.mult, ALU.add),
                    reads=[("PT2", pk), ("xin", pk)],
                    writes=[("xpart", pk)], tag=f"pp_{cc}")
                P.dma("scalar", lambda e, cc=cc, pk=pk: e.dma_start(
                    out=part_s[cc * CH:(cc + 1) * CH, :], in_=dch(xpart, pk, D)),
                    reads=[("xpart", pk)], writes=[("parts", cc)], tag=f"ps_{cc}")
            for mc in range(NKC):
                bank = mc % 2
                def mm_o(e, mc=mc, bank=bank):
                    last = None
                    for kc in range(NKC):
                        last = e.matmul(PF[bank][:],
                                        woT[:, kc * D + mc * CH:kc * D + (mc + 1) * CH],
                                        dch(attnfm, kc, TT),
                                        start=(kc == 0), stop=(kc == NKC - 1))
                    return last
                P.add("tensor", mm_o,
                      reads=[("woT", kc2) for kc2 in range(NKC)]
                      + [("atf", c2) for c2 in range(TCH)],
                      writes=[f"QF{bank}"], tag=f"o_{t}_{mc}")
                P.add("vector", lambda e, mc=mc, bank=bank, xch=xch: e.scalar_tensor_tensor(
                    dch(x1b, mc, TT), PF[bank][:], B_OUT[:, mc:mc + 1],
                    xch(mc), ALU.add, ALU.add),
                    reads=[f"QF{bank}"] + [("xnfm", xnk, c2) for c2 in range(TCH)],
                    writes=[("x1b", mc)], tag=f"x1_{t}_{mc}")
                P.add("scalar", lambda e, mc=mc: e.activation(
                    dch(x1sq, mc, TT), dch(x1b, mc, TT), AF.Square),
                    reads=[("x1b", mc)], writes=[("shA", mc)], tag=f"x1q_{t}_{mc}")
        ln1_tail(NTL - 1)
        P.emit(nc, sems)
      except _Skip:
        pass

    with contextlib.ExitStack() as p3:
      try:
        _guard(3)
        sb3 = lambda n, s, d_: p3.enter_context(nc.sbuf_tensor(n, s, d_))
        ps3 = lambda n, s, d_: p3.enter_context(nc.psum_tensor(n, s, d_))
        w2r = sb3("w2r", [CH, NMF * D], BF16)
        w1b = sb3("w1b", [CH, 2 * D], BF16)
        x1nb = sb3("x1v", [CH, 2 * NKC * TT], BF16)
        hbuf = sb3("hbf", [CH, NMF * TT], BF16)
        x2b = sb3("x2b", [CH, NKC * TT], BF16)
        x2sq = sb3("x2q", [CH, NKC * TT], BF16)
        aob = sb3("aob", [CH, NKC * TT], BF16)
        rows = sb3("rw3", [1, 4 * TT], F32)
        bc1 = sb3("bc3", [CH, TT], F32)
        bc2 = sb3("bc4", [CH, TT], F32)
        tsc = sb3("tsc", [CH, TT], F32)
        xpart = sb3("xp3", [CH, 2 * D], F32)
        outb = sb3("oub", [CH, 2 * D], F32)
        PT3 = ps3("PT3", [CH, 2 * D], BF16)
        PF = [ps3(f"RF{i}", [CH, 512], F32) for i in range(5)]
        P = Plan(state, drop,
                 psum_keys=[f"RF{i}" for i in range(5)] + [("PT3", 0), ("PT3", 1)])

        def row(i):
            return rows[:, i * TT:(i + 1) * TT]

        for kc in range(NMF):
            P.dma("sync", lambda e, kc=kc: e.dma_start(
                out=dch(w2r, kc, D), in_=w2s_s[kc * CH:(kc + 1) * CH, :]),
                writes=[("w2r", kc)], tag=f"lw2_{kc}")

        def ln2_tail(t):
                def mm_t1(e):
                    last = None
                    for kc in range(NKC):
                        last = e.matmul(PF[4][0:1, :], onesc[:], dch(x2b, kc, TT),
                                        start=(kc == 0), stop=(kc == NKC - 1))
                    return last
                P.add("tensor", mm_t1, reads=[("x2b", kc2) for kc2 in range(NKC)],
                      writes=["RF4"], tag=f"t1_{t}")
                P.add("vector", lambda e: e.tensor_scalar_mul(
                    row(0), PF[4][0:1, :], 1.0 / D),
                    reads=["RF4"], writes=[("rw", 0)], tag=f"t1e_{t}")
                def mm_t2(e):
                    last = None
                    for kc in range(NKC):
                        last = e.matmul(PF[4][0:1, :], onesc[:], dch(x2sq, kc, TT),
                                        start=(kc == 0), stop=(kc == NKC - 1))
                    return last
                P.add("tensor", mm_t2, reads=[("x2sq", kc2) for kc2 in range(NKC)],
                      writes=["RF4"], tag=f"t2_{t}")
                P.add("vector", lambda e: e.tensor_scalar_mul(
                    row(1), PF[4][0:1, :], 1.0 / D),
                    reads=["RF4"], writes=[("rw", 1)], tag=f"t2e_{t}")
                P.add("vector", lambda e: e.tensor_mul(row(3), row(0), row(0)),
                      reads=[("rw", 0)], writes=[("rw", 3)], tag=f"u2_{t}")
                P.add("vector", lambda e: e.tensor_sub(row(2), row(1), row(3)),
                      reads=[("rw", 1), ("rw", 3)], writes=[("rw", 2)], tag=f"u3_{t}")
                P.add("scalar", lambda e: e.activation(
                    row(3), row(2), AF.Ln, bias=EPS1r),
                    reads=[("rw", 2)], writes=[("rw", 3)], tag=f"u4_{t}")
                P.add("scalar", lambda e: e.activation(
                    row(2), row(3), AF.Exp, scale=-0.5),
                    reads=[("rw", 3)], writes=[("rw", 2)], tag=f"r2_{t}")
                P.add("vector", lambda e: e.scalar_tensor_tensor(
                    row(3), row(0), -1.0, row(2), ALU.mult, ALU.mult),
                    reads=[("rw", 0), ("rw", 2)], writes=[("rw", 3)], tag=f"q2_{t}")
                P.add("tensor", lambda e: e.matmul(PF[4][:], onesr[:], row(2),
                                                   start=True, stop=True),
                      reads=[("rw", 2)], writes=["RF4"], tag=f"w1b_{t}")
                P.add("vector", lambda e: e.tensor_copy(bc1[:], PF[4][:]),
                      reads=["RF4"], writes=["bc1"], tag=f"w1e_{t}")
                P.add("tensor", lambda e: e.matmul(PF[4][:], onesr[:], row(3),
                                                   start=True, stop=True),
                      reads=[("rw", 3)], writes=["RF4"], tag=f"w2b_{t}")
                P.add("vector", lambda e: e.tensor_copy(bc2[:], PF[4][:]),
                      reads=["RF4"], writes=["bc2"], tag=f"w2e_{t}")
                for mc in range(NKC):
                    P.add("gpsimd", lambda e, mc=mc: e.tensor_mul(
                        tsc[:], dch(x2b, mc, TT), bc1[:]),
                        reads=[("x2b", mc), "bc1"], writes=["tsc"], tag=f"m1_{t}_{mc}")
                    P.add("gpsimd", lambda e: e.tensor_add(tsc[:], tsc[:], bc2[:]),
                          reads=["tsc", "bc2"], writes=["tsc"], tag=f"m2_{t}_{mc}")
                    P.add("gpsimd", lambda e, mc=mc: e.tensor_scalar(
                        dch(aob, mc, TT), tsc[:], GAM[:, mc:mc + 1], BET[:, mc:mc + 1],
                        ALU.mult, ALU.add),
                        reads=["tsc"], writes=[("aob", mc)], tag=f"m3_{t}_{mc}")
                for c in range(TCH):
                    cc = t * TCH + c
                    pk = c % 2
                    for kc in range(NKC):
                        P.add("tensor", lambda e, kc=kc, pk=pk, c=c: e.transpose(
                            PT3[:, pk * D + kc * CH:pk * D + (kc + 1) * CH],
                            aob[:, kc * TT + c * CH:kc * TT + (c + 1) * CH], iden[:]),
                            reads=[("aob", kc)], writes=[("PT3", pk)],
                            tag=f"oT_{t}_{c}_{kc}")
                    xs = dch(xpart, pk, D)
                    P.dma("gpsimd", lambda e, cc=cc, xs=xs: e.dma_start(
                        out=xs, in_=part_s[cc * CH:(cc + 1) * CH, :]),
                        writes=[("xp3", pk)], tag=f"lp_{cc}")
                    P.add("vector", lambda e, pk=pk: e.tensor_add(
                        dch(outb, pk, D), PT3[:, pk * D:(pk + 1) * D], dch(xpart, pk, D)),
                        reads=[("PT3", pk), ("xp3", pk)], writes=[("oub", pk)],
                        tag=f"os_{cc}")
                    P.dma("gpsimd", lambda e, cc=cc, pk=pk: e.dma_start(
                        out=out_d[cc * CH:(cc + 1) * CH, :], in_=dch(outb, pk, D)),
                        reads=[("oub", pk)], writes=[("outd", cc)], tag=f"ow_{cc}")
        for t in range(NTL):
            tb = t % 2
            for mc in range(NKC):
                P.dma("sync", lambda e, t=t, mc=mc, tb=tb: e.dma_start(
                    out=x1nb[:, (tb * NKC + mc) * TT:(tb * NKC + mc + 1) * TT],
                    in_=x1n_s[mc * CH:(mc + 1) * CH, t * TT:(t + 1) * TT]),
                    writes=[("x1v", tb, mc)], tag=f"lx1_{t}_{mc}")
            if t > 0:
                ln2_tail(t - 1)
            for m in range(NMF):
                wb = m % 2
                bank = m % 2
                P.dma("sync", lambda e, m=m, wb=wb: e.dma_start(
                    out=dch(w1b, wb, D), in_=w1s_s[m]),
                    writes=[("w1b", wb)], tag=f"lw1_{t}_{m}")
                def mm_h(e, m=m, wb=wb, bank=bank, tb=tb):
                    last = None
                    for kc in range(NKC):
                        last = e.matmul(
                            PF[bank][:],
                            w1b[:, wb * D + kc * CH:wb * D + (kc + 1) * CH],
                            x1nb[:, (tb * NKC + kc) * TT:(tb * NKC + kc + 1) * TT],
                            start=(kc == 0), stop=(kc == NKC - 1))
                    return last
                P.add("tensor", mm_h,
                      reads=[("w1b", wb)] + [("x1v", tb, kc2) for kc2 in range(NKC)],
                      writes=[f"RF{bank}"], tag=f"h_{t}_{m}")
                P.add("scalar", lambda e, m=m, bank=bank: e.activation(
                    dch(hbuf, m, TT), PF[bank][:], AF.Gelu, bias=b1c[:, m:m + 1]),
                    reads=[f"RF{bank}"], writes=[("h", m)], tag=f"he_{t}_{m}")
            for mc in range(NKC):
                bank = 2 + mc % 2
                def mm_x2(e, mc=mc, bank=bank):
                    last = None
                    for kc in range(NMF):
                        last = e.matmul(
                            PF[bank][:],
                            w2r[:, kc * D + mc * CH:kc * D + (mc + 1) * CH],
                            dch(hbuf, kc, TT),
                            start=(kc == 0), stop=(kc == NMF - 1))
                    return last
                P.add("tensor", mm_x2,
                      reads=[("w2r", kc2) for kc2 in range(NMF)]
                      + [("h", kc2) for kc2 in range(NMF)],
                      writes=[f"RF{bank}"], tag=f"x2_{t}_{mc}")
                P.add("vector",
                      lambda e, mc=mc, bank=bank, tb=tb: e.scalar_tensor_tensor(
                          dch(x2b, mc, TT), PF[bank][:], B2[:, mc:mc + 1],
                          x1nb[:, (tb * NKC + mc) * TT:(tb * NKC + mc + 1) * TT],
                          ALU.add, ALU.add),
                      reads=[f"RF{bank}", ("x1v", tb, mc)],
                      writes=[("x2b", mc)], tag=f"x2e_{t}_{mc}")
                P.add("scalar", lambda e, mc=mc: e.activation(
                    dch(x2sq, mc, TT), dch(x2b, mc, TT), AF.Square),
                    reads=[("x2b", mc)], writes=[("x2sq", mc)], tag=f"x2q_{t}_{mc}")
        ln2_tail(NTL - 1)
        P.emit(nc, sems)
      except _Skip:
        pass

    if debug:
        P = Plan(state)
        P.dma("sync", lambda e: e.dma_start(out=dbg["scal"], in_=scal[:]), tag="dbg_scal")
        P.dma("sync", lambda e: e.dma_start(
            out=dbg["gsc"], in_=build_nc._stash["sred"][:]), tag="dbg_gsc")
        P.dma("sync", lambda e: e.dma_start(out=dbg["kvl"], in_=kvl[:]), tag="dbg_kvl")
        P.dma("sync", lambda e: e.dma_start(out=dbg["x1n"], in_=x1n_s), tag="dbg_x1n")
        P.dma("sync", lambda e: e.dma_start(out=dbg["part"], in_=part_s), tag="dbg_part")
        P.dma("sync", lambda e: e.dma_start(out=dbg["h2row"], in_=h2row_s), tag="dbg_h2row")
        P.dma("sync", lambda e: e.dma_start(out=dbg["stH"], in_=statsH2[:]), tag="dbg_stH")
        P.dma("sync", lambda e: e.dma_start(out=dbg["stR"], in_=statsR[:]), tag="dbg_stR")
        P.dma("sync", lambda e: e.dma_start(out=dbg["stM"], in_=statsMUR[:]), tag="dbg_stM")
        P.dma("sync", lambda e: e.dma_start(out=dbg["mwq"], in_=mwq[:]), tag="dbg_mwq")
        P.emit(nc, sems)

    es.close()
    return nc


def _shard_inputs(L0_attn_x, L0_attn_proj, L0_attn_w_qkv, L0_attn_w_out,
                  L0_attn_b_out, L0_attn_gamma, L0_attn_beta, L0_attn_w1,
                  L0_attn_b1, L0_attn_w2, L0_attn_b2, L0_ffn_weight,
                  L0_ffn_ln_gamma, L0_ffn_ln_beta, NT):
    import ml_dtypes
    f32 = lambda a: np.ascontiguousarray(np.asarray(a, dtype=np.float32))
    x = f32(L0_attn_x)
    wqkvT = np.ascontiguousarray(f32(L0_attn_w_qkv).T)
    pj = f32(L0_attn_proj)
    projP = np.zeros((H // 2, CH, 2 * FF), np.float32)
    for pr in range(H // 2):
        projP[pr, 0:64, 0:FF] = pj[2 * pr]
        projP[pr, 64:128, FF:2 * FF] = pj[2 * pr + 1]
    projQ = np.zeros((CH, 2 * H * CH), np.float32)
    for h0 in range(H):
        for fc in range(2):
            sl = h0 * 2 + fc
            projQ[(h0 % 2) * 64:(h0 % 2) * 64 + 64, sl * CH:(sl + 1) * CH] = \
                pj[h0][:, fc * CH:(fc + 1) * CH]
    woutT = np.ascontiguousarray(f32(L0_attn_w_out).T)
    w1T = np.ascontiguousarray(f32(L0_attn_w1).T)
    w2T = np.ascontiguousarray(f32(L0_attn_w2).T)
    fw = f32(L0_ffn_weight)
    vecs = np.zeros((CH, 64), np.float32)
    for i, v in enumerate((L0_attn_b_out, L0_attn_gamma, L0_attn_beta,
                           L0_attn_b2, L0_ffn_ln_gamma, L0_ffn_ln_beta)):
        vecs[:, i * 8:(i + 1) * 8] = f32(v).reshape(NKC, CH).T
    b1c = np.ascontiguousarray(f32(L0_attn_b1).reshape(NMF, CH).T)
    vecs[:, 48] = EPS_LN
    vecs[:, 49] = LN_EPS_FFN
    lngbf = np.ascontiguousarray(np.broadcast_to(f32(L0_ffn_ln_gamma), (CH, D)))
    lnbbf = np.ascontiguousarray(np.broadcast_to(f32(L0_ffn_ln_beta), (CH, D)))
    iden = np.eye(CH, dtype=ml_dtypes.bfloat16)
    onesr = np.ones((1, CH), np.float32)
    onescb = np.ones((CH, 1), ml_dtypes.bfloat16)
    shared = dict(wqkvT=wqkvT, projP=projP, projQ=projQ, woutT=woutT, w1T=w1T, w2T=w2T,
                  fw=fw, vecs=vecs, b1c=b1c, lngbf=lngbf, lnbbf=lnbbf,
                  iden=iden, onesr=onesr, onescb=onescb)
    return [dict(shared, x=np.ascontiguousarray(x[i, :NT])) for i in range(NC)]


def kernel(**inputs):
    NT = 8192
    in_maps = _shard_inputs(NT=NT, **inputs)
    nc = build_nc(NT=NT)
    res = run_bass_kernel_spmd(nc, in_maps, core_ids=list(range(NC)))
    return np.stack([np.asarray(r["out"]) for r in res.results]).astype(np.float32)


# version 12
# speedup vs baseline: 1.2231x; 1.2231x over previous
import contextlib
import numpy as np
import concourse.bass as bass
import concourse.mybir as mybir
from concourse.bass_utils import run_bass_kernel_spmd

F32 = mybir.dt.float32
BF16 = mybir.dt.bfloat16
AF = mybir.ActivationFunctionType
ALU = mybir.AluOpType
AX = mybir.AxisListType.X

D = 1024
H = 16
FF = 256
DFF = 4096
NC = 8
CH = 128
NKC = D // CH
NMF = DFF // CH

EPS_LN = 1e-5
EPS_ATTN = 1e-8
LN_EPS_FFN = 1e-3
EPS_FFN = 1e-5
QB = 128.0
CLIP = QB - EPS_FFN

TT = 512
TCH = TT // CH


class Plan:

    ENGS = ("sync", "tensor", "vector", "scalar", "gpsimd")

    NLANES = {"sync": 8, "scalar": 6, "gpsimd": 4}

    def __init__(self, state, drop=(), psum_keys=()):
        self.state = state
        self.drop = tuple(drop)
        self.psum_keys = set(psum_keys)
        self.ops = []
        self.last_w = {}
        self.readers = {}

    def add(self, eng, emit, reads=(), writes=(), kind="compute", tag="",
            late=False):
        if any(tag.startswith(p) for p in self.drop):
            return None
        idx = len(self.ops)
        deps = set()
        for b in reads:
            w = self.last_w.get(b)
            if w is not None:
                deps.add(w)
            if b in self.psum_keys:
                for r in self.readers.get(b, ()):
                    if self.ops[r]["eng"] != eng:
                        deps.add(r)
        for b in writes:
            w = self.last_w.get(b)
            if w is not None:
                deps.add(w)
            for r in self.readers.get(b, ()):
                deps.add(r)
        for b in reads:
            self.readers.setdefault(b, []).append(idx)
        for b in writes:
            self.last_w[b] = idx
            self.readers[b] = []
        deps.discard(idx)
        lane = None
        if kind == "dma":
            lc = self.state.setdefault("lanecnt", {})
            lc[eng] = lc.get(eng, 0) + 1
            lane = lc[eng] % self.NLANES[eng]
        self.ops.append(dict(eng=eng, emit=emit, kind=kind, deps=deps, tag=tag,
                             late=late, inc=(kind != "compute"), incv=None,
                             waits=None, lane=lane, idx=idx))
        return idx

    def dma(self, eng, emit, reads=(), writes=(), tag=""):
        return self.add(eng, emit, reads, writes, kind="dma", tag=tag)

    def _sname(self, op):
        if op["kind"] == "dma":
            return "q_%s%d" % (op["eng"], op["lane"])
        if op["kind"] == "cc":
            return "cc"
        return "e_" + op["eng"]

    def _nowait(self, dep, op):
        if dep.get("late"):
            return False
        if dep["eng"] != op["eng"]:
            return False
        if dep["kind"] == "dma" or dep["kind"] == "cc":
            return dep["kind"] == op["kind"]
        return dep["eng"] == "tensor"

    FULL_WAITS = False

    def resolve(self):
        counts = self.state["counts"]
        ops = self.ops
        for op in ops:
            for d in op["deps"]:
                dep = ops[d]
                if not self._nowait(dep, op):
                    dep["inc"] = True
        for op in ops:
            if op["inc"]:
                sn = self._sname(op)
                counts[sn] = counts.get(sn, 0) + (16 if op["kind"] == "dma" else 1)
                op["incv"] = counts[sn]
        clocks = self.state["clocks"]
        opclock = [None] * len(ops)
        for op in ops:
            ec = clocks.setdefault(op["eng"], {})
            waits = []
            for d in sorted(op["deps"]):
                dep = ops[d]
                if self._nowait(dep, op):
                    continue
                sn, v = self._sname(dep), dep["incv"]
                assert v is not None, (op["tag"], dep["tag"])
                if ec.get(sn, 0) >= v and not Plan.FULL_WAITS:
                    continue
                if ec.get(sn, 0) >= v:
                    ec[sn] = max(ec.get(sn, 0), v)
                    waits.append((sn, v))
                    continue
                waits.append((sn, v))
                ec[sn] = v
                dc = opclock[d]
                if dc:
                    for s2, v2 in dc.items():
                        if ec.get(s2, 0) < v2:
                            ec[s2] = v2
            op["waits"] = waits
            opclock[op["idx"]] = dict(ec)

    def emit(self, nc, sems):
        self.resolve()
        counts = self.state["counts"]
        qsems = ["q_sync%d" % i for i in range(8)] + \
                ["q_scalar%d" % i for i in range(6)] + \
                ["q_gpsimd%d" % i for i in range(4)] + ["cc"]
        drain = [(sn, counts[sn]) for sn in qsems if counts.get(sn)]
        with nc.Block() as blk:
            for eng_name in self.ENGS:
                ops = [o for o in self.ops if o["eng"] == eng_name]
                if not ops and eng_name != "vector":
                    continue

                def make(ops, eng_name=eng_name):
                    def body(eng):
                        for op in ops:
                            for sn, v in op["waits"]:
                                eng.wait_ge(sems[sn], v)
                            inst = op["emit"](eng)
                            if op["inc"]:
                                assert inst is not None, op["tag"]
                                inst.then_inc(sems[self._sname(op)],
                                              16 if op["kind"] == "dma" else 1)
                        if eng_name == "vector":
                            ec = self.state["clocks"].setdefault("vector", {})
                            for sn, v in drain:
                                if ec.get(sn, 0) < v:
                                    eng.wait_ge(sems[sn], v)
                                    ec[sn] = v
                    return body
                getattr(blk, eng_name)(make(ops))


def build_nc(NT=8192, race=False, passes=(1, 2, 3), drop=(), debug=False):
    NTL = NT // TT
    NCH = NT // CH

    nc = bass.Bass("TRN2", target_bir_lowering=False,
                   detect_race_conditions=race)
    dt = nc.dram_tensor
    x_d = dt("x", [NT, D], F32, kind="ExternalInput").ap()
    wqkvT_d = dt("wqkvT", [D, 3 * D], F32, kind="ExternalInput").ap()
    projP_d = dt("projP", [H // 2, CH, 2 * FF], F32, kind="ExternalInput").ap()
    projQ_d = dt("projQ", [CH, 2 * H * CH], F32, kind="ExternalInput").ap()
    woutT_d = dt("woutT", [D, D], F32, kind="ExternalInput").ap()
    w1T_d = dt("w1T", [D, DFF], F32, kind="ExternalInput").ap()
    w2T_d = dt("w2T", [DFF, D], F32, kind="ExternalInput").ap()
    fw_d = dt("fw", [D, D], F32, kind="ExternalInput").ap()
    vecs_d = dt("vecs", [CH, 64], F32, kind="ExternalInput").ap()
    b1c_d = dt("b1c", [CH, NMF], F32, kind="ExternalInput").ap()
    iden_d = dt("iden", [CH, CH], BF16, kind="ExternalInput").ap()
    onesr_d = dt("onesr", [1, CH], F32, kind="ExternalInput").ap()
    onesc_d = dt("onescb", [CH, 1], BF16, kind="ExternalInput").ap()
    out_d = dt("out", [NT, D], F32, kind="ExternalOutput").ap()

    x1n_s = dt("x1n_s", [D, NT], BF16, kind="Internal").ap()
    part_s = dt("part_s", [NT, D], F32, kind="Internal").ap()
    w1s_s = dt("w1s_s", [NMF, CH, D], BF16, kind="Internal").ap()
    w2s_s = dt("w2s_s", [DFF, D], BF16, kind="Internal").ap()
    h2row_s = dt("h2row_s", [1, NT], F32, kind="Internal").ap()
    cc_in = dt("cc_in", [1, 1], F32, kind="Internal").ap()
    cc_out = dt("cc_out", [1, 1], F32, kind="Internal", addr_space="Shared").ap()

    dbg = {}
    if debug:
        dbg["scal"] = dt("dbg_scal", [1, 16], F32, kind="ExternalOutput").ap()
        dbg["gsc"] = dt("dbg_gsc", [CH, 8], F32, kind="ExternalOutput").ap()
        dbg["kvl"] = dt("dbg_kvl", [CH, 2 * H * 66], BF16, kind="ExternalOutput").ap()
        dbg["x1n"] = dt("dbg_x1n", [D, NT], BF16, kind="ExternalOutput").ap()
        dbg["part"] = dt("dbg_part", [NT, D], F32, kind="ExternalOutput").ap()
        dbg["h2row"] = dt("dbg_h2row", [1, NT], F32, kind="ExternalOutput").ap()
        dbg["stH"] = dt("dbg_stH", [CH, NT // CH], F32, kind="ExternalOutput").ap()
        dbg["stR"] = dt("dbg_stR", [CH, NT // CH], F32, kind="ExternalOutput").ap()
        dbg["stM"] = dt("dbg_stM", [CH, NT // CH], F32, kind="ExternalOutput").ap()
        dbg["mwq"] = dt("dbg_mwq", [CH, NKC * D], BF16, kind="ExternalOutput").ap()
    state = dict(counts={}, clocks={})
    es = contextlib.ExitStack()
    sems = {}
    for sn in (["e_tensor", "e_vector", "e_scalar", "e_gpsimd", "cc"]
               + ["q_sync%d" % i for i in range(8)]
               + ["q_scalar%d" % i for i in range(6)]
               + ["q_gpsimd%d" % i for i in range(4)]):
        sems[sn] = es.enter_context(nc.semaphore(sn))
    sb = lambda n, s, d_: es.enter_context(nc.sbuf_tensor(n, s, d_))

    iden = sb("idn", [CH, CH], BF16)
    onesr = sb("onr", [1, CH], F32)
    onesc = sb("onc", [CH, 1], BF16)
    vecs = sb("vcs", [CH, 64], F32)
    b1c = sb("b1cs", [CH, NMF], F32)
    statsR = sb("stR", [CH, NCH], F32)
    statsMUR = sb("stM", [CH, NCH], F32)
    statsH2 = sb("stH", [CH, NCH], F32)
    kvl = sb("kvl", [CH, 2 * H * 66], BF16)
    mwq = sb("mwq", [CH, NKC * D], BF16)
    gsc = sb("gsc", [CH, 8], F32)
    lngyc = sb("lgy", [CH, NKC], F32)
    lnbyc = sb("lby", [CH, NKC], F32)
    grow = sb("grw", [1, CH], F32)
    scal = sb("scl", [1, 16], F32)
    gacc = sb("gac", [CH, 1], F32)
    srow = sb("srw", [1, CH], F32)

    B_OUT = vecs[:, 0:8]
    GAM = vecs[:, 8:16]
    BET = vecs[:, 16:24]
    B2 = vecs[:, 24:32]
    LNG = vecs[:, 32:40]
    LNB = vecs[:, 40:48]
    EPS1 = vecs[:, 48:49]
    EPSF = vecs[:, 49:50]
    EPS1r = vecs[0:1, 48:49]

    def dch(ap2d, c, w=CH):
        return ap2d[:, c * w:(c + 1) * w]

    class _Skip(Exception):
        pass

    def _guard(n):
        if n not in passes:
            raise _Skip()

    with contextlib.ExitStack() as p1:
      try:
        _guard(1)
        sb1 = lambda n, s, d_: p1.enter_context(nc.sbuf_tensor(n, s, d_))
        ps1 = lambda n, s, d_: p1.enter_context(nc.psum_tensor(n, s, d_))
        kT = sb1("kT", [CH, NKC * D], BF16)
        vT = sb1("vT", [CH, NKC * D], BF16)
        lngb = sb1("lgb", [CH, D], BF16)
        lnbb = sb1("lbb", [CH, D], BF16)
        kvacc = sb1("kva", [65, H * FF], F32)
        projBD = sb1("pbd", [CH, (H // 2) * 2 * FF], BF16)
        wst = sb1("wst", [CH, DFF], F32)
        wst2 = sb1("ws2", [CH, DFF], BF16)
        rstg = sb1("rsg", [CH, D], F32)
        xin = sb1("xin", [CH, 2 * D], F32)
        xntm = sb1("xnt", [CH, 2 * D], BF16)
        xnfm = sb1("xnf", [CH, NKC * TT], BF16)
        kfm = sb1("kfm", [CH, NKC * TT], BF16)
        vext = sb1("vex", [CH, TCH * H * 66], BF16)
        kftm = sb1("kft", [CH, TCH * H * FF], BF16)
        ebuf = sb1("ebf", [CH, 2 * 2 * FF], BF16)
        tbuf = sb1("tbf", [CH, 2 * 2 * FF], BF16)
        gscr = sb1("gs1", [CH, D], BF16)
        gscr2 = sb1("gs2", [CH, D], BF16)
        sqd = sb1("sqd", [CH, D], F32)
        sred = sb1("srd", [CH, 8], F32)
        stS = sb1("stS", [CH, 16], F32)
        PT = ps1("PT", [CH, 2 * D], BF16)
        PF = [ps1(f"PF{i}", [CH, 512], F32) for i in range(6)]
        P = Plan(state, drop,
                 psum_keys=[f"PF{i}" for i in range(6)] + [("PT", 0), ("PT", 1)])

        for name, dstv, srcv in (("iden", iden, iden_d), ("onesr", onesr, onesr_d),
                                 ("onesc", onesc, onesc_d), ("vecs", vecs, vecs_d),
                                 ("b1c", b1c, b1c_d)):
            P.dma("sync", lambda e, a=dstv, b=srcv: e.dma_start(out=a[:], in_=b),
                  writes=[name], tag="ld_" + name)
        lngbf_d = dt("lngbf", [CH, D], F32, kind="ExternalInput").ap()
        lnbbf_d = dt("lnbbf", [CH, D], F32, kind="ExternalInput").ap()
        P.dma("sync", lambda e: e.dma_start(out=wst[:, 0:D], in_=lngbf_d),
              writes=[("wst", 0)], tag="ld_lngf")
        P.dma("sync", lambda e: e.dma_start(out=wst[:, D:2 * D], in_=lnbbf_d),
              writes=[("wst", 1)], tag="ld_lnbf")
        P.add("vector", lambda e: e.tensor_copy(lngb[:], wst[:, 0:D]),
              reads=[("wst", 0)], writes=["lngb"], tag="c_lngb")
        P.add("vector", lambda e: e.tensor_copy(lnbb[:], wst[:, D:2 * D]),
              reads=[("wst", 1)], writes=["lnbb"], tag="c_lnbb")
        P.add("vector", lambda e: e.memset(gacc[:], 0.0), writes=["gacc"], tag="ms_g")
        P.add("vector", lambda e: e.memset(kvl[:], 0.0), writes=["kvl"], tag="ms_kvl")
        P.add("vector", lambda e: e.memset(vext[:], 1.0),
              writes=[("vext", c, vg) for c in range(TCH) for vg in range(2)],
              tag="ms_vx")

        def ldw(dkey, dsl, src, eng="vector"):
            P.dma("sync", lambda e, src=src: e.dma_start(out=rstg[:, 0:src.shape[1]], in_=src),
                  writes=["rstg"], tag="ld" + str(dkey))
            P.add(eng, lambda e, dsl=dsl, w=src.shape[1]: e.tensor_copy(dsl, rstg[:, 0:w]),
                  reads=["rstg"], writes=[dkey], tag="cast" + str(dkey))

        for kc in range(NKC):
            ldw(("kT", kc), dch(kT, kc, D), wqkvT_d[kc * CH:(kc + 1) * CH, D:2 * D])
        for kc in range(NKC):
            ldw(("vT", kc), dch(vT, kc, D), wqkvT_d[kc * CH:(kc + 1) * CH, 2 * D:3 * D])
        for pr in range(H // 2):
            P.dma("sync", lambda e, pr=pr: e.dma_start(
                out=wst[:, 0:2 * FF], in_=projP_d[pr]),
                writes=[("wst", 0)], tag=f"ldpj_{pr}")
            P.add("vector", lambda e, pr=pr: e.tensor_copy(
                dch(projBD, pr, 2 * FF), wst[:, 0:2 * FF]),
                reads=[("wst", 0)], writes=[("projBD", pr)], tag=f"cpj_{pr}")

        def emit_prep():
            fw_r = fw_d.rearrange("(a p) u -> p a u", p=CH)
            for hf in range(2):
                P.dma("scalar", lambda e, hf=hf: e.dma_start(
                    out=wst[:].rearrange("p (a u) -> p a u", u=D),
                    in_=fw_r[:, hf * 4:(hf + 1) * 4, :]),
                    writes=[("wst", q) for q in range(4)], tag=f"ld_fw{hf}")
                P.add("vector", lambda e, hf=hf: e.reduce_sum(
                    sred[:, hf:hf + 1], wst[:], axis=AX, apply_absolute_value=True),
                    reads=[("wst", q) for q in range(4)], writes=[("sred", hf)],
                    tag=f"asum{hf}", late=True)
            P.add("vector", lambda e: e.tensor_add(sred[:, 2:3], sred[:, 0:1], sred[:, 1:2]),
                  reads=[("sred", 0), ("sred", 1)], writes=[("sred", 2)], tag="asum2")
            P.dma("gpsimd", lambda e: e.dma_start(out=srow[:], in_=sred[:, 2:3], allow_slow_non_contiguous=True),
                  reads=[("sred", 2)], writes=["srow"], tag="fold_b")
            P.add("vector", lambda e: e.reduce_sum(scal[:, 0:1], srow[:], axis=AX),
                  reads=["srow"], writes=[("scal", 0)], tag="bsum", late=True)
            P.add("vector", lambda e: e.tensor_scalar(
                scal[:, 1:2], scal[:, 0:1], 0.5 / (D * D), 0.5 * EPS_FFN, ALU.mult, ALU.add),
                reads=[("scal", 0)], writes=[("scal", 1)], tag="hb")
            P.add("vector", lambda e: e.tensor_scalar_mul(scal[:, 2:3], scal[:, 1:2], -1.0),
                  reads=[("scal", 1)], writes=[("scal", 2)], tag="nhb")
            P.add("tensor", lambda e: e.matmul(PF[0][:, 0:2], onesr[:], scal[:, 1:3],
                                               start=True, stop=True),
                  reads=[("scal", 1), ("scal", 2), "onesr"], writes=["PF0"], tag="bc_hb")
            P.add("vector", lambda e: e.tensor_copy(gsc[:, 2:4], PF[0][:, 0:2]),
                  reads=["PF0"], writes=[("gsc", 2)], tag="ev_hb")
            for hf in range(2):
                P.dma("scalar", lambda e, hf=hf: e.dma_start(
                    out=wst[:].rearrange("p (a u) -> p a u", u=D),
                    in_=fw_r[:, hf * 4:(hf + 1) * 4, :]),
                    writes=[("wst", q) for q in range(4)], tag=f"ld_fwb{hf}")
                P.add("vector", lambda e: e.tensor_single_scalar(
                    wst2[:], wst[:], gsc[:, 2:3], ALU.is_gt),
                    reads=[("wst", q) for q in range(4)] + [("gsc", 2)],
                    writes=[("wst2", q) for q in range(4)], tag=f"gt{hf}")
                P.add("vector", lambda e, hf=hf: e.scalar_tensor_tensor(
                    mwq[:, hf * DFF:(hf + 1) * DFF], wst[:], gsc[:, 3:4], wst2[:],
                    ALU.is_lt, ALU.subtract),
                    reads=[("wst", q) for q in range(4)]
                    + [("wst2", q) for q in range(4)] + [("gsc", 2)],
                    writes=[("mwq", hf)], tag=f"mwq{hf}")

            for kc in range(NMF):
                q = kc % 4
                ssl = wst[:, q * D:(q + 1) * D]
                s2 = wst2[:, q * D:(q + 1) * D]
                P.dma("scalar", lambda e, kc=kc, ssl=ssl: e.dma_start(
                    out=ssl, in_=w2T_d[kc * CH:(kc + 1) * CH, :]),
                    writes=[("wst", q)], tag=f"ldw2_{kc}")
                P.add("gpsimd", lambda e, ssl=ssl, s2=s2: e.tensor_copy(s2, ssl),
                      reads=[("wst", q)], writes=[("wst2", q)], tag=f"cw2_{kc}")
                P.dma("gpsimd", lambda e, kc=kc, s2=s2: e.dma_start(
                    out=w2s_s[kc * CH:(kc + 1) * CH, :], in_=s2),
                    reads=[("wst2", q)], writes=["w2s"], tag=f"sw2_{kc}")
            for kc in range(NKC):
                P.dma("scalar", lambda e, kc=kc: e.dma_start(
                    out=wst[:], in_=w1T_d[kc * CH:(kc + 1) * CH, :]),
                    writes=[("wst", q) for q in range(4)], tag=f"ldw1_{kc}")
                P.add("gpsimd", lambda e: e.tensor_copy(wst2[:], wst[:]),
                      reads=[("wst", q) for q in range(4)],
                      writes=[("wst2", q) for q in range(4)], tag=f"cw1_{kc}")
                P.dma("gpsimd", lambda e, kc=kc: e.dma_start(
                    out=w1s_s[:, :, kc * CH:(kc + 1) * CH].rearrange("m p c -> p m c"),
                    in_=wst2[:].rearrange("p (m c) -> p m c", c=CH)),
                    reads=[("wst2", q) for q in range(4)], writes=["w1s"], tag=f"sw1_{kc}")


        for t in range(NTL):
            if t == min(2, NTL - 1):
                emit_prep()
            tc0 = t * TCH
            for c in range(TCH):
                cc = t * TCH + c
                pk = c % 2
                xs = dch(xin, pk, D)
                P.dma("sync", lambda e, cc=cc, xs=xs: e.dma_start(
                    out=xs, in_=x_d[cc * CH:(cc + 1) * CH, :]),
                    writes=[("xin", pk)], tag=f"ldx_{cc}")
                P.add("vector", lambda e, xs=xs, c=c: e.reduce_sum(
                    sred[:, c:c + 1], xs, axis=AX),
                    reads=[("xin", pk)], writes=[("sred", c)], tag=f"sum_{cc}",
                    late=True)
                P.add("vector", lambda e, xs=xs, c=c: e.scalar_tensor_tensor(
                    sqd[:], xs, 1.0, xs, ALU.mult, ALU.mult,
                    accum_out=sred[:, 4 + c:5 + c]),
                    reads=[("xin", pk)], writes=[("sred", 4 + c), "sqd"],
                    tag=f"sq_{cc}", late=True)
            SR = lambda a, b: stS[:, a:b]
            P.add("vector", lambda e: e.tensor_scalar_mul(
                SR(0, 4), sred[:, 0:4], 1.0 / D),
                reads=[("sred", c2) for c2 in range(4)], writes=[("stS", 0)],
                tag=f"bmu_{t}")
            P.add("vector", lambda e: e.tensor_scalar_mul(
                SR(4, 8), sred[:, 4:8], 1.0 / D),
                reads=[("sred", 4 + c2) for c2 in range(4)], writes=[("stS", 1)],
                tag=f"be2_{t}")
            P.add("vector", lambda e: e.tensor_mul(SR(8, 12), SR(0, 4), SR(0, 4)),
                  reads=[("stS", 0)], writes=[("stS", 2)], tag=f"bm2_{t}")
            P.add("vector", lambda e: e.tensor_sub(SR(12, 16), SR(4, 8), SR(8, 12)),
                  reads=[("stS", 1), ("stS", 2)], writes=[("stS", 3)], tag=f"bvr_{t}")
            P.add("scalar", lambda e: e.activation(
                SR(8, 12), SR(12, 16), AF.Ln, bias=EPS1),
                reads=[("stS", 3)], writes=[("stS", 2)], tag=f"bln_{t}")
            P.add("scalar", lambda e, tc0=tc0: e.activation(
                statsR[:, tc0:tc0 + 4], SR(8, 12), AF.Exp, scale=-0.5),
                reads=[("stS", 2)],
                writes=[("stR", tc0 + c2) for c2 in range(4)], tag=f"br_{t}")
            P.add("vector", lambda e, tc0=tc0: e.scalar_tensor_tensor(
                statsMUR[:, tc0:tc0 + 4], SR(0, 4), -1.0, statsR[:, tc0:tc0 + 4],
                ALU.mult, ALU.mult),
                reads=[("stS", 0)] + [("stR", tc0 + c2) for c2 in range(4)],
                writes=[("stM", tc0 + c2) for c2 in range(4)], tag=f"bmr_{t}")
            P.add("vector", lambda e, tc0=tc0: e.tensor_mul(
                SR(0, 4), statsR[:, tc0:tc0 + 4], statsR[:, tc0:tc0 + 4]),
                reads=[("stR", tc0 + c2) for c2 in range(4)], writes=[("stS", 0)],
                tag=f"brr_{t}")
            P.add("vector", lambda e: e.tensor_mul(SR(4, 8), SR(0, 4), SR(12, 16)),
                  reads=[("stS", 0), ("stS", 3)], writes=[("stS", 1)], tag=f"bv2_{t}")
            P.add("scalar", lambda e: e.activation(
                SR(8, 12), SR(4, 8), AF.Ln, bias=EPSF),
                reads=[("stS", 1)], writes=[("stS", 2)], tag=f"bl2_{t}")
            P.add("scalar", lambda e, tc0=tc0: e.activation(
                statsH2[:, tc0:tc0 + 4], SR(8, 12), AF.Exp, scale=-0.5),
                reads=[("stS", 2)],
                writes=[("stH", tc0 + c2) for c2 in range(4)], tag=f"bh2_{t}")
            for c in range(TCH):
                cc = t * TCH + c
                pk = c % 2
                xs = dch(xin, pk, D)
                P.dma("sync", lambda e, cc=cc, xs=xs: e.dma_start(
                    out=xs, in_=x_d[cc * CH:(cc + 1) * CH, :]),
                    writes=[("xin", pk)], tag=f"ldx2_{cc}")
                P.add("scalar", lambda e, xs=xs, pk=pk, cc=cc: e.activation(
                    dch(xntm, pk, D), xs, AF.Identity,
                    bias=statsMUR[:, cc:cc + 1], scale=statsR[:, cc:cc + 1]),
                    reads=[("xin", pk), ("stM", cc), ("stR", cc)],
                    writes=[("xntm", pk)], tag=f"xn_{cc}")
                P.add("vector", lambda e, pk=pk, cc=cc: e.scalar_tensor_tensor(
                    gscr[:], dch(xntm, pk, D), statsH2[:, cc:cc + 1], lngb[:],
                    ALU.mult, ALU.mult),
                    reads=[("xntm", pk), ("stH", cc), "lngb"], writes=["gscr"],
                    tag=f"g1_{cc}")
                P.add("gpsimd", lambda e: e.tensor_add(gscr2[:], gscr[:], lnbb[:]),
                      reads=["gscr", "lnbb"], writes=["gscr2"], tag=f"g2_{cc}")
                P.add("vector", lambda e: e.reduce_max(
                    sred[:, 0:1], gscr2[:], axis=AX, apply_absolute_value=True),
                    reads=["gscr2"], writes=[("sred", 0)], tag=f"g3_{cc}",
                    late=True)
                P.add("vector", lambda e: e.tensor_max(gacc[:], gacc[:], sred[:, 0:1]),
                      reads=[("sred", 0), "gacc"], writes=["gacc"], tag=f"g4_{cc}")
                for kc in range(NKC):
                    P.add("tensor", lambda e, kc=kc, pk=pk: e.transpose(
                        PT[:, pk * D + kc * CH:pk * D + (kc + 1) * CH],
                        xntm[:, pk * D + kc * CH:pk * D + (kc + 1) * CH], iden[:]),
                        reads=[("xntm", pk), "iden"], writes=[("PT", pk)],
                        tag=f"xT_{cc}_{kc}")
                P.add("vector", lambda e, c=c, pk=pk: e.tensor_copy(
                    xnfm[:].rearrange("p (k tt) -> p k tt", tt=TT)
                    [:, :, c * CH:(c + 1) * CH],
                    PT[:, pk * D:(pk + 1) * D].rearrange("p (k w) -> p k w", w=CH)),
                    reads=[("PT", pk)], writes=[("xnfm", c)], tag=f"xTe_{cc}")
            P.dma("gpsimd", lambda e, t=t: e.dma_start(
                out=h2row_s[:, t * TT:(t + 1) * TT].rearrange(
                    "o (c p) -> (o p) c", p=CH),
                in_=statsH2[:, t * TCH:(t + 1) * TCH],
                allow_slow_non_contiguous=True),
                reads=[("stH", t * TCH + c2) for c2 in range(TCH)],
                writes=["h2row"], tag=f"h2r_{t}")

            for mc in range(NKC):
                bank = mc % 2
                def mm_k(e, mc=mc, bank=bank):
                    last = None
                    for kc in range(NKC):
                        last = e.matmul(PF[bank][:],
                                        kT[:, kc * D + mc * CH:kc * D + (mc + 1) * CH],
                                        dch(xnfm, kc, TT),
                                        start=(kc == 0), stop=(kc == NKC - 1))
                    return last
                P.add("tensor", mm_k,
                      reads=[("kT", kc2) for kc2 in range(NKC)]
                      + [("xnfm", c2) for c2 in range(TCH)],
                      writes=[f"PF{bank}"], tag=f"k_{t}_{mc}")
                P.add("scalar", lambda e, mc=mc, bank=bank: e.activation(
                    dch(kfm, mc, TT), PF[bank][:], AF.Copy),
                    reads=[f"PF{bank}"], writes=[("kfm", mc)], tag=f"ke_{t}_{mc}")

            for c in range(TCH):
                for vg in range(2):
                    bank = 2 + vg
                    def mm_v(e, c=c, vg=vg, bank=bank):
                        last = None
                        for kc in range(NKC):
                            last = e.matmul(
                                PF[bank][:],
                                xnfm[:, kc * TT + c * CH:kc * TT + (c + 1) * CH],
                                vT[:, kc * D + vg * 512:kc * D + (vg + 1) * 512],
                                start=(kc == 0), stop=(kc == NKC - 1))
                        return last
                    P.add("tensor", mm_v,
                          reads=[("vT", kc2) for kc2 in range(NKC)] + [("xnfm", c)],
                          writes=[f"PF{bank}"], tag=f"v_{t}_{c}_{vg}")
                    def ev_v(e, c=c, vg=vg, bank=bank):
                        dst = vext[:].rearrange("p (cc h s) -> p cc h s", h=H, s=66)[
                            :, c, vg * 8:(vg + 1) * 8, 0:64]
                        return e.tensor_copy(dst, PF[bank][:].rearrange(
                            "p (h w) -> p h w", w=64))
                    P.add("vector", ev_v, reads=[f"PF{bank}"],
                          writes=[("vext", c, vg)], tag=f"ve_{t}_{c}_{vg}")
            for c in range(TCH):
                for pr in range(H // 2):
                    bank = (c * (H // 2) + pr) % 4
                    pkk = pr % 2
                    P.add("tensor", lambda e, c=c, pr=pr, bank=bank: e.matmul(
                        PF[bank][:],
                        kfm[:, pr * TT + c * CH:pr * TT + (c + 1) * CH],
                        dch(projBD, pr, 2 * FF),
                        start=True, stop=True),
                        reads=[("kfm", pr), ("projBD", pr)],
                        writes=[f"PF{bank}"], tag=f"kf_{t}_{c}_{pr}")
                    eb = dch(ebuf, pkk, 2 * FF)
                    tb = dch(tbuf, pkk, 2 * FF)
                    P.add("scalar", lambda e, bank=bank, eb=eb: e.activation(
                        eb, PF[bank][:], AF.Exp),
                        reads=[f"PF{bank}"], writes=[("eb", pkk)],
                        tag=f"kfE_{t}_{c}_{pr}")
                    P.add("vector", lambda e, bank=bank, tb=tb: e.tensor_scalar(
                        tb, PF[bank][:], 1.0, 1.0, ALU.add, ALU.max),
                        reads=[f"PF{bank}"], writes=[("tb", pkk)],
                        tag=f"kfT_{t}_{c}_{pr}")
                    P.add("vector", lambda e, c=c, pr=pr, eb=eb, tb=tb: e.tensor_tensor(
                        kftm[:, (c * H + pr * 2) * FF:(c * H + pr * 2 + 2) * FF],
                        eb, tb, ALU.min),
                        reads=[("eb", pkk), ("tb", pkk)],
                        writes=[("kftm", c, pr)], tag=f"kfM_{t}_{c}_{pr}")
            for pr in range(H // 2):
                bank = 4 + (pr % 2)
                def mm_kv(e, pr=pr, bank=bank):
                    last = None
                    for c in range(TCH):
                        for hh in range(2):
                            h0 = pr * 2 + hh
                            last = e.matmul(
                                PF[bank][0:65, hh * FF:(hh + 1) * FF],
                                vext[:, (c * H + h0) * 66:(c * H + h0) * 66 + 65],
                                kftm[:, (c * H + h0) * FF:(c * H + h0 + 1) * FF],
                                start=(c == 0 and hh == 0),
                                stop=(c == TCH - 1 and hh == 1))
                    return last
                P.add("tensor", mm_kv,
                      reads=[("vext", c2, vg) for c2 in range(TCH) for vg in range(2)]
                      + [("kftm", c2, pr) for c2 in range(TCH)],
                      writes=[f"PF{bank}"], tag=f"kv_{t}_{pr}")
                if t == 0:
                    P.add("vector", lambda e, pr=pr, bank=bank: e.tensor_copy(
                        kvacc[:, pr * 2 * FF:(pr * 2 + 2) * FF], PF[bank][0:65, :]),
                        reads=[f"PF{bank}"], writes=[("kva", pr)], tag=f"kve_{t}_{pr}")
                else:
                    P.add("vector", lambda e, pr=pr, bank=bank: e.tensor_add(
                        kvacc[:, pr * 2 * FF:(pr * 2 + 2) * FF],
                        kvacc[:, pr * 2 * FF:(pr * 2 + 2) * FF], PF[bank][0:65, :]),
                        reads=[f"PF{bank}", ("kva", pr)], writes=[("kva", pr)],
                        tag=f"kve_{t}_{pr}")

        def ptoff(s):
            return (s % 2) * 1024 + (s // 2) * 66
        P.add("vector", lambda e: e.tensor_copy(wst2[0:65, 0:H * FF], kvacc[:]),
              reads=[("kva", pr) for pr in range(H // 2)],
              writes=[("wst2", q) for q in range(4)], tag="kv_cast")
        for h0 in range(H):
            for fc in range(2):
                sl = h0 * 2 + fc
                hf = sl % 2
                P.add("tensor", lambda e, h0=h0, fc=fc, sl=sl: e.transpose(
                    PT[:, ptoff(sl % 16):ptoff(sl % 16) + 65],
                    wst2[0:65, h0 * FF + fc * CH:h0 * FF + (fc + 1) * CH],
                    iden[0:65, 0:65]),
                    reads=[("wst2", q) for q in range(4)] + ["iden"],
                    writes=[("PT", hf)], tag=f"kvT_{h0}_{fc}")
                P.add("vector", lambda e, sl=sl: e.tensor_copy(
                    kvl[:, sl * 66:sl * 66 + 65],
                    PT[:, ptoff(sl % 16):ptoff(sl % 16) + 65]),
                    reads=[("PT", hf)], writes=["kvl"], tag=f"kvle_{sl}")
        build_nc._stash = dict(sred=sred, wst=wst)
        P.emit(nc, sems)
      except _Skip:
        pass

    with contextlib.ExitStack() as p2:
      try:
        _guard(2)
        sb2 = lambda n, s, d_: p2.enter_context(nc.sbuf_tensor(n, s, d_))
        ps2 = lambda n, s, d_: p2.enter_context(nc.psum_tensor(n, s, d_))
        qT = sb2("qT", [CH, NKC * D], BF16)
        projQ = sb2("pjq", [CH, 2 * H * CH], BF16)
        woT = sb2("woT", [CH, NKC * D], BF16)
        wst = sb2("wsb", [CH, 2 * D], F32)
        xin = sb2("xnb", [CH, 2 * D], F32)
        xpin = xin
        xntm = sb2("xtb", [CH, 2 * D], BF16)
        xnfm = sb2("xfb", [CH, 2 * NKC * TT], BF16)
        qfm = sb2("qfm", [CH, NKC * TT], BF16)
        shA = qfm
        qf = sb2("qf", [CH, 2 * H * TT], BF16)
        ebf2 = sb2("eb2", [CH, 2 * TT], BF16)
        tbf2 = sb2("tb2", [CH, 2 * TT], BF16)
        attnr = sb2("atr", [CH, 2 * D], BF16)
        z16 = sb2("z16", [CH, 16], F32)
        zr16 = sb2("zrs", [CH, 16], F32)
        attnfm = sb2("atf", [CH, NKC * TT], BF16)
        x1b = sb2("x1b", [CH, NKC * TT], BF16)
        x1sq = shA
        x1n = sb2("x1m", [CH, NKC * TT], BF16)
        rows = sb2("rws", [1, 4 * TT], F32)
        bc1 = sb2("bc1", [CH, TT], F32)
        bc2 = sb2("bc2", [CH, TT], F32)
        xq1 = sb2("xq1", [CH, TT], F32)
        xq2 = sb2("xq2", [CH, TT], F32)
        xqb = sb2("xqb", [CH, NKC * TT], BF16)
        ybf = shA
        xpart = sb2("xpt", [CH, 2 * D], F32)
        h2r = sb2("h2r", [1, TT], F32)
        PT2 = ps2("PT2", [CH, 2 * D], BF16)
        PT2b = ps2("PT2b", [CH, D], BF16)
        PF = [ps2(f"QF{i}", [CH, 512], F32) for i in range(5)]
        P = Plan(state, drop,
                 psum_keys=[f"QF{i}" for i in range(5)]
                 + [("PT2", 0), ("PT2", 1), "PT2b"])

        def row(i):
            return rows[:, i * TT:(i + 1) * TT]

        for pq in range(4):
            q = pq % 2
            ssl = wst[:, q * D:(q + 1) * D]
            P.dma("sync", lambda e, pq=pq, ssl=ssl: e.dma_start(
                out=ssl, in_=projQ_d[:, pq * D:(pq + 1) * D]),
                writes=[("wst", q)], tag=f"ldpq_{pq}")
            P.add("vector", lambda e, pq=pq, ssl=ssl: e.tensor_copy(
                projQ[:, pq * D:(pq + 1) * D], ssl),
                reads=[("wst", q)], writes=["projQ"], tag=f"cpq_{pq}")
        for kc in range(NKC):
            q = kc % 2
            ssl = wst[:, q * D:(q + 1) * D]
            P.dma("sync", lambda e, kc=kc, ssl=ssl: e.dma_start(
                out=ssl, in_=wqkvT_d[kc * CH:(kc + 1) * CH, 0:D]),
                writes=[("wst", q)], tag=f"ldq_{kc}")
            P.add("vector", lambda e, kc=kc, ssl=ssl: e.tensor_copy(
                dch(qT, kc, D), ssl),
                reads=[("wst", q)], writes=[("qT", kc)], tag=f"cq_{kc}")
        for kc in range(NKC):
            q = kc % 2
            ssl = wst[:, q * D:(q + 1) * D]
            P.dma("sync", lambda e, kc=kc, ssl=ssl: e.dma_start(
                out=ssl, in_=woutT_d[kc * CH:(kc + 1) * CH, :]),
                writes=[("wst", q)], tag=f"ldo_{kc}")
            P.add("vector", lambda e, kc=kc, ssl=ssl: e.tensor_copy(
                dch(woT, kc, D), ssl),
                reads=[("wst", q)], writes=[("woT", kc)], tag=f"co_{kc}")

        P.dma("gpsimd", lambda e: e.dma_start(out=srow[:], in_=gacc[:], allow_slow_non_contiguous=True),
              reads=["gacc"], writes=["srow"], tag="fold_g")
        P.add("vector", lambda e: e.reduce_max(scal[:, 3:4], srow[:], axis=AX),
              reads=["srow"], writes=[("scal", 3)], tag="gloc", late=True)
        P.dma("gpsimd", lambda e: e.dma_start(out=cc_in, in_=scal[:, 3:4]),
              reads=[("scal", 3)], writes=["cc_in"], tag="st_cc")
        P.add("gpsimd", lambda e: e.collective_compute(
            "AllReduce", ALU.max, replica_groups=[list(range(NC))],
            ins=[cc_in], outs=[cc_out]),
            reads=["cc_in"], writes=["cc_out"], kind="cc", tag="ar_gamma")
        P.dma("gpsimd", lambda e: e.dma_start(out=scal[:, 4:5], in_=cc_out),
              reads=["cc_out"], writes=[("scal", 4)], tag="ld_cc")
        P.add("vector", lambda e: e.reciprocal(scal[:, 5:6], scal[:, 4:5]),
              reads=[("scal", 4)], writes=[("scal", 5)], tag="rg0", late=True)
        P.add("vector", lambda e: e.tensor_scalar_mul(scal[:, 6:7], scal[:, 5:6], QB),
              reads=[("scal", 5)], writes=[("scal", 6)], tag="rgam")
        P.add("vector", lambda e: e.tensor_mul(scal[:, 7:8], scal[:, 4:5], scal[:, 0:1]),
              reads=[("scal", 4), ("scal", 0)], writes=[("scal", 7)], tag="gb")
        P.add("vector", lambda e: e.tensor_scalar_mul(
            scal[:, 8:9], scal[:, 7:8], -1.0 / (QB * D * D)),
            reads=[("scal", 7)], writes=[("scal", 8)], tag="nys")
        P.add("tensor", lambda e: e.matmul(PF[0][:, 0:4], onesr[:], scal[:, 5:9],
                                           start=True, stop=True),
              reads=[("scal", 6), ("scal", 8), "onesr"], writes=["QF0"], tag="bc_g")
        P.add("vector", lambda e: e.tensor_copy(gsc[:, 0:1], PF[0][:, 1:2]),
              reads=["QF0"], writes=[("gsc", 0)], tag="ev_rg")
        P.add("vector", lambda e: e.tensor_copy(gsc[:, 1:2], PF[0][:, 3:4]),
              reads=["QF0"], writes=[("gsc", 1)], tag="ev_ny")
        P.add("vector", lambda e: e.tensor_single_scalar(
            grow[:], onesr[:], scal[:, 6:7], ALU.mult),
            reads=[("scal", 6), "onesr"], writes=["grow"], tag="grow")
        P.add("vector", lambda e: e.tensor_copy(lngyc[:], LNG),
              reads=["vecs"], writes=["lngyc"], tag="lngyc")
        P.add("vector", lambda e: e.tensor_single_scalar(
            lnbyc[:], LNB, gsc[:, 0:1], ALU.mult),
            reads=["vecs", ("gsc", 0)], writes=["lnbyc"], tag="lnbyc")

        def ln1_tail(t):
                def mm_s1(e):
                    last = None
                    for kc in range(NKC):
                        last = e.matmul(PF[4][0:1, :], onesc[:], dch(x1b, kc, TT),
                                        start=(kc == 0), stop=(kc == NKC - 1))
                    return last
                P.add("tensor", mm_s1,
                      reads=[("x1b", kc2) for kc2 in range(NKC)],
                      writes=["QF4"], tag=f"s1_{t}")
                P.add("vector", lambda e: e.tensor_scalar_mul(
                    row(0), PF[4][0:1, :], 1.0 / D),
                    reads=["QF4"], writes=[("rw", 0)], tag=f"s1e_{t}")
                def mm_s2(e):
                    last = None
                    for kc in range(NKC):
                        last = e.matmul(PF[4][0:1, :], onesc[:], dch(x1sq, kc, TT),
                                        start=(kc == 0), stop=(kc == NKC - 1))
                    return last
                P.add("tensor", mm_s2,
                      reads=[("shA", kc2) for kc2 in range(NKC)],
                      writes=["QF4"], tag=f"s2_{t}")
                P.add("vector", lambda e: e.tensor_scalar_mul(
                    row(1), PF[4][0:1, :], 1.0 / D),
                    reads=["QF4"], writes=[("rw", 1)], tag=f"s2e_{t}")
                P.add("vector", lambda e: e.tensor_mul(row(3), row(0), row(0)),
                      reads=[("rw", 0)], writes=[("rw", 3)], tag=f"v2_{t}")
                P.add("vector", lambda e: e.tensor_sub(row(2), row(1), row(3)),
                      reads=[("rw", 1), ("rw", 3)], writes=[("rw", 2)], tag=f"v3_{t}")
                P.add("scalar", lambda e: e.activation(
                    row(3), row(2), AF.Ln, bias=EPS1r),
                    reads=[("rw", 2)], writes=[("rw", 3)], tag=f"v4_{t}")
                P.add("scalar", lambda e: e.activation(
                    row(2), row(3), AF.Exp, scale=-0.5),
                    reads=[("rw", 3)], writes=[("rw", 2)], tag=f"r1_{t}")
                P.add("vector", lambda e: e.scalar_tensor_tensor(
                    row(3), row(0), -1.0, row(2), ALU.mult, ALU.mult),
                    reads=[("rw", 0), ("rw", 2)], writes=[("rw", 3)], tag=f"q1_{t}")
                P.add("tensor", lambda e: e.matmul(PF[4][:], onesr[:], row(2),
                                                   start=True, stop=True),
                      reads=[("rw", 2)], writes=["QF4"], tag=f"b1_{t}")
                P.add("vector", lambda e: e.tensor_copy(bc1[:], PF[4][:]),
                      reads=["QF4"], writes=["bc1"], tag=f"b1e_{t}")
                P.add("tensor", lambda e: e.matmul(PF[4][:], onesr[:], row(3),
                                                   start=True, stop=True),
                      reads=[("rw", 3)], writes=["QF4"], tag=f"b2_{t}")
                P.add("vector", lambda e: e.tensor_copy(bc2[:], PF[4][:]),
                      reads=["QF4"], writes=["bc2"], tag=f"b2e_{t}")
                for mc in range(NKC):
                    P.add("vector", lambda e, mc=mc: e.tensor_mul(
                        xq2[:], dch(x1b, mc, TT), bc1[:]),
                        reads=[("x1b", mc), "bc1"], writes=["xq2"], tag=f"n1_{t}_{mc}")
                    P.add("vector", lambda e: e.tensor_add(xq2[:], xq2[:], bc2[:]),
                          reads=["xq2", "bc2"], writes=["xq2"], tag=f"n2_{t}_{mc}")
                    P.add("vector", lambda e, mc=mc: e.tensor_scalar(
                        dch(x1n, mc, TT), xq2[:], GAM[:, mc:mc + 1], BET[:, mc:mc + 1],
                        ALU.mult, ALU.add),
                        reads=["xq2"], writes=[("x1n", mc)], tag=f"n3_{t}_{mc}")
                    P.dma("gpsimd", lambda e, mc=mc, t=t: e.dma_start(
                        out=x1n_s[mc * CH:(mc + 1) * CH, t * TT:(t + 1) * TT],
                        in_=dch(x1n, mc, TT)),
                        reads=[("x1n", mc)], writes=[("x1ns", t, mc)], tag=f"nS_{t}_{mc}")

        for t in range(NTL):
            xo = (t % 2) * NKC * TT
            xnk = t % 2

            def xch(kc, xo=None, t=t):
                o = (t % 2) * NKC * TT
                return xnfm[:, o + kc * TT:o + (kc + 1) * TT]
            for c in range(TCH):
                cc = t * TCH + c
                pk = c % 2
                xs = dch(xin, pk, D)
                P.dma("sync", lambda e, cc=cc, xs=xs: e.dma_start(
                    out=xs, in_=x_d[cc * CH:(cc + 1) * CH, :]),
                    writes=[("xin", pk)], tag=f"p2x_{cc}")
                P.add("scalar", lambda e, xs=xs, pk=pk, cc=cc: e.activation(
                    dch(xntm, pk, D), xs, AF.Identity,
                    bias=statsMUR[:, cc:cc + 1], scale=statsR[:, cc:cc + 1]),
                    reads=[("xin", pk)], writes=[("xntm", pk)], tag=f"p2n_{cc}")
                for kc in range(NKC):
                    P.add("tensor", lambda e, kc=kc, pk=pk: e.transpose(
                        PT2b[:, kc * CH:(kc + 1) * CH],
                        xntm[:, pk * D + kc * CH:pk * D + (kc + 1) * CH], iden[:]),
                        reads=[("xntm", pk)], writes=["PT2b"],
                        tag=f"p2T_{cc}_{kc}")
                P.add("vector", lambda e, c=c, pk=pk, xo=xo: e.tensor_copy(
                    xnfm[:, xo:xo + NKC * TT].rearrange("p (k tt) -> p k tt", tt=TT)
                    [:, :, c * CH:(c + 1) * CH],
                    PT2b[:].rearrange("p (k w) -> p k w", w=CH)),
                    reads=["PT2b"], writes=[("xnfm", xnk, c)], tag=f"p2e_{cc}")
            if t > 0:
                ln1_tail(t - 1)
            P.dma("sync", lambda e, t=t: e.dma_start(
                out=h2r[:], in_=h2row_s[:, t * TT:(t + 1) * TT]),
                writes=["h2r"], tag=f"fh_{t}")
            P.add("tensor", lambda e: e.matmul(PF[4][:], grow[:], h2r[:],
                                               start=True, stop=True),
                  reads=["h2r"], writes=["QF4"], tag=f"fb_{t}")
            for mc in range(NKC):
                P.add("vector", lambda e, mc=mc, xch=xch: e.scalar_tensor_tensor(
                    xq1[:], xch(mc), lngyc[:, mc:mc + 1], PF[4][:],
                    ALU.mult, ALU.mult),
                    reads=[("xnfm", xnk, c2) for c2 in range(TCH)] + ["QF4"],
                    writes=["xq1"], tag=f"f1_{t}_{mc}")
                P.add("vector", lambda e, mc=mc: e.tensor_scalar(
                    xq1[:], xq1[:], lnbyc[:, mc:mc + 1], CLIP, ALU.add, ALU.min),
                    reads=["xq1"], writes=["xq1"], tag=f"f2_{t}_{mc}")
                P.add("vector", lambda e, mc=mc: e.tensor_scalar_max(
                    dch(xqb, mc, TT), xq1[:], -CLIP),
                    reads=["xq1"], writes=[("xqb", mc)], tag=f"f3_{t}_{mc}")
            for mc in range(NKC):
                bank = mc % 2
                def mm_q(e, mc=mc, bank=bank, xch=xch):
                    last = None
                    for kc in range(NKC):
                        last = e.matmul(PF[bank][:],
                                        qT[:, kc * D + mc * CH:kc * D + (mc + 1) * CH],
                                        xch(kc),
                                        start=(kc == 0), stop=(kc == NKC - 1))
                    return last
                P.add("tensor", mm_q,
                      reads=[("qT", kc2) for kc2 in range(NKC)]
                      + [("xnfm", xnk, c2) for c2 in range(TCH)],
                      writes=[f"QF{bank}"], tag=f"q_{t}_{mc}")
                P.add("scalar", lambda e, mc=mc, bank=bank: e.activation(
                    dch(qfm, mc, TT), PF[bank][:], AF.Copy),
                    reads=[f"QF{bank}"], writes=[("shA", mc)], tag=f"qe_{t}_{mc}")
            for fc in range(2):
                for h0 in range(H):
                    pr, hh = h0 // 2, h0 % 2
                    base = hh * 64
                    sl = h0 * 2 + fc
                    bank = (fc * H + h0) % 4
                    P.add("tensor",
                          lambda e, pr=pr, sl=sl, bank=bank: e.matmul(
                              PF[bank][:],
                              dch(projQ, sl, CH),
                              dch(qfm, pr, TT),
                              start=True, stop=True),
                          reads=[("shA", pr), "projQ"], writes=[f"QF{bank}"],
                          tag=f"qf_{t}_{h0}_{fc}")
                    pe2 = sl % 2
                    P.add("scalar", lambda e, bank=bank, pe2=pe2: e.activation(
                        dch(ebf2, pe2, TT), PF[bank][:], AF.Exp),
                        reads=[f"QF{bank}"], writes=[("ebf2", pe2)],
                        tag=f"qfE_{t}_{h0}_{fc}")
                    P.add("vector", lambda e, bank=bank, pe2=pe2: e.tensor_scalar(
                        dch(tbf2, pe2, TT), PF[bank][:], 1.0, 1.0, ALU.add, ALU.max),
                        reads=[f"QF{bank}"], writes=[("tbf2", pe2)],
                        tag=f"qfT_{t}_{h0}_{fc}")
                    P.add("vector", lambda e, sl=sl, pe2=pe2: e.tensor_tensor(
                        qf[:, sl * TT:(sl + 1) * TT], dch(ebf2, pe2, TT),
                        dch(tbf2, pe2, TT), ALU.min),
                        reads=[("ebf2", pe2), ("tbf2", pe2)], writes=[("qf", sl)],
                        tag=f"qfM_{t}_{h0}_{fc}")
            for c in range(TCH):
                pk = c % 2
                for bl in range(4):
                    bank = 2 + bl % 2
                    def mm_a(e, c=c, bl=bl, bank=bank):
                        last = None
                        for s in range(4):
                            h0 = bl * 4 + s
                            for fc in range(2):
                                sl = h0 * 2 + fc
                                last = e.matmul(
                                    PF[bank][:, s * CH:s * CH + 65],
                                    qf[:, sl * TT + c * CH:sl * TT + (c + 1) * CH],
                                    kvl[:, sl * 66:sl * 66 + 65],
                                    start=(fc == 0), stop=(fc == 1))
                        return last
                    P.add("tensor", mm_a,
                          reads=[("qf", (bl * 4 + s) * 2 + fc)
                                 for s in range(4) for fc in range(2)],
                          writes=[f"QF{bank}"], tag=f"a_{t}_{c}_{bl}")
                    P.add("scalar", lambda e, bl=bl, bank=bank, pk=pk: e.activation(
                        attnr[:, pk * D + bl * 4 * 64:pk * D + (bl + 1) * 4 * 64]
                        .rearrange("p (s w) -> p s w", w=64),
                        PF[bank][:].rearrange("p (s w) -> p s w", w=CH)[:, :, 0:64],
                        AF.Copy),
                        reads=[f"QF{bank}"], writes=[("atr", pk)],
                        tag=f"ae_{t}_{c}_{bl}")
                    P.add("vector", lambda e, bl=bl, bank=bank: e.tensor_scalar_add(
                        z16[:, bl * 4:(bl + 1) * 4].rearrange(
                            "p (s o) -> p s o", o=1),
                        PF[bank][:].rearrange("p (s w) -> p s w", w=CH)[:, :, 64:65],
                        EPS_ATTN),
                        reads=[f"QF{bank}"], writes=[("z16", bl)],
                        tag=f"az_{t}_{c}_{bl}")
                P.add("vector", lambda e: e.reciprocal(zr16[:], z16[:]),
                      reads=[("z16", bl2) for bl2 in range(4)], writes=["zr16"],
                      tag=f"ar_{t}_{c}", late=True)
                P.add("vector", lambda e, pk=pk: e.tensor_tensor(
                    dch(attnr, pk, D).rearrange("p (h w) -> p h w", w=64),
                    dch(attnr, pk, D).rearrange("p (h w) -> p h w", w=64),
                    zr16[:].rearrange("p (h o) -> p h o", o=1).broadcast_to(
                        (CH, H, 64)), ALU.mult),
                    reads=["zr16", ("atr", pk)], writes=[("atr", pk)],
                    tag=f"as_{t}_{c}")
                for kc in range(NKC):
                    P.add("tensor", lambda e, kc=kc, pk=pk: e.transpose(
                        PT2[:, pk * D + kc * CH:pk * D + (kc + 1) * CH],
                        attnr[:, pk * D + kc * CH:pk * D + (kc + 1) * CH], iden[:]),
                        reads=[("atr", pk)], writes=[("PT2", pk)],
                        tag=f"aT_{t}_{c}_{kc}")
                P.add("vector", lambda e, c=c, pk=pk: e.tensor_copy(
                    attnfm[:].rearrange("p (k tt) -> p k tt", tt=TT)
                    [:, :, c * CH:(c + 1) * CH],
                    PT2[:, pk * D:(pk + 1) * D].rearrange("p (k w) -> p k w", w=CH)),
                    reads=[("PT2", pk)], writes=[("atf", c)], tag=f"aTe_{t}_{c}")
            for mc in range(NKC):
                bank = mc % 2
                def mm_y(e, mc=mc, bank=bank):
                    last = None
                    for kc in range(NKC):
                        last = e.matmul(PF[bank][:],
                                        mwq[:, kc * D + mc * CH:kc * D + (mc + 1) * CH],
                                        dch(xqb, kc, TT),
                                        start=(kc == 0), stop=(kc == NKC - 1))
                    return last
                P.add("tensor", mm_y,
                      reads=[("xqb", kc2) for kc2 in range(NKC)],
                      writes=[f"QF{bank}"], tag=f"y_{t}_{mc}")
                P.add("scalar", lambda e, mc=mc, bank=bank: e.activation(
                    dch(ybf, mc, TT), PF[bank][:], AF.Copy),
                    reads=[f"QF{bank}"], writes=[("shA", mc)], tag=f"ye_{t}_{mc}")
            for c in range(TCH):
                cc = t * TCH + c
                pk = c % 2
                for kc in range(NKC):
                    P.add("tensor", lambda e, kc=kc, pk=pk, c=c: e.transpose(
                        PT2[:, pk * D + kc * CH:pk * D + (kc + 1) * CH],
                        ybf[:, kc * TT + c * CH:kc * TT + (c + 1) * CH], iden[:]),
                        reads=[("shA", kc)], writes=[("PT2", pk)],
                        tag=f"yT_{t}_{c}_{kc}")
                xs = dch(xpin, pk, D)
                P.dma("sync", lambda e, cc=cc, xs=xs: e.dma_start(
                    out=xs, in_=x_d[cc * CH:(cc + 1) * CH, :]),
                    writes=[("xin", pk)], tag=f"px_{cc}")
                P.add("vector", lambda e, pk=pk: e.scalar_tensor_tensor(
                    dch(xpart, pk, D), PT2[:, pk * D:(pk + 1) * D], gsc[:, 1:2],
                    dch(xpin, pk, D), ALU.mult, ALU.add),
                    reads=[("PT2", pk), ("xin", pk)],
                    writes=[("xpart", pk)], tag=f"pp_{cc}")
                P.dma("scalar", lambda e, cc=cc, pk=pk: e.dma_start(
                    out=part_s[cc * CH:(cc + 1) * CH, :], in_=dch(xpart, pk, D)),
                    reads=[("xpart", pk)], writes=[("parts", cc)], tag=f"ps_{cc}")
            for mc in range(NKC):
                bank = mc % 2
                def mm_o(e, mc=mc, bank=bank):
                    last = None
                    for kc in range(NKC):
                        last = e.matmul(PF[bank][:],
                                        woT[:, kc * D + mc * CH:kc * D + (mc + 1) * CH],
                                        dch(attnfm, kc, TT),
                                        start=(kc == 0), stop=(kc == NKC - 1))
                    return last
                P.add("tensor", mm_o,
                      reads=[("woT", kc2) for kc2 in range(NKC)]
                      + [("atf", c2) for c2 in range(TCH)],
                      writes=[f"QF{bank}"], tag=f"o_{t}_{mc}")
                P.add("vector", lambda e, mc=mc, bank=bank, xch=xch: e.scalar_tensor_tensor(
                    dch(x1b, mc, TT), PF[bank][:], B_OUT[:, mc:mc + 1],
                    xch(mc), ALU.add, ALU.add),
                    reads=[f"QF{bank}"] + [("xnfm", xnk, c2) for c2 in range(TCH)],
                    writes=[("x1b", mc)], tag=f"x1_{t}_{mc}")
                P.add("scalar", lambda e, mc=mc: e.activation(
                    dch(x1sq, mc, TT), dch(x1b, mc, TT), AF.Square),
                    reads=[("x1b", mc)], writes=[("shA", mc)], tag=f"x1q_{t}_{mc}")
        ln1_tail(NTL - 1)
        P.emit(nc, sems)
      except _Skip:
        pass

    with contextlib.ExitStack() as p3:
      try:
        _guard(3)
        sb3 = lambda n, s, d_: p3.enter_context(nc.sbuf_tensor(n, s, d_))
        ps3 = lambda n, s, d_: p3.enter_context(nc.psum_tensor(n, s, d_))
        w2r = sb3("w2r", [CH, NMF * D], BF16)
        w1b = sb3("w1b", [CH, 2 * D], BF16)
        x1nb = sb3("x1v", [CH, 2 * NKC * TT], BF16)
        hbuf = sb3("hbf", [CH, NMF * TT], BF16)
        x2b = sb3("x2b", [CH, NKC * TT], BF16)
        x2sq = sb3("x2q", [CH, NKC * TT], BF16)
        aob = sb3("aob", [CH, NKC * TT], BF16)
        rows = sb3("rw3", [1, 4 * TT], F32)
        bc1 = sb3("bc3", [CH, TT], F32)
        bc2 = sb3("bc4", [CH, TT], F32)
        tsc = sb3("tsc", [CH, TT], F32)
        xpart = sb3("xp3", [CH, 2 * D], F32)
        outb = sb3("oub", [CH, 2 * D], F32)
        PT3 = ps3("PT3", [CH, 2 * D], BF16)
        PF = [ps3(f"RF{i}", [CH, 512], F32) for i in range(5)]
        P = Plan(state, drop,
                 psum_keys=[f"RF{i}" for i in range(5)] + [("PT3", 0), ("PT3", 1)])

        def row(i):
            return rows[:, i * TT:(i + 1) * TT]

        for kc in range(NMF):
            P.dma("sync", lambda e, kc=kc: e.dma_start(
                out=dch(w2r, kc, D), in_=w2s_s[kc * CH:(kc + 1) * CH, :]),
                writes=[("w2r", kc)], tag=f"lw2_{kc}")

        def ln2_tail(t):
                def mm_t1(e):
                    last = None
                    for kc in range(NKC):
                        last = e.matmul(PF[4][0:1, :], onesc[:], dch(x2b, kc, TT),
                                        start=(kc == 0), stop=(kc == NKC - 1))
                    return last
                P.add("tensor", mm_t1, reads=[("x2b", kc2) for kc2 in range(NKC)],
                      writes=["RF4"], tag=f"t1_{t}")
                P.add("vector", lambda e: e.tensor_scalar_mul(
                    row(0), PF[4][0:1, :], 1.0 / D),
                    reads=["RF4"], writes=[("rw", 0)], tag=f"t1e_{t}")
                def mm_t2(e):
                    last = None
                    for kc in range(NKC):
                        last = e.matmul(PF[4][0:1, :], onesc[:], dch(x2sq, kc, TT),
                                        start=(kc == 0), stop=(kc == NKC - 1))
                    return last
                P.add("tensor", mm_t2, reads=[("x2sq", kc2) for kc2 in range(NKC)],
                      writes=["RF4"], tag=f"t2_{t}")
                P.add("vector", lambda e: e.tensor_scalar_mul(
                    row(1), PF[4][0:1, :], 1.0 / D),
                    reads=["RF4"], writes=[("rw", 1)], tag=f"t2e_{t}")
                P.add("vector", lambda e: e.tensor_mul(row(3), row(0), row(0)),
                      reads=[("rw", 0)], writes=[("rw", 3)], tag=f"u2_{t}")
                P.add("vector", lambda e: e.tensor_sub(row(2), row(1), row(3)),
                      reads=[("rw", 1), ("rw", 3)], writes=[("rw", 2)], tag=f"u3_{t}")
                P.add("scalar", lambda e: e.activation(
                    row(3), row(2), AF.Ln, bias=EPS1r),
                    reads=[("rw", 2)], writes=[("rw", 3)], tag=f"u4_{t}")
                P.add("scalar", lambda e: e.activation(
                    row(2), row(3), AF.Exp, scale=-0.5),
                    reads=[("rw", 3)], writes=[("rw", 2)], tag=f"r2_{t}")
                P.add("vector", lambda e: e.scalar_tensor_tensor(
                    row(3), row(0), -1.0, row(2), ALU.mult, ALU.mult),
                    reads=[("rw", 0), ("rw", 2)], writes=[("rw", 3)], tag=f"q2_{t}")
                P.add("tensor", lambda e: e.matmul(PF[4][:], onesr[:], row(2),
                                                   start=True, stop=True),
                      reads=[("rw", 2)], writes=["RF4"], tag=f"w1b_{t}")
                P.add("vector", lambda e: e.tensor_copy(bc1[:], PF[4][:]),
                      reads=["RF4"], writes=["bc1"], tag=f"w1e_{t}")
                P.add("tensor", lambda e: e.matmul(PF[4][:], onesr[:], row(3),
                                                   start=True, stop=True),
                      reads=[("rw", 3)], writes=["RF4"], tag=f"w2b_{t}")
                P.add("vector", lambda e: e.tensor_copy(bc2[:], PF[4][:]),
                      reads=["RF4"], writes=["bc2"], tag=f"w2e_{t}")
                for mc in range(NKC):
                    P.add("vector", lambda e, mc=mc: e.tensor_mul(
                        tsc[:], dch(x2b, mc, TT), bc1[:]),
                        reads=[("x2b", mc), "bc1"], writes=["tsc"], tag=f"m1_{t}_{mc}")
                    P.add("vector", lambda e: e.tensor_add(tsc[:], tsc[:], bc2[:]),
                          reads=["tsc", "bc2"], writes=["tsc"], tag=f"m2_{t}_{mc}")
                    P.add("vector", lambda e, mc=mc: e.tensor_scalar(
                        dch(aob, mc, TT), tsc[:], GAM[:, mc:mc + 1], BET[:, mc:mc + 1],
                        ALU.mult, ALU.add),
                        reads=["tsc"], writes=[("aob", mc)], tag=f"m3_{t}_{mc}")
                for c in range(TCH):
                    cc = t * TCH + c
                    pk = c % 2
                    for kc in range(NKC):
                        P.add("tensor", lambda e, kc=kc, pk=pk, c=c: e.transpose(
                            PT3[:, pk * D + kc * CH:pk * D + (kc + 1) * CH],
                            aob[:, kc * TT + c * CH:kc * TT + (c + 1) * CH], iden[:]),
                            reads=[("aob", kc)], writes=[("PT3", pk)],
                            tag=f"oT_{t}_{c}_{kc}")
                    xs = dch(xpart, pk, D)
                    P.dma("gpsimd", lambda e, cc=cc, xs=xs: e.dma_start(
                        out=xs, in_=part_s[cc * CH:(cc + 1) * CH, :]),
                        writes=[("xp3", pk)], tag=f"lp_{cc}")
                    P.add("vector", lambda e, pk=pk: e.tensor_add(
                        dch(outb, pk, D), PT3[:, pk * D:(pk + 1) * D], dch(xpart, pk, D)),
                        reads=[("PT3", pk), ("xp3", pk)], writes=[("oub", pk)],
                        tag=f"os_{cc}")
                    P.dma("gpsimd", lambda e, cc=cc, pk=pk: e.dma_start(
                        out=out_d[cc * CH:(cc + 1) * CH, :], in_=dch(outb, pk, D)),
                        reads=[("oub", pk)], writes=[("outd", cc)], tag=f"ow_{cc}")
        for t in range(NTL):
            tb = t % 2
            for mc in range(NKC):
                P.dma("sync", lambda e, t=t, mc=mc, tb=tb: e.dma_start(
                    out=x1nb[:, (tb * NKC + mc) * TT:(tb * NKC + mc + 1) * TT],
                    in_=x1n_s[mc * CH:(mc + 1) * CH, t * TT:(t + 1) * TT]),
                    writes=[("x1v", tb, mc)], tag=f"lx1_{t}_{mc}")
            if t > 0:
                ln2_tail(t - 1)
            for m in range(NMF):
                wb = m % 2
                bank = m % 2
                P.dma("sync", lambda e, m=m, wb=wb: e.dma_start(
                    out=dch(w1b, wb, D), in_=w1s_s[m]),
                    writes=[("w1b", wb)], tag=f"lw1_{t}_{m}")
                def mm_h(e, m=m, wb=wb, bank=bank, tb=tb):
                    last = None
                    for kc in range(NKC):
                        last = e.matmul(
                            PF[bank][:],
                            w1b[:, wb * D + kc * CH:wb * D + (kc + 1) * CH],
                            x1nb[:, (tb * NKC + kc) * TT:(tb * NKC + kc + 1) * TT],
                            start=(kc == 0), stop=(kc == NKC - 1))
                    return last
                P.add("tensor", mm_h,
                      reads=[("w1b", wb)] + [("x1v", tb, kc2) for kc2 in range(NKC)],
                      writes=[f"RF{bank}"], tag=f"h_{t}_{m}")
                P.add("scalar", lambda e, m=m, bank=bank: e.activation(
                    dch(hbuf, m, TT), PF[bank][:], AF.Gelu, bias=b1c[:, m:m + 1]),
                    reads=[f"RF{bank}"], writes=[("h", m)], tag=f"he_{t}_{m}")
            for mc in range(NKC):
                bank = 2 + mc % 2
                def mm_x2(e, mc=mc, bank=bank):
                    last = None
                    for kc in range(NMF):
                        last = e.matmul(
                            PF[bank][:],
                            w2r[:, kc * D + mc * CH:kc * D + (mc + 1) * CH],
                            dch(hbuf, kc, TT),
                            start=(kc == 0), stop=(kc == NMF - 1))
                    return last
                P.add("tensor", mm_x2,
                      reads=[("w2r", kc2) for kc2 in range(NMF)]
                      + [("h", kc2) for kc2 in range(NMF)],
                      writes=[f"RF{bank}"], tag=f"x2_{t}_{mc}")
                P.add("vector",
                      lambda e, mc=mc, bank=bank, tb=tb: e.scalar_tensor_tensor(
                          dch(x2b, mc, TT), PF[bank][:], B2[:, mc:mc + 1],
                          x1nb[:, (tb * NKC + mc) * TT:(tb * NKC + mc + 1) * TT],
                          ALU.add, ALU.add),
                      reads=[f"RF{bank}", ("x1v", tb, mc)],
                      writes=[("x2b", mc)], tag=f"x2e_{t}_{mc}")
                P.add("scalar", lambda e, mc=mc: e.activation(
                    dch(x2sq, mc, TT), dch(x2b, mc, TT), AF.Square),
                    reads=[("x2b", mc)], writes=[("x2sq", mc)], tag=f"x2q_{t}_{mc}")
        ln2_tail(NTL - 1)
        P.emit(nc, sems)
      except _Skip:
        pass

    if debug:
        P = Plan(state)
        P.dma("sync", lambda e: e.dma_start(out=dbg["scal"], in_=scal[:]), tag="dbg_scal")
        P.dma("sync", lambda e: e.dma_start(
            out=dbg["gsc"], in_=build_nc._stash["sred"][:]), tag="dbg_gsc")
        P.dma("sync", lambda e: e.dma_start(out=dbg["kvl"], in_=kvl[:]), tag="dbg_kvl")
        P.dma("sync", lambda e: e.dma_start(out=dbg["x1n"], in_=x1n_s), tag="dbg_x1n")
        P.dma("sync", lambda e: e.dma_start(out=dbg["part"], in_=part_s), tag="dbg_part")
        P.dma("sync", lambda e: e.dma_start(out=dbg["h2row"], in_=h2row_s), tag="dbg_h2row")
        P.dma("sync", lambda e: e.dma_start(out=dbg["stH"], in_=statsH2[:]), tag="dbg_stH")
        P.dma("sync", lambda e: e.dma_start(out=dbg["stR"], in_=statsR[:]), tag="dbg_stR")
        P.dma("sync", lambda e: e.dma_start(out=dbg["stM"], in_=statsMUR[:]), tag="dbg_stM")
        P.dma("sync", lambda e: e.dma_start(out=dbg["mwq"], in_=mwq[:]), tag="dbg_mwq")
        P.emit(nc, sems)

    es.close()
    return nc


def _shard_inputs(L0_attn_x, L0_attn_proj, L0_attn_w_qkv, L0_attn_w_out,
                  L0_attn_b_out, L0_attn_gamma, L0_attn_beta, L0_attn_w1,
                  L0_attn_b1, L0_attn_w2, L0_attn_b2, L0_ffn_weight,
                  L0_ffn_ln_gamma, L0_ffn_ln_beta, NT):
    import ml_dtypes
    f32 = lambda a: np.ascontiguousarray(np.asarray(a, dtype=np.float32))
    x = f32(L0_attn_x)
    wqkvT = np.ascontiguousarray(f32(L0_attn_w_qkv).T)
    pj = f32(L0_attn_proj)
    projP = np.zeros((H // 2, CH, 2 * FF), np.float32)
    for pr in range(H // 2):
        projP[pr, 0:64, 0:FF] = pj[2 * pr]
        projP[pr, 64:128, FF:2 * FF] = pj[2 * pr + 1]
    projQ = np.zeros((CH, 2 * H * CH), np.float32)
    for h0 in range(H):
        for fc in range(2):
            sl = h0 * 2 + fc
            projQ[(h0 % 2) * 64:(h0 % 2) * 64 + 64, sl * CH:(sl + 1) * CH] = \
                pj[h0][:, fc * CH:(fc + 1) * CH]
    woutT = np.ascontiguousarray(f32(L0_attn_w_out).T)
    w1T = np.ascontiguousarray(f32(L0_attn_w1).T)
    w2T = np.ascontiguousarray(f32(L0_attn_w2).T)
    fw = f32(L0_ffn_weight)
    vecs = np.zeros((CH, 64), np.float32)
    for i, v in enumerate((L0_attn_b_out, L0_attn_gamma, L0_attn_beta,
                           L0_attn_b2, L0_ffn_ln_gamma, L0_ffn_ln_beta)):
        vecs[:, i * 8:(i + 1) * 8] = f32(v).reshape(NKC, CH).T
    b1c = np.ascontiguousarray(f32(L0_attn_b1).reshape(NMF, CH).T)
    vecs[:, 48] = EPS_LN
    vecs[:, 49] = LN_EPS_FFN
    lngbf = np.ascontiguousarray(np.broadcast_to(f32(L0_ffn_ln_gamma), (CH, D)))
    lnbbf = np.ascontiguousarray(np.broadcast_to(f32(L0_ffn_ln_beta), (CH, D)))
    iden = np.eye(CH, dtype=ml_dtypes.bfloat16)
    onesr = np.ones((1, CH), np.float32)
    onescb = np.ones((CH, 1), ml_dtypes.bfloat16)
    shared = dict(wqkvT=wqkvT, projP=projP, projQ=projQ, woutT=woutT, w1T=w1T, w2T=w2T,
                  fw=fw, vecs=vecs, b1c=b1c, lngbf=lngbf, lnbbf=lnbbf,
                  iden=iden, onesr=onesr, onescb=onescb)
    return [dict(shared, x=np.ascontiguousarray(x[i, :NT])) for i in range(NC)]


def kernel(**inputs):
    NT = 8192
    in_maps = _shard_inputs(NT=NT, **inputs)
    nc = build_nc(NT=NT)
    res = run_bass_kernel_spmd(nc, in_maps, core_ids=list(range(NC)))
    return np.stack([np.asarray(r["out"]) for r in res.results]).astype(np.float32)


# version 13
# speedup vs baseline: 1.2380x; 1.0122x over previous
import contextlib
import numpy as np
import concourse.bass as bass
import concourse.mybir as mybir
from concourse.bass_utils import run_bass_kernel_spmd

F32 = mybir.dt.float32
BF16 = mybir.dt.bfloat16
AF = mybir.ActivationFunctionType
ALU = mybir.AluOpType
AX = mybir.AxisListType.X

D = 1024
H = 16
FF = 256
DFF = 4096
NC = 8
CH = 128
NKC = D // CH
NMF = DFF // CH

EPS_LN = 1e-5
EPS_ATTN = 1e-8
LN_EPS_FFN = 1e-3
EPS_FFN = 1e-5
QB = 128.0
CLIP = QB - EPS_FFN

TT = 512
TCH = TT // CH


class Plan:

    ENGS = ("sync", "tensor", "vector", "scalar", "gpsimd")

    NLANES = {"sync": 8, "scalar": 6, "gpsimd": 4}

    def __init__(self, state, drop=(), psum_keys=()):
        self.state = state
        self.drop = tuple(drop)
        self.psum_keys = set(psum_keys)
        self.ops = []
        self.last_w = {}
        self.readers = {}

    def add(self, eng, emit, reads=(), writes=(), kind="compute", tag="",
            late=False):
        if any(tag.startswith(p) for p in self.drop):
            return None
        idx = len(self.ops)
        deps = set()
        for b in reads:
            w = self.last_w.get(b)
            if w is not None:
                deps.add(w)
            if b in self.psum_keys:
                for r in self.readers.get(b, ()):
                    if self.ops[r]["eng"] != eng:
                        deps.add(r)
        for b in writes:
            w = self.last_w.get(b)
            if w is not None:
                deps.add(w)
            for r in self.readers.get(b, ()):
                deps.add(r)
        for b in reads:
            self.readers.setdefault(b, []).append(idx)
        for b in writes:
            self.last_w[b] = idx
            self.readers[b] = []
        deps.discard(idx)
        lane = None
        if kind == "dma":
            lc = self.state.setdefault("lanecnt", {})
            lc[eng] = lc.get(eng, 0) + 1
            lane = lc[eng] % self.NLANES[eng]
        self.ops.append(dict(eng=eng, emit=emit, kind=kind, deps=deps, tag=tag,
                             late=late, inc=(kind != "compute"), incv=None,
                             waits=None, lane=lane, idx=idx))
        return idx

    def dma(self, eng, emit, reads=(), writes=(), tag=""):
        return self.add(eng, emit, reads, writes, kind="dma", tag=tag)

    def _sname(self, op):
        if op["kind"] == "dma":
            return "q_%s%d" % (op["eng"], op["lane"])
        if op["kind"] == "cc":
            return "cc"
        return "e_" + op["eng"]

    def _nowait(self, dep, op):
        if dep.get("late"):
            return False
        if dep["eng"] != op["eng"]:
            return False
        if dep["kind"] == "dma" or dep["kind"] == "cc":
            return dep["kind"] == op["kind"]
        return dep["eng"] == "tensor"

    FULL_WAITS = False

    def resolve(self):
        counts = self.state["counts"]
        ops = self.ops
        for op in ops:
            for d in op["deps"]:
                dep = ops[d]
                if not self._nowait(dep, op):
                    dep["inc"] = True
        for op in ops:
            if op["inc"]:
                sn = self._sname(op)
                counts[sn] = counts.get(sn, 0) + (16 if op["kind"] == "dma" else 1)
                op["incv"] = counts[sn]
        clocks = self.state["clocks"]
        opclock = [None] * len(ops)
        for op in ops:
            ec = clocks.setdefault(op["eng"], {})
            waits = []
            for d in sorted(op["deps"]):
                dep = ops[d]
                if self._nowait(dep, op):
                    continue
                sn, v = self._sname(dep), dep["incv"]
                assert v is not None, (op["tag"], dep["tag"])
                if ec.get(sn, 0) >= v and not Plan.FULL_WAITS:
                    continue
                if ec.get(sn, 0) >= v:
                    ec[sn] = max(ec.get(sn, 0), v)
                    waits.append((sn, v))
                    continue
                waits.append((sn, v))
                ec[sn] = v
                dc = opclock[d]
                if dc:
                    for s2, v2 in dc.items():
                        if ec.get(s2, 0) < v2:
                            ec[s2] = v2
            op["waits"] = waits
            opclock[op["idx"]] = dict(ec)

    def emit(self, nc, sems):
        self.resolve()
        counts = self.state["counts"]
        qsems = ["q_sync%d" % i for i in range(8)] + \
                ["q_scalar%d" % i for i in range(6)] + \
                ["q_gpsimd%d" % i for i in range(4)] + ["cc"]
        drain = [(sn, counts[sn]) for sn in qsems if counts.get(sn)]
        with nc.Block() as blk:
            for eng_name in self.ENGS:
                ops = [o for o in self.ops if o["eng"] == eng_name]
                if not ops and eng_name != "vector":
                    continue

                def make(ops, eng_name=eng_name):
                    def body(eng):
                        for op in ops:
                            for sn, v in op["waits"]:
                                eng.wait_ge(sems[sn], v)
                            inst = op["emit"](eng)
                            if op["inc"]:
                                assert inst is not None, op["tag"]
                                inst.then_inc(sems[self._sname(op)],
                                              16 if op["kind"] == "dma" else 1)
                        if eng_name == "vector":
                            ec = self.state["clocks"].setdefault("vector", {})
                            for sn, v in drain:
                                if ec.get(sn, 0) < v:
                                    eng.wait_ge(sems[sn], v)
                                    ec[sn] = v
                    return body
                getattr(blk, eng_name)(make(ops))


def build_nc(NT=8192, race=False, passes=(1, 2, 3), drop=(), debug=False):
    NTL = NT // TT
    NCH = NT // CH

    nc = bass.Bass("TRN2", target_bir_lowering=False,
                   detect_race_conditions=race)
    dt = nc.dram_tensor
    x_d = dt("x", [NT, D], F32, kind="ExternalInput").ap()
    wqkvT_d = dt("wqkvT", [D, 3 * D], F32, kind="ExternalInput").ap()
    projP_d = dt("projP", [H // 2, CH, 2 * FF], F32, kind="ExternalInput").ap()
    projQ_d = dt("projQ", [CH, 2 * H * CH], F32, kind="ExternalInput").ap()
    woutT_d = dt("woutT", [D, D], F32, kind="ExternalInput").ap()
    w1T_d = dt("w1T", [D, DFF], F32, kind="ExternalInput").ap()
    w2T_d = dt("w2T", [DFF, D], F32, kind="ExternalInput").ap()
    fw_d = dt("fw", [D, D], F32, kind="ExternalInput").ap()
    vecs_d = dt("vecs", [CH, 64], F32, kind="ExternalInput").ap()
    b1c_d = dt("b1c", [CH, NMF], F32, kind="ExternalInput").ap()
    iden_d = dt("iden", [CH, CH], BF16, kind="ExternalInput").ap()
    onesr_d = dt("onesr", [1, CH], F32, kind="ExternalInput").ap()
    onesc_d = dt("onescb", [CH, 1], BF16, kind="ExternalInput").ap()
    out_d = dt("out", [NT, D], F32, kind="ExternalOutput").ap()

    x1n_s = dt("x1n_s", [D, NT], BF16, kind="Internal").ap()
    part_s = dt("part_s", [NT, D], F32, kind="Internal").ap()
    w1s_s = dt("w1s_s", [NMF, CH, D], BF16, kind="Internal").ap()
    w2s_s = dt("w2s_s", [DFF, D], BF16, kind="Internal").ap()
    h2row_s = dt("h2row_s", [1, NT], F32, kind="Internal").ap()
    cc_in = dt("cc_in", [1, 1], F32, kind="Internal").ap()
    cc_out = dt("cc_out", [1, 1], F32, kind="Internal", addr_space="Shared").ap()

    dbg = {}
    if debug:
        dbg["scal"] = dt("dbg_scal", [1, 16], F32, kind="ExternalOutput").ap()
        dbg["gsc"] = dt("dbg_gsc", [CH, 8], F32, kind="ExternalOutput").ap()
        dbg["kvl"] = dt("dbg_kvl", [CH, 2 * H * 66], BF16, kind="ExternalOutput").ap()
        dbg["x1n"] = dt("dbg_x1n", [D, NT], BF16, kind="ExternalOutput").ap()
        dbg["part"] = dt("dbg_part", [NT, D], F32, kind="ExternalOutput").ap()
        dbg["h2row"] = dt("dbg_h2row", [1, NT], F32, kind="ExternalOutput").ap()
        dbg["stH"] = dt("dbg_stH", [CH, NT // CH], F32, kind="ExternalOutput").ap()
        dbg["stR"] = dt("dbg_stR", [CH, NT // CH], F32, kind="ExternalOutput").ap()
        dbg["stM"] = dt("dbg_stM", [CH, NT // CH], F32, kind="ExternalOutput").ap()
        dbg["mwq"] = dt("dbg_mwq", [CH, NKC * D], BF16, kind="ExternalOutput").ap()
    state = dict(counts={}, clocks={})
    es = contextlib.ExitStack()
    sems = {}
    for sn in (["e_tensor", "e_vector", "e_scalar", "e_gpsimd", "cc"]
               + ["q_sync%d" % i for i in range(8)]
               + ["q_scalar%d" % i for i in range(6)]
               + ["q_gpsimd%d" % i for i in range(4)]):
        sems[sn] = es.enter_context(nc.semaphore(sn))
    sb = lambda n, s, d_: es.enter_context(nc.sbuf_tensor(n, s, d_))

    iden = sb("idn", [CH, CH], BF16)
    onesr = sb("onr", [1, CH], F32)
    onesc = sb("onc", [CH, 1], BF16)
    vecs = sb("vcs", [CH, 64], F32)
    b1c = sb("b1cs", [CH, NMF], F32)
    statsR = sb("stR", [CH, NCH], F32)
    statsMUR = sb("stM", [CH, NCH], F32)
    statsH2 = sb("stH", [CH, NCH], F32)
    kvl = sb("kvl", [CH, 2 * H * 66], BF16)
    mwq = sb("mwq", [CH, NKC * D], BF16)
    gsc = sb("gsc", [CH, 8], F32)
    lngyc = sb("lgy", [CH, NKC], F32)
    lnbyc = sb("lby", [CH, NKC], F32)
    grow = sb("grw", [1, CH], F32)
    scal = sb("scl", [1, 16], F32)
    gacc = sb("gac", [CH, 1], F32)
    srow = sb("srw", [1, CH], F32)

    B_OUT = vecs[:, 0:8]
    GAM = vecs[:, 8:16]
    BET = vecs[:, 16:24]
    B2 = vecs[:, 24:32]
    LNG = vecs[:, 32:40]
    LNB = vecs[:, 40:48]
    EPS1 = vecs[:, 48:49]
    EPSF = vecs[:, 49:50]
    EPS1r = vecs[0:1, 48:49]

    def dch(ap2d, c, w=CH):
        return ap2d[:, c * w:(c + 1) * w]

    class _Skip(Exception):
        pass

    def _guard(n):
        if n not in passes:
            raise _Skip()

    with contextlib.ExitStack() as p1:
      try:
        _guard(1)
        sb1 = lambda n, s, d_: p1.enter_context(nc.sbuf_tensor(n, s, d_))
        ps1 = lambda n, s, d_: p1.enter_context(nc.psum_tensor(n, s, d_))
        kT = sb1("kT", [CH, NKC * D], BF16)
        vT = sb1("vT", [CH, NKC * D], BF16)
        lngb = sb1("lgb", [CH, D], BF16)
        lnbb = sb1("lbb", [CH, D], BF16)
        kvacc = sb1("kva", [65, H * FF], F32)
        projBD = sb1("pbd", [CH, (H // 2) * 2 * FF], BF16)
        wst = sb1("wst", [CH, DFF], F32)
        wst2 = sb1("ws2", [CH, DFF], BF16)
        rstg = sb1("rsg", [CH, 2 * D], F32)
        xin = sb1("xin", [CH, 2 * D], F32)
        xntm = sb1("xnt", [CH, 2 * D], BF16)
        xnfm = sb1("xnf", [CH, NKC * TT], BF16)
        kfm = sb1("kfm", [CH, NKC * TT], BF16)
        vext = sb1("vex", [CH, TCH * H * 66], BF16)
        kftm = sb1("kft", [CH, TCH * H * FF], BF16)
        ebuf = sb1("ebf", [CH, 2 * 2 * FF], BF16)
        tbuf = sb1("tbf", [CH, 2 * 2 * FF], BF16)
        gscr = sb1("gs1", [CH, D], BF16)
        gscr2 = sb1("gs2", [CH, D], BF16)
        sqd = sb1("sqd", [CH, D], F32)
        sred = sb1("srd", [CH, 8], F32)
        stS = sb1("stS", [CH, 16], F32)
        PT = ps1("PT", [CH, 2 * D], BF16)
        PF = [ps1(f"PF{i}", [CH, 512], F32) for i in range(6)]
        P = Plan(state, drop,
                 psum_keys=[f"PF{i}" for i in range(6)] + [("PT", 0), ("PT", 1)])

        for name, dstv, srcv in (("iden", iden, iden_d), ("onesr", onesr, onesr_d),
                                 ("onesc", onesc, onesc_d), ("vecs", vecs, vecs_d),
                                 ("b1c", b1c, b1c_d)):
            P.dma("sync", lambda e, a=dstv, b=srcv: e.dma_start(out=a[:], in_=b),
                  writes=[name], tag="ld_" + name)
        lngbf_d = dt("lngbf", [CH, D], F32, kind="ExternalInput").ap()
        lnbbf_d = dt("lnbbf", [CH, D], F32, kind="ExternalInput").ap()
        P.dma("sync", lambda e: e.dma_start(out=wst[:, 0:D], in_=lngbf_d),
              writes=[("wst", 0)], tag="ld_lngf")
        P.dma("sync", lambda e: e.dma_start(out=wst[:, D:2 * D], in_=lnbbf_d),
              writes=[("wst", 1)], tag="ld_lnbf")
        P.add("vector", lambda e: e.tensor_copy(lngb[:], wst[:, 0:D]),
              reads=[("wst", 0)], writes=["lngb"], tag="c_lngb")
        P.add("vector", lambda e: e.tensor_copy(lnbb[:], wst[:, D:2 * D]),
              reads=[("wst", 1)], writes=["lnbb"], tag="c_lnbb")
        P.add("vector", lambda e: e.memset(gacc[:], 0.0), writes=["gacc"], tag="ms_g")
        P.add("vector", lambda e: e.memset(kvl[:], 0.0), writes=["kvl"], tag="ms_kvl")
        P.add("vector", lambda e: e.memset(vext[:], 1.0),
              writes=[("vext", c, vg) for c in range(TCH) for vg in range(2)],
              tag="ms_vx")

        _ldn = [0]

        def ldw(dkey, dsl, src, eng="vector"):
            q = _ldn[0] % 2
            _ldn[0] += 1
            sl = rstg[:, q * D:q * D + src.shape[1]]
            P.dma("sync", lambda e, src=src, sl=sl: e.dma_start(out=sl, in_=src),
                  writes=[("rstg", q)], tag="ld" + str(dkey))
            P.add(eng, lambda e, dsl=dsl, sl=sl: e.tensor_copy(dsl, sl),
                  reads=[("rstg", q)], writes=[dkey], tag="cast" + str(dkey))

        for kc in range(NKC):
            ldw(("kT", kc), dch(kT, kc, D), wqkvT_d[kc * CH:(kc + 1) * CH, D:2 * D])
        for kc in range(NKC):
            ldw(("vT", kc), dch(vT, kc, D), wqkvT_d[kc * CH:(kc + 1) * CH, 2 * D:3 * D])
        for pr in range(H // 2):
            P.dma("sync", lambda e, pr=pr: e.dma_start(
                out=wst[:, 0:2 * FF], in_=projP_d[pr]),
                writes=[("wst", 0)], tag=f"ldpj_{pr}")
            P.add("vector", lambda e, pr=pr: e.tensor_copy(
                dch(projBD, pr, 2 * FF), wst[:, 0:2 * FF]),
                reads=[("wst", 0)], writes=[("projBD", pr)], tag=f"cpj_{pr}")

        def emit_prep():
            fw_r = fw_d.rearrange("(a p) u -> p a u", p=CH)
            for hf in range(2):
                P.dma("scalar", lambda e, hf=hf: e.dma_start(
                    out=wst[:].rearrange("p (a u) -> p a u", u=D),
                    in_=fw_r[:, hf * 4:(hf + 1) * 4, :]),
                    writes=[("wst", q) for q in range(4)], tag=f"ld_fw{hf}")
                P.add("vector", lambda e, hf=hf: e.reduce_sum(
                    sred[:, hf:hf + 1], wst[:], axis=AX, apply_absolute_value=True),
                    reads=[("wst", q) for q in range(4)], writes=[("sred", hf)],
                    tag=f"asum{hf}", late=True)
            P.add("vector", lambda e: e.tensor_add(sred[:, 2:3], sred[:, 0:1], sred[:, 1:2]),
                  reads=[("sred", 0), ("sred", 1)], writes=[("sred", 2)], tag="asum2")
            P.dma("gpsimd", lambda e: e.dma_start(out=srow[:], in_=sred[:, 2:3], allow_slow_non_contiguous=True),
                  reads=[("sred", 2)], writes=["srow"], tag="fold_b")
            P.add("vector", lambda e: e.reduce_sum(scal[:, 0:1], srow[:], axis=AX),
                  reads=["srow"], writes=[("scal", 0)], tag="bsum", late=True)
            P.add("vector", lambda e: e.tensor_scalar(
                scal[:, 1:2], scal[:, 0:1], 0.5 / (D * D), 0.5 * EPS_FFN, ALU.mult, ALU.add),
                reads=[("scal", 0)], writes=[("scal", 1)], tag="hb")
            P.add("vector", lambda e: e.tensor_scalar_mul(scal[:, 2:3], scal[:, 1:2], -1.0),
                  reads=[("scal", 1)], writes=[("scal", 2)], tag="nhb")
            P.add("tensor", lambda e: e.matmul(PF[0][:, 0:2], onesr[:], scal[:, 1:3],
                                               start=True, stop=True),
                  reads=[("scal", 1), ("scal", 2), "onesr"], writes=["PF0"], tag="bc_hb")
            P.add("vector", lambda e: e.tensor_copy(gsc[:, 2:4], PF[0][:, 0:2]),
                  reads=["PF0"], writes=[("gsc", 2)], tag="ev_hb")
            for hf in range(2):
                P.dma("scalar", lambda e, hf=hf: e.dma_start(
                    out=wst[:].rearrange("p (a u) -> p a u", u=D),
                    in_=fw_r[:, hf * 4:(hf + 1) * 4, :]),
                    writes=[("wst", q) for q in range(4)], tag=f"ld_fwb{hf}")
                P.add("vector", lambda e: e.tensor_single_scalar(
                    wst2[:], wst[:], gsc[:, 2:3], ALU.is_gt),
                    reads=[("wst", q) for q in range(4)] + [("gsc", 2)],
                    writes=[("wst2", q) for q in range(4)], tag=f"gt{hf}")
                P.add("vector", lambda e, hf=hf: e.scalar_tensor_tensor(
                    mwq[:, hf * DFF:(hf + 1) * DFF], wst[:], gsc[:, 3:4], wst2[:],
                    ALU.is_lt, ALU.subtract),
                    reads=[("wst", q) for q in range(4)]
                    + [("wst2", q) for q in range(4)] + [("gsc", 2)],
                    writes=[("mwq", hf)], tag=f"mwq{hf}")

            for kc in range(NMF):
                q = kc % 4
                ssl = wst[:, q * D:(q + 1) * D]
                s2 = wst2[:, q * D:(q + 1) * D]
                P.dma("scalar", lambda e, kc=kc, ssl=ssl: e.dma_start(
                    out=ssl, in_=w2T_d[kc * CH:(kc + 1) * CH, :]),
                    writes=[("wst", q)], tag=f"ldw2_{kc}")
                P.add("gpsimd", lambda e, ssl=ssl, s2=s2: e.tensor_copy(s2, ssl),
                      reads=[("wst", q)], writes=[("wst2", q)], tag=f"cw2_{kc}")
                P.dma("gpsimd", lambda e, kc=kc, s2=s2: e.dma_start(
                    out=w2s_s[kc * CH:(kc + 1) * CH, :], in_=s2),
                    reads=[("wst2", q)], writes=["w2s"], tag=f"sw2_{kc}")
            for kc in range(NKC):
                P.dma("scalar", lambda e, kc=kc: e.dma_start(
                    out=wst[:], in_=w1T_d[kc * CH:(kc + 1) * CH, :]),
                    writes=[("wst", q) for q in range(4)], tag=f"ldw1_{kc}")
                P.add("gpsimd", lambda e: e.tensor_copy(wst2[:], wst[:]),
                      reads=[("wst", q) for q in range(4)],
                      writes=[("wst2", q) for q in range(4)], tag=f"cw1_{kc}")
                P.dma("gpsimd", lambda e, kc=kc: e.dma_start(
                    out=w1s_s[:, :, kc * CH:(kc + 1) * CH].rearrange("m p c -> p m c"),
                    in_=wst2[:].rearrange("p (m c) -> p m c", c=CH)),
                    reads=[("wst2", q) for q in range(4)], writes=["w1s"], tag=f"sw1_{kc}")


        for t in range(NTL):
            if t == min(2, NTL - 1):
                emit_prep()
            tc0 = t * TCH
            for c in range(TCH):
                cc = t * TCH + c
                pk = c % 2
                xs = dch(xin, pk, D)
                P.dma("sync", lambda e, cc=cc, xs=xs: e.dma_start(
                    out=xs, in_=x_d[cc * CH:(cc + 1) * CH, :]),
                    writes=[("xin", pk)], tag=f"ldx_{cc}")
                P.add("vector", lambda e, xs=xs, c=c: e.reduce_sum(
                    sred[:, c:c + 1], xs, axis=AX),
                    reads=[("xin", pk)], writes=[("sred", c)], tag=f"sum_{cc}",
                    late=True)
                P.add("vector", lambda e, xs=xs, c=c: e.scalar_tensor_tensor(
                    sqd[:], xs, 1.0, xs, ALU.mult, ALU.mult,
                    accum_out=sred[:, 4 + c:5 + c]),
                    reads=[("xin", pk)], writes=[("sred", 4 + c), "sqd"],
                    tag=f"sq_{cc}", late=True)
            SR = lambda a, b: stS[:, a:b]
            P.add("vector", lambda e: e.tensor_scalar_mul(
                SR(0, 4), sred[:, 0:4], 1.0 / D),
                reads=[("sred", c2) for c2 in range(4)], writes=[("stS", 0)],
                tag=f"bmu_{t}")
            P.add("vector", lambda e: e.tensor_scalar_mul(
                SR(4, 8), sred[:, 4:8], 1.0 / D),
                reads=[("sred", 4 + c2) for c2 in range(4)], writes=[("stS", 1)],
                tag=f"be2_{t}")
            P.add("vector", lambda e: e.tensor_mul(SR(8, 12), SR(0, 4), SR(0, 4)),
                  reads=[("stS", 0)], writes=[("stS", 2)], tag=f"bm2_{t}")
            P.add("vector", lambda e: e.tensor_sub(SR(12, 16), SR(4, 8), SR(8, 12)),
                  reads=[("stS", 1), ("stS", 2)], writes=[("stS", 3)], tag=f"bvr_{t}")
            P.add("scalar", lambda e: e.activation(
                SR(8, 12), SR(12, 16), AF.Ln, bias=EPS1),
                reads=[("stS", 3)], writes=[("stS", 2)], tag=f"bln_{t}")
            P.add("scalar", lambda e, tc0=tc0: e.activation(
                statsR[:, tc0:tc0 + 4], SR(8, 12), AF.Exp, scale=-0.5),
                reads=[("stS", 2)],
                writes=[("stR", tc0 + c2) for c2 in range(4)], tag=f"br_{t}")
            P.add("vector", lambda e, tc0=tc0: e.scalar_tensor_tensor(
                statsMUR[:, tc0:tc0 + 4], SR(0, 4), -1.0, statsR[:, tc0:tc0 + 4],
                ALU.mult, ALU.mult),
                reads=[("stS", 0)] + [("stR", tc0 + c2) for c2 in range(4)],
                writes=[("stM", tc0 + c2) for c2 in range(4)], tag=f"bmr_{t}")
            P.add("vector", lambda e, tc0=tc0: e.tensor_mul(
                SR(0, 4), statsR[:, tc0:tc0 + 4], statsR[:, tc0:tc0 + 4]),
                reads=[("stR", tc0 + c2) for c2 in range(4)], writes=[("stS", 0)],
                tag=f"brr_{t}")
            P.add("vector", lambda e: e.tensor_mul(SR(4, 8), SR(0, 4), SR(12, 16)),
                  reads=[("stS", 0), ("stS", 3)], writes=[("stS", 1)], tag=f"bv2_{t}")
            P.add("scalar", lambda e: e.activation(
                SR(8, 12), SR(4, 8), AF.Ln, bias=EPSF),
                reads=[("stS", 1)], writes=[("stS", 2)], tag=f"bl2_{t}")
            P.add("scalar", lambda e, tc0=tc0: e.activation(
                statsH2[:, tc0:tc0 + 4], SR(8, 12), AF.Exp, scale=-0.5),
                reads=[("stS", 2)],
                writes=[("stH", tc0 + c2) for c2 in range(4)], tag=f"bh2_{t}")
            for c in range(TCH):
                cc = t * TCH + c
                pk = c % 2
                xs = dch(xin, pk, D)
                P.dma("sync", lambda e, cc=cc, xs=xs: e.dma_start(
                    out=xs, in_=x_d[cc * CH:(cc + 1) * CH, :]),
                    writes=[("xin", pk)], tag=f"ldx2_{cc}")
                P.add("scalar", lambda e, xs=xs, pk=pk, cc=cc: e.activation(
                    dch(xntm, pk, D), xs, AF.Identity,
                    bias=statsMUR[:, cc:cc + 1], scale=statsR[:, cc:cc + 1]),
                    reads=[("xin", pk), ("stM", cc), ("stR", cc)],
                    writes=[("xntm", pk)], tag=f"xn_{cc}")
                P.add("vector", lambda e, pk=pk, cc=cc: e.scalar_tensor_tensor(
                    gscr[:], dch(xntm, pk, D), statsH2[:, cc:cc + 1], lngb[:],
                    ALU.mult, ALU.mult),
                    reads=[("xntm", pk), ("stH", cc), "lngb"], writes=["gscr"],
                    tag=f"g1_{cc}")
                P.add("gpsimd", lambda e: e.tensor_add(gscr2[:], gscr[:], lnbb[:]),
                      reads=["gscr", "lnbb"], writes=["gscr2"], tag=f"g2_{cc}")
                P.add("vector", lambda e: e.reduce_max(
                    sred[:, 0:1], gscr2[:], axis=AX, apply_absolute_value=True),
                    reads=["gscr2"], writes=[("sred", 0)], tag=f"g3_{cc}",
                    late=True)
                P.add("vector", lambda e: e.tensor_max(gacc[:], gacc[:], sred[:, 0:1]),
                      reads=[("sred", 0), "gacc"], writes=["gacc"], tag=f"g4_{cc}")
                for kc in range(NKC):
                    P.add("tensor", lambda e, kc=kc, pk=pk: e.transpose(
                        PT[:, pk * D + kc * CH:pk * D + (kc + 1) * CH],
                        xntm[:, pk * D + kc * CH:pk * D + (kc + 1) * CH], iden[:]),
                        reads=[("xntm", pk), "iden"], writes=[("PT", pk)],
                        tag=f"xT_{cc}_{kc}")
                P.add("vector", lambda e, c=c, pk=pk: e.tensor_copy(
                    xnfm[:].rearrange("p (k tt) -> p k tt", tt=TT)
                    [:, :, c * CH:(c + 1) * CH],
                    PT[:, pk * D:(pk + 1) * D].rearrange("p (k w) -> p k w", w=CH)),
                    reads=[("PT", pk)], writes=[("xnfm", c)], tag=f"xTe_{cc}")
            P.dma("gpsimd", lambda e, t=t: e.dma_start(
                out=h2row_s[:, t * TT:(t + 1) * TT].rearrange(
                    "o (c p) -> (o p) c", p=CH),
                in_=statsH2[:, t * TCH:(t + 1) * TCH],
                allow_slow_non_contiguous=True),
                reads=[("stH", t * TCH + c2) for c2 in range(TCH)],
                writes=["h2row"], tag=f"h2r_{t}")

            for mc in range(NKC):
                bank = mc % 2
                def mm_k(e, mc=mc, bank=bank):
                    last = None
                    for kc in range(NKC):
                        last = e.matmul(PF[bank][:],
                                        kT[:, kc * D + mc * CH:kc * D + (mc + 1) * CH],
                                        dch(xnfm, kc, TT),
                                        start=(kc == 0), stop=(kc == NKC - 1))
                    return last
                P.add("tensor", mm_k,
                      reads=[("kT", kc2) for kc2 in range(NKC)]
                      + [("xnfm", c2) for c2 in range(TCH)],
                      writes=[f"PF{bank}"], tag=f"k_{t}_{mc}")
                P.add("scalar", lambda e, mc=mc, bank=bank: e.activation(
                    dch(kfm, mc, TT), PF[bank][:], AF.Copy),
                    reads=[f"PF{bank}"], writes=[("kfm", mc)], tag=f"ke_{t}_{mc}")

            for c in range(TCH):
                for vg in range(2):
                    bank = 2 + vg
                    def mm_v(e, c=c, vg=vg, bank=bank):
                        last = None
                        for kc in range(NKC):
                            last = e.matmul(
                                PF[bank][:],
                                xnfm[:, kc * TT + c * CH:kc * TT + (c + 1) * CH],
                                vT[:, kc * D + vg * 512:kc * D + (vg + 1) * 512],
                                start=(kc == 0), stop=(kc == NKC - 1))
                        return last
                    P.add("tensor", mm_v,
                          reads=[("vT", kc2) for kc2 in range(NKC)] + [("xnfm", c)],
                          writes=[f"PF{bank}"], tag=f"v_{t}_{c}_{vg}")
                    def ev_v(e, c=c, vg=vg, bank=bank):
                        dst = vext[:].rearrange("p (cc h s) -> p cc h s", h=H, s=66)[
                            :, c, vg * 8:(vg + 1) * 8, 0:64]
                        return e.tensor_copy(dst, PF[bank][:].rearrange(
                            "p (h w) -> p h w", w=64))
                    P.add("vector", ev_v, reads=[f"PF{bank}"],
                          writes=[("vext", c, vg)], tag=f"ve_{t}_{c}_{vg}")
            for c in range(TCH):
                for pr in range(H // 2):
                    bank = (c * (H // 2) + pr) % 4
                    pkk = pr % 2
                    P.add("tensor", lambda e, c=c, pr=pr, bank=bank: e.matmul(
                        PF[bank][:],
                        kfm[:, pr * TT + c * CH:pr * TT + (c + 1) * CH],
                        dch(projBD, pr, 2 * FF),
                        start=True, stop=True),
                        reads=[("kfm", pr), ("projBD", pr)],
                        writes=[f"PF{bank}"], tag=f"kf_{t}_{c}_{pr}")
                    eb = dch(ebuf, pkk, 2 * FF)
                    tb = dch(tbuf, pkk, 2 * FF)
                    P.add("scalar", lambda e, bank=bank, eb=eb: e.activation(
                        eb, PF[bank][:], AF.Exp),
                        reads=[f"PF{bank}"], writes=[("eb", pkk)],
                        tag=f"kfE_{t}_{c}_{pr}")
                    P.add("vector", lambda e, bank=bank, tb=tb: e.tensor_scalar(
                        tb, PF[bank][:], 1.0, 1.0, ALU.add, ALU.max),
                        reads=[f"PF{bank}"], writes=[("tb", pkk)],
                        tag=f"kfT_{t}_{c}_{pr}")
                    P.add("vector", lambda e, c=c, pr=pr, eb=eb, tb=tb: e.tensor_tensor(
                        kftm[:, (c * H + pr * 2) * FF:(c * H + pr * 2 + 2) * FF],
                        eb, tb, ALU.min),
                        reads=[("eb", pkk), ("tb", pkk)],
                        writes=[("kftm", c, pr)], tag=f"kfM_{t}_{c}_{pr}")
            for pr in range(H // 2):
                bank = 4 + (pr % 2)
                def mm_kv(e, pr=pr, bank=bank):
                    last = None
                    for c in range(TCH):
                        for hh in range(2):
                            h0 = pr * 2 + hh
                            last = e.matmul(
                                PF[bank][0:65, hh * FF:(hh + 1) * FF],
                                vext[:, (c * H + h0) * 66:(c * H + h0) * 66 + 65],
                                kftm[:, (c * H + h0) * FF:(c * H + h0 + 1) * FF],
                                start=(c == 0 and hh == 0),
                                stop=(c == TCH - 1 and hh == 1))
                    return last
                P.add("tensor", mm_kv,
                      reads=[("vext", c2, vg) for c2 in range(TCH) for vg in range(2)]
                      + [("kftm", c2, pr) for c2 in range(TCH)],
                      writes=[f"PF{bank}"], tag=f"kv_{t}_{pr}")
                if t == 0:
                    P.add("vector", lambda e, pr=pr, bank=bank: e.tensor_copy(
                        kvacc[:, pr * 2 * FF:(pr * 2 + 2) * FF], PF[bank][0:65, :]),
                        reads=[f"PF{bank}"], writes=[("kva", pr)], tag=f"kve_{t}_{pr}")
                else:
                    P.add("vector", lambda e, pr=pr, bank=bank: e.tensor_add(
                        kvacc[:, pr * 2 * FF:(pr * 2 + 2) * FF],
                        kvacc[:, pr * 2 * FF:(pr * 2 + 2) * FF], PF[bank][0:65, :]),
                        reads=[f"PF{bank}", ("kva", pr)], writes=[("kva", pr)],
                        tag=f"kve_{t}_{pr}")

        def ptoff(s):
            return (s % 2) * 1024 + (s // 2) * 66
        P.add("vector", lambda e: e.tensor_copy(wst2[0:65, 0:H * FF], kvacc[:]),
              reads=[("kva", pr) for pr in range(H // 2)],
              writes=[("wst2", q) for q in range(4)], tag="kv_cast")
        for h0 in range(H):
            for fc in range(2):
                sl = h0 * 2 + fc
                hf = sl % 2
                P.add("tensor", lambda e, h0=h0, fc=fc, sl=sl: e.transpose(
                    PT[:, ptoff(sl % 16):ptoff(sl % 16) + 65],
                    wst2[0:65, h0 * FF + fc * CH:h0 * FF + (fc + 1) * CH],
                    iden[0:65, 0:65]),
                    reads=[("wst2", q) for q in range(4)] + ["iden"],
                    writes=[("PT", hf)], tag=f"kvT_{h0}_{fc}")
                P.add("vector", lambda e, sl=sl: e.tensor_copy(
                    kvl[:, sl * 66:sl * 66 + 65],
                    PT[:, ptoff(sl % 16):ptoff(sl % 16) + 65]),
                    reads=[("PT", hf)], writes=["kvl"], tag=f"kvle_{sl}")
        build_nc._stash = dict(sred=sred, wst=wst)
        P.emit(nc, sems)
      except _Skip:
        pass

    with contextlib.ExitStack() as p2:
      try:
        _guard(2)
        sb2 = lambda n, s, d_: p2.enter_context(nc.sbuf_tensor(n, s, d_))
        ps2 = lambda n, s, d_: p2.enter_context(nc.psum_tensor(n, s, d_))
        qT = sb2("qT", [CH, NKC * D], BF16)
        projQ = sb2("pjq", [CH, 2 * H * CH], BF16)
        woT = sb2("woT", [CH, NKC * D], BF16)
        wst = sb2("wsb", [CH, 2 * D], F32)
        xin = sb2("xnb", [CH, 2 * D], F32)
        xpin = xin
        xntm = sb2("xtb", [CH, 2 * D], BF16)
        xnfm = sb2("xfb", [CH, 2 * NKC * TT], BF16)
        qfm = sb2("qfm", [CH, NKC * TT], BF16)
        shA = qfm
        qf = sb2("qf", [CH, 2 * H * TT], BF16)
        ebf2 = sb2("eb2", [CH, 2 * TT], BF16)
        tbf2 = sb2("tb2", [CH, 2 * TT], BF16)
        attnr = sb2("atr", [CH, 2 * D], BF16)
        z16 = sb2("z16", [CH, 16], F32)
        zr16 = sb2("zrs", [CH, 16], F32)
        attnfm = sb2("atf", [CH, NKC * TT], BF16)
        x1b = sb2("x1b", [CH, NKC * TT], BF16)
        x1sq = shA
        x1n = sb2("x1m", [CH, NKC * TT], BF16)
        rows = sb2("rws", [1, 4 * TT], F32)
        bc1 = sb2("bc1", [CH, TT], F32)
        bc2 = sb2("bc2", [CH, TT], F32)
        xq1 = sb2("xq1", [CH, TT], F32)
        xq2 = sb2("xq2", [CH, TT], F32)
        xqb = sb2("xqb", [CH, NKC * TT], BF16)
        ybf = shA
        xpart = sb2("xpt", [CH, 2 * D], F32)
        h2r = sb2("h2r", [1, TT], F32)
        PT2 = ps2("PT2", [CH, 2 * D], BF16)
        PT2b = ps2("PT2b", [CH, D], BF16)
        PF = [ps2(f"QF{i}", [CH, 512], F32) for i in range(5)]
        P = Plan(state, drop,
                 psum_keys=[f"QF{i}" for i in range(5)]
                 + [("PT2", 0), ("PT2", 1), "PT2b"])

        def row(i):
            return rows[:, i * TT:(i + 1) * TT]

        for pq in range(4):
            q = pq % 2
            ssl = wst[:, q * D:(q + 1) * D]
            P.dma("scalar", lambda e, pq=pq, ssl=ssl: e.dma_start(
                out=ssl, in_=projQ_d[:, pq * D:(pq + 1) * D]),
                writes=[("wst", q)], tag=f"ldpq_{pq}")
            P.add("vector", lambda e, pq=pq, ssl=ssl: e.tensor_copy(
                projQ[:, pq * D:(pq + 1) * D], ssl),
                reads=[("wst", q)], writes=["projQ"], tag=f"cpq_{pq}")
        for kc in range(NKC):
            q = kc % 2
            ssl = wst[:, q * D:(q + 1) * D]
            P.dma("scalar", lambda e, kc=kc, ssl=ssl: e.dma_start(
                out=ssl, in_=wqkvT_d[kc * CH:(kc + 1) * CH, 0:D]),
                writes=[("wst", q)], tag=f"ldq_{kc}")
            P.add("vector", lambda e, kc=kc, ssl=ssl: e.tensor_copy(
                dch(qT, kc, D), ssl),
                reads=[("wst", q)], writes=[("qT", kc)], tag=f"cq_{kc}")
        for kc in range(NKC):
            q = kc % 2
            ssl = wst[:, q * D:(q + 1) * D]
            P.dma("scalar", lambda e, kc=kc, ssl=ssl: e.dma_start(
                out=ssl, in_=woutT_d[kc * CH:(kc + 1) * CH, :]),
                writes=[("wst", q)], tag=f"ldo_{kc}")
            P.add("vector", lambda e, kc=kc, ssl=ssl: e.tensor_copy(
                dch(woT, kc, D), ssl),
                reads=[("wst", q)], writes=[("woT", kc)], tag=f"co_{kc}")

        P.dma("gpsimd", lambda e: e.dma_start(out=srow[:], in_=gacc[:], allow_slow_non_contiguous=True),
              reads=["gacc"], writes=["srow"], tag="fold_g")
        P.add("vector", lambda e: e.reduce_max(scal[:, 3:4], srow[:], axis=AX),
              reads=["srow"], writes=[("scal", 3)], tag="gloc", late=True)
        P.dma("gpsimd", lambda e: e.dma_start(out=cc_in, in_=scal[:, 3:4]),
              reads=[("scal", 3)], writes=["cc_in"], tag="st_cc")
        P.add("gpsimd", lambda e: e.collective_compute(
            "AllReduce", ALU.max, replica_groups=[list(range(NC))],
            ins=[cc_in], outs=[cc_out]),
            reads=["cc_in"], writes=["cc_out"], kind="cc", tag="ar_gamma")
        P.dma("gpsimd", lambda e: e.dma_start(out=scal[:, 4:5], in_=cc_out),
              reads=["cc_out"], writes=[("scal", 4)], tag="ld_cc")
        P.add("vector", lambda e: e.reciprocal(scal[:, 5:6], scal[:, 4:5]),
              reads=[("scal", 4)], writes=[("scal", 5)], tag="rg0", late=True)
        P.add("vector", lambda e: e.tensor_scalar_mul(scal[:, 6:7], scal[:, 5:6], QB),
              reads=[("scal", 5)], writes=[("scal", 6)], tag="rgam")
        P.add("vector", lambda e: e.tensor_mul(scal[:, 7:8], scal[:, 4:5], scal[:, 0:1]),
              reads=[("scal", 4), ("scal", 0)], writes=[("scal", 7)], tag="gb")
        P.add("vector", lambda e: e.tensor_scalar_mul(
            scal[:, 8:9], scal[:, 7:8], -1.0 / (QB * D * D)),
            reads=[("scal", 7)], writes=[("scal", 8)], tag="nys")
        P.add("tensor", lambda e: e.matmul(PF[0][:, 0:4], onesr[:], scal[:, 5:9],
                                           start=True, stop=True),
              reads=[("scal", 6), ("scal", 8), "onesr"], writes=["QF0"], tag="bc_g")
        P.add("vector", lambda e: e.tensor_copy(gsc[:, 0:1], PF[0][:, 1:2]),
              reads=["QF0"], writes=[("gsc", 0)], tag="ev_rg")
        P.add("vector", lambda e: e.tensor_copy(gsc[:, 1:2], PF[0][:, 3:4]),
              reads=["QF0"], writes=[("gsc", 1)], tag="ev_ny")
        P.add("vector", lambda e: e.tensor_single_scalar(
            grow[:], onesr[:], scal[:, 6:7], ALU.mult),
            reads=[("scal", 6), "onesr"], writes=["grow"], tag="grow")
        P.add("vector", lambda e: e.tensor_copy(lngyc[:], LNG),
              reads=["vecs"], writes=["lngyc"], tag="lngyc")
        P.add("vector", lambda e: e.tensor_single_scalar(
            lnbyc[:], LNB, gsc[:, 0:1], ALU.mult),
            reads=["vecs", ("gsc", 0)], writes=["lnbyc"], tag="lnbyc")

        def ln1_tail(t):
                def mm_s1(e):
                    last = None
                    for kc in range(NKC):
                        last = e.matmul(PF[4][0:1, :], onesc[:], dch(x1b, kc, TT),
                                        start=(kc == 0), stop=(kc == NKC - 1))
                    return last
                P.add("tensor", mm_s1,
                      reads=[("x1b", kc2) for kc2 in range(NKC)],
                      writes=["QF4"], tag=f"s1_{t}")
                P.add("vector", lambda e: e.tensor_scalar_mul(
                    row(0), PF[4][0:1, :], 1.0 / D),
                    reads=["QF4"], writes=[("rw", 0)], tag=f"s1e_{t}")
                def mm_s2(e):
                    last = None
                    for kc in range(NKC):
                        last = e.matmul(PF[4][0:1, :], onesc[:], dch(x1sq, kc, TT),
                                        start=(kc == 0), stop=(kc == NKC - 1))
                    return last
                P.add("tensor", mm_s2,
                      reads=[("shA", kc2) for kc2 in range(NKC)],
                      writes=["QF4"], tag=f"s2_{t}")
                P.add("vector", lambda e: e.tensor_scalar_mul(
                    row(1), PF[4][0:1, :], 1.0 / D),
                    reads=["QF4"], writes=[("rw", 1)], tag=f"s2e_{t}")
                P.add("vector", lambda e: e.tensor_mul(row(3), row(0), row(0)),
                      reads=[("rw", 0)], writes=[("rw", 3)], tag=f"v2_{t}")
                P.add("vector", lambda e: e.tensor_sub(row(2), row(1), row(3)),
                      reads=[("rw", 1), ("rw", 3)], writes=[("rw", 2)], tag=f"v3_{t}")
                P.add("scalar", lambda e: e.activation(
                    row(3), row(2), AF.Ln, bias=EPS1r),
                    reads=[("rw", 2)], writes=[("rw", 3)], tag=f"v4_{t}")
                P.add("scalar", lambda e: e.activation(
                    row(2), row(3), AF.Exp, scale=-0.5),
                    reads=[("rw", 3)], writes=[("rw", 2)], tag=f"r1_{t}")
                P.add("vector", lambda e: e.scalar_tensor_tensor(
                    row(3), row(0), -1.0, row(2), ALU.mult, ALU.mult),
                    reads=[("rw", 0), ("rw", 2)], writes=[("rw", 3)], tag=f"q1_{t}")
                P.add("tensor", lambda e: e.matmul(PF[4][:], onesr[:], row(2),
                                                   start=True, stop=True),
                      reads=[("rw", 2)], writes=["QF4"], tag=f"b1_{t}")
                P.add("vector", lambda e: e.tensor_copy(bc1[:], PF[4][:]),
                      reads=["QF4"], writes=["bc1"], tag=f"b1e_{t}")
                P.add("tensor", lambda e: e.matmul(PF[4][:], onesr[:], row(3),
                                                   start=True, stop=True),
                      reads=[("rw", 3)], writes=["QF4"], tag=f"b2_{t}")
                P.add("vector", lambda e: e.tensor_copy(bc2[:], PF[4][:]),
                      reads=["QF4"], writes=["bc2"], tag=f"b2e_{t}")
                for mc in range(NKC):
                    P.add("vector", lambda e, mc=mc: e.tensor_mul(
                        xq2[:], dch(x1b, mc, TT), bc1[:]),
                        reads=[("x1b", mc), "bc1"], writes=["xq2"], tag=f"n1_{t}_{mc}")
                    P.add("vector", lambda e: e.tensor_add(xq2[:], xq2[:], bc2[:]),
                          reads=["xq2", "bc2"], writes=["xq2"], tag=f"n2_{t}_{mc}")
                    P.add("vector", lambda e, mc=mc: e.tensor_scalar(
                        dch(x1n, mc, TT), xq2[:], GAM[:, mc:mc + 1], BET[:, mc:mc + 1],
                        ALU.mult, ALU.add),
                        reads=["xq2"], writes=[("x1n", mc)], tag=f"n3_{t}_{mc}")
                    P.dma("gpsimd", lambda e, mc=mc, t=t: e.dma_start(
                        out=x1n_s[mc * CH:(mc + 1) * CH, t * TT:(t + 1) * TT],
                        in_=dch(x1n, mc, TT)),
                        reads=[("x1n", mc)], writes=[("x1ns", t, mc)], tag=f"nS_{t}_{mc}")

        for t in range(NTL):
            xo = (t % 2) * NKC * TT
            xnk = t % 2

            def xch(kc, xo=None, t=t):
                o = (t % 2) * NKC * TT
                return xnfm[:, o + kc * TT:o + (kc + 1) * TT]
            for c in range(TCH):
                cc = t * TCH + c
                pk = c % 2
                xs = dch(xin, pk, D)
                P.dma("sync", lambda e, cc=cc, xs=xs: e.dma_start(
                    out=xs, in_=x_d[cc * CH:(cc + 1) * CH, :]),
                    writes=[("xin", pk)], tag=f"p2x_{cc}")
                P.add("scalar", lambda e, xs=xs, pk=pk, cc=cc: e.activation(
                    dch(xntm, pk, D), xs, AF.Identity,
                    bias=statsMUR[:, cc:cc + 1], scale=statsR[:, cc:cc + 1]),
                    reads=[("xin", pk)], writes=[("xntm", pk)], tag=f"p2n_{cc}")
                for kc in range(NKC):
                    P.add("tensor", lambda e, kc=kc, pk=pk: e.transpose(
                        PT2b[:, kc * CH:(kc + 1) * CH],
                        xntm[:, pk * D + kc * CH:pk * D + (kc + 1) * CH], iden[:]),
                        reads=[("xntm", pk)], writes=["PT2b"],
                        tag=f"p2T_{cc}_{kc}")
                P.add("vector", lambda e, c=c, pk=pk, xo=xo: e.tensor_copy(
                    xnfm[:, xo:xo + NKC * TT].rearrange("p (k tt) -> p k tt", tt=TT)
                    [:, :, c * CH:(c + 1) * CH],
                    PT2b[:].rearrange("p (k w) -> p k w", w=CH)),
                    reads=["PT2b"], writes=[("xnfm", xnk, c)], tag=f"p2e_{cc}")
            if t > 0:
                ln1_tail(t - 1)
            P.dma("sync", lambda e, t=t: e.dma_start(
                out=h2r[:], in_=h2row_s[:, t * TT:(t + 1) * TT]),
                writes=["h2r"], tag=f"fh_{t}")
            P.add("tensor", lambda e: e.matmul(PF[4][:], grow[:], h2r[:],
                                               start=True, stop=True),
                  reads=["h2r"], writes=["QF4"], tag=f"fb_{t}")
            for mc in range(NKC):
                P.add("vector", lambda e, mc=mc, xch=xch: e.scalar_tensor_tensor(
                    xq1[:], xch(mc), lngyc[:, mc:mc + 1], PF[4][:],
                    ALU.mult, ALU.mult),
                    reads=[("xnfm", xnk, c2) for c2 in range(TCH)] + ["QF4"],
                    writes=["xq1"], tag=f"f1_{t}_{mc}")
                P.add("vector", lambda e, mc=mc: e.tensor_scalar(
                    xq1[:], xq1[:], lnbyc[:, mc:mc + 1], CLIP, ALU.add, ALU.min),
                    reads=["xq1"], writes=["xq1"], tag=f"f2_{t}_{mc}")
                P.add("vector", lambda e, mc=mc: e.tensor_scalar_max(
                    dch(xqb, mc, TT), xq1[:], -CLIP),
                    reads=["xq1"], writes=[("xqb", mc)], tag=f"f3_{t}_{mc}")
            for mc in range(NKC):
                bank = mc % 2
                def mm_q(e, mc=mc, bank=bank, xch=xch):
                    last = None
                    for kc in range(NKC):
                        last = e.matmul(PF[bank][:],
                                        qT[:, kc * D + mc * CH:kc * D + (mc + 1) * CH],
                                        xch(kc),
                                        start=(kc == 0), stop=(kc == NKC - 1))
                    return last
                P.add("tensor", mm_q,
                      reads=[("qT", kc2) for kc2 in range(NKC)]
                      + [("xnfm", xnk, c2) for c2 in range(TCH)],
                      writes=[f"QF{bank}"], tag=f"q_{t}_{mc}")
                P.add("scalar", lambda e, mc=mc, bank=bank: e.activation(
                    dch(qfm, mc, TT), PF[bank][:], AF.Copy),
                    reads=[f"QF{bank}"], writes=[("shA", mc)], tag=f"qe_{t}_{mc}")
            for fc in range(2):
                for h0 in range(H):
                    pr, hh = h0 // 2, h0 % 2
                    base = hh * 64
                    sl = h0 * 2 + fc
                    bank = (fc * H + h0) % 4
                    P.add("tensor",
                          lambda e, pr=pr, sl=sl, bank=bank: e.matmul(
                              PF[bank][:],
                              dch(projQ, sl, CH),
                              dch(qfm, pr, TT),
                              start=True, stop=True),
                          reads=[("shA", pr), "projQ"], writes=[f"QF{bank}"],
                          tag=f"qf_{t}_{h0}_{fc}")
                    pe2 = sl % 2
                    P.add("scalar", lambda e, bank=bank, pe2=pe2: e.activation(
                        dch(ebf2, pe2, TT), PF[bank][:], AF.Exp),
                        reads=[f"QF{bank}"], writes=[("ebf2", pe2)],
                        tag=f"qfE_{t}_{h0}_{fc}")
                    P.add("vector", lambda e, bank=bank, pe2=pe2: e.tensor_scalar(
                        dch(tbf2, pe2, TT), PF[bank][:], 1.0, 1.0, ALU.add, ALU.max),
                        reads=[f"QF{bank}"], writes=[("tbf2", pe2)],
                        tag=f"qfT_{t}_{h0}_{fc}")
                    P.add("vector", lambda e, sl=sl, pe2=pe2: e.tensor_tensor(
                        qf[:, sl * TT:(sl + 1) * TT], dch(ebf2, pe2, TT),
                        dch(tbf2, pe2, TT), ALU.min),
                        reads=[("ebf2", pe2), ("tbf2", pe2)], writes=[("qf", sl)],
                        tag=f"qfM_{t}_{h0}_{fc}")
            for c in range(TCH):
                pk = c % 2
                for bl in range(4):
                    bank = 2 + bl % 2
                    def mm_a(e, c=c, bl=bl, bank=bank):
                        last = None
                        for s in range(4):
                            h0 = bl * 4 + s
                            for fc in range(2):
                                sl = h0 * 2 + fc
                                last = e.matmul(
                                    PF[bank][:, s * CH:s * CH + 65],
                                    qf[:, sl * TT + c * CH:sl * TT + (c + 1) * CH],
                                    kvl[:, sl * 66:sl * 66 + 65],
                                    start=(fc == 0), stop=(fc == 1))
                        return last
                    P.add("tensor", mm_a,
                          reads=[("qf", (bl * 4 + s) * 2 + fc)
                                 for s in range(4) for fc in range(2)],
                          writes=[f"QF{bank}"], tag=f"a_{t}_{c}_{bl}")
                    P.add("scalar", lambda e, bl=bl, bank=bank, pk=pk: e.activation(
                        attnr[:, pk * D + bl * 4 * 64:pk * D + (bl + 1) * 4 * 64]
                        .rearrange("p (s w) -> p s w", w=64),
                        PF[bank][:].rearrange("p (s w) -> p s w", w=CH)[:, :, 0:64],
                        AF.Copy),
                        reads=[f"QF{bank}"], writes=[("atr", pk)],
                        tag=f"ae_{t}_{c}_{bl}")
                    P.add("vector", lambda e, bl=bl, bank=bank: e.tensor_scalar_add(
                        z16[:, bl * 4:(bl + 1) * 4].rearrange(
                            "p (s o) -> p s o", o=1),
                        PF[bank][:].rearrange("p (s w) -> p s w", w=CH)[:, :, 64:65],
                        EPS_ATTN),
                        reads=[f"QF{bank}"], writes=[("z16", bl)],
                        tag=f"az_{t}_{c}_{bl}")
                P.add("vector", lambda e: e.reciprocal(zr16[:], z16[:]),
                      reads=[("z16", bl2) for bl2 in range(4)], writes=["zr16"],
                      tag=f"ar_{t}_{c}", late=True)
                P.add("vector", lambda e, pk=pk: e.tensor_tensor(
                    dch(attnr, pk, D).rearrange("p (h w) -> p h w", w=64),
                    dch(attnr, pk, D).rearrange("p (h w) -> p h w", w=64),
                    zr16[:].rearrange("p (h o) -> p h o", o=1).broadcast_to(
                        (CH, H, 64)), ALU.mult),
                    reads=["zr16", ("atr", pk)], writes=[("atr", pk)],
                    tag=f"as_{t}_{c}")
                for kc in range(NKC):
                    P.add("tensor", lambda e, kc=kc, pk=pk: e.transpose(
                        PT2[:, pk * D + kc * CH:pk * D + (kc + 1) * CH],
                        attnr[:, pk * D + kc * CH:pk * D + (kc + 1) * CH], iden[:]),
                        reads=[("atr", pk)], writes=[("PT2", pk)],
                        tag=f"aT_{t}_{c}_{kc}")
                P.add("vector", lambda e, c=c, pk=pk: e.tensor_copy(
                    attnfm[:].rearrange("p (k tt) -> p k tt", tt=TT)
                    [:, :, c * CH:(c + 1) * CH],
                    PT2[:, pk * D:(pk + 1) * D].rearrange("p (k w) -> p k w", w=CH)),
                    reads=[("PT2", pk)], writes=[("atf", c)], tag=f"aTe_{t}_{c}")
            for mc in range(NKC):
                bank = mc % 2
                def mm_y(e, mc=mc, bank=bank):
                    last = None
                    for kc in range(NKC):
                        last = e.matmul(PF[bank][:],
                                        mwq[:, kc * D + mc * CH:kc * D + (mc + 1) * CH],
                                        dch(xqb, kc, TT),
                                        start=(kc == 0), stop=(kc == NKC - 1))
                    return last
                P.add("tensor", mm_y,
                      reads=[("xqb", kc2) for kc2 in range(NKC)],
                      writes=[f"QF{bank}"], tag=f"y_{t}_{mc}")
                P.add("scalar", lambda e, mc=mc, bank=bank: e.activation(
                    dch(ybf, mc, TT), PF[bank][:], AF.Copy),
                    reads=[f"QF{bank}"], writes=[("shA", mc)], tag=f"ye_{t}_{mc}")
            for c in range(TCH):
                cc = t * TCH + c
                pk = c % 2
                for kc in range(NKC):
                    P.add("tensor", lambda e, kc=kc, pk=pk, c=c: e.transpose(
                        PT2[:, pk * D + kc * CH:pk * D + (kc + 1) * CH],
                        ybf[:, kc * TT + c * CH:kc * TT + (c + 1) * CH], iden[:]),
                        reads=[("shA", kc)], writes=[("PT2", pk)],
                        tag=f"yT_{t}_{c}_{kc}")
                xs = dch(xpin, pk, D)
                P.dma("sync", lambda e, cc=cc, xs=xs: e.dma_start(
                    out=xs, in_=x_d[cc * CH:(cc + 1) * CH, :]),
                    writes=[("xin", pk)], tag=f"px_{cc}")
                P.add("vector", lambda e, pk=pk: e.scalar_tensor_tensor(
                    dch(xpart, pk, D), PT2[:, pk * D:(pk + 1) * D], gsc[:, 1:2],
                    dch(xpin, pk, D), ALU.mult, ALU.add),
                    reads=[("PT2", pk), ("xin", pk)],
                    writes=[("xpart", pk)], tag=f"pp_{cc}")
                P.dma("scalar", lambda e, cc=cc, pk=pk: e.dma_start(
                    out=part_s[cc * CH:(cc + 1) * CH, :], in_=dch(xpart, pk, D)),
                    reads=[("xpart", pk)], writes=[("parts", cc)], tag=f"ps_{cc}")
            for mc in range(NKC):
                bank = mc % 2
                def mm_o(e, mc=mc, bank=bank):
                    last = None
                    for kc in range(NKC):
                        last = e.matmul(PF[bank][:],
                                        woT[:, kc * D + mc * CH:kc * D + (mc + 1) * CH],
                                        dch(attnfm, kc, TT),
                                        start=(kc == 0), stop=(kc == NKC - 1))
                    return last
                P.add("tensor", mm_o,
                      reads=[("woT", kc2) for kc2 in range(NKC)]
                      + [("atf", c2) for c2 in range(TCH)],
                      writes=[f"QF{bank}"], tag=f"o_{t}_{mc}")
                P.add("vector", lambda e, mc=mc, bank=bank, xch=xch: e.scalar_tensor_tensor(
                    dch(x1b, mc, TT), PF[bank][:], B_OUT[:, mc:mc + 1],
                    xch(mc), ALU.add, ALU.add),
                    reads=[f"QF{bank}"] + [("xnfm", xnk, c2) for c2 in range(TCH)],
                    writes=[("x1b", mc)], tag=f"x1_{t}_{mc}")
                P.add("scalar", lambda e, mc=mc: e.activation(
                    dch(x1sq, mc, TT), dch(x1b, mc, TT), AF.Square),
                    reads=[("x1b", mc)], writes=[("shA", mc)], tag=f"x1q_{t}_{mc}")
        ln1_tail(NTL - 1)
        P.emit(nc, sems)
      except _Skip:
        pass

    with contextlib.ExitStack() as p3:
      try:
        _guard(3)
        sb3 = lambda n, s, d_: p3.enter_context(nc.sbuf_tensor(n, s, d_))
        ps3 = lambda n, s, d_: p3.enter_context(nc.psum_tensor(n, s, d_))
        w2r = sb3("w2r", [CH, NMF * D], BF16)
        w1b = sb3("w1b", [CH, 2 * D], BF16)
        x1nb = sb3("x1v", [CH, 2 * NKC * TT], BF16)
        hbuf = sb3("hbf", [CH, NMF * TT], BF16)
        x2b = sb3("x2b", [CH, NKC * TT], BF16)
        x2sq = sb3("x2q", [CH, NKC * TT], BF16)
        aob = sb3("aob", [CH, NKC * TT], BF16)
        rows = sb3("rw3", [1, 4 * TT], F32)
        bc1 = sb3("bc3", [CH, TT], F32)
        bc2 = sb3("bc4", [CH, TT], F32)
        tsc = sb3("tsc", [CH, TT], F32)
        xpart = sb3("xp3", [CH, 2 * D], F32)
        outb = sb3("oub", [CH, 2 * D], F32)
        PT3 = ps3("PT3", [CH, 2 * D], BF16)
        PF = [ps3(f"RF{i}", [CH, 512], F32) for i in range(5)]
        P = Plan(state, drop,
                 psum_keys=[f"RF{i}" for i in range(5)] + [("PT3", 0), ("PT3", 1)])

        def row(i):
            return rows[:, i * TT:(i + 1) * TT]

        for kc in range(NMF):
            P.dma("sync", lambda e, kc=kc: e.dma_start(
                out=dch(w2r, kc, D), in_=w2s_s[kc * CH:(kc + 1) * CH, :]),
                writes=[("w2r", kc)], tag=f"lw2_{kc}")

        def ln2_tail(t):
                def mm_t1(e):
                    last = None
                    for kc in range(NKC):
                        last = e.matmul(PF[4][0:1, :], onesc[:], dch(x2b, kc, TT),
                                        start=(kc == 0), stop=(kc == NKC - 1))
                    return last
                P.add("tensor", mm_t1, reads=[("x2b", kc2) for kc2 in range(NKC)],
                      writes=["RF4"], tag=f"t1_{t}")
                P.add("vector", lambda e: e.tensor_scalar_mul(
                    row(0), PF[4][0:1, :], 1.0 / D),
                    reads=["RF4"], writes=[("rw", 0)], tag=f"t1e_{t}")
                def mm_t2(e):
                    last = None
                    for kc in range(NKC):
                        last = e.matmul(PF[4][0:1, :], onesc[:], dch(x2sq, kc, TT),
                                        start=(kc == 0), stop=(kc == NKC - 1))
                    return last
                P.add("tensor", mm_t2, reads=[("x2sq", kc2) for kc2 in range(NKC)],
                      writes=["RF4"], tag=f"t2_{t}")
                P.add("vector", lambda e: e.tensor_scalar_mul(
                    row(1), PF[4][0:1, :], 1.0 / D),
                    reads=["RF4"], writes=[("rw", 1)], tag=f"t2e_{t}")
                P.add("vector", lambda e: e.tensor_mul(row(3), row(0), row(0)),
                      reads=[("rw", 0)], writes=[("rw", 3)], tag=f"u2_{t}")
                P.add("vector", lambda e: e.tensor_sub(row(2), row(1), row(3)),
                      reads=[("rw", 1), ("rw", 3)], writes=[("rw", 2)], tag=f"u3_{t}")
                P.add("scalar", lambda e: e.activation(
                    row(3), row(2), AF.Ln, bias=EPS1r),
                    reads=[("rw", 2)], writes=[("rw", 3)], tag=f"u4_{t}")
                P.add("scalar", lambda e: e.activation(
                    row(2), row(3), AF.Exp, scale=-0.5),
                    reads=[("rw", 3)], writes=[("rw", 2)], tag=f"r2_{t}")
                P.add("vector", lambda e: e.scalar_tensor_tensor(
                    row(3), row(0), -1.0, row(2), ALU.mult, ALU.mult),
                    reads=[("rw", 0), ("rw", 2)], writes=[("rw", 3)], tag=f"q2_{t}")
                P.add("tensor", lambda e: e.matmul(PF[4][:], onesr[:], row(2),
                                                   start=True, stop=True),
                      reads=[("rw", 2)], writes=["RF4"], tag=f"w1b_{t}")
                P.add("vector", lambda e: e.tensor_copy(bc1[:], PF[4][:]),
                      reads=["RF4"], writes=["bc1"], tag=f"w1e_{t}")
                P.add("tensor", lambda e: e.matmul(PF[4][:], onesr[:], row(3),
                                                   start=True, stop=True),
                      reads=[("rw", 3)], writes=["RF4"], tag=f"w2b_{t}")
                P.add("vector", lambda e: e.tensor_copy(bc2[:], PF[4][:]),
                      reads=["RF4"], writes=["bc2"], tag=f"w2e_{t}")
                for mc in range(NKC):
                    P.add("vector", lambda e, mc=mc: e.tensor_mul(
                        tsc[:], dch(x2b, mc, TT), bc1[:]),
                        reads=[("x2b", mc), "bc1"], writes=["tsc"], tag=f"m1_{t}_{mc}")
                    P.add("vector", lambda e: e.tensor_add(tsc[:], tsc[:], bc2[:]),
                          reads=["tsc", "bc2"], writes=["tsc"], tag=f"m2_{t}_{mc}")
                    P.add("vector", lambda e, mc=mc: e.tensor_scalar(
                        dch(aob, mc, TT), tsc[:], GAM[:, mc:mc + 1], BET[:, mc:mc + 1],
                        ALU.mult, ALU.add),
                        reads=["tsc"], writes=[("aob", mc)], tag=f"m3_{t}_{mc}")
                for c in range(TCH):
                    cc = t * TCH + c
                    pk = c % 2
                    for kc in range(NKC):
                        P.add("tensor", lambda e, kc=kc, pk=pk, c=c: e.transpose(
                            PT3[:, pk * D + kc * CH:pk * D + (kc + 1) * CH],
                            aob[:, kc * TT + c * CH:kc * TT + (c + 1) * CH], iden[:]),
                            reads=[("aob", kc)], writes=[("PT3", pk)],
                            tag=f"oT_{t}_{c}_{kc}")
                    xs = dch(xpart, pk, D)
                    P.dma("gpsimd", lambda e, cc=cc, xs=xs: e.dma_start(
                        out=xs, in_=part_s[cc * CH:(cc + 1) * CH, :]),
                        writes=[("xp3", pk)], tag=f"lp_{cc}")
                    P.add("vector", lambda e, pk=pk: e.tensor_add(
                        dch(outb, pk, D), PT3[:, pk * D:(pk + 1) * D], dch(xpart, pk, D)),
                        reads=[("PT3", pk), ("xp3", pk)], writes=[("oub", pk)],
                        tag=f"os_{cc}")
                    P.dma("gpsimd", lambda e, cc=cc, pk=pk: e.dma_start(
                        out=out_d[cc * CH:(cc + 1) * CH, :], in_=dch(outb, pk, D)),
                        reads=[("oub", pk)], writes=[("outd", cc)], tag=f"ow_{cc}")
        for t in range(NTL):
            tb = t % 2
            for mc in range(NKC):
                P.dma("sync", lambda e, t=t, mc=mc, tb=tb: e.dma_start(
                    out=x1nb[:, (tb * NKC + mc) * TT:(tb * NKC + mc + 1) * TT],
                    in_=x1n_s[mc * CH:(mc + 1) * CH, t * TT:(t + 1) * TT]),
                    writes=[("x1v", tb, mc)], tag=f"lx1_{t}_{mc}")
            if t > 0:
                ln2_tail(t - 1)
            for m in range(NMF):
                wb = m % 2
                bank = m % 2
                P.dma("sync", lambda e, m=m, wb=wb: e.dma_start(
                    out=dch(w1b, wb, D), in_=w1s_s[m]),
                    writes=[("w1b", wb)], tag=f"lw1_{t}_{m}")
                def mm_h(e, m=m, wb=wb, bank=bank, tb=tb):
                    last = None
                    for kc in range(NKC):
                        last = e.matmul(
                            PF[bank][:],
                            w1b[:, wb * D + kc * CH:wb * D + (kc + 1) * CH],
                            x1nb[:, (tb * NKC + kc) * TT:(tb * NKC + kc + 1) * TT],
                            start=(kc == 0), stop=(kc == NKC - 1))
                    return last
                P.add("tensor", mm_h,
                      reads=[("w1b", wb)] + [("x1v", tb, kc2) for kc2 in range(NKC)],
                      writes=[f"RF{bank}"], tag=f"h_{t}_{m}")
                P.add("scalar", lambda e, m=m, bank=bank: e.activation(
                    dch(hbuf, m, TT), PF[bank][:], AF.Gelu, bias=b1c[:, m:m + 1]),
                    reads=[f"RF{bank}"], writes=[("h", m)], tag=f"he_{t}_{m}")
            for mc in range(NKC):
                bank = 2 + mc % 2
                def mm_x2(e, mc=mc, bank=bank):
                    last = None
                    for kc in range(NMF):
                        last = e.matmul(
                            PF[bank][:],
                            w2r[:, kc * D + mc * CH:kc * D + (mc + 1) * CH],
                            dch(hbuf, kc, TT),
                            start=(kc == 0), stop=(kc == NMF - 1))
                    return last
                P.add("tensor", mm_x2,
                      reads=[("w2r", kc2) for kc2 in range(NMF)]
                      + [("h", kc2) for kc2 in range(NMF)],
                      writes=[f"RF{bank}"], tag=f"x2_{t}_{mc}")
                P.add("vector",
                      lambda e, mc=mc, bank=bank, tb=tb: e.scalar_tensor_tensor(
                          dch(x2b, mc, TT), PF[bank][:], B2[:, mc:mc + 1],
                          x1nb[:, (tb * NKC + mc) * TT:(tb * NKC + mc + 1) * TT],
                          ALU.add, ALU.add),
                      reads=[f"RF{bank}", ("x1v", tb, mc)],
                      writes=[("x2b", mc)], tag=f"x2e_{t}_{mc}")
                P.add("scalar", lambda e, mc=mc: e.activation(
                    dch(x2sq, mc, TT), dch(x2b, mc, TT), AF.Square),
                    reads=[("x2b", mc)], writes=[("x2sq", mc)], tag=f"x2q_{t}_{mc}")
        ln2_tail(NTL - 1)
        P.emit(nc, sems)
      except _Skip:
        pass

    if debug:
        P = Plan(state)
        P.dma("sync", lambda e: e.dma_start(out=dbg["scal"], in_=scal[:]), tag="dbg_scal")
        P.dma("sync", lambda e: e.dma_start(
            out=dbg["gsc"], in_=build_nc._stash["sred"][:]), tag="dbg_gsc")
        P.dma("sync", lambda e: e.dma_start(out=dbg["kvl"], in_=kvl[:]), tag="dbg_kvl")
        P.dma("sync", lambda e: e.dma_start(out=dbg["x1n"], in_=x1n_s), tag="dbg_x1n")
        P.dma("sync", lambda e: e.dma_start(out=dbg["part"], in_=part_s), tag="dbg_part")
        P.dma("sync", lambda e: e.dma_start(out=dbg["h2row"], in_=h2row_s), tag="dbg_h2row")
        P.dma("sync", lambda e: e.dma_start(out=dbg["stH"], in_=statsH2[:]), tag="dbg_stH")
        P.dma("sync", lambda e: e.dma_start(out=dbg["stR"], in_=statsR[:]), tag="dbg_stR")
        P.dma("sync", lambda e: e.dma_start(out=dbg["stM"], in_=statsMUR[:]), tag="dbg_stM")
        P.dma("sync", lambda e: e.dma_start(out=dbg["mwq"], in_=mwq[:]), tag="dbg_mwq")
        P.emit(nc, sems)

    es.close()
    return nc


def _shard_inputs(L0_attn_x, L0_attn_proj, L0_attn_w_qkv, L0_attn_w_out,
                  L0_attn_b_out, L0_attn_gamma, L0_attn_beta, L0_attn_w1,
                  L0_attn_b1, L0_attn_w2, L0_attn_b2, L0_ffn_weight,
                  L0_ffn_ln_gamma, L0_ffn_ln_beta, NT):
    import ml_dtypes
    f32 = lambda a: np.ascontiguousarray(np.asarray(a, dtype=np.float32))
    x = f32(L0_attn_x)
    wqkvT = np.ascontiguousarray(f32(L0_attn_w_qkv).T)
    pj = f32(L0_attn_proj)
    projP = np.zeros((H // 2, CH, 2 * FF), np.float32)
    for pr in range(H // 2):
        projP[pr, 0:64, 0:FF] = pj[2 * pr]
        projP[pr, 64:128, FF:2 * FF] = pj[2 * pr + 1]
    projQ = np.zeros((CH, 2 * H * CH), np.float32)
    for h0 in range(H):
        for fc in range(2):
            sl = h0 * 2 + fc
            projQ[(h0 % 2) * 64:(h0 % 2) * 64 + 64, sl * CH:(sl + 1) * CH] = \
                pj[h0][:, fc * CH:(fc + 1) * CH]
    woutT = np.ascontiguousarray(f32(L0_attn_w_out).T)
    w1T = np.ascontiguousarray(f32(L0_attn_w1).T)
    w2T = np.ascontiguousarray(f32(L0_attn_w2).T)
    fw = f32(L0_ffn_weight)
    vecs = np.zeros((CH, 64), np.float32)
    for i, v in enumerate((L0_attn_b_out, L0_attn_gamma, L0_attn_beta,
                           L0_attn_b2, L0_ffn_ln_gamma, L0_ffn_ln_beta)):
        vecs[:, i * 8:(i + 1) * 8] = f32(v).reshape(NKC, CH).T
    b1c = np.ascontiguousarray(f32(L0_attn_b1).reshape(NMF, CH).T)
    vecs[:, 48] = EPS_LN
    vecs[:, 49] = LN_EPS_FFN
    lngbf = np.ascontiguousarray(np.broadcast_to(f32(L0_ffn_ln_gamma), (CH, D)))
    lnbbf = np.ascontiguousarray(np.broadcast_to(f32(L0_ffn_ln_beta), (CH, D)))
    iden = np.eye(CH, dtype=ml_dtypes.bfloat16)
    onesr = np.ones((1, CH), np.float32)
    onescb = np.ones((CH, 1), ml_dtypes.bfloat16)
    shared = dict(wqkvT=wqkvT, projP=projP, projQ=projQ, woutT=woutT, w1T=w1T, w2T=w2T,
                  fw=fw, vecs=vecs, b1c=b1c, lngbf=lngbf, lnbbf=lnbbf,
                  iden=iden, onesr=onesr, onescb=onescb)
    return [dict(shared, x=np.ascontiguousarray(x[i, :NT])) for i in range(NC)]


def kernel(**inputs):
    NT = 8192
    in_maps = _shard_inputs(NT=NT, **inputs)
    nc = build_nc(NT=NT)
    res = run_bass_kernel_spmd(nc, in_maps, core_ids=list(range(NC)))
    return np.stack([np.asarray(r["out"]) for r in res.results]).astype(np.float32)


# version 14
# speedup vs baseline: 1.2537x; 1.0127x over previous
import contextlib
import numpy as np
import concourse.bass as bass
import concourse.mybir as mybir
from concourse.bass_utils import run_bass_kernel_spmd

F32 = mybir.dt.float32
BF16 = mybir.dt.bfloat16
AF = mybir.ActivationFunctionType
ALU = mybir.AluOpType
AX = mybir.AxisListType.X

D = 1024
H = 16
FF = 256
DFF = 4096
NC = 8
CH = 128
NKC = D // CH
NMF = DFF // CH

EPS_LN = 1e-5
EPS_ATTN = 1e-8
LN_EPS_FFN = 1e-3
EPS_FFN = 1e-5
QB = 128.0
CLIP = QB - EPS_FFN

TT = 512
TCH = TT // CH


class Plan:

    ENGS = ("sync", "tensor", "vector", "scalar", "gpsimd")

    NLANES = {"sync": 8, "scalar": 6, "gpsimd": 4}

    def __init__(self, state, drop=(), psum_keys=()):
        self.state = state
        self.drop = tuple(drop)
        self.psum_keys = set(psum_keys)
        self.ops = []
        self.last_w = {}
        self.readers = {}

    def add(self, eng, emit, reads=(), writes=(), kind="compute", tag="",
            late=False):
        if any(tag.startswith(p) for p in self.drop):
            return None
        idx = len(self.ops)
        deps = set()
        for b in reads:
            w = self.last_w.get(b)
            if w is not None:
                deps.add(w)
            if b in self.psum_keys:
                for r in self.readers.get(b, ()):
                    if self.ops[r]["eng"] != eng:
                        deps.add(r)
        for b in writes:
            w = self.last_w.get(b)
            if w is not None:
                deps.add(w)
            for r in self.readers.get(b, ()):
                deps.add(r)
        for b in reads:
            self.readers.setdefault(b, []).append(idx)
        for b in writes:
            self.last_w[b] = idx
            self.readers[b] = []
        deps.discard(idx)
        lane = None
        if kind == "dma":
            lc = self.state.setdefault("lanecnt", {})
            lc[eng] = lc.get(eng, 0) + 1
            lane = lc[eng] % self.NLANES[eng]
        self.ops.append(dict(eng=eng, emit=emit, kind=kind, deps=deps, tag=tag,
                             late=late, inc=(kind != "compute"), incv=None,
                             waits=None, lane=lane, idx=idx))
        return idx

    def dma(self, eng, emit, reads=(), writes=(), tag=""):
        return self.add(eng, emit, reads, writes, kind="dma", tag=tag)

    def _sname(self, op):
        if op["kind"] == "dma":
            return "q_%s%d" % (op["eng"], op["lane"])
        if op["kind"] == "cc":
            return "cc"
        return "e_" + op["eng"]

    def _nowait(self, dep, op):
        if dep.get("late"):
            return False
        if dep["eng"] != op["eng"]:
            return False
        if dep["kind"] == "dma" or dep["kind"] == "cc":
            return dep["kind"] == op["kind"]
        return dep["eng"] == "tensor"

    FULL_WAITS = False

    def resolve(self):
        counts = self.state["counts"]
        ops = self.ops
        for op in ops:
            for d in op["deps"]:
                dep = ops[d]
                if not self._nowait(dep, op):
                    dep["inc"] = True
        for op in ops:
            if op["inc"]:
                sn = self._sname(op)
                counts[sn] = counts.get(sn, 0) + (16 if op["kind"] == "dma" else 1)
                op["incv"] = counts[sn]
        clocks = self.state["clocks"]
        opclock = [None] * len(ops)
        for op in ops:
            ec = clocks.setdefault(op["eng"], {})
            waits = []
            for d in sorted(op["deps"]):
                dep = ops[d]
                if self._nowait(dep, op):
                    continue
                sn, v = self._sname(dep), dep["incv"]
                assert v is not None, (op["tag"], dep["tag"])
                if ec.get(sn, 0) >= v and not Plan.FULL_WAITS:
                    continue
                if ec.get(sn, 0) >= v:
                    ec[sn] = max(ec.get(sn, 0), v)
                    waits.append((sn, v))
                    continue
                waits.append((sn, v))
                ec[sn] = v
                dc = opclock[d]
                if dc:
                    for s2, v2 in dc.items():
                        if ec.get(s2, 0) < v2:
                            ec[s2] = v2
            op["waits"] = waits
            opclock[op["idx"]] = dict(ec)

    def emit(self, nc, sems):
        self.resolve()
        counts = self.state["counts"]
        qsems = ["q_sync%d" % i for i in range(8)] + \
                ["q_scalar%d" % i for i in range(6)] + \
                ["q_gpsimd%d" % i for i in range(4)] + ["cc"]
        drain = [(sn, counts[sn]) for sn in qsems if counts.get(sn)]
        with nc.Block() as blk:
            for eng_name in self.ENGS:
                ops = [o for o in self.ops if o["eng"] == eng_name]
                if not ops and eng_name != "vector":
                    continue

                def make(ops, eng_name=eng_name):
                    def body(eng):
                        for op in ops:
                            for sn, v in op["waits"]:
                                eng.wait_ge(sems[sn], v)
                            inst = op["emit"](eng)
                            if op["inc"]:
                                assert inst is not None, op["tag"]
                                inst.then_inc(sems[self._sname(op)],
                                              16 if op["kind"] == "dma" else 1)
                        if eng_name == "vector":
                            ec = self.state["clocks"].setdefault("vector", {})
                            for sn, v in drain:
                                if ec.get(sn, 0) < v:
                                    eng.wait_ge(sems[sn], v)
                                    ec[sn] = v
                    return body
                getattr(blk, eng_name)(make(ops))


def build_nc(NT=8192, race=False, passes=(1, 2, 3), drop=(), debug=False):
    NTL = NT // TT
    NCH = NT // CH

    nc = bass.Bass("TRN2", target_bir_lowering=False,
                   detect_race_conditions=race)
    dt = nc.dram_tensor
    x_d = dt("x", [NT, D], F32, kind="ExternalInput").ap()
    wqkvT_d = dt("wqkvT", [D, 3 * D], F32, kind="ExternalInput").ap()
    projP_d = dt("projP", [H // 2, CH, 2 * FF], F32, kind="ExternalInput").ap()
    projQ_d = dt("projQ", [CH, 2 * H * CH], F32, kind="ExternalInput").ap()
    woutT_d = dt("woutT", [D, D], F32, kind="ExternalInput").ap()
    w1T_d = dt("w1T", [D, DFF], F32, kind="ExternalInput").ap()
    w2T_d = dt("w2T", [DFF, D], F32, kind="ExternalInput").ap()
    fw_d = dt("fw", [D, D], F32, kind="ExternalInput").ap()
    vecs_d = dt("vecs", [CH, 64], F32, kind="ExternalInput").ap()
    b1c_d = dt("b1c", [CH, NMF], F32, kind="ExternalInput").ap()
    iden_d = dt("iden", [CH, CH], BF16, kind="ExternalInput").ap()
    onesr_d = dt("onesr", [1, CH], F32, kind="ExternalInput").ap()
    onesc_d = dt("onescb", [CH, 1], BF16, kind="ExternalInput").ap()
    out_d = dt("out", [NT, D], F32, kind="ExternalOutput").ap()

    x1n_s = dt("x1n_s", [D, NT], BF16, kind="Internal").ap()
    part_s = dt("part_s", [NT, D], F32, kind="Internal").ap()
    w1s_s = dt("w1s_s", [NMF, CH, D], BF16, kind="Internal").ap()
    w2s_s = dt("w2s_s", [DFF, D], BF16, kind="Internal").ap()
    h2row_s = dt("h2row_s", [1, NT], F32, kind="Internal").ap()
    cc_in = dt("cc_in", [1, 1], F32, kind="Internal").ap()
    cc_out = dt("cc_out", [1, 1], F32, kind="Internal", addr_space="Shared").ap()

    dbg = {}
    if debug:
        dbg["scal"] = dt("dbg_scal", [1, 16], F32, kind="ExternalOutput").ap()
        dbg["gsc"] = dt("dbg_gsc", [CH, 8], F32, kind="ExternalOutput").ap()
        dbg["kvl"] = dt("dbg_kvl", [CH, 2 * H * 66], BF16, kind="ExternalOutput").ap()
        dbg["x1n"] = dt("dbg_x1n", [D, NT], BF16, kind="ExternalOutput").ap()
        dbg["part"] = dt("dbg_part", [NT, D], F32, kind="ExternalOutput").ap()
        dbg["h2row"] = dt("dbg_h2row", [1, NT], F32, kind="ExternalOutput").ap()
        dbg["stH"] = dt("dbg_stH", [CH, NT // CH], F32, kind="ExternalOutput").ap()
        dbg["stR"] = dt("dbg_stR", [CH, NT // CH], F32, kind="ExternalOutput").ap()
        dbg["stM"] = dt("dbg_stM", [CH, NT // CH], F32, kind="ExternalOutput").ap()
        dbg["mwq"] = dt("dbg_mwq", [CH, NKC * D], BF16, kind="ExternalOutput").ap()
    state = dict(counts={}, clocks={})
    es = contextlib.ExitStack()
    sems = {}
    for sn in (["e_tensor", "e_vector", "e_scalar", "e_gpsimd", "cc"]
               + ["q_sync%d" % i for i in range(8)]
               + ["q_scalar%d" % i for i in range(6)]
               + ["q_gpsimd%d" % i for i in range(4)]):
        sems[sn] = es.enter_context(nc.semaphore(sn))
    sb = lambda n, s, d_: es.enter_context(nc.sbuf_tensor(n, s, d_))

    iden = sb("idn", [CH, CH], BF16)
    onesr = sb("onr", [1, CH], F32)
    onesc = sb("onc", [CH, 1], BF16)
    vecs = sb("vcs", [CH, 64], F32)
    b1c = sb("b1cs", [CH, NMF], F32)
    statsR = sb("stR", [CH, NCH], F32)
    statsMUR = sb("stM", [CH, NCH], F32)
    statsH2 = sb("stH", [CH, NCH], F32)
    kvl = sb("kvl", [CH, 2 * H * 66], BF16)
    mwq = sb("mwq", [CH, NKC * D], BF16)
    gsc = sb("gsc", [CH, 8], F32)
    lngyc = sb("lgy", [CH, NKC], F32)
    lnbyc = sb("lby", [CH, NKC], F32)
    grow = sb("grw", [1, CH], F32)
    scal = sb("scl", [1, 16], F32)
    gacc = sb("gac", [CH, 1], F32)
    srow = sb("srw", [1, CH], F32)

    B_OUT = vecs[:, 0:8]
    GAM = vecs[:, 8:16]
    BET = vecs[:, 16:24]
    B2 = vecs[:, 24:32]
    LNG = vecs[:, 32:40]
    LNB = vecs[:, 40:48]
    EPS1 = vecs[:, 48:49]
    EPSF = vecs[:, 49:50]
    EPS1r = vecs[0:1, 48:49]

    def dch(ap2d, c, w=CH):
        return ap2d[:, c * w:(c + 1) * w]

    class _Skip(Exception):
        pass

    def _guard(n):
        if n not in passes:
            raise _Skip()

    with contextlib.ExitStack() as p1:
      try:
        _guard(1)
        sb1 = lambda n, s, d_: p1.enter_context(nc.sbuf_tensor(n, s, d_))
        ps1 = lambda n, s, d_: p1.enter_context(nc.psum_tensor(n, s, d_))
        kT = sb1("kT", [CH, NKC * D], BF16)
        vT = sb1("vT", [CH, NKC * D], BF16)
        lngb = sb1("lgb", [CH, D], BF16)
        lnbb = sb1("lbb", [CH, D], BF16)
        kvacc = sb1("kva", [65, H * FF], F32)
        projBD = sb1("pbd", [CH, (H // 2) * 2 * FF], BF16)
        wst = sb1("wst", [CH, DFF], F32)
        wst2 = sb1("ws2", [CH, DFF], BF16)
        rstg = sb1("rsg", [CH, 2 * D], F32)
        xin = sb1("xin", [CH, 2 * D], F32)
        xntm = sb1("xnt", [CH, 2 * D], BF16)
        xnfm = sb1("xnf", [CH, NKC * TT], BF16)
        kfm = sb1("kfm", [CH, NKC * TT], BF16)
        vext = sb1("vex", [CH, TCH * H * 66], BF16)
        kftm = sb1("kft", [CH, TCH * H * FF], BF16)
        ebuf = sb1("ebf", [CH, 2 * 2 * FF], BF16)
        tbuf = sb1("tbf", [CH, 2 * 2 * FF], BF16)
        gscr = sb1("gs1", [CH, D], BF16)
        gscr2 = sb1("gs2", [CH, D], BF16)
        sqd = sb1("sqd", [CH, D], F32)
        sred = sb1("srd", [CH, 8], F32)
        stS = sb1("stS", [CH, 16], F32)
        PT = ps1("PT", [CH, 2 * D], BF16)
        PF = [ps1(f"PF{i}", [CH, 512], F32) for i in range(6)]
        P = Plan(state, drop,
                 psum_keys=[f"PF{i}" for i in range(6)] + [("PT", 0), ("PT", 1)])

        for name, dstv, srcv in (("iden", iden, iden_d), ("onesr", onesr, onesr_d),
                                 ("onesc", onesc, onesc_d), ("vecs", vecs, vecs_d),
                                 ("b1c", b1c, b1c_d)):
            P.dma("sync", lambda e, a=dstv, b=srcv: e.dma_start(out=a[:], in_=b),
                  writes=[name], tag="ld_" + name)
        lngbf_d = dt("lngbf", [CH, D], F32, kind="ExternalInput").ap()
        lnbbf_d = dt("lnbbf", [CH, D], F32, kind="ExternalInput").ap()
        P.dma("sync", lambda e: e.dma_start(out=wst[:, 0:D], in_=lngbf_d),
              writes=[("wst", 0)], tag="ld_lngf")
        P.dma("sync", lambda e: e.dma_start(out=wst[:, D:2 * D], in_=lnbbf_d),
              writes=[("wst", 1)], tag="ld_lnbf")
        P.add("vector", lambda e: e.tensor_copy(lngb[:], wst[:, 0:D]),
              reads=[("wst", 0)], writes=["lngb"], tag="c_lngb")
        P.add("vector", lambda e: e.tensor_copy(lnbb[:], wst[:, D:2 * D]),
              reads=[("wst", 1)], writes=["lnbb"], tag="c_lnbb")
        P.add("vector", lambda e: e.memset(gacc[:], 0.0), writes=["gacc"], tag="ms_g")
        P.add("vector", lambda e: e.memset(kvl[:], 0.0), writes=["kvl"], tag="ms_kvl")
        P.add("vector", lambda e: e.memset(vext[:], 1.0),
              writes=[("vext", c, vg) for c in range(TCH) for vg in range(2)],
              tag="ms_vx")

        _ldn = [0]

        def ldw(dkey, dsl, src, eng="vector"):
            q = _ldn[0] % 2
            _ldn[0] += 1
            sl = rstg[:, q * D:q * D + src.shape[1]]
            P.dma("sync", lambda e, src=src, sl=sl: e.dma_start(out=sl, in_=src),
                  writes=[("rstg", q)], tag="ld" + str(dkey))
            P.add(eng, lambda e, dsl=dsl, sl=sl: e.tensor_copy(dsl, sl),
                  reads=[("rstg", q)], writes=[dkey], tag="cast" + str(dkey))

        for kc in range(NKC):
            ldw(("kT", kc), dch(kT, kc, D), wqkvT_d[kc * CH:(kc + 1) * CH, D:2 * D])
        for kc in range(NKC):
            ldw(("vT", kc), dch(vT, kc, D), wqkvT_d[kc * CH:(kc + 1) * CH, 2 * D:3 * D])
        for pr in range(H // 2):
            P.dma("sync", lambda e, pr=pr: e.dma_start(
                out=wst[:, 0:2 * FF], in_=projP_d[pr]),
                writes=[("wst", 0)], tag=f"ldpj_{pr}")
            P.add("vector", lambda e, pr=pr: e.tensor_copy(
                dch(projBD, pr, 2 * FF), wst[:, 0:2 * FF]),
                reads=[("wst", 0)], writes=[("projBD", pr)], tag=f"cpj_{pr}")

        def prep_gen():
            fw_r = fw_d.rearrange("(a p) u -> p a u", p=CH)
            for hf in range(2):
                P.dma("scalar", lambda e, hf=hf: e.dma_start(
                    out=wst[:].rearrange("p (a u) -> p a u", u=D),
                    in_=fw_r[:, hf * 4:(hf + 1) * 4, :]),
                    writes=[("wst", q) for q in range(4)], tag=f"ld_fw{hf}")
                P.add("vector", lambda e, hf=hf: e.reduce_sum(
                    sred[:, hf:hf + 1], wst[:], axis=AX, apply_absolute_value=True),
                    reads=[("wst", q) for q in range(4)], writes=[("sred", hf)],
                    tag=f"asum{hf}", late=True)
            P.add("vector", lambda e: e.tensor_add(sred[:, 2:3], sred[:, 0:1], sred[:, 1:2]),
                  reads=[("sred", 0), ("sred", 1)], writes=[("sred", 2)], tag="asum2")
            P.dma("gpsimd", lambda e: e.dma_start(out=srow[:], in_=sred[:, 2:3], allow_slow_non_contiguous=True),
                  reads=[("sred", 2)], writes=["srow"], tag="fold_b")
            P.add("vector", lambda e: e.reduce_sum(scal[:, 0:1], srow[:], axis=AX),
                  reads=["srow"], writes=[("scal", 0)], tag="bsum", late=True)
            P.add("vector", lambda e: e.tensor_scalar(
                scal[:, 1:2], scal[:, 0:1], 0.5 / (D * D), 0.5 * EPS_FFN, ALU.mult, ALU.add),
                reads=[("scal", 0)], writes=[("scal", 1)], tag="hb")
            P.add("vector", lambda e: e.tensor_scalar_mul(scal[:, 2:3], scal[:, 1:2], -1.0),
                  reads=[("scal", 1)], writes=[("scal", 2)], tag="nhb")
            P.add("tensor", lambda e: e.matmul(PF[0][:, 0:2], onesr[:], scal[:, 1:3],
                                               start=True, stop=True),
                  reads=[("scal", 1), ("scal", 2), "onesr"], writes=["PF0"], tag="bc_hb")
            P.add("vector", lambda e: e.tensor_copy(gsc[:, 2:4], PF[0][:, 0:2]),
                  reads=["PF0"], writes=[("gsc", 2)], tag="ev_hb")
            for hf in range(2):
                P.dma("scalar", lambda e, hf=hf: e.dma_start(
                    out=wst[:].rearrange("p (a u) -> p a u", u=D),
                    in_=fw_r[:, hf * 4:(hf + 1) * 4, :]),
                    writes=[("wst", q) for q in range(4)], tag=f"ld_fwb{hf}")
                P.add("vector", lambda e: e.tensor_single_scalar(
                    wst2[:], wst[:], gsc[:, 2:3], ALU.is_gt),
                    reads=[("wst", q) for q in range(4)] + [("gsc", 2)],
                    writes=[("wst2", q) for q in range(4)], tag=f"gt{hf}")
                P.add("vector", lambda e, hf=hf: e.scalar_tensor_tensor(
                    mwq[:, hf * DFF:(hf + 1) * DFF], wst[:], gsc[:, 3:4], wst2[:],
                    ALU.is_lt, ALU.subtract),
                    reads=[("wst", q) for q in range(4)]
                    + [("wst2", q) for q in range(4)] + [("gsc", 2)],
                    writes=[("mwq", hf)], tag=f"mwq{hf}")
                yield

            for kc in range(NMF):
                q = kc % 4
                ssl = wst[:, q * D:(q + 1) * D]
                s2 = wst2[:, q * D:(q + 1) * D]
                P.dma("scalar", lambda e, kc=kc, ssl=ssl: e.dma_start(
                    out=ssl, in_=w2T_d[kc * CH:(kc + 1) * CH, :]),
                    writes=[("wst", q)], tag=f"ldw2_{kc}")
                P.add("gpsimd", lambda e, ssl=ssl, s2=s2: e.tensor_copy(s2, ssl),
                      reads=[("wst", q)], writes=[("wst2", q)], tag=f"cw2_{kc}")
                P.dma("gpsimd", lambda e, kc=kc, s2=s2: e.dma_start(
                    out=w2s_s[kc * CH:(kc + 1) * CH, :], in_=s2),
                    reads=[("wst2", q)], writes=["w2s"], tag=f"sw2_{kc}")
                yield
            for kc in range(NKC):
                P.dma("scalar", lambda e, kc=kc: e.dma_start(
                    out=wst[:], in_=w1T_d[kc * CH:(kc + 1) * CH, :]),
                    writes=[("wst", q) for q in range(4)], tag=f"ldw1_{kc}")
                P.add("gpsimd", lambda e: e.tensor_copy(wst2[:], wst[:]),
                      reads=[("wst", q) for q in range(4)],
                      writes=[("wst2", q) for q in range(4)], tag=f"cw1_{kc}")
                P.dma("gpsimd", lambda e, kc=kc: e.dma_start(
                    out=w1s_s[:, :, kc * CH:(kc + 1) * CH].rearrange("m p c -> p m c"),
                    in_=wst2[:].rearrange("p (m c) -> p m c", c=CH)),
                    reads=[("wst2", q) for q in range(4)], writes=["w1s"], tag=f"sw1_{kc}")
                yield


        _pg = prep_gen()
        for t in range(NTL):
            if t >= 1:
                for _ in range(4):
                    next(_pg, None)
            tc0 = t * TCH
            for c in range(TCH):
                cc = t * TCH + c
                pk = c % 2
                xs = dch(xin, pk, D)
                P.dma("sync", lambda e, cc=cc, xs=xs: e.dma_start(
                    out=xs, in_=x_d[cc * CH:(cc + 1) * CH, :]),
                    writes=[("xin", pk)], tag=f"ldx_{cc}")
                P.add("vector", lambda e, xs=xs, c=c: e.reduce_sum(
                    sred[:, c:c + 1], xs, axis=AX),
                    reads=[("xin", pk)], writes=[("sred", c)], tag=f"sum_{cc}",
                    late=True)
                P.add("vector", lambda e, xs=xs, c=c: e.scalar_tensor_tensor(
                    sqd[:], xs, 1.0, xs, ALU.mult, ALU.mult,
                    accum_out=sred[:, 4 + c:5 + c]),
                    reads=[("xin", pk)], writes=[("sred", 4 + c), "sqd"],
                    tag=f"sq_{cc}", late=True)
            SR = lambda a, b: stS[:, a:b]
            P.add("vector", lambda e: e.tensor_scalar_mul(
                SR(0, 4), sred[:, 0:4], 1.0 / D),
                reads=[("sred", c2) for c2 in range(4)], writes=[("stS", 0)],
                tag=f"bmu_{t}")
            P.add("vector", lambda e: e.tensor_scalar_mul(
                SR(4, 8), sred[:, 4:8], 1.0 / D),
                reads=[("sred", 4 + c2) for c2 in range(4)], writes=[("stS", 1)],
                tag=f"be2_{t}")
            P.add("vector", lambda e: e.tensor_mul(SR(8, 12), SR(0, 4), SR(0, 4)),
                  reads=[("stS", 0)], writes=[("stS", 2)], tag=f"bm2_{t}")
            P.add("vector", lambda e: e.tensor_sub(SR(12, 16), SR(4, 8), SR(8, 12)),
                  reads=[("stS", 1), ("stS", 2)], writes=[("stS", 3)], tag=f"bvr_{t}")
            P.add("scalar", lambda e: e.activation(
                SR(8, 12), SR(12, 16), AF.Ln, bias=EPS1),
                reads=[("stS", 3)], writes=[("stS", 2)], tag=f"bln_{t}")
            P.add("scalar", lambda e, tc0=tc0: e.activation(
                statsR[:, tc0:tc0 + 4], SR(8, 12), AF.Exp, scale=-0.5),
                reads=[("stS", 2)],
                writes=[("stR", tc0 + c2) for c2 in range(4)], tag=f"br_{t}")
            P.add("vector", lambda e, tc0=tc0: e.scalar_tensor_tensor(
                statsMUR[:, tc0:tc0 + 4], SR(0, 4), -1.0, statsR[:, tc0:tc0 + 4],
                ALU.mult, ALU.mult),
                reads=[("stS", 0)] + [("stR", tc0 + c2) for c2 in range(4)],
                writes=[("stM", tc0 + c2) for c2 in range(4)], tag=f"bmr_{t}")
            P.add("vector", lambda e, tc0=tc0: e.tensor_mul(
                SR(0, 4), statsR[:, tc0:tc0 + 4], statsR[:, tc0:tc0 + 4]),
                reads=[("stR", tc0 + c2) for c2 in range(4)], writes=[("stS", 0)],
                tag=f"brr_{t}")
            P.add("vector", lambda e: e.tensor_mul(SR(4, 8), SR(0, 4), SR(12, 16)),
                  reads=[("stS", 0), ("stS", 3)], writes=[("stS", 1)], tag=f"bv2_{t}")
            P.add("scalar", lambda e: e.activation(
                SR(8, 12), SR(4, 8), AF.Ln, bias=EPSF),
                reads=[("stS", 1)], writes=[("stS", 2)], tag=f"bl2_{t}")
            P.add("scalar", lambda e, tc0=tc0: e.activation(
                statsH2[:, tc0:tc0 + 4], SR(8, 12), AF.Exp, scale=-0.5),
                reads=[("stS", 2)],
                writes=[("stH", tc0 + c2) for c2 in range(4)], tag=f"bh2_{t}")
            for c in range(TCH):
                cc = t * TCH + c
                pk = c % 2
                xs = dch(xin, pk, D)
                P.dma("sync", lambda e, cc=cc, xs=xs: e.dma_start(
                    out=xs, in_=x_d[cc * CH:(cc + 1) * CH, :]),
                    writes=[("xin", pk)], tag=f"ldx2_{cc}")
                P.add("scalar", lambda e, xs=xs, pk=pk, cc=cc: e.activation(
                    dch(xntm, pk, D), xs, AF.Identity,
                    bias=statsMUR[:, cc:cc + 1], scale=statsR[:, cc:cc + 1]),
                    reads=[("xin", pk), ("stM", cc), ("stR", cc)],
                    writes=[("xntm", pk)], tag=f"xn_{cc}")
                P.add("vector", lambda e, pk=pk, cc=cc: e.scalar_tensor_tensor(
                    gscr[:], dch(xntm, pk, D), statsH2[:, cc:cc + 1], lngb[:],
                    ALU.mult, ALU.mult),
                    reads=[("xntm", pk), ("stH", cc), "lngb"], writes=["gscr"],
                    tag=f"g1_{cc}")
                P.add("gpsimd", lambda e: e.tensor_add(gscr2[:], gscr[:], lnbb[:]),
                      reads=["gscr", "lnbb"], writes=["gscr2"], tag=f"g2_{cc}")
                P.add("vector", lambda e: e.reduce_max(
                    sred[:, 0:1], gscr2[:], axis=AX, apply_absolute_value=True),
                    reads=["gscr2"], writes=[("sred", 0)], tag=f"g3_{cc}",
                    late=True)
                P.add("vector", lambda e: e.tensor_max(gacc[:], gacc[:], sred[:, 0:1]),
                      reads=[("sred", 0), "gacc"], writes=["gacc"], tag=f"g4_{cc}")
                for kc in range(NKC):
                    P.add("tensor", lambda e, kc=kc, pk=pk: e.transpose(
                        PT[:, pk * D + kc * CH:pk * D + (kc + 1) * CH],
                        xntm[:, pk * D + kc * CH:pk * D + (kc + 1) * CH], iden[:]),
                        reads=[("xntm", pk), "iden"], writes=[("PT", pk)],
                        tag=f"xT_{cc}_{kc}")
                P.add("vector", lambda e, c=c, pk=pk: e.tensor_copy(
                    xnfm[:].rearrange("p (k tt) -> p k tt", tt=TT)
                    [:, :, c * CH:(c + 1) * CH],
                    PT[:, pk * D:(pk + 1) * D].rearrange("p (k w) -> p k w", w=CH)),
                    reads=[("PT", pk)], writes=[("xnfm", c)], tag=f"xTe_{cc}")
            P.dma("gpsimd", lambda e, t=t: e.dma_start(
                out=h2row_s[:, t * TT:(t + 1) * TT].rearrange(
                    "o (c p) -> (o p) c", p=CH),
                in_=statsH2[:, t * TCH:(t + 1) * TCH],
                allow_slow_non_contiguous=True),
                reads=[("stH", t * TCH + c2) for c2 in range(TCH)],
                writes=["h2row"], tag=f"h2r_{t}")

            for mc in range(NKC):
                bank = mc % 2
                def mm_k(e, mc=mc, bank=bank):
                    last = None
                    for kc in range(NKC):
                        last = e.matmul(PF[bank][:],
                                        kT[:, kc * D + mc * CH:kc * D + (mc + 1) * CH],
                                        dch(xnfm, kc, TT),
                                        start=(kc == 0), stop=(kc == NKC - 1))
                    return last
                P.add("tensor", mm_k,
                      reads=[("kT", kc2) for kc2 in range(NKC)]
                      + [("xnfm", c2) for c2 in range(TCH)],
                      writes=[f"PF{bank}"], tag=f"k_{t}_{mc}")
                P.add("scalar", lambda e, mc=mc, bank=bank: e.activation(
                    dch(kfm, mc, TT), PF[bank][:], AF.Copy),
                    reads=[f"PF{bank}"], writes=[("kfm", mc)], tag=f"ke_{t}_{mc}")

            for c in range(TCH):
                for vg in range(2):
                    bank = 2 + vg
                    def mm_v(e, c=c, vg=vg, bank=bank):
                        last = None
                        for kc in range(NKC):
                            last = e.matmul(
                                PF[bank][:],
                                xnfm[:, kc * TT + c * CH:kc * TT + (c + 1) * CH],
                                vT[:, kc * D + vg * 512:kc * D + (vg + 1) * 512],
                                start=(kc == 0), stop=(kc == NKC - 1))
                        return last
                    P.add("tensor", mm_v,
                          reads=[("vT", kc2) for kc2 in range(NKC)] + [("xnfm", c)],
                          writes=[f"PF{bank}"], tag=f"v_{t}_{c}_{vg}")
                    def ev_v(e, c=c, vg=vg, bank=bank):
                        dst = vext[:].rearrange("p (cc h s) -> p cc h s", h=H, s=66)[
                            :, c, vg * 8:(vg + 1) * 8, 0:64]
                        return e.tensor_copy(dst, PF[bank][:].rearrange(
                            "p (h w) -> p h w", w=64))
                    P.add("vector", ev_v, reads=[f"PF{bank}"],
                          writes=[("vext", c, vg)], tag=f"ve_{t}_{c}_{vg}")
            for c in range(TCH):
                for pr in range(H // 2):
                    bank = (c * (H // 2) + pr) % 4
                    pkk = pr % 2
                    P.add("tensor", lambda e, c=c, pr=pr, bank=bank: e.matmul(
                        PF[bank][:],
                        kfm[:, pr * TT + c * CH:pr * TT + (c + 1) * CH],
                        dch(projBD, pr, 2 * FF),
                        start=True, stop=True),
                        reads=[("kfm", pr), ("projBD", pr)],
                        writes=[f"PF{bank}"], tag=f"kf_{t}_{c}_{pr}")
                    eb = dch(ebuf, pkk, 2 * FF)
                    tb = dch(tbuf, pkk, 2 * FF)
                    P.add("scalar", lambda e, bank=bank, eb=eb: e.activation(
                        eb, PF[bank][:], AF.Exp),
                        reads=[f"PF{bank}"], writes=[("eb", pkk)],
                        tag=f"kfE_{t}_{c}_{pr}")
                    P.add("vector", lambda e, bank=bank, tb=tb: e.tensor_scalar(
                        tb, PF[bank][:], 1.0, 1.0, ALU.add, ALU.max),
                        reads=[f"PF{bank}"], writes=[("tb", pkk)],
                        tag=f"kfT_{t}_{c}_{pr}")
                    P.add("vector", lambda e, c=c, pr=pr, eb=eb, tb=tb: e.tensor_tensor(
                        kftm[:, (c * H + pr * 2) * FF:(c * H + pr * 2 + 2) * FF],
                        eb, tb, ALU.min),
                        reads=[("eb", pkk), ("tb", pkk)],
                        writes=[("kftm", c, pr)], tag=f"kfM_{t}_{c}_{pr}")
            for pr in range(H // 2):
                bank = 4 + (pr % 2)
                def mm_kv(e, pr=pr, bank=bank):
                    last = None
                    for c in range(TCH):
                        for hh in range(2):
                            h0 = pr * 2 + hh
                            last = e.matmul(
                                PF[bank][0:65, hh * FF:(hh + 1) * FF],
                                vext[:, (c * H + h0) * 66:(c * H + h0) * 66 + 65],
                                kftm[:, (c * H + h0) * FF:(c * H + h0 + 1) * FF],
                                start=(c == 0 and hh == 0),
                                stop=(c == TCH - 1 and hh == 1))
                    return last
                P.add("tensor", mm_kv,
                      reads=[("vext", c2, vg) for c2 in range(TCH) for vg in range(2)]
                      + [("kftm", c2, pr) for c2 in range(TCH)],
                      writes=[f"PF{bank}"], tag=f"kv_{t}_{pr}")
                if t == 0:
                    P.add("vector", lambda e, pr=pr, bank=bank: e.tensor_copy(
                        kvacc[:, pr * 2 * FF:(pr * 2 + 2) * FF], PF[bank][0:65, :]),
                        reads=[f"PF{bank}"], writes=[("kva", pr)], tag=f"kve_{t}_{pr}")
                else:
                    P.add("vector", lambda e, pr=pr, bank=bank: e.tensor_add(
                        kvacc[:, pr * 2 * FF:(pr * 2 + 2) * FF],
                        kvacc[:, pr * 2 * FF:(pr * 2 + 2) * FF], PF[bank][0:65, :]),
                        reads=[f"PF{bank}", ("kva", pr)], writes=[("kva", pr)],
                        tag=f"kve_{t}_{pr}")

        for _ in _pg:
            pass
        def ptoff(s):
            return (s % 2) * 1024 + (s // 2) * 66
        P.add("vector", lambda e: e.tensor_copy(wst2[0:65, 0:H * FF], kvacc[:]),
              reads=[("kva", pr) for pr in range(H // 2)],
              writes=[("wst2", q) for q in range(4)], tag="kv_cast")
        for h0 in range(H):
            for fc in range(2):
                sl = h0 * 2 + fc
                hf = sl % 2
                P.add("tensor", lambda e, h0=h0, fc=fc, sl=sl: e.transpose(
                    PT[:, ptoff(sl % 16):ptoff(sl % 16) + 65],
                    wst2[0:65, h0 * FF + fc * CH:h0 * FF + (fc + 1) * CH],
                    iden[0:65, 0:65]),
                    reads=[("wst2", q) for q in range(4)] + ["iden"],
                    writes=[("PT", hf)], tag=f"kvT_{h0}_{fc}")
                P.add("vector", lambda e, sl=sl: e.tensor_copy(
                    kvl[:, sl * 66:sl * 66 + 65],
                    PT[:, ptoff(sl % 16):ptoff(sl % 16) + 65]),
                    reads=[("PT", hf)], writes=["kvl"], tag=f"kvle_{sl}")
        build_nc._stash = dict(sred=sred, wst=wst)
        P.emit(nc, sems)
      except _Skip:
        pass

    with contextlib.ExitStack() as p2:
      try:
        _guard(2)
        sb2 = lambda n, s, d_: p2.enter_context(nc.sbuf_tensor(n, s, d_))
        ps2 = lambda n, s, d_: p2.enter_context(nc.psum_tensor(n, s, d_))
        qT = sb2("qT", [CH, NKC * D], BF16)
        projQ = sb2("pjq", [CH, 2 * H * CH], BF16)
        woT = sb2("woT", [CH, NKC * D], BF16)
        wst = sb2("wsb", [CH, 2 * D], F32)
        xin = sb2("xnb", [CH, 2 * D], F32)
        xpin = xin
        xntm = sb2("xtb", [CH, 2 * D], BF16)
        xnfm = sb2("xfb", [CH, 2 * NKC * TT], BF16)
        qfm = sb2("qfm", [CH, NKC * TT], BF16)
        shA = qfm
        qf = sb2("qf", [CH, 2 * H * TT], BF16)
        ebf2 = sb2("eb2", [CH, 2 * TT], BF16)
        tbf2 = sb2("tb2", [CH, 2 * TT], BF16)
        attnr = sb2("atr", [CH, 2 * D], BF16)
        z16 = sb2("z16", [CH, 16], F32)
        zr16 = sb2("zrs", [CH, 16], F32)
        attnfm = sb2("atf", [CH, NKC * TT], BF16)
        x1b = sb2("x1b", [CH, NKC * TT], BF16)
        x1sq = shA
        x1n = sb2("x1m", [CH, NKC * TT], BF16)
        rows = sb2("rws", [1, 4 * TT], F32)
        bc1 = sb2("bc1", [CH, TT], F32)
        bc2 = sb2("bc2", [CH, TT], F32)
        xq1 = sb2("xq1", [CH, TT], F32)
        xq2 = sb2("xq2", [CH, TT], F32)
        xqb = sb2("xqb", [CH, NKC * TT], BF16)
        ybf = shA
        xpart = sb2("xpt", [CH, 2 * D], F32)
        h2r = sb2("h2r", [1, TT], F32)
        PT2 = ps2("PT2", [CH, 2 * D], BF16)
        PT2b = ps2("PT2b", [CH, D], BF16)
        PF = [ps2(f"QF{i}", [CH, 512], F32) for i in range(5)]
        P = Plan(state, drop,
                 psum_keys=[f"QF{i}" for i in range(5)]
                 + [("PT2", 0), ("PT2", 1), "PT2b"])

        def row(i):
            return rows[:, i * TT:(i + 1) * TT]

        for pq in range(4):
            q = pq % 2
            ssl = wst[:, q * D:(q + 1) * D]
            P.dma("scalar", lambda e, pq=pq, ssl=ssl: e.dma_start(
                out=ssl, in_=projQ_d[:, pq * D:(pq + 1) * D]),
                writes=[("wst", q)], tag=f"ldpq_{pq}")
            P.add("vector", lambda e, pq=pq, ssl=ssl: e.tensor_copy(
                projQ[:, pq * D:(pq + 1) * D], ssl),
                reads=[("wst", q)], writes=["projQ"], tag=f"cpq_{pq}")
        for kc in range(NKC):
            q = kc % 2
            ssl = wst[:, q * D:(q + 1) * D]
            P.dma("scalar", lambda e, kc=kc, ssl=ssl: e.dma_start(
                out=ssl, in_=wqkvT_d[kc * CH:(kc + 1) * CH, 0:D]),
                writes=[("wst", q)], tag=f"ldq_{kc}")
            P.add("vector", lambda e, kc=kc, ssl=ssl: e.tensor_copy(
                dch(qT, kc, D), ssl),
                reads=[("wst", q)], writes=[("qT", kc)], tag=f"cq_{kc}")
        for kc in range(NKC):
            q = kc % 2
            ssl = wst[:, q * D:(q + 1) * D]
            P.dma("scalar", lambda e, kc=kc, ssl=ssl: e.dma_start(
                out=ssl, in_=woutT_d[kc * CH:(kc + 1) * CH, :]),
                writes=[("wst", q)], tag=f"ldo_{kc}")
            P.add("vector", lambda e, kc=kc, ssl=ssl: e.tensor_copy(
                dch(woT, kc, D), ssl),
                reads=[("wst", q)], writes=[("woT", kc)], tag=f"co_{kc}")

        P.dma("gpsimd", lambda e: e.dma_start(out=srow[:], in_=gacc[:], allow_slow_non_contiguous=True),
              reads=["gacc"], writes=["srow"], tag="fold_g")
        P.add("vector", lambda e: e.reduce_max(scal[:, 3:4], srow[:], axis=AX),
              reads=["srow"], writes=[("scal", 3)], tag="gloc", late=True)
        P.dma("gpsimd", lambda e: e.dma_start(out=cc_in, in_=scal[:, 3:4]),
              reads=[("scal", 3)], writes=["cc_in"], tag="st_cc")
        P.add("gpsimd", lambda e: e.collective_compute(
            "AllReduce", ALU.max, replica_groups=[list(range(NC))],
            ins=[cc_in], outs=[cc_out]),
            reads=["cc_in"], writes=["cc_out"], kind="cc", tag="ar_gamma")
        P.dma("gpsimd", lambda e: e.dma_start(out=scal[:, 4:5], in_=cc_out),
              reads=["cc_out"], writes=[("scal", 4)], tag="ld_cc")
        P.add("vector", lambda e: e.reciprocal(scal[:, 5:6], scal[:, 4:5]),
              reads=[("scal", 4)], writes=[("scal", 5)], tag="rg0", late=True)
        P.add("vector", lambda e: e.tensor_scalar_mul(scal[:, 6:7], scal[:, 5:6], QB),
              reads=[("scal", 5)], writes=[("scal", 6)], tag="rgam")
        P.add("vector", lambda e: e.tensor_mul(scal[:, 7:8], scal[:, 4:5], scal[:, 0:1]),
              reads=[("scal", 4), ("scal", 0)], writes=[("scal", 7)], tag="gb")
        P.add("vector", lambda e: e.tensor_scalar_mul(
            scal[:, 8:9], scal[:, 7:8], -1.0 / (QB * D * D)),
            reads=[("scal", 7)], writes=[("scal", 8)], tag="nys")
        P.add("tensor", lambda e: e.matmul(PF[0][:, 0:4], onesr[:], scal[:, 5:9],
                                           start=True, stop=True),
              reads=[("scal", 6), ("scal", 8), "onesr"], writes=["QF0"], tag="bc_g")
        P.add("vector", lambda e: e.tensor_copy(gsc[:, 0:1], PF[0][:, 1:2]),
              reads=["QF0"], writes=[("gsc", 0)], tag="ev_rg")
        P.add("vector", lambda e: e.tensor_copy(gsc[:, 1:2], PF[0][:, 3:4]),
              reads=["QF0"], writes=[("gsc", 1)], tag="ev_ny")
        P.add("vector", lambda e: e.tensor_single_scalar(
            grow[:], onesr[:], scal[:, 6:7], ALU.mult),
            reads=[("scal", 6), "onesr"], writes=["grow"], tag="grow")
        P.add("vector", lambda e: e.tensor_copy(lngyc[:], LNG),
              reads=["vecs"], writes=["lngyc"], tag="lngyc")
        P.add("vector", lambda e: e.tensor_single_scalar(
            lnbyc[:], LNB, gsc[:, 0:1], ALU.mult),
            reads=["vecs", ("gsc", 0)], writes=["lnbyc"], tag="lnbyc")

        def ln1_tail(t):
                def mm_s1(e):
                    last = None
                    for kc in range(NKC):
                        last = e.matmul(PF[4][0:1, :], onesc[:], dch(x1b, kc, TT),
                                        start=(kc == 0), stop=(kc == NKC - 1))
                    return last
                P.add("tensor", mm_s1,
                      reads=[("x1b", kc2) for kc2 in range(NKC)],
                      writes=["QF4"], tag=f"s1_{t}")
                P.add("vector", lambda e: e.tensor_scalar_mul(
                    row(0), PF[4][0:1, :], 1.0 / D),
                    reads=["QF4"], writes=[("rw", 0)], tag=f"s1e_{t}")
                def mm_s2(e):
                    last = None
                    for kc in range(NKC):
                        last = e.matmul(PF[4][0:1, :], onesc[:], dch(x1sq, kc, TT),
                                        start=(kc == 0), stop=(kc == NKC - 1))
                    return last
                P.add("tensor", mm_s2,
                      reads=[("shA", kc2) for kc2 in range(NKC)],
                      writes=["QF4"], tag=f"s2_{t}")
                P.add("vector", lambda e: e.tensor_scalar_mul(
                    row(1), PF[4][0:1, :], 1.0 / D),
                    reads=["QF4"], writes=[("rw", 1)], tag=f"s2e_{t}")
                P.add("vector", lambda e: e.tensor_mul(row(3), row(0), row(0)),
                      reads=[("rw", 0)], writes=[("rw", 3)], tag=f"v2_{t}")
                P.add("vector", lambda e: e.tensor_sub(row(2), row(1), row(3)),
                      reads=[("rw", 1), ("rw", 3)], writes=[("rw", 2)], tag=f"v3_{t}")
                P.add("scalar", lambda e: e.activation(
                    row(3), row(2), AF.Ln, bias=EPS1r),
                    reads=[("rw", 2)], writes=[("rw", 3)], tag=f"v4_{t}")
                P.add("scalar", lambda e: e.activation(
                    row(2), row(3), AF.Exp, scale=-0.5),
                    reads=[("rw", 3)], writes=[("rw", 2)], tag=f"r1_{t}")
                P.add("vector", lambda e: e.scalar_tensor_tensor(
                    row(3), row(0), -1.0, row(2), ALU.mult, ALU.mult),
                    reads=[("rw", 0), ("rw", 2)], writes=[("rw", 3)], tag=f"q1_{t}")
                P.add("tensor", lambda e: e.matmul(PF[4][:], onesr[:], row(2),
                                                   start=True, stop=True),
                      reads=[("rw", 2)], writes=["QF4"], tag=f"b1_{t}")
                P.add("vector", lambda e: e.tensor_copy(bc1[:], PF[4][:]),
                      reads=["QF4"], writes=["bc1"], tag=f"b1e_{t}")
                P.add("tensor", lambda e: e.matmul(PF[4][:], onesr[:], row(3),
                                                   start=True, stop=True),
                      reads=[("rw", 3)], writes=["QF4"], tag=f"b2_{t}")
                P.add("vector", lambda e: e.tensor_copy(bc2[:], PF[4][:]),
                      reads=["QF4"], writes=["bc2"], tag=f"b2e_{t}")
                for mc in range(NKC):
                    P.add("vector", lambda e, mc=mc: e.tensor_mul(
                        xq2[:], dch(x1b, mc, TT), bc1[:]),
                        reads=[("x1b", mc), "bc1"], writes=["xq2"], tag=f"n1_{t}_{mc}")
                    P.add("vector", lambda e: e.tensor_add(xq2[:], xq2[:], bc2[:]),
                          reads=["xq2", "bc2"], writes=["xq2"], tag=f"n2_{t}_{mc}")
                    P.add("vector", lambda e, mc=mc: e.tensor_scalar(
                        dch(x1n, mc, TT), xq2[:], GAM[:, mc:mc + 1], BET[:, mc:mc + 1],
                        ALU.mult, ALU.add),
                        reads=["xq2"], writes=[("x1n", mc)], tag=f"n3_{t}_{mc}")
                    P.dma("gpsimd", lambda e, mc=mc, t=t: e.dma_start(
                        out=x1n_s[mc * CH:(mc + 1) * CH, t * TT:(t + 1) * TT],
                        in_=dch(x1n, mc, TT)),
                        reads=[("x1n", mc)], writes=[("x1ns", t, mc)], tag=f"nS_{t}_{mc}")

        for t in range(NTL):
            xo = (t % 2) * NKC * TT
            xnk = t % 2

            def xch(kc, xo=None, t=t):
                o = (t % 2) * NKC * TT
                return xnfm[:, o + kc * TT:o + (kc + 1) * TT]
            for c in range(TCH):
                cc = t * TCH + c
                pk = c % 2
                xs = dch(xin, pk, D)
                P.dma("sync", lambda e, cc=cc, xs=xs: e.dma_start(
                    out=xs, in_=x_d[cc * CH:(cc + 1) * CH, :]),
                    writes=[("xin", pk)], tag=f"p2x_{cc}")
                P.add("scalar", lambda e, xs=xs, pk=pk, cc=cc: e.activation(
                    dch(xntm, pk, D), xs, AF.Identity,
                    bias=statsMUR[:, cc:cc + 1], scale=statsR[:, cc:cc + 1]),
                    reads=[("xin", pk)], writes=[("xntm", pk)], tag=f"p2n_{cc}")
                for kc in range(NKC):
                    P.add("tensor", lambda e, kc=kc, pk=pk: e.transpose(
                        PT2b[:, kc * CH:(kc + 1) * CH],
                        xntm[:, pk * D + kc * CH:pk * D + (kc + 1) * CH], iden[:]),
                        reads=[("xntm", pk)], writes=["PT2b"],
                        tag=f"p2T_{cc}_{kc}")
                P.add("vector", lambda e, c=c, pk=pk, xo=xo: e.tensor_copy(
                    xnfm[:, xo:xo + NKC * TT].rearrange("p (k tt) -> p k tt", tt=TT)
                    [:, :, c * CH:(c + 1) * CH],
                    PT2b[:].rearrange("p (k w) -> p k w", w=CH)),
                    reads=["PT2b"], writes=[("xnfm", xnk, c)], tag=f"p2e_{cc}")
            if t > 0:
                ln1_tail(t - 1)
            P.dma("sync", lambda e, t=t: e.dma_start(
                out=h2r[:], in_=h2row_s[:, t * TT:(t + 1) * TT]),
                writes=["h2r"], tag=f"fh_{t}")
            P.add("tensor", lambda e: e.matmul(PF[4][:], grow[:], h2r[:],
                                               start=True, stop=True),
                  reads=["h2r"], writes=["QF4"], tag=f"fb_{t}")
            for mc in range(NKC):
                P.add("vector", lambda e, mc=mc, xch=xch: e.scalar_tensor_tensor(
                    xq1[:], xch(mc), lngyc[:, mc:mc + 1], PF[4][:],
                    ALU.mult, ALU.mult),
                    reads=[("xnfm", xnk, c2) for c2 in range(TCH)] + ["QF4"],
                    writes=["xq1"], tag=f"f1_{t}_{mc}")
                P.add("vector", lambda e, mc=mc: e.tensor_scalar(
                    xq1[:], xq1[:], lnbyc[:, mc:mc + 1], CLIP, ALU.add, ALU.min),
                    reads=["xq1"], writes=["xq1"], tag=f"f2_{t}_{mc}")
                P.add("vector", lambda e, mc=mc: e.tensor_scalar_max(
                    dch(xqb, mc, TT), xq1[:], -CLIP),
                    reads=["xq1"], writes=[("xqb", mc)], tag=f"f3_{t}_{mc}")
            for mc in range(NKC):
                bank = mc % 2
                def mm_q(e, mc=mc, bank=bank, xch=xch):
                    last = None
                    for kc in range(NKC):
                        last = e.matmul(PF[bank][:],
                                        qT[:, kc * D + mc * CH:kc * D + (mc + 1) * CH],
                                        xch(kc),
                                        start=(kc == 0), stop=(kc == NKC - 1))
                    return last
                P.add("tensor", mm_q,
                      reads=[("qT", kc2) for kc2 in range(NKC)]
                      + [("xnfm", xnk, c2) for c2 in range(TCH)],
                      writes=[f"QF{bank}"], tag=f"q_{t}_{mc}")
                P.add("scalar", lambda e, mc=mc, bank=bank: e.activation(
                    dch(qfm, mc, TT), PF[bank][:], AF.Copy),
                    reads=[f"QF{bank}"], writes=[("shA", mc)], tag=f"qe_{t}_{mc}")
            for fc in range(2):
                for h0 in range(H):
                    pr, hh = h0 // 2, h0 % 2
                    base = hh * 64
                    sl = h0 * 2 + fc
                    bank = (fc * H + h0) % 4
                    P.add("tensor",
                          lambda e, pr=pr, sl=sl, bank=bank: e.matmul(
                              PF[bank][:],
                              dch(projQ, sl, CH),
                              dch(qfm, pr, TT),
                              start=True, stop=True),
                          reads=[("shA", pr), "projQ"], writes=[f"QF{bank}"],
                          tag=f"qf_{t}_{h0}_{fc}")
                    pe2 = sl % 2
                    P.add("scalar", lambda e, bank=bank, pe2=pe2: e.activation(
                        dch(ebf2, pe2, TT), PF[bank][:], AF.Exp),
                        reads=[f"QF{bank}"], writes=[("ebf2", pe2)],
                        tag=f"qfE_{t}_{h0}_{fc}")
                    P.add("vector", lambda e, bank=bank, pe2=pe2: e.tensor_scalar(
                        dch(tbf2, pe2, TT), PF[bank][:], 1.0, 1.0, ALU.add, ALU.max),
                        reads=[f"QF{bank}"], writes=[("tbf2", pe2)],
                        tag=f"qfT_{t}_{h0}_{fc}")
                    P.add("vector", lambda e, sl=sl, pe2=pe2: e.tensor_tensor(
                        qf[:, sl * TT:(sl + 1) * TT], dch(ebf2, pe2, TT),
                        dch(tbf2, pe2, TT), ALU.min),
                        reads=[("ebf2", pe2), ("tbf2", pe2)], writes=[("qf", sl)],
                        tag=f"qfM_{t}_{h0}_{fc}")
            for c in range(TCH):
                pk = c % 2
                for bl in range(4):
                    bank = 2 + bl % 2
                    def mm_a(e, c=c, bl=bl, bank=bank):
                        last = None
                        for s in range(4):
                            h0 = bl * 4 + s
                            for fc in range(2):
                                sl = h0 * 2 + fc
                                last = e.matmul(
                                    PF[bank][:, s * CH:s * CH + 65],
                                    qf[:, sl * TT + c * CH:sl * TT + (c + 1) * CH],
                                    kvl[:, sl * 66:sl * 66 + 65],
                                    start=(fc == 0), stop=(fc == 1))
                        return last
                    P.add("tensor", mm_a,
                          reads=[("qf", (bl * 4 + s) * 2 + fc)
                                 for s in range(4) for fc in range(2)],
                          writes=[f"QF{bank}"], tag=f"a_{t}_{c}_{bl}")
                    P.add("scalar", lambda e, bl=bl, bank=bank, pk=pk: e.activation(
                        attnr[:, pk * D + bl * 4 * 64:pk * D + (bl + 1) * 4 * 64]
                        .rearrange("p (s w) -> p s w", w=64),
                        PF[bank][:].rearrange("p (s w) -> p s w", w=CH)[:, :, 0:64],
                        AF.Copy),
                        reads=[f"QF{bank}"], writes=[("atr", pk)],
                        tag=f"ae_{t}_{c}_{bl}")
                    P.add("vector", lambda e, bl=bl, bank=bank: e.tensor_scalar_add(
                        z16[:, bl * 4:(bl + 1) * 4].rearrange(
                            "p (s o) -> p s o", o=1),
                        PF[bank][:].rearrange("p (s w) -> p s w", w=CH)[:, :, 64:65],
                        EPS_ATTN),
                        reads=[f"QF{bank}"], writes=[("z16", bl)],
                        tag=f"az_{t}_{c}_{bl}")
                P.add("vector", lambda e: e.reciprocal(zr16[:], z16[:]),
                      reads=[("z16", bl2) for bl2 in range(4)], writes=["zr16"],
                      tag=f"ar_{t}_{c}", late=True)
                P.add("vector", lambda e, pk=pk: e.tensor_tensor(
                    dch(attnr, pk, D).rearrange("p (h w) -> p h w", w=64),
                    dch(attnr, pk, D).rearrange("p (h w) -> p h w", w=64),
                    zr16[:].rearrange("p (h o) -> p h o", o=1).broadcast_to(
                        (CH, H, 64)), ALU.mult),
                    reads=["zr16", ("atr", pk)], writes=[("atr", pk)],
                    tag=f"as_{t}_{c}")
                for kc in range(NKC):
                    P.add("tensor", lambda e, kc=kc, pk=pk: e.transpose(
                        PT2[:, pk * D + kc * CH:pk * D + (kc + 1) * CH],
                        attnr[:, pk * D + kc * CH:pk * D + (kc + 1) * CH], iden[:]),
                        reads=[("atr", pk)], writes=[("PT2", pk)],
                        tag=f"aT_{t}_{c}_{kc}")
                P.add("vector", lambda e, c=c, pk=pk: e.tensor_copy(
                    attnfm[:].rearrange("p (k tt) -> p k tt", tt=TT)
                    [:, :, c * CH:(c + 1) * CH],
                    PT2[:, pk * D:(pk + 1) * D].rearrange("p (k w) -> p k w", w=CH)),
                    reads=[("PT2", pk)], writes=[("atf", c)], tag=f"aTe_{t}_{c}")
            for mc in range(NKC):
                bank = mc % 2
                def mm_y(e, mc=mc, bank=bank):
                    last = None
                    for kc in range(NKC):
                        last = e.matmul(PF[bank][:],
                                        mwq[:, kc * D + mc * CH:kc * D + (mc + 1) * CH],
                                        dch(xqb, kc, TT),
                                        start=(kc == 0), stop=(kc == NKC - 1))
                    return last
                P.add("tensor", mm_y,
                      reads=[("xqb", kc2) for kc2 in range(NKC)],
                      writes=[f"QF{bank}"], tag=f"y_{t}_{mc}")
                P.add("scalar", lambda e, mc=mc, bank=bank: e.activation(
                    dch(ybf, mc, TT), PF[bank][:], AF.Copy),
                    reads=[f"QF{bank}"], writes=[("shA", mc)], tag=f"ye_{t}_{mc}")
            for c in range(TCH):
                cc = t * TCH + c
                pk = c % 2
                for kc in range(NKC):
                    P.add("tensor", lambda e, kc=kc, pk=pk, c=c: e.transpose(
                        PT2[:, pk * D + kc * CH:pk * D + (kc + 1) * CH],
                        ybf[:, kc * TT + c * CH:kc * TT + (c + 1) * CH], iden[:]),
                        reads=[("shA", kc)], writes=[("PT2", pk)],
                        tag=f"yT_{t}_{c}_{kc}")
                xs = dch(xpin, pk, D)
                P.dma("sync", lambda e, cc=cc, xs=xs: e.dma_start(
                    out=xs, in_=x_d[cc * CH:(cc + 1) * CH, :]),
                    writes=[("xin", pk)], tag=f"px_{cc}")
                P.add("vector", lambda e, pk=pk: e.scalar_tensor_tensor(
                    dch(xpart, pk, D), PT2[:, pk * D:(pk + 1) * D], gsc[:, 1:2],
                    dch(xpin, pk, D), ALU.mult, ALU.add),
                    reads=[("PT2", pk), ("xin", pk)],
                    writes=[("xpart", pk)], tag=f"pp_{cc}")
                P.dma("scalar", lambda e, cc=cc, pk=pk: e.dma_start(
                    out=part_s[cc * CH:(cc + 1) * CH, :], in_=dch(xpart, pk, D)),
                    reads=[("xpart", pk)], writes=[("parts", cc)], tag=f"ps_{cc}")
            for mc in range(NKC):
                bank = mc % 2
                def mm_o(e, mc=mc, bank=bank):
                    last = None
                    for kc in range(NKC):
                        last = e.matmul(PF[bank][:],
                                        woT[:, kc * D + mc * CH:kc * D + (mc + 1) * CH],
                                        dch(attnfm, kc, TT),
                                        start=(kc == 0), stop=(kc == NKC - 1))
                    return last
                P.add("tensor", mm_o,
                      reads=[("woT", kc2) for kc2 in range(NKC)]
                      + [("atf", c2) for c2 in range(TCH)],
                      writes=[f"QF{bank}"], tag=f"o_{t}_{mc}")
                P.add("vector", lambda e, mc=mc, bank=bank, xch=xch: e.scalar_tensor_tensor(
                    dch(x1b, mc, TT), PF[bank][:], B_OUT[:, mc:mc + 1],
                    xch(mc), ALU.add, ALU.add),
                    reads=[f"QF{bank}"] + [("xnfm", xnk, c2) for c2 in range(TCH)],
                    writes=[("x1b", mc)], tag=f"x1_{t}_{mc}")
                P.add("scalar", lambda e, mc=mc: e.activation(
                    dch(x1sq, mc, TT), dch(x1b, mc, TT), AF.Square),
                    reads=[("x1b", mc)], writes=[("shA", mc)], tag=f"x1q_{t}_{mc}")
        ln1_tail(NTL - 1)
        P.emit(nc, sems)
      except _Skip:
        pass

    with contextlib.ExitStack() as p3:
      try:
        _guard(3)
        sb3 = lambda n, s, d_: p3.enter_context(nc.sbuf_tensor(n, s, d_))
        ps3 = lambda n, s, d_: p3.enter_context(nc.psum_tensor(n, s, d_))
        w2r = sb3("w2r", [CH, NMF * D], BF16)
        w1b = sb3("w1b", [CH, 2 * D], BF16)
        x1nb = sb3("x1v", [CH, 2 * NKC * TT], BF16)
        hbuf = sb3("hbf", [CH, NMF * TT], BF16)
        x2b = sb3("x2b", [CH, NKC * TT], BF16)
        x2sq = sb3("x2q", [CH, NKC * TT], BF16)
        aob = sb3("aob", [CH, NKC * TT], BF16)
        rows = sb3("rw3", [1, 4 * TT], F32)
        bc1 = sb3("bc3", [CH, TT], F32)
        bc2 = sb3("bc4", [CH, TT], F32)
        tsc = sb3("tsc", [CH, TT], F32)
        xpart = sb3("xp3", [CH, 2 * D], F32)
        outb = sb3("oub", [CH, 2 * D], F32)
        PT3 = ps3("PT3", [CH, 2 * D], BF16)
        PF = [ps3(f"RF{i}", [CH, 512], F32) for i in range(5)]
        P = Plan(state, drop,
                 psum_keys=[f"RF{i}" for i in range(5)] + [("PT3", 0), ("PT3", 1)])

        def row(i):
            return rows[:, i * TT:(i + 1) * TT]

        for kc in range(NMF):
            P.dma("sync", lambda e, kc=kc: e.dma_start(
                out=dch(w2r, kc, D), in_=w2s_s[kc * CH:(kc + 1) * CH, :]),
                writes=[("w2r", kc)], tag=f"lw2_{kc}")

        def ln2_tail(t):
                def mm_t1(e):
                    last = None
                    for kc in range(NKC):
                        last = e.matmul(PF[4][0:1, :], onesc[:], dch(x2b, kc, TT),
                                        start=(kc == 0), stop=(kc == NKC - 1))
                    return last
                P.add("tensor", mm_t1, reads=[("x2b", kc2) for kc2 in range(NKC)],
                      writes=["RF4"], tag=f"t1_{t}")
                P.add("vector", lambda e: e.tensor_scalar_mul(
                    row(0), PF[4][0:1, :], 1.0 / D),
                    reads=["RF4"], writes=[("rw", 0)], tag=f"t1e_{t}")
                def mm_t2(e):
                    last = None
                    for kc in range(NKC):
                        last = e.matmul(PF[4][0:1, :], onesc[:], dch(x2sq, kc, TT),
                                        start=(kc == 0), stop=(kc == NKC - 1))
                    return last
                P.add("tensor", mm_t2, reads=[("x2sq", kc2) for kc2 in range(NKC)],
                      writes=["RF4"], tag=f"t2_{t}")
                P.add("vector", lambda e: e.tensor_scalar_mul(
                    row(1), PF[4][0:1, :], 1.0 / D),
                    reads=["RF4"], writes=[("rw", 1)], tag=f"t2e_{t}")
                P.add("vector", lambda e: e.tensor_mul(row(3), row(0), row(0)),
                      reads=[("rw", 0)], writes=[("rw", 3)], tag=f"u2_{t}")
                P.add("vector", lambda e: e.tensor_sub(row(2), row(1), row(3)),
                      reads=[("rw", 1), ("rw", 3)], writes=[("rw", 2)], tag=f"u3_{t}")
                P.add("scalar", lambda e: e.activation(
                    row(3), row(2), AF.Ln, bias=EPS1r),
                    reads=[("rw", 2)], writes=[("rw", 3)], tag=f"u4_{t}")
                P.add("scalar", lambda e: e.activation(
                    row(2), row(3), AF.Exp, scale=-0.5),
                    reads=[("rw", 3)], writes=[("rw", 2)], tag=f"r2_{t}")
                P.add("vector", lambda e: e.scalar_tensor_tensor(
                    row(3), row(0), -1.0, row(2), ALU.mult, ALU.mult),
                    reads=[("rw", 0), ("rw", 2)], writes=[("rw", 3)], tag=f"q2_{t}")
                P.add("tensor", lambda e: e.matmul(PF[4][:], onesr[:], row(2),
                                                   start=True, stop=True),
                      reads=[("rw", 2)], writes=["RF4"], tag=f"w1b_{t}")
                P.add("vector", lambda e: e.tensor_copy(bc1[:], PF[4][:]),
                      reads=["RF4"], writes=["bc1"], tag=f"w1e_{t}")
                P.add("tensor", lambda e: e.matmul(PF[4][:], onesr[:], row(3),
                                                   start=True, stop=True),
                      reads=[("rw", 3)], writes=["RF4"], tag=f"w2b_{t}")
                P.add("vector", lambda e: e.tensor_copy(bc2[:], PF[4][:]),
                      reads=["RF4"], writes=["bc2"], tag=f"w2e_{t}")
                for mc in range(NKC):
                    P.add("vector", lambda e, mc=mc: e.tensor_mul(
                        tsc[:], dch(x2b, mc, TT), bc1[:]),
                        reads=[("x2b", mc), "bc1"], writes=["tsc"], tag=f"m1_{t}_{mc}")
                    P.add("vector", lambda e: e.tensor_add(tsc[:], tsc[:], bc2[:]),
                          reads=["tsc", "bc2"], writes=["tsc"], tag=f"m2_{t}_{mc}")
                    P.add("vector", lambda e, mc=mc: e.tensor_scalar(
                        dch(aob, mc, TT), tsc[:], GAM[:, mc:mc + 1], BET[:, mc:mc + 1],
                        ALU.mult, ALU.add),
                        reads=["tsc"], writes=[("aob", mc)], tag=f"m3_{t}_{mc}")
                for c in range(TCH):
                    cc = t * TCH + c
                    pk = c % 2
                    for kc in range(NKC):
                        P.add("tensor", lambda e, kc=kc, pk=pk, c=c: e.transpose(
                            PT3[:, pk * D + kc * CH:pk * D + (kc + 1) * CH],
                            aob[:, kc * TT + c * CH:kc * TT + (c + 1) * CH], iden[:]),
                            reads=[("aob", kc)], writes=[("PT3", pk)],
                            tag=f"oT_{t}_{c}_{kc}")
                    xs = dch(xpart, pk, D)
                    P.dma("gpsimd", lambda e, cc=cc, xs=xs: e.dma_start(
                        out=xs, in_=part_s[cc * CH:(cc + 1) * CH, :]),
                        writes=[("xp3", pk)], tag=f"lp_{cc}")
                    P.add("vector", lambda e, pk=pk: e.tensor_add(
                        dch(outb, pk, D), PT3[:, pk * D:(pk + 1) * D], dch(xpart, pk, D)),
                        reads=[("PT3", pk), ("xp3", pk)], writes=[("oub", pk)],
                        tag=f"os_{cc}")
                    P.dma("gpsimd", lambda e, cc=cc, pk=pk: e.dma_start(
                        out=out_d[cc * CH:(cc + 1) * CH, :], in_=dch(outb, pk, D)),
                        reads=[("oub", pk)], writes=[("outd", cc)], tag=f"ow_{cc}")
        for t in range(NTL):
            tb = t % 2
            for mc in range(NKC):
                P.dma("sync", lambda e, t=t, mc=mc, tb=tb: e.dma_start(
                    out=x1nb[:, (tb * NKC + mc) * TT:(tb * NKC + mc + 1) * TT],
                    in_=x1n_s[mc * CH:(mc + 1) * CH, t * TT:(t + 1) * TT]),
                    writes=[("x1v", tb, mc)], tag=f"lx1_{t}_{mc}")
            if t > 0:
                ln2_tail(t - 1)
            for m in range(NMF):
                wb = m % 2
                bank = m % 2
                P.dma("sync", lambda e, m=m, wb=wb: e.dma_start(
                    out=dch(w1b, wb, D), in_=w1s_s[m]),
                    writes=[("w1b", wb)], tag=f"lw1_{t}_{m}")
                def mm_h(e, m=m, wb=wb, bank=bank, tb=tb):
                    last = None
                    for kc in range(NKC):
                        last = e.matmul(
                            PF[bank][:],
                            w1b[:, wb * D + kc * CH:wb * D + (kc + 1) * CH],
                            x1nb[:, (tb * NKC + kc) * TT:(tb * NKC + kc + 1) * TT],
                            start=(kc == 0), stop=(kc == NKC - 1))
                    return last
                P.add("tensor", mm_h,
                      reads=[("w1b", wb)] + [("x1v", tb, kc2) for kc2 in range(NKC)],
                      writes=[f"RF{bank}"], tag=f"h_{t}_{m}")
                P.add("scalar", lambda e, m=m, bank=bank: e.activation(
                    dch(hbuf, m, TT), PF[bank][:], AF.Gelu, bias=b1c[:, m:m + 1]),
                    reads=[f"RF{bank}"], writes=[("h", m)], tag=f"he_{t}_{m}")
            for mc in range(NKC):
                bank = 2 + mc % 2
                def mm_x2(e, mc=mc, bank=bank):
                    last = None
                    for kc in range(NMF):
                        last = e.matmul(
                            PF[bank][:],
                            w2r[:, kc * D + mc * CH:kc * D + (mc + 1) * CH],
                            dch(hbuf, kc, TT),
                            start=(kc == 0), stop=(kc == NMF - 1))
                    return last
                P.add("tensor", mm_x2,
                      reads=[("w2r", kc2) for kc2 in range(NMF)]
                      + [("h", kc2) for kc2 in range(NMF)],
                      writes=[f"RF{bank}"], tag=f"x2_{t}_{mc}")
                P.add("vector",
                      lambda e, mc=mc, bank=bank, tb=tb: e.scalar_tensor_tensor(
                          dch(x2b, mc, TT), PF[bank][:], B2[:, mc:mc + 1],
                          x1nb[:, (tb * NKC + mc) * TT:(tb * NKC + mc + 1) * TT],
                          ALU.add, ALU.add),
                      reads=[f"RF{bank}", ("x1v", tb, mc)],
                      writes=[("x2b", mc)], tag=f"x2e_{t}_{mc}")
                P.add("scalar", lambda e, mc=mc: e.activation(
                    dch(x2sq, mc, TT), dch(x2b, mc, TT), AF.Square),
                    reads=[("x2b", mc)], writes=[("x2sq", mc)], tag=f"x2q_{t}_{mc}")
        ln2_tail(NTL - 1)
        P.emit(nc, sems)
      except _Skip:
        pass

    if debug:
        P = Plan(state)
        P.dma("sync", lambda e: e.dma_start(out=dbg["scal"], in_=scal[:]), tag="dbg_scal")
        P.dma("sync", lambda e: e.dma_start(
            out=dbg["gsc"], in_=build_nc._stash["sred"][:]), tag="dbg_gsc")
        P.dma("sync", lambda e: e.dma_start(out=dbg["kvl"], in_=kvl[:]), tag="dbg_kvl")
        P.dma("sync", lambda e: e.dma_start(out=dbg["x1n"], in_=x1n_s), tag="dbg_x1n")
        P.dma("sync", lambda e: e.dma_start(out=dbg["part"], in_=part_s), tag="dbg_part")
        P.dma("sync", lambda e: e.dma_start(out=dbg["h2row"], in_=h2row_s), tag="dbg_h2row")
        P.dma("sync", lambda e: e.dma_start(out=dbg["stH"], in_=statsH2[:]), tag="dbg_stH")
        P.dma("sync", lambda e: e.dma_start(out=dbg["stR"], in_=statsR[:]), tag="dbg_stR")
        P.dma("sync", lambda e: e.dma_start(out=dbg["stM"], in_=statsMUR[:]), tag="dbg_stM")
        P.dma("sync", lambda e: e.dma_start(out=dbg["mwq"], in_=mwq[:]), tag="dbg_mwq")
        P.emit(nc, sems)

    es.close()
    return nc


def _shard_inputs(L0_attn_x, L0_attn_proj, L0_attn_w_qkv, L0_attn_w_out,
                  L0_attn_b_out, L0_attn_gamma, L0_attn_beta, L0_attn_w1,
                  L0_attn_b1, L0_attn_w2, L0_attn_b2, L0_ffn_weight,
                  L0_ffn_ln_gamma, L0_ffn_ln_beta, NT):
    import ml_dtypes
    f32 = lambda a: np.ascontiguousarray(np.asarray(a, dtype=np.float32))
    x = f32(L0_attn_x)
    wqkvT = np.ascontiguousarray(f32(L0_attn_w_qkv).T)
    pj = f32(L0_attn_proj)
    projP = np.zeros((H // 2, CH, 2 * FF), np.float32)
    for pr in range(H // 2):
        projP[pr, 0:64, 0:FF] = pj[2 * pr]
        projP[pr, 64:128, FF:2 * FF] = pj[2 * pr + 1]
    projQ = np.zeros((CH, 2 * H * CH), np.float32)
    for h0 in range(H):
        for fc in range(2):
            sl = h0 * 2 + fc
            projQ[(h0 % 2) * 64:(h0 % 2) * 64 + 64, sl * CH:(sl + 1) * CH] = \
                pj[h0][:, fc * CH:(fc + 1) * CH]
    woutT = np.ascontiguousarray(f32(L0_attn_w_out).T)
    w1T = np.ascontiguousarray(f32(L0_attn_w1).T)
    w2T = np.ascontiguousarray(f32(L0_attn_w2).T)
    fw = f32(L0_ffn_weight)
    vecs = np.zeros((CH, 64), np.float32)
    for i, v in enumerate((L0_attn_b_out, L0_attn_gamma, L0_attn_beta,
                           L0_attn_b2, L0_ffn_ln_gamma, L0_ffn_ln_beta)):
        vecs[:, i * 8:(i + 1) * 8] = f32(v).reshape(NKC, CH).T
    b1c = np.ascontiguousarray(f32(L0_attn_b1).reshape(NMF, CH).T)
    vecs[:, 48] = EPS_LN
    vecs[:, 49] = LN_EPS_FFN
    lngbf = np.ascontiguousarray(np.broadcast_to(f32(L0_ffn_ln_gamma), (CH, D)))
    lnbbf = np.ascontiguousarray(np.broadcast_to(f32(L0_ffn_ln_beta), (CH, D)))
    iden = np.eye(CH, dtype=ml_dtypes.bfloat16)
    onesr = np.ones((1, CH), np.float32)
    onescb = np.ones((CH, 1), ml_dtypes.bfloat16)
    shared = dict(wqkvT=wqkvT, projP=projP, projQ=projQ, woutT=woutT, w1T=w1T, w2T=w2T,
                  fw=fw, vecs=vecs, b1c=b1c, lngbf=lngbf, lnbbf=lnbbf,
                  iden=iden, onesr=onesr, onescb=onescb)
    return [dict(shared, x=np.ascontiguousarray(x[i, :NT])) for i in range(NC)]


def kernel(**inputs):
    NT = 8192
    in_maps = _shard_inputs(NT=NT, **inputs)
    nc = build_nc(NT=NT)
    res = run_bass_kernel_spmd(nc, in_maps, core_ids=list(range(NC)))
    return np.stack([np.asarray(r["out"]) for r in res.results]).astype(np.float32)


# version 15
# speedup vs baseline: 1.2746x; 1.0167x over previous
import contextlib
import numpy as np
import concourse.bass as bass
import concourse.mybir as mybir
from concourse.bass_utils import run_bass_kernel_spmd

F32 = mybir.dt.float32
BF16 = mybir.dt.bfloat16
AF = mybir.ActivationFunctionType
ALU = mybir.AluOpType
AX = mybir.AxisListType.X

D = 1024
H = 16
FF = 256
DFF = 4096
NC = 8
CH = 128
NKC = D // CH
NMF = DFF // CH

EPS_LN = 1e-5
EPS_ATTN = 1e-8
LN_EPS_FFN = 1e-3
EPS_FFN = 1e-5
QB = 128.0
CLIP = QB - EPS_FFN

TT = 512
TCH = TT // CH


class Plan:

    ENGS = ("sync", "tensor", "vector", "scalar", "gpsimd")

    NLANES = {"sync": 8, "scalar": 6, "gpsimd": 4}

    def __init__(self, state, drop=(), psum_keys=()):
        self.state = state
        self.drop = tuple(drop)
        self.psum_keys = set(psum_keys)
        self.ops = []
        self.last_w = {}
        self.readers = {}

    def add(self, eng, emit, reads=(), writes=(), kind="compute", tag="",
            late=False):
        if any(tag.startswith(p) for p in self.drop):
            return None
        idx = len(self.ops)
        deps = set()
        for b in reads:
            w = self.last_w.get(b)
            if w is not None:
                deps.add(w)
            if b in self.psum_keys:
                for r in self.readers.get(b, ()):
                    if self.ops[r]["eng"] != eng:
                        deps.add(r)
        for b in writes:
            w = self.last_w.get(b)
            if w is not None:
                deps.add(w)
            for r in self.readers.get(b, ()):
                deps.add(r)
        for b in reads:
            self.readers.setdefault(b, []).append(idx)
        for b in writes:
            self.last_w[b] = idx
            self.readers[b] = []
        deps.discard(idx)
        lane = None
        if kind == "dma":
            lc = self.state.setdefault("lanecnt", {})
            lc[eng] = lc.get(eng, 0) + 1
            lane = lc[eng] % self.NLANES[eng]
        self.ops.append(dict(eng=eng, emit=emit, kind=kind, deps=deps, tag=tag,
                             late=late, inc=(kind != "compute"), incv=None,
                             waits=None, lane=lane, idx=idx))
        return idx

    def dma(self, eng, emit, reads=(), writes=(), tag=""):
        return self.add(eng, emit, reads, writes, kind="dma", tag=tag)

    def _sname(self, op):
        if op["kind"] == "dma":
            return "q_%s%d" % (op["eng"], op["lane"])
        if op["kind"] == "cc":
            return "cc"
        return "e_" + op["eng"]

    def _nowait(self, dep, op):
        if dep.get("late"):
            return False
        if dep["eng"] != op["eng"]:
            return False
        if dep["kind"] == "dma" or dep["kind"] == "cc":
            return dep["kind"] == op["kind"]
        return dep["eng"] == "tensor"

    FULL_WAITS = False

    def resolve(self):
        counts = self.state["counts"]
        ops = self.ops
        for op in ops:
            for d in op["deps"]:
                dep = ops[d]
                if not self._nowait(dep, op):
                    dep["inc"] = True
        for op in ops:
            if op["inc"]:
                sn = self._sname(op)
                counts[sn] = counts.get(sn, 0) + (16 if op["kind"] == "dma" else 1)
                op["incv"] = counts[sn]
        clocks = self.state["clocks"]
        opclock = [None] * len(ops)
        for op in ops:
            ec = clocks.setdefault(op["eng"], {})
            waits = []
            for d in sorted(op["deps"]):
                dep = ops[d]
                if self._nowait(dep, op):
                    continue
                sn, v = self._sname(dep), dep["incv"]
                assert v is not None, (op["tag"], dep["tag"])
                if ec.get(sn, 0) >= v and not Plan.FULL_WAITS:
                    continue
                if ec.get(sn, 0) >= v:
                    ec[sn] = max(ec.get(sn, 0), v)
                    waits.append((sn, v))
                    continue
                waits.append((sn, v))
                ec[sn] = v
                dc = opclock[d]
                if dc:
                    for s2, v2 in dc.items():
                        if ec.get(s2, 0) < v2:
                            ec[s2] = v2
            op["waits"] = waits
            opclock[op["idx"]] = dict(ec)

    def emit(self, nc, sems):
        self.resolve()
        counts = self.state["counts"]
        qsems = ["q_sync%d" % i for i in range(8)] + \
                ["q_scalar%d" % i for i in range(6)] + \
                ["q_gpsimd%d" % i for i in range(4)] + ["cc"]
        drain = [(sn, counts[sn]) for sn in qsems if counts.get(sn)]
        with nc.Block() as blk:
            for eng_name in self.ENGS:
                ops = [o for o in self.ops if o["eng"] == eng_name]
                if not ops and eng_name != "vector":
                    continue

                def make(ops, eng_name=eng_name):
                    def body(eng):
                        for op in ops:
                            for sn, v in op["waits"]:
                                eng.wait_ge(sems[sn], v)
                            inst = op["emit"](eng)
                            if op["inc"]:
                                assert inst is not None, op["tag"]
                                inst.then_inc(sems[self._sname(op)],
                                              16 if op["kind"] == "dma" else 1)
                        if eng_name == "vector":
                            ec = self.state["clocks"].setdefault("vector", {})
                            for sn, v in drain:
                                if ec.get(sn, 0) < v:
                                    eng.wait_ge(sems[sn], v)
                                    ec[sn] = v
                    return body
                getattr(blk, eng_name)(make(ops))


def build_nc(NT=8192, race=False, passes=(1, 2, 3), drop=(), debug=False):
    NTL = NT // TT
    NCH = NT // CH

    nc = bass.Bass("TRN2", target_bir_lowering=False,
                   detect_race_conditions=race)
    dt = nc.dram_tensor
    x_d = dt("x", [NT, D], F32, kind="ExternalInput").ap()
    wqkvT_d = dt("wqkvT", [D, 3 * D], F32, kind="ExternalInput").ap()
    projP_d = dt("projP", [H // 2, CH, 2 * FF], F32, kind="ExternalInput").ap()
    projQ_d = dt("projQ", [CH, 2 * H * CH], F32, kind="ExternalInput").ap()
    woutT_d = dt("woutT", [D, D], F32, kind="ExternalInput").ap()
    w1T_d = dt("w1T", [D, DFF], F32, kind="ExternalInput").ap()
    w2T_d = dt("w2T", [DFF, D], F32, kind="ExternalInput").ap()
    fw_d = dt("fw", [D, D], F32, kind="ExternalInput").ap()
    vecs_d = dt("vecs", [CH, 64], F32, kind="ExternalInput").ap()
    b1c_d = dt("b1c", [CH, NMF], F32, kind="ExternalInput").ap()
    iden_d = dt("iden", [CH, CH], BF16, kind="ExternalInput").ap()
    onesr_d = dt("onesr", [1, CH], F32, kind="ExternalInput").ap()
    onesc_d = dt("onescb", [CH, 1], BF16, kind="ExternalInput").ap()
    out_d = dt("out", [NT, D], F32, kind="ExternalOutput").ap()

    x1n_s = dt("x1n_s", [D, NT], BF16, kind="Internal").ap()
    part_s = dt("part_s", [NT, D], F32, kind="Internal").ap()
    w1s_s = dt("w1s_s", [NMF, CH, D], BF16, kind="Internal").ap()
    w2s_s = dt("w2s_s", [DFF, D], BF16, kind="Internal").ap()
    h2row_s = dt("h2row_s", [1, NT], F32, kind="Internal").ap()
    cc_in = dt("cc_in", [1, 1], F32, kind="Internal").ap()
    cc_out = dt("cc_out", [1, 1], F32, kind="Internal", addr_space="Shared").ap()

    dbg = {}
    if debug:
        dbg["scal"] = dt("dbg_scal", [1, 16], F32, kind="ExternalOutput").ap()
        dbg["gsc"] = dt("dbg_gsc", [CH, 8], F32, kind="ExternalOutput").ap()
        dbg["kvl"] = dt("dbg_kvl", [CH, 2 * H * 66], BF16, kind="ExternalOutput").ap()
        dbg["x1n"] = dt("dbg_x1n", [D, NT], BF16, kind="ExternalOutput").ap()
        dbg["part"] = dt("dbg_part", [NT, D], F32, kind="ExternalOutput").ap()
        dbg["h2row"] = dt("dbg_h2row", [1, NT], F32, kind="ExternalOutput").ap()
        dbg["stH"] = dt("dbg_stH", [CH, NT // CH], F32, kind="ExternalOutput").ap()
        dbg["stR"] = dt("dbg_stR", [CH, NT // CH], F32, kind="ExternalOutput").ap()
        dbg["stM"] = dt("dbg_stM", [CH, NT // CH], F32, kind="ExternalOutput").ap()
        dbg["mwq"] = dt("dbg_mwq", [CH, NKC * D], BF16, kind="ExternalOutput").ap()
    state = dict(counts={}, clocks={})
    es = contextlib.ExitStack()
    sems = {}
    for sn in (["e_tensor", "e_vector", "e_scalar", "e_gpsimd", "cc"]
               + ["q_sync%d" % i for i in range(8)]
               + ["q_scalar%d" % i for i in range(6)]
               + ["q_gpsimd%d" % i for i in range(4)]):
        sems[sn] = es.enter_context(nc.semaphore(sn))
    sb = lambda n, s, d_: es.enter_context(nc.sbuf_tensor(n, s, d_))

    iden = sb("idn", [CH, CH], BF16)
    onesr = sb("onr", [1, CH], F32)
    onesc = sb("onc", [CH, 1], BF16)
    vecs = sb("vcs", [CH, 64], F32)
    b1c = sb("b1cs", [CH, NMF], F32)
    statsR = sb("stR", [CH, NCH], F32)
    statsMUR = sb("stM", [CH, NCH], F32)
    statsH2 = sb("stH", [CH, NCH], F32)
    kvl = sb("kvl", [CH, 2 * H * 66], BF16)
    mwq = sb("mwq", [CH, NKC * D], BF16)
    gsc = sb("gsc", [CH, 8], F32)
    lngyc = sb("lgy", [CH, NKC], F32)
    lnbyc = sb("lby", [CH, NKC], F32)
    grow = sb("grw", [1, CH], F32)
    scal = sb("scl", [1, 16], F32)
    gacc = sb("gac", [CH, 1], F32)
    srow = sb("srw", [1, CH], F32)

    B_OUT = vecs[:, 0:8]
    GAM = vecs[:, 8:16]
    BET = vecs[:, 16:24]
    B2 = vecs[:, 24:32]
    LNG = vecs[:, 32:40]
    LNB = vecs[:, 40:48]
    EPS1 = vecs[:, 48:49]
    EPSF = vecs[:, 49:50]
    EPS1r = vecs[0:1, 48:49]

    def dch(ap2d, c, w=CH):
        return ap2d[:, c * w:(c + 1) * w]

    class _Skip(Exception):
        pass

    def _guard(n):
        if n not in passes:
            raise _Skip()

    with contextlib.ExitStack() as p1:
      try:
        _guard(1)
        sb1 = lambda n, s, d_: p1.enter_context(nc.sbuf_tensor(n, s, d_))
        ps1 = lambda n, s, d_: p1.enter_context(nc.psum_tensor(n, s, d_))
        kT = sb1("kT", [CH, NKC * D], BF16)
        vT = sb1("vT", [CH, NKC * D], BF16)
        lngb = sb1("lgb", [CH, D], BF16)
        lnbb = sb1("lbb", [CH, D], BF16)
        kvacc = sb1("kva", [65, H * FF], F32)
        projBD = sb1("pbd", [CH, (H // 2) * 2 * FF], BF16)
        wst = sb1("wst", [CH, DFF], F32)
        wst2 = sb1("ws2", [CH, DFF], BF16)
        rstg = sb1("rsg", [CH, 2 * D], F32)
        xin = sb1("xin", [CH, 2 * D], F32)
        xntm = sb1("xnt", [CH, 2 * D], BF16)
        xnfm = sb1("xnf", [CH, NKC * TT], BF16)
        kfm = sb1("kfm", [CH, NKC * TT], BF16)
        vext = sb1("vex", [CH, TCH * H * 66], BF16)
        kftm = sb1("kft", [CH, TCH * H * FF], BF16)
        ebuf = sb1("ebf", [CH, 2 * 2 * FF], BF16)
        tbuf = sb1("tbf", [CH, 2 * 2 * FF], BF16)
        gscr = sb1("gs1", [CH, D], BF16)
        gscr2 = sb1("gs2", [CH, D], BF16)
        sqd = sb1("sqd", [CH, D], F32)
        sred = sb1("srd", [CH, 8], F32)
        stS = sb1("stS", [CH, 16], F32)
        PT = ps1("PT", [CH, 2 * D], BF16)
        PF = [ps1(f"PF{i}", [CH, 512], F32) for i in range(6)]
        P = Plan(state, drop,
                 psum_keys=[f"PF{i}" for i in range(6)] + [("PT", 0), ("PT", 1)])

        for name, dstv, srcv in (("iden", iden, iden_d), ("onesr", onesr, onesr_d),
                                 ("onesc", onesc, onesc_d), ("vecs", vecs, vecs_d),
                                 ("b1c", b1c, b1c_d)):
            P.dma("sync", lambda e, a=dstv, b=srcv: e.dma_start(out=a[:], in_=b),
                  writes=[name], tag="ld_" + name)
        lngbf_d = dt("lngbf", [CH, D], F32, kind="ExternalInput").ap()
        lnbbf_d = dt("lnbbf", [CH, D], F32, kind="ExternalInput").ap()
        P.dma("sync", lambda e: e.dma_start(out=wst[:, 0:D], in_=lngbf_d),
              writes=[("wst", 0)], tag="ld_lngf")
        P.dma("sync", lambda e: e.dma_start(out=wst[:, D:2 * D], in_=lnbbf_d),
              writes=[("wst", 1)], tag="ld_lnbf")
        P.add("vector", lambda e: e.tensor_copy(lngb[:], wst[:, 0:D]),
              reads=[("wst", 0)], writes=["lngb"], tag="c_lngb")
        P.add("vector", lambda e: e.tensor_copy(lnbb[:], wst[:, D:2 * D]),
              reads=[("wst", 1)], writes=["lnbb"], tag="c_lnbb")
        P.add("vector", lambda e: e.memset(gacc[:], 0.0), writes=["gacc"], tag="ms_g")
        P.add("vector", lambda e: e.memset(kvl[:], 0.0), writes=["kvl"], tag="ms_kvl")
        P.add("vector", lambda e: e.memset(vext[:], 1.0),
              writes=[("vext", c, vg) for c in range(TCH) for vg in range(2)],
              tag="ms_vx")

        _ldn = [0]

        def ldw(dkey, dsl, src, eng="vector"):
            q = _ldn[0] % 2
            _ldn[0] += 1
            sl = rstg[:, q * D:q * D + src.shape[1]]
            P.dma("sync", lambda e, src=src, sl=sl: e.dma_start(out=sl, in_=src),
                  writes=[("rstg", q)], tag="ld" + str(dkey))
            P.add(eng, lambda e, dsl=dsl, sl=sl: e.tensor_copy(dsl, sl),
                  reads=[("rstg", q)], writes=[dkey], tag="cast" + str(dkey))

        for kc in range(NKC):
            ldw(("kT", kc), dch(kT, kc, D), wqkvT_d[kc * CH:(kc + 1) * CH, D:2 * D])
        for kc in range(NKC):
            ldw(("vT", kc), dch(vT, kc, D), wqkvT_d[kc * CH:(kc + 1) * CH, 2 * D:3 * D])
        for pr in range(H // 2):
            P.dma("sync", lambda e, pr=pr: e.dma_start(
                out=wst[:, 0:2 * FF], in_=projP_d[pr]),
                writes=[("wst", 0)], tag=f"ldpj_{pr}")
            P.add("vector", lambda e, pr=pr: e.tensor_copy(
                dch(projBD, pr, 2 * FF), wst[:, 0:2 * FF]),
                reads=[("wst", 0)], writes=[("projBD", pr)], tag=f"cpj_{pr}")

        def prep_gen():
            fw_r = fw_d.rearrange("(a p) u -> p a u", p=CH)
            for hf in range(2):
                P.dma("scalar", lambda e, hf=hf: e.dma_start(
                    out=wst[:].rearrange("p (a u) -> p a u", u=D),
                    in_=fw_r[:, hf * 4:(hf + 1) * 4, :]),
                    writes=[("wst", q) for q in range(4)], tag=f"ld_fw{hf}")
                P.add("vector", lambda e, hf=hf: e.reduce_sum(
                    sred[:, hf:hf + 1], wst[:], axis=AX, apply_absolute_value=True),
                    reads=[("wst", q) for q in range(4)], writes=[("sred", hf)],
                    tag=f"asum{hf}", late=True)
            P.add("vector", lambda e: e.tensor_add(sred[:, 2:3], sred[:, 0:1], sred[:, 1:2]),
                  reads=[("sred", 0), ("sred", 1)], writes=[("sred", 2)], tag="asum2")
            P.dma("gpsimd", lambda e: e.dma_start(out=srow[:], in_=sred[:, 2:3], allow_slow_non_contiguous=True),
                  reads=[("sred", 2)], writes=["srow"], tag="fold_b")
            P.add("vector", lambda e: e.reduce_sum(scal[:, 0:1], srow[:], axis=AX),
                  reads=["srow"], writes=[("scal", 0)], tag="bsum", late=True)
            P.add("vector", lambda e: e.tensor_scalar(
                scal[:, 1:2], scal[:, 0:1], 0.5 / (D * D), 0.5 * EPS_FFN, ALU.mult, ALU.add),
                reads=[("scal", 0)], writes=[("scal", 1)], tag="hb")
            P.add("vector", lambda e: e.tensor_scalar_mul(scal[:, 2:3], scal[:, 1:2], -1.0),
                  reads=[("scal", 1)], writes=[("scal", 2)], tag="nhb")
            P.add("tensor", lambda e: e.matmul(PF[0][:, 0:2], onesr[:], scal[:, 1:3],
                                               start=True, stop=True),
                  reads=[("scal", 1), ("scal", 2), "onesr"], writes=["PF0"], tag="bc_hb")
            P.add("vector", lambda e: e.tensor_copy(gsc[:, 2:4], PF[0][:, 0:2]),
                  reads=["PF0"], writes=[("gsc", 2)], tag="ev_hb")
            for hf in range(2):
                P.dma("scalar", lambda e, hf=hf: e.dma_start(
                    out=wst[:].rearrange("p (a u) -> p a u", u=D),
                    in_=fw_r[:, hf * 4:(hf + 1) * 4, :]),
                    writes=[("wst", q) for q in range(4)], tag=f"ld_fwb{hf}")
                P.add("vector", lambda e: e.tensor_single_scalar(
                    wst2[:], wst[:], gsc[:, 2:3], ALU.is_gt),
                    reads=[("wst", q) for q in range(4)] + [("gsc", 2)],
                    writes=[("wst2", q) for q in range(4)], tag=f"gt{hf}")
                P.add("vector", lambda e, hf=hf: e.scalar_tensor_tensor(
                    mwq[:, hf * DFF:(hf + 1) * DFF], wst[:], gsc[:, 3:4], wst2[:],
                    ALU.is_lt, ALU.subtract),
                    reads=[("wst", q) for q in range(4)]
                    + [("wst2", q) for q in range(4)] + [("gsc", 2)],
                    writes=[("mwq", hf)], tag=f"mwq{hf}")
                yield

            for kc in range(NMF):
                q = kc % 4
                ssl = wst[:, q * D:(q + 1) * D]
                s2 = wst2[:, q * D:(q + 1) * D]
                P.dma("scalar", lambda e, kc=kc, ssl=ssl: e.dma_start(
                    out=ssl, in_=w2T_d[kc * CH:(kc + 1) * CH, :]),
                    writes=[("wst", q)], tag=f"ldw2_{kc}")
                P.add("gpsimd", lambda e, ssl=ssl, s2=s2: e.tensor_copy(s2, ssl),
                      reads=[("wst", q)], writes=[("wst2", q)], tag=f"cw2_{kc}")
                P.dma("gpsimd", lambda e, kc=kc, s2=s2: e.dma_start(
                    out=w2s_s[kc * CH:(kc + 1) * CH, :], in_=s2),
                    reads=[("wst2", q)], writes=["w2s"], tag=f"sw2_{kc}")
                yield
            for kc in range(NKC):
                P.dma("scalar", lambda e, kc=kc: e.dma_start(
                    out=wst[:], in_=w1T_d[kc * CH:(kc + 1) * CH, :]),
                    writes=[("wst", q) for q in range(4)], tag=f"ldw1_{kc}")
                P.add("gpsimd", lambda e: e.tensor_copy(wst2[:], wst[:]),
                      reads=[("wst", q) for q in range(4)],
                      writes=[("wst2", q) for q in range(4)], tag=f"cw1_{kc}")
                P.dma("gpsimd", lambda e, kc=kc: e.dma_start(
                    out=w1s_s[:, :, kc * CH:(kc + 1) * CH].rearrange("m p c -> p m c"),
                    in_=wst2[:].rearrange("p (m c) -> p m c", c=CH)),
                    reads=[("wst2", q) for q in range(4)], writes=["w1s"], tag=f"sw1_{kc}")
                yield


        _pg = prep_gen()
        pend_kv = []
        for t in range(NTL):
            if t >= 1:
                for _ in range(4):
                    next(_pg, None)
            tc0 = t * TCH
            for c in range(TCH):
                cc = t * TCH + c
                pk = c % 2
                xs = dch(xin, pk, D)
                P.dma("sync", lambda e, cc=cc, xs=xs: e.dma_start(
                    out=xs, in_=x_d[cc * CH:(cc + 1) * CH, :]),
                    writes=[("xin", pk)], tag=f"ldx_{cc}")
                P.add("vector", lambda e, xs=xs, c=c: e.reduce_sum(
                    sred[:, c:c + 1], xs, axis=AX),
                    reads=[("xin", pk)], writes=[("sred", c)], tag=f"sum_{cc}",
                    late=True)
                P.add("vector", lambda e, xs=xs, c=c: e.scalar_tensor_tensor(
                    sqd[:], xs, 1.0, xs, ALU.mult, ALU.mult,
                    accum_out=sred[:, 4 + c:5 + c]),
                    reads=[("xin", pk)], writes=[("sred", 4 + c), "sqd"],
                    tag=f"sq_{cc}", late=True)
            SR = lambda a, b: stS[:, a:b]
            P.add("vector", lambda e: e.tensor_scalar_mul(
                SR(0, 4), sred[:, 0:4], 1.0 / D),
                reads=[("sred", c2) for c2 in range(4)], writes=[("stS", 0)],
                tag=f"bmu_{t}")
            P.add("vector", lambda e: e.tensor_scalar_mul(
                SR(4, 8), sred[:, 4:8], 1.0 / D),
                reads=[("sred", 4 + c2) for c2 in range(4)], writes=[("stS", 1)],
                tag=f"be2_{t}")
            P.add("vector", lambda e: e.tensor_mul(SR(8, 12), SR(0, 4), SR(0, 4)),
                  reads=[("stS", 0)], writes=[("stS", 2)], tag=f"bm2_{t}")
            P.add("vector", lambda e: e.tensor_sub(SR(12, 16), SR(4, 8), SR(8, 12)),
                  reads=[("stS", 1), ("stS", 2)], writes=[("stS", 3)], tag=f"bvr_{t}")
            P.add("scalar", lambda e: e.activation(
                SR(8, 12), SR(12, 16), AF.Ln, bias=EPS1),
                reads=[("stS", 3)], writes=[("stS", 2)], tag=f"bln_{t}")
            P.add("scalar", lambda e, tc0=tc0: e.activation(
                statsR[:, tc0:tc0 + 4], SR(8, 12), AF.Exp, scale=-0.5),
                reads=[("stS", 2)],
                writes=[("stR", tc0 + c2) for c2 in range(4)], tag=f"br_{t}")
            P.add("vector", lambda e, tc0=tc0: e.scalar_tensor_tensor(
                statsMUR[:, tc0:tc0 + 4], SR(0, 4), -1.0, statsR[:, tc0:tc0 + 4],
                ALU.mult, ALU.mult),
                reads=[("stS", 0)] + [("stR", tc0 + c2) for c2 in range(4)],
                writes=[("stM", tc0 + c2) for c2 in range(4)], tag=f"bmr_{t}")
            P.add("vector", lambda e, tc0=tc0: e.tensor_mul(
                SR(0, 4), statsR[:, tc0:tc0 + 4], statsR[:, tc0:tc0 + 4]),
                reads=[("stR", tc0 + c2) for c2 in range(4)], writes=[("stS", 0)],
                tag=f"brr_{t}")
            P.add("vector", lambda e: e.tensor_mul(SR(4, 8), SR(0, 4), SR(12, 16)),
                  reads=[("stS", 0), ("stS", 3)], writes=[("stS", 1)], tag=f"bv2_{t}")
            P.add("scalar", lambda e: e.activation(
                SR(8, 12), SR(4, 8), AF.Ln, bias=EPSF),
                reads=[("stS", 1)], writes=[("stS", 2)], tag=f"bl2_{t}")
            P.add("scalar", lambda e, tc0=tc0: e.activation(
                statsH2[:, tc0:tc0 + 4], SR(8, 12), AF.Exp, scale=-0.5),
                reads=[("stS", 2)],
                writes=[("stH", tc0 + c2) for c2 in range(4)], tag=f"bh2_{t}")
            for c in range(TCH):
                cc = t * TCH + c
                pk = c % 2
                xs = dch(xin, pk, D)
                P.dma("sync", lambda e, cc=cc, xs=xs: e.dma_start(
                    out=xs, in_=x_d[cc * CH:(cc + 1) * CH, :]),
                    writes=[("xin", pk)], tag=f"ldx2_{cc}")
                P.add("scalar", lambda e, xs=xs, pk=pk, cc=cc: e.activation(
                    dch(xntm, pk, D), xs, AF.Identity,
                    bias=statsMUR[:, cc:cc + 1], scale=statsR[:, cc:cc + 1]),
                    reads=[("xin", pk), ("stM", cc), ("stR", cc)],
                    writes=[("xntm", pk)], tag=f"xn_{cc}")
                P.add("vector", lambda e, pk=pk, cc=cc: e.scalar_tensor_tensor(
                    gscr[:], dch(xntm, pk, D), statsH2[:, cc:cc + 1], lngb[:],
                    ALU.mult, ALU.mult),
                    reads=[("xntm", pk), ("stH", cc), "lngb"], writes=["gscr"],
                    tag=f"g1_{cc}")
                P.add("gpsimd", lambda e: e.tensor_add(gscr2[:], gscr[:], lnbb[:]),
                      reads=["gscr", "lnbb"], writes=["gscr2"], tag=f"g2_{cc}")
                P.add("vector", lambda e: e.reduce_max(
                    sred[:, 0:1], gscr2[:], axis=AX, apply_absolute_value=True),
                    reads=["gscr2"], writes=[("sred", 0)], tag=f"g3_{cc}",
                    late=True)
                P.add("vector", lambda e: e.tensor_max(gacc[:], gacc[:], sred[:, 0:1]),
                      reads=[("sred", 0), "gacc"], writes=["gacc"], tag=f"g4_{cc}")
                for kc in range(NKC):
                    P.add("tensor", lambda e, kc=kc, pk=pk: e.transpose(
                        PT[:, pk * D + kc * CH:pk * D + (kc + 1) * CH],
                        xntm[:, pk * D + kc * CH:pk * D + (kc + 1) * CH], iden[:]),
                        reads=[("xntm", pk), "iden"], writes=[("PT", pk)],
                        tag=f"xT_{cc}_{kc}")
                P.add("vector", lambda e, c=c, pk=pk: e.tensor_copy(
                    xnfm[:].rearrange("p (k tt) -> p k tt", tt=TT)
                    [:, :, c * CH:(c + 1) * CH],
                    PT[:, pk * D:(pk + 1) * D].rearrange("p (k w) -> p k w", w=CH)),
                    reads=[("PT", pk)], writes=[("xnfm", c)], tag=f"xTe_{cc}")
            P.dma("gpsimd", lambda e, t=t: e.dma_start(
                out=h2row_s[:, t * TT:(t + 1) * TT].rearrange(
                    "o (c p) -> (o p) c", p=CH),
                in_=statsH2[:, t * TCH:(t + 1) * TCH],
                allow_slow_non_contiguous=True),
                reads=[("stH", t * TCH + c2) for c2 in range(TCH)],
                writes=["h2row"], tag=f"h2r_{t}")

            for fn in pend_kv:
                fn()
            pend_kv = []
            for mc in range(NKC):
                bank = mc % 2
                def mm_k(e, mc=mc, bank=bank):
                    last = None
                    for kc in range(NKC):
                        last = e.matmul(PF[bank][:],
                                        kT[:, kc * D + mc * CH:kc * D + (mc + 1) * CH],
                                        dch(xnfm, kc, TT),
                                        start=(kc == 0), stop=(kc == NKC - 1))
                    return last
                P.add("tensor", mm_k,
                      reads=[("kT", kc2) for kc2 in range(NKC)]
                      + [("xnfm", c2) for c2 in range(TCH)],
                      writes=[f"PF{bank}"], tag=f"k_{t}_{mc}")
                P.add("scalar", lambda e, mc=mc, bank=bank: e.activation(
                    dch(kfm, mc, TT), PF[bank][:], AF.Copy),
                    reads=[f"PF{bank}"], writes=[("kfm", mc)], tag=f"ke_{t}_{mc}")

            for c in range(TCH):
                for vg in range(2):
                    bank = 2 + vg
                    def mm_v(e, c=c, vg=vg, bank=bank):
                        last = None
                        for kc in range(NKC):
                            last = e.matmul(
                                PF[bank][:],
                                xnfm[:, kc * TT + c * CH:kc * TT + (c + 1) * CH],
                                vT[:, kc * D + vg * 512:kc * D + (vg + 1) * 512],
                                start=(kc == 0), stop=(kc == NKC - 1))
                        return last
                    P.add("tensor", mm_v,
                          reads=[("vT", kc2) for kc2 in range(NKC)] + [("xnfm", c)],
                          writes=[f"PF{bank}"], tag=f"v_{t}_{c}_{vg}")
                    def ev_v(e, c=c, vg=vg, bank=bank):
                        dst = vext[:].rearrange("p (cc h s) -> p cc h s", h=H, s=66)[
                            :, c, vg * 8:(vg + 1) * 8, 0:64]
                        return e.tensor_copy(dst, PF[bank][:].rearrange(
                            "p (h w) -> p h w", w=64))
                    P.add("vector", ev_v, reads=[f"PF{bank}"],
                          writes=[("vext", c, vg)], tag=f"ve_{t}_{c}_{vg}")
            for c in range(TCH):
                for pr in range(H // 2):
                    bank = (c * (H // 2) + pr) % 4
                    pkk = pr % 2
                    P.add("tensor", lambda e, c=c, pr=pr, bank=bank: e.matmul(
                        PF[bank][:],
                        kfm[:, pr * TT + c * CH:pr * TT + (c + 1) * CH],
                        dch(projBD, pr, 2 * FF),
                        start=True, stop=True),
                        reads=[("kfm", pr), ("projBD", pr)],
                        writes=[f"PF{bank}"], tag=f"kf_{t}_{c}_{pr}")
                    eb = dch(ebuf, pkk, 2 * FF)
                    tb = dch(tbuf, pkk, 2 * FF)
                    P.add("scalar", lambda e, bank=bank, eb=eb: e.activation(
                        eb, PF[bank][:], AF.Exp),
                        reads=[f"PF{bank}"], writes=[("eb", pkk)],
                        tag=f"kfE_{t}_{c}_{pr}")
                    P.add("vector", lambda e, bank=bank, tb=tb: e.tensor_scalar(
                        tb, PF[bank][:], 1.0, 1.0, ALU.add, ALU.max),
                        reads=[f"PF{bank}"], writes=[("tb", pkk)],
                        tag=f"kfT_{t}_{c}_{pr}")
                    P.add("vector", lambda e, c=c, pr=pr, eb=eb, tb=tb: e.tensor_tensor(
                        kftm[:, (c * H + pr * 2) * FF:(c * H + pr * 2 + 2) * FF],
                        eb, tb, ALU.min),
                        reads=[("eb", pkk), ("tb", pkk)],
                        writes=[("kftm", c, pr)], tag=f"kfM_{t}_{c}_{pr}")
            for pr in range(H // 2):
                bank = 4 + (pr % 2)
                def mm_kv(e, pr=pr, bank=bank):
                    last = None
                    for c in range(TCH):
                        for hh in range(2):
                            h0 = pr * 2 + hh
                            last = e.matmul(
                                PF[bank][0:65, hh * FF:(hh + 1) * FF],
                                vext[:, (c * H + h0) * 66:(c * H + h0) * 66 + 65],
                                kftm[:, (c * H + h0) * FF:(c * H + h0 + 1) * FF],
                                start=(c == 0 and hh == 0),
                                stop=(c == TCH - 1 and hh == 1))
                    return last
                P.add("tensor", mm_kv,
                      reads=[("vext", c2, vg) for c2 in range(TCH) for vg in range(2)]
                      + [("kftm", c2, pr) for c2 in range(TCH)],
                      writes=[f"PF{bank}"], tag=f"kv_{t}_{pr}")
                if t == 0:
                    P.add("vector", lambda e, pr=pr, bank=bank: e.tensor_copy(
                        kvacc[:, pr * 2 * FF:(pr * 2 + 2) * FF], PF[bank][0:65, :]),
                        reads=[f"PF{bank}"], writes=[("kva", pr)], tag=f"kve_{t}_{pr}")
                else:
                    P.add("vector", lambda e, pr=pr, bank=bank: e.tensor_add(
                        kvacc[:, pr * 2 * FF:(pr * 2 + 2) * FF],
                        kvacc[:, pr * 2 * FF:(pr * 2 + 2) * FF], PF[bank][0:65, :]),
                        reads=[f"PF{bank}", ("kva", pr)], writes=[("kva", pr)],
                        tag=f"kve_{t}_{pr}")

        for fn in pend_kv:
            fn()
        pend_kv = []
        for _ in _pg:
            pass
        def ptoff(s):
            return (s % 2) * 1024 + (s // 2) * 66
        P.add("vector", lambda e: e.tensor_copy(wst2[0:65, 0:H * FF], kvacc[:]),
              reads=[("kva", pr) for pr in range(H // 2)],
              writes=[("wst2", q) for q in range(4)], tag="kv_cast")
        for h0 in range(H):
            for fc in range(2):
                sl = h0 * 2 + fc
                hf = sl % 2
                P.add("tensor", lambda e, h0=h0, fc=fc, sl=sl: e.transpose(
                    PT[:, ptoff(sl % 16):ptoff(sl % 16) + 65],
                    wst2[0:65, h0 * FF + fc * CH:h0 * FF + (fc + 1) * CH],
                    iden[0:65, 0:65]),
                    reads=[("wst2", q) for q in range(4)] + ["iden"],
                    writes=[("PT", hf)], tag=f"kvT_{h0}_{fc}")
                P.add("vector", lambda e, sl=sl: e.tensor_copy(
                    kvl[:, sl * 66:sl * 66 + 65],
                    PT[:, ptoff(sl % 16):ptoff(sl % 16) + 65]),
                    reads=[("PT", hf)], writes=["kvl"], tag=f"kvle_{sl}")
        build_nc._stash = dict(sred=sred, wst=wst)
        P.emit(nc, sems)
      except _Skip:
        pass

    with contextlib.ExitStack() as p2:
      try:
        _guard(2)
        sb2 = lambda n, s, d_: p2.enter_context(nc.sbuf_tensor(n, s, d_))
        ps2 = lambda n, s, d_: p2.enter_context(nc.psum_tensor(n, s, d_))
        qT = sb2("qT", [CH, NKC * D], BF16)
        projQ = sb2("pjq", [CH, 2 * H * CH], BF16)
        woT = sb2("woT", [CH, NKC * D], BF16)
        wst = sb2("wsb", [CH, 2 * D], F32)
        xin = sb2("xnb", [CH, 2 * D], F32)
        xpin = xin
        xntm = sb2("xtb", [CH, 2 * D], BF16)
        xnfm = sb2("xfb", [CH, 2 * NKC * TT], BF16)
        qfm = sb2("qfm", [CH, NKC * TT], BF16)
        shA = qfm
        qf = sb2("qf", [CH, 2 * H * TT], BF16)
        ebf2 = sb2("eb2", [CH, 2 * TT], BF16)
        tbf2 = sb2("tb2", [CH, 2 * TT], BF16)
        attnr = sb2("atr", [CH, 2 * D], BF16)
        z16 = sb2("z16", [CH, 16], F32)
        zr16 = sb2("zrs", [CH, 16], F32)
        attnfm = sb2("atf", [CH, NKC * TT], BF16)
        x1b = sb2("x1b", [CH, NKC * TT], BF16)
        x1sq = shA
        x1n = sb2("x1m", [CH, NKC * TT], BF16)
        rows = sb2("rws", [1, 4 * TT], F32)
        bc1 = sb2("bc1", [CH, TT], F32)
        bc2 = sb2("bc2", [CH, TT], F32)
        xq1 = sb2("xq1", [CH, TT], F32)
        xq2 = sb2("xq2", [CH, TT], F32)
        xqb = sb2("xqb", [CH, NKC * TT], BF16)
        ybf = shA
        xpart = sb2("xpt", [CH, 2 * D], F32)
        h2r = sb2("h2r", [1, TT], F32)
        PT2 = ps2("PT2", [CH, 2 * D], BF16)
        PT2b = ps2("PT2b", [CH, D], BF16)
        PF = [ps2(f"QF{i}", [CH, 512], F32) for i in range(5)]
        P = Plan(state, drop,
                 psum_keys=[f"QF{i}" for i in range(5)]
                 + [("PT2", 0), ("PT2", 1), "PT2b"])

        def row(i):
            return rows[:, i * TT:(i + 1) * TT]

        for pq in range(4):
            q = pq % 2
            ssl = wst[:, q * D:(q + 1) * D]
            P.dma("scalar", lambda e, pq=pq, ssl=ssl: e.dma_start(
                out=ssl, in_=projQ_d[:, pq * D:(pq + 1) * D]),
                writes=[("wst", q)], tag=f"ldpq_{pq}")
            P.add("vector", lambda e, pq=pq, ssl=ssl: e.tensor_copy(
                projQ[:, pq * D:(pq + 1) * D], ssl),
                reads=[("wst", q)], writes=["projQ"], tag=f"cpq_{pq}")
        for kc in range(NKC):
            q = kc % 2
            ssl = wst[:, q * D:(q + 1) * D]
            P.dma("scalar", lambda e, kc=kc, ssl=ssl: e.dma_start(
                out=ssl, in_=wqkvT_d[kc * CH:(kc + 1) * CH, 0:D]),
                writes=[("wst", q)], tag=f"ldq_{kc}")
            P.add("vector", lambda e, kc=kc, ssl=ssl: e.tensor_copy(
                dch(qT, kc, D), ssl),
                reads=[("wst", q)], writes=[("qT", kc)], tag=f"cq_{kc}")
        for kc in range(NKC):
            q = kc % 2
            ssl = wst[:, q * D:(q + 1) * D]
            P.dma("scalar", lambda e, kc=kc, ssl=ssl: e.dma_start(
                out=ssl, in_=woutT_d[kc * CH:(kc + 1) * CH, :]),
                writes=[("wst", q)], tag=f"ldo_{kc}")
            P.add("vector", lambda e, kc=kc, ssl=ssl: e.tensor_copy(
                dch(woT, kc, D), ssl),
                reads=[("wst", q)], writes=[("woT", kc)], tag=f"co_{kc}")

        P.dma("gpsimd", lambda e: e.dma_start(out=srow[:], in_=gacc[:], allow_slow_non_contiguous=True),
              reads=["gacc"], writes=["srow"], tag="fold_g")
        P.add("vector", lambda e: e.reduce_max(scal[:, 3:4], srow[:], axis=AX),
              reads=["srow"], writes=[("scal", 3)], tag="gloc", late=True)
        P.dma("gpsimd", lambda e: e.dma_start(out=cc_in, in_=scal[:, 3:4]),
              reads=[("scal", 3)], writes=["cc_in"], tag="st_cc")
        P.add("gpsimd", lambda e: e.collective_compute(
            "AllReduce", ALU.max, replica_groups=[list(range(NC))],
            ins=[cc_in], outs=[cc_out]),
            reads=["cc_in"], writes=["cc_out"], kind="cc", tag="ar_gamma")
        P.dma("gpsimd", lambda e: e.dma_start(out=scal[:, 4:5], in_=cc_out),
              reads=["cc_out"], writes=[("scal", 4)], tag="ld_cc")
        P.add("vector", lambda e: e.reciprocal(scal[:, 5:6], scal[:, 4:5]),
              reads=[("scal", 4)], writes=[("scal", 5)], tag="rg0", late=True)
        P.add("vector", lambda e: e.tensor_scalar_mul(scal[:, 6:7], scal[:, 5:6], QB),
              reads=[("scal", 5)], writes=[("scal", 6)], tag="rgam")
        P.add("vector", lambda e: e.tensor_mul(scal[:, 7:8], scal[:, 4:5], scal[:, 0:1]),
              reads=[("scal", 4), ("scal", 0)], writes=[("scal", 7)], tag="gb")
        P.add("vector", lambda e: e.tensor_scalar_mul(
            scal[:, 8:9], scal[:, 7:8], -1.0 / (QB * D * D)),
            reads=[("scal", 7)], writes=[("scal", 8)], tag="nys")
        P.add("tensor", lambda e: e.matmul(PF[0][:, 0:4], onesr[:], scal[:, 5:9],
                                           start=True, stop=True),
              reads=[("scal", 6), ("scal", 8), "onesr"], writes=["QF0"], tag="bc_g")
        P.add("vector", lambda e: e.tensor_copy(gsc[:, 0:1], PF[0][:, 1:2]),
              reads=["QF0"], writes=[("gsc", 0)], tag="ev_rg")
        P.add("vector", lambda e: e.tensor_copy(gsc[:, 1:2], PF[0][:, 3:4]),
              reads=["QF0"], writes=[("gsc", 1)], tag="ev_ny")
        P.add("vector", lambda e: e.tensor_single_scalar(
            grow[:], onesr[:], scal[:, 6:7], ALU.mult),
            reads=[("scal", 6), "onesr"], writes=["grow"], tag="grow")
        P.add("vector", lambda e: e.tensor_copy(lngyc[:], LNG),
              reads=["vecs"], writes=["lngyc"], tag="lngyc")
        P.add("vector", lambda e: e.tensor_single_scalar(
            lnbyc[:], LNB, gsc[:, 0:1], ALU.mult),
            reads=["vecs", ("gsc", 0)], writes=["lnbyc"], tag="lnbyc")

        def ln1_tail(t):
                def mm_s1(e):
                    last = None
                    for kc in range(NKC):
                        last = e.matmul(PF[4][0:1, :], onesc[:], dch(x1b, kc, TT),
                                        start=(kc == 0), stop=(kc == NKC - 1))
                    return last
                P.add("tensor", mm_s1,
                      reads=[("x1b", kc2) for kc2 in range(NKC)],
                      writes=["QF4"], tag=f"s1_{t}")
                P.add("vector", lambda e: e.tensor_scalar_mul(
                    row(0), PF[4][0:1, :], 1.0 / D),
                    reads=["QF4"], writes=[("rw", 0)], tag=f"s1e_{t}")
                def mm_s2(e):
                    last = None
                    for kc in range(NKC):
                        last = e.matmul(PF[4][0:1, :], onesc[:], dch(x1sq, kc, TT),
                                        start=(kc == 0), stop=(kc == NKC - 1))
                    return last
                P.add("tensor", mm_s2,
                      reads=[("shA", kc2) for kc2 in range(NKC)],
                      writes=["QF4"], tag=f"s2_{t}")
                P.add("vector", lambda e: e.tensor_scalar_mul(
                    row(1), PF[4][0:1, :], 1.0 / D),
                    reads=["QF4"], writes=[("rw", 1)], tag=f"s2e_{t}")
                P.add("vector", lambda e: e.tensor_mul(row(3), row(0), row(0)),
                      reads=[("rw", 0)], writes=[("rw", 3)], tag=f"v2_{t}")
                P.add("vector", lambda e: e.tensor_sub(row(2), row(1), row(3)),
                      reads=[("rw", 1), ("rw", 3)], writes=[("rw", 2)], tag=f"v3_{t}")
                P.add("scalar", lambda e: e.activation(
                    row(3), row(2), AF.Ln, bias=EPS1r),
                    reads=[("rw", 2)], writes=[("rw", 3)], tag=f"v4_{t}")
                P.add("scalar", lambda e: e.activation(
                    row(2), row(3), AF.Exp, scale=-0.5),
                    reads=[("rw", 3)], writes=[("rw", 2)], tag=f"r1_{t}")
                P.add("vector", lambda e: e.scalar_tensor_tensor(
                    row(3), row(0), -1.0, row(2), ALU.mult, ALU.mult),
                    reads=[("rw", 0), ("rw", 2)], writes=[("rw", 3)], tag=f"q1_{t}")
                P.add("tensor", lambda e: e.matmul(PF[4][:], onesr[:], row(2),
                                                   start=True, stop=True),
                      reads=[("rw", 2)], writes=["QF4"], tag=f"b1_{t}")
                P.add("vector", lambda e: e.tensor_copy(bc1[:], PF[4][:]),
                      reads=["QF4"], writes=["bc1"], tag=f"b1e_{t}")
                P.add("tensor", lambda e: e.matmul(PF[4][:], onesr[:], row(3),
                                                   start=True, stop=True),
                      reads=[("rw", 3)], writes=["QF4"], tag=f"b2_{t}")
                P.add("vector", lambda e: e.tensor_copy(bc2[:], PF[4][:]),
                      reads=["QF4"], writes=["bc2"], tag=f"b2e_{t}")
                for mc in range(NKC):
                    P.add("vector", lambda e, mc=mc: e.tensor_mul(
                        xq2[:], dch(x1b, mc, TT), bc1[:]),
                        reads=[("x1b", mc), "bc1"], writes=["xq2"], tag=f"n1_{t}_{mc}")
                    P.add("vector", lambda e: e.tensor_add(xq2[:], xq2[:], bc2[:]),
                          reads=["xq2", "bc2"], writes=["xq2"], tag=f"n2_{t}_{mc}")
                    P.add("vector", lambda e, mc=mc: e.tensor_scalar(
                        dch(x1n, mc, TT), xq2[:], GAM[:, mc:mc + 1], BET[:, mc:mc + 1],
                        ALU.mult, ALU.add),
                        reads=["xq2"], writes=[("x1n", mc)], tag=f"n3_{t}_{mc}")
                    P.dma("gpsimd", lambda e, mc=mc, t=t: e.dma_start(
                        out=x1n_s[mc * CH:(mc + 1) * CH, t * TT:(t + 1) * TT],
                        in_=dch(x1n, mc, TT)),
                        reads=[("x1n", mc)], writes=[("x1ns", t, mc)], tag=f"nS_{t}_{mc}")

        for t in range(NTL):
            xo = (t % 2) * NKC * TT
            xnk = t % 2

            def xch(kc, xo=None, t=t):
                o = (t % 2) * NKC * TT
                return xnfm[:, o + kc * TT:o + (kc + 1) * TT]
            for c in range(TCH):
                cc = t * TCH + c
                pk = c % 2
                xs = dch(xin, pk, D)
                P.dma("sync", lambda e, cc=cc, xs=xs: e.dma_start(
                    out=xs, in_=x_d[cc * CH:(cc + 1) * CH, :]),
                    writes=[("xin", pk)], tag=f"p2x_{cc}")
                P.add("scalar", lambda e, xs=xs, pk=pk, cc=cc: e.activation(
                    dch(xntm, pk, D), xs, AF.Identity,
                    bias=statsMUR[:, cc:cc + 1], scale=statsR[:, cc:cc + 1]),
                    reads=[("xin", pk)], writes=[("xntm", pk)], tag=f"p2n_{cc}")
                for kc in range(NKC):
                    P.add("tensor", lambda e, kc=kc, pk=pk: e.transpose(
                        PT2b[:, kc * CH:(kc + 1) * CH],
                        xntm[:, pk * D + kc * CH:pk * D + (kc + 1) * CH], iden[:]),
                        reads=[("xntm", pk)], writes=["PT2b"],
                        tag=f"p2T_{cc}_{kc}")
                P.add("vector", lambda e, c=c, pk=pk, xo=xo: e.tensor_copy(
                    xnfm[:, xo:xo + NKC * TT].rearrange("p (k tt) -> p k tt", tt=TT)
                    [:, :, c * CH:(c + 1) * CH],
                    PT2b[:].rearrange("p (k w) -> p k w", w=CH)),
                    reads=["PT2b"], writes=[("xnfm", xnk, c)], tag=f"p2e_{cc}")
            if t > 0:
                ln1_tail(t - 1)
            P.dma("sync", lambda e, t=t: e.dma_start(
                out=h2r[:], in_=h2row_s[:, t * TT:(t + 1) * TT]),
                writes=["h2r"], tag=f"fh_{t}")
            P.add("tensor", lambda e: e.matmul(PF[4][:], grow[:], h2r[:],
                                               start=True, stop=True),
                  reads=["h2r"], writes=["QF4"], tag=f"fb_{t}")
            for mc in range(NKC):
                P.add("vector", lambda e, mc=mc, xch=xch: e.scalar_tensor_tensor(
                    xq1[:], xch(mc), lngyc[:, mc:mc + 1], PF[4][:],
                    ALU.mult, ALU.mult),
                    reads=[("xnfm", xnk, c2) for c2 in range(TCH)] + ["QF4"],
                    writes=["xq1"], tag=f"f1_{t}_{mc}")
                P.add("vector", lambda e, mc=mc: e.tensor_scalar(
                    xq1[:], xq1[:], lnbyc[:, mc:mc + 1], CLIP, ALU.add, ALU.min),
                    reads=["xq1"], writes=["xq1"], tag=f"f2_{t}_{mc}")
                P.add("vector", lambda e, mc=mc: e.tensor_scalar_max(
                    dch(xqb, mc, TT), xq1[:], -CLIP),
                    reads=["xq1"], writes=[("xqb", mc)], tag=f"f3_{t}_{mc}")
            for mc in range(NKC):
                bank = mc % 2
                def mm_q(e, mc=mc, bank=bank, xch=xch):
                    last = None
                    for kc in range(NKC):
                        last = e.matmul(PF[bank][:],
                                        qT[:, kc * D + mc * CH:kc * D + (mc + 1) * CH],
                                        xch(kc),
                                        start=(kc == 0), stop=(kc == NKC - 1))
                    return last
                P.add("tensor", mm_q,
                      reads=[("qT", kc2) for kc2 in range(NKC)]
                      + [("xnfm", xnk, c2) for c2 in range(TCH)],
                      writes=[f"QF{bank}"], tag=f"q_{t}_{mc}")
                P.add("scalar", lambda e, mc=mc, bank=bank: e.activation(
                    dch(qfm, mc, TT), PF[bank][:], AF.Copy),
                    reads=[f"QF{bank}"], writes=[("shA", mc)], tag=f"qe_{t}_{mc}")
            for fc in range(2):
                for h0 in range(H):
                    pr, hh = h0 // 2, h0 % 2
                    base = hh * 64
                    sl = h0 * 2 + fc
                    bank = (fc * H + h0) % 4
                    P.add("tensor",
                          lambda e, pr=pr, sl=sl, bank=bank: e.matmul(
                              PF[bank][:],
                              dch(projQ, sl, CH),
                              dch(qfm, pr, TT),
                              start=True, stop=True),
                          reads=[("shA", pr), "projQ"], writes=[f"QF{bank}"],
                          tag=f"qf_{t}_{h0}_{fc}")
                    pe2 = sl % 2
                    P.add("scalar", lambda e, bank=bank, pe2=pe2: e.activation(
                        dch(ebf2, pe2, TT), PF[bank][:], AF.Exp),
                        reads=[f"QF{bank}"], writes=[("ebf2", pe2)],
                        tag=f"qfE_{t}_{h0}_{fc}")
                    P.add("vector", lambda e, bank=bank, pe2=pe2: e.tensor_scalar(
                        dch(tbf2, pe2, TT), PF[bank][:], 1.0, 1.0, ALU.add, ALU.max),
                        reads=[f"QF{bank}"], writes=[("tbf2", pe2)],
                        tag=f"qfT_{t}_{h0}_{fc}")
                    P.add("vector", lambda e, sl=sl, pe2=pe2: e.tensor_tensor(
                        qf[:, sl * TT:(sl + 1) * TT], dch(ebf2, pe2, TT),
                        dch(tbf2, pe2, TT), ALU.min),
                        reads=[("ebf2", pe2), ("tbf2", pe2)], writes=[("qf", sl)],
                        tag=f"qfM_{t}_{h0}_{fc}")
            for c in range(TCH):
                pk = c % 2
                for bl in range(4):
                    bank = 2 + bl % 2
                    def mm_a(e, c=c, bl=bl, bank=bank):
                        last = None
                        for s in range(4):
                            h0 = bl * 4 + s
                            for fc in range(2):
                                sl = h0 * 2 + fc
                                last = e.matmul(
                                    PF[bank][:, s * CH:s * CH + 65],
                                    qf[:, sl * TT + c * CH:sl * TT + (c + 1) * CH],
                                    kvl[:, sl * 66:sl * 66 + 65],
                                    start=(fc == 0), stop=(fc == 1))
                        return last
                    P.add("tensor", mm_a,
                          reads=[("qf", (bl * 4 + s) * 2 + fc)
                                 for s in range(4) for fc in range(2)],
                          writes=[f"QF{bank}"], tag=f"a_{t}_{c}_{bl}")
                    P.add("scalar", lambda e, bl=bl, bank=bank, pk=pk: e.activation(
                        attnr[:, pk * D + bl * 4 * 64:pk * D + (bl + 1) * 4 * 64]
                        .rearrange("p (s w) -> p s w", w=64),
                        PF[bank][:].rearrange("p (s w) -> p s w", w=CH)[:, :, 0:64],
                        AF.Copy),
                        reads=[f"QF{bank}"], writes=[("atr", pk)],
                        tag=f"ae_{t}_{c}_{bl}")
                    P.add("vector", lambda e, bl=bl, bank=bank: e.tensor_scalar_add(
                        z16[:, bl * 4:(bl + 1) * 4].rearrange(
                            "p (s o) -> p s o", o=1),
                        PF[bank][:].rearrange("p (s w) -> p s w", w=CH)[:, :, 64:65],
                        EPS_ATTN),
                        reads=[f"QF{bank}"], writes=[("z16", bl)],
                        tag=f"az_{t}_{c}_{bl}")
                P.add("vector", lambda e: e.reciprocal(zr16[:], z16[:]),
                      reads=[("z16", bl2) for bl2 in range(4)], writes=["zr16"],
                      tag=f"ar_{t}_{c}", late=True)
                P.add("vector", lambda e, pk=pk: e.tensor_tensor(
                    dch(attnr, pk, D).rearrange("p (h w) -> p h w", w=64),
                    dch(attnr, pk, D).rearrange("p (h w) -> p h w", w=64),
                    zr16[:].rearrange("p (h o) -> p h o", o=1).broadcast_to(
                        (CH, H, 64)), ALU.mult),
                    reads=["zr16", ("atr", pk)], writes=[("atr", pk)],
                    tag=f"as_{t}_{c}")
                for kc in range(NKC):
                    P.add("tensor", lambda e, kc=kc, pk=pk: e.transpose(
                        PT2[:, pk * D + kc * CH:pk * D + (kc + 1) * CH],
                        attnr[:, pk * D + kc * CH:pk * D + (kc + 1) * CH], iden[:]),
                        reads=[("atr", pk)], writes=[("PT2", pk)],
                        tag=f"aT_{t}_{c}_{kc}")
                P.add("vector", lambda e, c=c, pk=pk: e.tensor_copy(
                    attnfm[:].rearrange("p (k tt) -> p k tt", tt=TT)
                    [:, :, c * CH:(c + 1) * CH],
                    PT2[:, pk * D:(pk + 1) * D].rearrange("p (k w) -> p k w", w=CH)),
                    reads=[("PT2", pk)], writes=[("atf", c)], tag=f"aTe_{t}_{c}")
            for mc in range(NKC):
                bank = mc % 2
                def mm_y(e, mc=mc, bank=bank):
                    last = None
                    for kc in range(NKC):
                        last = e.matmul(PF[bank][:],
                                        mwq[:, kc * D + mc * CH:kc * D + (mc + 1) * CH],
                                        dch(xqb, kc, TT),
                                        start=(kc == 0), stop=(kc == NKC - 1))
                    return last
                P.add("tensor", mm_y,
                      reads=[("xqb", kc2) for kc2 in range(NKC)],
                      writes=[f"QF{bank}"], tag=f"y_{t}_{mc}")
                P.add("scalar", lambda e, mc=mc, bank=bank: e.activation(
                    dch(ybf, mc, TT), PF[bank][:], AF.Copy),
                    reads=[f"QF{bank}"], writes=[("shA", mc)], tag=f"ye_{t}_{mc}")
            for c in range(TCH):
                cc = t * TCH + c
                pk = c % 2
                for kc in range(NKC):
                    P.add("tensor", lambda e, kc=kc, pk=pk, c=c: e.transpose(
                        PT2[:, pk * D + kc * CH:pk * D + (kc + 1) * CH],
                        ybf[:, kc * TT + c * CH:kc * TT + (c + 1) * CH], iden[:]),
                        reads=[("shA", kc)], writes=[("PT2", pk)],
                        tag=f"yT_{t}_{c}_{kc}")
                xs = dch(xpin, pk, D)
                P.dma("sync", lambda e, cc=cc, xs=xs: e.dma_start(
                    out=xs, in_=x_d[cc * CH:(cc + 1) * CH, :]),
                    writes=[("xin", pk)], tag=f"px_{cc}")
                P.add("vector", lambda e, pk=pk: e.scalar_tensor_tensor(
                    dch(xpart, pk, D), PT2[:, pk * D:(pk + 1) * D], gsc[:, 1:2],
                    dch(xpin, pk, D), ALU.mult, ALU.add),
                    reads=[("PT2", pk), ("xin", pk)],
                    writes=[("xpart", pk)], tag=f"pp_{cc}")
                P.dma("scalar", lambda e, cc=cc, pk=pk: e.dma_start(
                    out=part_s[cc * CH:(cc + 1) * CH, :], in_=dch(xpart, pk, D)),
                    reads=[("xpart", pk)], writes=[("parts", cc)], tag=f"ps_{cc}")
            for mc in range(NKC):
                bank = mc % 2
                def mm_o(e, mc=mc, bank=bank):
                    last = None
                    for kc in range(NKC):
                        last = e.matmul(PF[bank][:],
                                        woT[:, kc * D + mc * CH:kc * D + (mc + 1) * CH],
                                        dch(attnfm, kc, TT),
                                        start=(kc == 0), stop=(kc == NKC - 1))
                    return last
                P.add("tensor", mm_o,
                      reads=[("woT", kc2) for kc2 in range(NKC)]
                      + [("atf", c2) for c2 in range(TCH)],
                      writes=[f"QF{bank}"], tag=f"o_{t}_{mc}")
                P.add("vector", lambda e, mc=mc, bank=bank, xch=xch: e.scalar_tensor_tensor(
                    dch(x1b, mc, TT), PF[bank][:], B_OUT[:, mc:mc + 1],
                    xch(mc), ALU.add, ALU.add),
                    reads=[f"QF{bank}"] + [("xnfm", xnk, c2) for c2 in range(TCH)],
                    writes=[("x1b", mc)], tag=f"x1_{t}_{mc}")
                P.add("scalar", lambda e, mc=mc: e.activation(
                    dch(x1sq, mc, TT), dch(x1b, mc, TT), AF.Square),
                    reads=[("x1b", mc)], writes=[("shA", mc)], tag=f"x1q_{t}_{mc}")
        ln1_tail(NTL - 1)
        P.emit(nc, sems)
      except _Skip:
        pass

    with contextlib.ExitStack() as p3:
      try:
        _guard(3)
        sb3 = lambda n, s, d_: p3.enter_context(nc.sbuf_tensor(n, s, d_))
        ps3 = lambda n, s, d_: p3.enter_context(nc.psum_tensor(n, s, d_))
        w2r = sb3("w2r", [CH, NMF * D], BF16)
        w1b = sb3("w1b", [CH, 2 * D], BF16)
        x1nb = sb3("x1v", [CH, 2 * NKC * TT], BF16)
        hbuf = sb3("hbf", [CH, NMF * TT], BF16)
        x2b = sb3("x2b", [CH, NKC * TT], BF16)
        x2sq = sb3("x2q", [CH, NKC * TT], BF16)
        aob = sb3("aob", [CH, NKC * TT], BF16)
        rows = sb3("rw3", [1, 4 * TT], F32)
        bc1 = sb3("bc3", [CH, TT], F32)
        bc2 = sb3("bc4", [CH, TT], F32)
        tsc = sb3("tsc", [CH, TT], F32)
        xpart = sb3("xp3", [CH, 2 * D], F32)
        outb = sb3("oub", [CH, 2 * D], F32)
        PT3 = ps3("PT3", [CH, 2 * D], BF16)
        PF = [ps3(f"RF{i}", [CH, 512], F32) for i in range(5)]
        P = Plan(state, drop,
                 psum_keys=[f"RF{i}" for i in range(5)] + [("PT3", 0), ("PT3", 1)])

        def row(i):
            return rows[:, i * TT:(i + 1) * TT]

        for kc in range(NMF):
            P.dma("sync", lambda e, kc=kc: e.dma_start(
                out=dch(w2r, kc, D), in_=w2s_s[kc * CH:(kc + 1) * CH, :]),
                writes=[("w2r", kc)], tag=f"lw2_{kc}")

        def ln2_tail(t):
                def mm_t1(e):
                    last = None
                    for kc in range(NKC):
                        last = e.matmul(PF[4][0:1, :], onesc[:], dch(x2b, kc, TT),
                                        start=(kc == 0), stop=(kc == NKC - 1))
                    return last
                P.add("tensor", mm_t1, reads=[("x2b", kc2) for kc2 in range(NKC)],
                      writes=["RF4"], tag=f"t1_{t}")
                P.add("vector", lambda e: e.tensor_scalar_mul(
                    row(0), PF[4][0:1, :], 1.0 / D),
                    reads=["RF4"], writes=[("rw", 0)], tag=f"t1e_{t}")
                def mm_t2(e):
                    last = None
                    for kc in range(NKC):
                        last = e.matmul(PF[4][0:1, :], onesc[:], dch(x2sq, kc, TT),
                                        start=(kc == 0), stop=(kc == NKC - 1))
                    return last
                P.add("tensor", mm_t2, reads=[("x2sq", kc2) for kc2 in range(NKC)],
                      writes=["RF4"], tag=f"t2_{t}")
                P.add("vector", lambda e: e.tensor_scalar_mul(
                    row(1), PF[4][0:1, :], 1.0 / D),
                    reads=["RF4"], writes=[("rw", 1)], tag=f"t2e_{t}")
                P.add("vector", lambda e: e.tensor_mul(row(3), row(0), row(0)),
                      reads=[("rw", 0)], writes=[("rw", 3)], tag=f"u2_{t}")
                P.add("vector", lambda e: e.tensor_sub(row(2), row(1), row(3)),
                      reads=[("rw", 1), ("rw", 3)], writes=[("rw", 2)], tag=f"u3_{t}")
                P.add("scalar", lambda e: e.activation(
                    row(3), row(2), AF.Ln, bias=EPS1r),
                    reads=[("rw", 2)], writes=[("rw", 3)], tag=f"u4_{t}")
                P.add("scalar", lambda e: e.activation(
                    row(2), row(3), AF.Exp, scale=-0.5),
                    reads=[("rw", 3)], writes=[("rw", 2)], tag=f"r2_{t}")
                P.add("vector", lambda e: e.scalar_tensor_tensor(
                    row(3), row(0), -1.0, row(2), ALU.mult, ALU.mult),
                    reads=[("rw", 0), ("rw", 2)], writes=[("rw", 3)], tag=f"q2_{t}")
                P.add("tensor", lambda e: e.matmul(PF[4][:], onesr[:], row(2),
                                                   start=True, stop=True),
                      reads=[("rw", 2)], writes=["RF4"], tag=f"w1b_{t}")
                P.add("vector", lambda e: e.tensor_copy(bc1[:], PF[4][:]),
                      reads=["RF4"], writes=["bc1"], tag=f"w1e_{t}")
                P.add("tensor", lambda e: e.matmul(PF[4][:], onesr[:], row(3),
                                                   start=True, stop=True),
                      reads=[("rw", 3)], writes=["RF4"], tag=f"w2b_{t}")
                P.add("vector", lambda e: e.tensor_copy(bc2[:], PF[4][:]),
                      reads=["RF4"], writes=["bc2"], tag=f"w2e_{t}")
                for mc in range(NKC):
                    P.add("vector", lambda e, mc=mc: e.tensor_mul(
                        tsc[:], dch(x2b, mc, TT), bc1[:]),
                        reads=[("x2b", mc), "bc1"], writes=["tsc"], tag=f"m1_{t}_{mc}")
                    P.add("vector", lambda e: e.tensor_add(tsc[:], tsc[:], bc2[:]),
                          reads=["tsc", "bc2"], writes=["tsc"], tag=f"m2_{t}_{mc}")
                    P.add("vector", lambda e, mc=mc: e.tensor_scalar(
                        dch(aob, mc, TT), tsc[:], GAM[:, mc:mc + 1], BET[:, mc:mc + 1],
                        ALU.mult, ALU.add),
                        reads=["tsc"], writes=[("aob", mc)], tag=f"m3_{t}_{mc}")
                for c in range(TCH):
                    cc = t * TCH + c
                    pk = c % 2
                    for kc in range(NKC):
                        P.add("tensor", lambda e, kc=kc, pk=pk, c=c: e.transpose(
                            PT3[:, pk * D + kc * CH:pk * D + (kc + 1) * CH],
                            aob[:, kc * TT + c * CH:kc * TT + (c + 1) * CH], iden[:]),
                            reads=[("aob", kc)], writes=[("PT3", pk)],
                            tag=f"oT_{t}_{c}_{kc}")
                    xs = dch(xpart, pk, D)
                    P.dma("gpsimd", lambda e, cc=cc, xs=xs: e.dma_start(
                        out=xs, in_=part_s[cc * CH:(cc + 1) * CH, :]),
                        writes=[("xp3", pk)], tag=f"lp_{cc}")
                    P.add("vector", lambda e, pk=pk: e.tensor_add(
                        dch(outb, pk, D), PT3[:, pk * D:(pk + 1) * D], dch(xpart, pk, D)),
                        reads=[("PT3", pk), ("xp3", pk)], writes=[("oub", pk)],
                        tag=f"os_{cc}")
                    P.dma("gpsimd", lambda e, cc=cc, pk=pk: e.dma_start(
                        out=out_d[cc * CH:(cc + 1) * CH, :], in_=dch(outb, pk, D)),
                        reads=[("oub", pk)], writes=[("outd", cc)], tag=f"ow_{cc}")
        for t in range(NTL):
            tb = t % 2
            for mc in range(NKC):
                P.dma("sync", lambda e, t=t, mc=mc, tb=tb: e.dma_start(
                    out=x1nb[:, (tb * NKC + mc) * TT:(tb * NKC + mc + 1) * TT],
                    in_=x1n_s[mc * CH:(mc + 1) * CH, t * TT:(t + 1) * TT]),
                    writes=[("x1v", tb, mc)], tag=f"lx1_{t}_{mc}")
            if t > 0:
                ln2_tail(t - 1)
            for m in range(NMF):
                wb = m % 2
                bank = m % 2
                P.dma("sync", lambda e, m=m, wb=wb: e.dma_start(
                    out=dch(w1b, wb, D), in_=w1s_s[m]),
                    writes=[("w1b", wb)], tag=f"lw1_{t}_{m}")
                def mm_h(e, m=m, wb=wb, bank=bank, tb=tb):
                    last = None
                    for kc in range(NKC):
                        last = e.matmul(
                            PF[bank][:],
                            w1b[:, wb * D + kc * CH:wb * D + (kc + 1) * CH],
                            x1nb[:, (tb * NKC + kc) * TT:(tb * NKC + kc + 1) * TT],
                            start=(kc == 0), stop=(kc == NKC - 1))
                    return last
                P.add("tensor", mm_h,
                      reads=[("w1b", wb)] + [("x1v", tb, kc2) for kc2 in range(NKC)],
                      writes=[f"RF{bank}"], tag=f"h_{t}_{m}")
                P.add("scalar", lambda e, m=m, bank=bank: e.activation(
                    dch(hbuf, m, TT), PF[bank][:], AF.Gelu, bias=b1c[:, m:m + 1]),
                    reads=[f"RF{bank}"], writes=[("h", m)], tag=f"he_{t}_{m}")
            for mc in range(NKC):
                bank = 2 + mc % 2
                def mm_x2(e, mc=mc, bank=bank):
                    last = None
                    for kc in range(NMF):
                        last = e.matmul(
                            PF[bank][:],
                            w2r[:, kc * D + mc * CH:kc * D + (mc + 1) * CH],
                            dch(hbuf, kc, TT),
                            start=(kc == 0), stop=(kc == NMF - 1))
                    return last
                P.add("tensor", mm_x2,
                      reads=[("w2r", kc2) for kc2 in range(NMF)]
                      + [("h", kc2) for kc2 in range(NMF)],
                      writes=[f"RF{bank}"], tag=f"x2_{t}_{mc}")
                P.add("vector",
                      lambda e, mc=mc, bank=bank, tb=tb: e.scalar_tensor_tensor(
                          dch(x2b, mc, TT), PF[bank][:], B2[:, mc:mc + 1],
                          x1nb[:, (tb * NKC + mc) * TT:(tb * NKC + mc + 1) * TT],
                          ALU.add, ALU.add),
                      reads=[f"RF{bank}", ("x1v", tb, mc)],
                      writes=[("x2b", mc)], tag=f"x2e_{t}_{mc}")
                P.add("scalar", lambda e, mc=mc: e.activation(
                    dch(x2sq, mc, TT), dch(x2b, mc, TT), AF.Square),
                    reads=[("x2b", mc)], writes=[("x2sq", mc)], tag=f"x2q_{t}_{mc}")
        ln2_tail(NTL - 1)
        P.emit(nc, sems)
      except _Skip:
        pass

    if debug:
        P = Plan(state)
        P.dma("sync", lambda e: e.dma_start(out=dbg["scal"], in_=scal[:]), tag="dbg_scal")
        P.dma("sync", lambda e: e.dma_start(
            out=dbg["gsc"], in_=build_nc._stash["sred"][:]), tag="dbg_gsc")
        P.dma("sync", lambda e: e.dma_start(out=dbg["kvl"], in_=kvl[:]), tag="dbg_kvl")
        P.dma("sync", lambda e: e.dma_start(out=dbg["x1n"], in_=x1n_s), tag="dbg_x1n")
        P.dma("sync", lambda e: e.dma_start(out=dbg["part"], in_=part_s), tag="dbg_part")
        P.dma("sync", lambda e: e.dma_start(out=dbg["h2row"], in_=h2row_s), tag="dbg_h2row")
        P.dma("sync", lambda e: e.dma_start(out=dbg["stH"], in_=statsH2[:]), tag="dbg_stH")
        P.dma("sync", lambda e: e.dma_start(out=dbg["stR"], in_=statsR[:]), tag="dbg_stR")
        P.dma("sync", lambda e: e.dma_start(out=dbg["stM"], in_=statsMUR[:]), tag="dbg_stM")
        P.dma("sync", lambda e: e.dma_start(out=dbg["mwq"], in_=mwq[:]), tag="dbg_mwq")
        P.emit(nc, sems)

    es.close()
    return nc


def _shard_inputs(L0_attn_x, L0_attn_proj, L0_attn_w_qkv, L0_attn_w_out,
                  L0_attn_b_out, L0_attn_gamma, L0_attn_beta, L0_attn_w1,
                  L0_attn_b1, L0_attn_w2, L0_attn_b2, L0_ffn_weight,
                  L0_ffn_ln_gamma, L0_ffn_ln_beta, NT):
    import ml_dtypes
    f32 = lambda a: np.ascontiguousarray(np.asarray(a, dtype=np.float32))
    x = f32(L0_attn_x)
    wqkvT = np.ascontiguousarray(f32(L0_attn_w_qkv).T)
    pj = f32(L0_attn_proj)
    projP = np.zeros((H // 2, CH, 2 * FF), np.float32)
    for pr in range(H // 2):
        projP[pr, 0:64, 0:FF] = pj[2 * pr]
        projP[pr, 64:128, FF:2 * FF] = pj[2 * pr + 1]
    projQ = np.zeros((CH, 2 * H * CH), np.float32)
    for h0 in range(H):
        for fc in range(2):
            sl = h0 * 2 + fc
            projQ[(h0 % 2) * 64:(h0 % 2) * 64 + 64, sl * CH:(sl + 1) * CH] = \
                pj[h0][:, fc * CH:(fc + 1) * CH]
    woutT = np.ascontiguousarray(f32(L0_attn_w_out).T)
    w1T = np.ascontiguousarray(f32(L0_attn_w1).T)
    w2T = np.ascontiguousarray(f32(L0_attn_w2).T)
    fw = f32(L0_ffn_weight)
    vecs = np.zeros((CH, 64), np.float32)
    for i, v in enumerate((L0_attn_b_out, L0_attn_gamma, L0_attn_beta,
                           L0_attn_b2, L0_ffn_ln_gamma, L0_ffn_ln_beta)):
        vecs[:, i * 8:(i + 1) * 8] = f32(v).reshape(NKC, CH).T
    b1c = np.ascontiguousarray(f32(L0_attn_b1).reshape(NMF, CH).T)
    vecs[:, 48] = EPS_LN
    vecs[:, 49] = LN_EPS_FFN
    lngbf = np.ascontiguousarray(np.broadcast_to(f32(L0_ffn_ln_gamma), (CH, D)))
    lnbbf = np.ascontiguousarray(np.broadcast_to(f32(L0_ffn_ln_beta), (CH, D)))
    iden = np.eye(CH, dtype=ml_dtypes.bfloat16)
    onesr = np.ones((1, CH), np.float32)
    onescb = np.ones((CH, 1), ml_dtypes.bfloat16)
    shared = dict(wqkvT=wqkvT, projP=projP, projQ=projQ, woutT=woutT, w1T=w1T, w2T=w2T,
                  fw=fw, vecs=vecs, b1c=b1c, lngbf=lngbf, lnbbf=lnbbf,
                  iden=iden, onesr=onesr, onescb=onescb)
    return [dict(shared, x=np.ascontiguousarray(x[i, :NT])) for i in range(NC)]


def kernel(**inputs):
    NT = 8192
    in_maps = _shard_inputs(NT=NT, **inputs)
    nc = build_nc(NT=NT)
    res = run_bass_kernel_spmd(nc, in_maps, core_ids=list(range(NC)))
    return np.stack([np.asarray(r["out"]) for r in res.results]).astype(np.float32)


# version 16 (final)
# speedup vs baseline: 1.2758x; 1.0009x over previous
import contextlib
import numpy as np
import concourse.bass as bass
import concourse.mybir as mybir
from concourse.bass_utils import run_bass_kernel_spmd

F32 = mybir.dt.float32
BF16 = mybir.dt.bfloat16
AF = mybir.ActivationFunctionType
ALU = mybir.AluOpType
AX = mybir.AxisListType.X

D = 1024
H = 16
FF = 256
DFF = 4096
NC = 8
CH = 128
NKC = D // CH
NMF = DFF // CH

EPS_LN = 1e-5
EPS_ATTN = 1e-8
LN_EPS_FFN = 1e-3
EPS_FFN = 1e-5
QB = 128.0
CLIP = QB - EPS_FFN

TT = 512
TCH = TT // CH


class Plan:

    ENGS = ("sync", "tensor", "vector", "scalar", "gpsimd")

    NLANES = {"sync": 8, "scalar": 6, "gpsimd": 4}

    def __init__(self, state, drop=(), psum_keys=()):
        self.state = state
        self.drop = tuple(drop)
        self.psum_keys = set(psum_keys)
        self.ops = []
        self.last_w = {}
        self.readers = {}

    def add(self, eng, emit, reads=(), writes=(), kind="compute", tag="",
            late=False):
        if any(tag.startswith(p) for p in self.drop):
            return None
        idx = len(self.ops)
        deps = set()
        for b in reads:
            w = self.last_w.get(b)
            if w is not None:
                deps.add(w)
            if b in self.psum_keys:
                for r in self.readers.get(b, ()):
                    if self.ops[r]["eng"] != eng:
                        deps.add(r)
        for b in writes:
            w = self.last_w.get(b)
            if w is not None:
                deps.add(w)
            for r in self.readers.get(b, ()):
                deps.add(r)
        for b in reads:
            self.readers.setdefault(b, []).append(idx)
        for b in writes:
            self.last_w[b] = idx
            self.readers[b] = []
        deps.discard(idx)
        lane = None
        if kind == "dma":
            lc = self.state.setdefault("lanecnt", {})
            lc[eng] = lc.get(eng, 0) + 1
            lane = lc[eng] % self.NLANES[eng]
        self.ops.append(dict(eng=eng, emit=emit, kind=kind, deps=deps, tag=tag,
                             late=late, inc=(kind != "compute"), incv=None,
                             waits=None, lane=lane, idx=idx))
        return idx

    def dma(self, eng, emit, reads=(), writes=(), tag=""):
        return self.add(eng, emit, reads, writes, kind="dma", tag=tag)

    def _sname(self, op):
        if op["kind"] == "dma":
            return "q_%s%d" % (op["eng"], op["lane"])
        if op["kind"] == "cc":
            return "cc"
        return "e_" + op["eng"]

    def _nowait(self, dep, op):
        if dep.get("late"):
            return False
        if dep["eng"] != op["eng"]:
            return False
        if dep["kind"] == "dma" or dep["kind"] == "cc":
            return dep["kind"] == op["kind"]
        return dep["eng"] == "tensor"

    FULL_WAITS = False

    def resolve(self):
        counts = self.state["counts"]
        ops = self.ops
        for op in ops:
            for d in op["deps"]:
                dep = ops[d]
                if not self._nowait(dep, op):
                    dep["inc"] = True
        for op in ops:
            if op["inc"]:
                sn = self._sname(op)
                counts[sn] = counts.get(sn, 0) + (16 if op["kind"] == "dma" else 1)
                op["incv"] = counts[sn]
        clocks = self.state["clocks"]
        opclock = [None] * len(ops)
        for op in ops:
            ec = clocks.setdefault(op["eng"], {})
            waits = []
            for d in sorted(op["deps"]):
                dep = ops[d]
                if self._nowait(dep, op):
                    continue
                sn, v = self._sname(dep), dep["incv"]
                assert v is not None, (op["tag"], dep["tag"])
                if ec.get(sn, 0) >= v and not Plan.FULL_WAITS:
                    continue
                if ec.get(sn, 0) >= v:
                    ec[sn] = max(ec.get(sn, 0), v)
                    waits.append((sn, v))
                    continue
                waits.append((sn, v))
                ec[sn] = v
                dc = opclock[d]
                if dc:
                    for s2, v2 in dc.items():
                        if ec.get(s2, 0) < v2:
                            ec[s2] = v2
            op["waits"] = waits
            opclock[op["idx"]] = dict(ec)

    def emit(self, nc, sems):
        self.resolve()
        counts = self.state["counts"]
        qsems = ["q_sync%d" % i for i in range(8)] + \
                ["q_scalar%d" % i for i in range(6)] + \
                ["q_gpsimd%d" % i for i in range(4)] + ["cc"]
        drain = [(sn, counts[sn]) for sn in qsems if counts.get(sn)]
        with nc.Block() as blk:
            for eng_name in self.ENGS:
                ops = [o for o in self.ops if o["eng"] == eng_name]
                if not ops and eng_name != "vector":
                    continue

                def make(ops, eng_name=eng_name):
                    def body(eng):
                        for op in ops:
                            for sn, v in op["waits"]:
                                eng.wait_ge(sems[sn], v)
                            inst = op["emit"](eng)
                            if op["inc"]:
                                assert inst is not None, op["tag"]
                                inst.then_inc(sems[self._sname(op)],
                                              16 if op["kind"] == "dma" else 1)
                        if eng_name == "vector":
                            ec = self.state["clocks"].setdefault("vector", {})
                            for sn, v in drain:
                                if ec.get(sn, 0) < v:
                                    eng.wait_ge(sems[sn], v)
                                    ec[sn] = v
                    return body
                getattr(blk, eng_name)(make(ops))


def build_nc(NT=8192, race=False, passes=(1, 2, 3), drop=(), debug=False):
    NTL = NT // TT
    NCH = NT // CH

    nc = bass.Bass("TRN2", target_bir_lowering=False,
                   detect_race_conditions=race)
    dt = nc.dram_tensor
    x_d = dt("x", [NT, D], F32, kind="ExternalInput").ap()
    wqkvT_d = dt("wqkvT", [D, 3 * D], F32, kind="ExternalInput").ap()
    projP_d = dt("projP", [H // 2, CH, 2 * FF], F32, kind="ExternalInput").ap()
    projQ_d = dt("projQ", [CH, 2 * H * CH], F32, kind="ExternalInput").ap()
    woutT_d = dt("woutT", [D, D], F32, kind="ExternalInput").ap()
    w1T_d = dt("w1T", [D, DFF], F32, kind="ExternalInput").ap()
    w2T_d = dt("w2T", [DFF, D], F32, kind="ExternalInput").ap()
    fw_d = dt("fw", [D, D], F32, kind="ExternalInput").ap()
    vecs_d = dt("vecs", [CH, 64], F32, kind="ExternalInput").ap()
    b1c_d = dt("b1c", [CH, NMF], F32, kind="ExternalInput").ap()
    iden_d = dt("iden", [CH, CH], BF16, kind="ExternalInput").ap()
    onesr_d = dt("onesr", [1, CH], F32, kind="ExternalInput").ap()
    onesc_d = dt("onescb", [CH, 1], BF16, kind="ExternalInput").ap()
    out_d = dt("out", [NT, D], F32, kind="ExternalOutput").ap()

    x1n_s = dt("x1n_s", [D, NT], BF16, kind="Internal").ap()
    part_s = dt("part_s", [NT, D], F32, kind="Internal").ap()
    w1s_s = dt("w1s_s", [NMF, CH, D], BF16, kind="Internal").ap()
    w2s_s = dt("w2s_s", [DFF, D], BF16, kind="Internal").ap()
    h2row_s = dt("h2row_s", [1, NT], F32, kind="Internal").ap()
    cc_in = dt("cc_in", [1, 1], F32, kind="Internal").ap()
    cc_out = dt("cc_out", [1, 1], F32, kind="Internal", addr_space="Shared").ap()

    dbg = {}
    if debug:
        dbg["scal"] = dt("dbg_scal", [1, 16], F32, kind="ExternalOutput").ap()
        dbg["gsc"] = dt("dbg_gsc", [CH, 8], F32, kind="ExternalOutput").ap()
        dbg["kvl"] = dt("dbg_kvl", [CH, 2 * H * 66], BF16, kind="ExternalOutput").ap()
        dbg["x1n"] = dt("dbg_x1n", [D, NT], BF16, kind="ExternalOutput").ap()
        dbg["part"] = dt("dbg_part", [NT, D], F32, kind="ExternalOutput").ap()
        dbg["h2row"] = dt("dbg_h2row", [1, NT], F32, kind="ExternalOutput").ap()
        dbg["stH"] = dt("dbg_stH", [CH, NT // CH], F32, kind="ExternalOutput").ap()
        dbg["stR"] = dt("dbg_stR", [CH, NT // CH], F32, kind="ExternalOutput").ap()
        dbg["stM"] = dt("dbg_stM", [CH, NT // CH], F32, kind="ExternalOutput").ap()
        dbg["mwq"] = dt("dbg_mwq", [CH, NKC * D], BF16, kind="ExternalOutput").ap()
    state = dict(counts={}, clocks={})
    es = contextlib.ExitStack()
    sems = {}
    for sn in (["e_tensor", "e_vector", "e_scalar", "e_gpsimd", "cc"]
               + ["q_sync%d" % i for i in range(8)]
               + ["q_scalar%d" % i for i in range(6)]
               + ["q_gpsimd%d" % i for i in range(4)]):
        sems[sn] = es.enter_context(nc.semaphore(sn))
    sb = lambda n, s, d_: es.enter_context(nc.sbuf_tensor(n, s, d_))

    iden = sb("idn", [CH, CH], BF16)
    onesr = sb("onr", [1, CH], F32)
    onesc = sb("onc", [CH, 1], BF16)
    vecs = sb("vcs", [CH, 64], F32)
    b1c = sb("b1cs", [CH, NMF], F32)
    statsR = sb("stR", [CH, NCH], F32)
    statsMUR = sb("stM", [CH, NCH], F32)
    statsH2 = sb("stH", [CH, NCH], F32)
    kvl = sb("kvl", [CH, 2 * H * 66], BF16)
    mwq = sb("mwq", [CH, NKC * D], BF16)
    gsc = sb("gsc", [CH, 8], F32)
    lngyc = sb("lgy", [CH, NKC], F32)
    lnbyc = sb("lby", [CH, NKC], F32)
    grow = sb("grw", [1, CH], F32)
    scal = sb("scl", [1, 16], F32)
    gacc = sb("gac", [CH, 1], F32)
    srow = sb("srw", [1, CH], F32)

    B_OUT = vecs[:, 0:8]
    GAM = vecs[:, 8:16]
    BET = vecs[:, 16:24]
    B2 = vecs[:, 24:32]
    LNG = vecs[:, 32:40]
    LNB = vecs[:, 40:48]
    EPS1 = vecs[:, 48:49]
    EPSF = vecs[:, 49:50]
    EPS1r = vecs[0:1, 48:49]

    def dch(ap2d, c, w=CH):
        return ap2d[:, c * w:(c + 1) * w]

    class _Skip(Exception):
        pass

    def _guard(n):
        if n not in passes:
            raise _Skip()

    with contextlib.ExitStack() as p1:
      try:
        _guard(1)
        sb1 = lambda n, s, d_: p1.enter_context(nc.sbuf_tensor(n, s, d_))
        ps1 = lambda n, s, d_: p1.enter_context(nc.psum_tensor(n, s, d_))
        kT = sb1("kT", [CH, NKC * D], BF16)
        vT = sb1("vT", [CH, NKC * D], BF16)
        lngb = sb1("lgb", [CH, D], BF16)
        lnbb = sb1("lbb", [CH, D], BF16)
        kvacc = sb1("kva", [65, H * FF], F32)
        projBD = sb1("pbd", [CH, (H // 2) * 2 * FF], BF16)
        wst = sb1("wst", [CH, DFF], F32)
        wst2 = sb1("ws2", [CH, DFF], BF16)
        rstg = sb1("rsg", [CH, 2 * D], F32)
        xin = sb1("xin", [CH, 2 * D], F32)
        xntm = sb1("xnt", [CH, 2 * D], BF16)
        xnfm = sb1("xnf", [CH, NKC * TT], BF16)
        kfm = sb1("kfm", [CH, NKC * TT], BF16)
        vext = sb1("vex", [CH, TCH * H * 66], BF16)
        kftm = sb1("kft", [CH, TCH * H * FF], BF16)
        ebuf = sb1("ebf", [CH, 2 * 2 * FF], BF16)
        tbuf = sb1("tbf", [CH, 2 * 2 * FF], BF16)
        gscr = sb1("gs1", [CH, D], BF16)
        gscr2 = sb1("gs2", [CH, D], BF16)
        sqd = sb1("sqd", [CH, D], F32)
        sred = sb1("srd", [CH, 8], F32)
        stS = sb1("stS", [CH, 16], F32)
        PT = ps1("PT", [CH, 2 * D], BF16)
        PF = [ps1(f"PF{i}", [CH, 512], F32) for i in range(6)]
        P = Plan(state, drop,
                 psum_keys=[f"PF{i}" for i in range(6)] + [("PT", 0), ("PT", 1)])

        for name, dstv, srcv in (("iden", iden, iden_d), ("onesr", onesr, onesr_d),
                                 ("onesc", onesc, onesc_d), ("vecs", vecs, vecs_d),
                                 ("b1c", b1c, b1c_d)):
            P.dma("sync", lambda e, a=dstv, b=srcv: e.dma_start(out=a[:], in_=b),
                  writes=[name], tag="ld_" + name)
        lngbf_d = dt("lngbf", [CH, D], F32, kind="ExternalInput").ap()
        lnbbf_d = dt("lnbbf", [CH, D], F32, kind="ExternalInput").ap()
        P.dma("scalar", lambda e: e.dma_start(out=wst[:, 0:D], in_=lngbf_d),
              writes=[("wst", 0)], tag="ld_lngf")
        P.dma("scalar", lambda e: e.dma_start(out=wst[:, D:2 * D], in_=lnbbf_d),
              writes=[("wst", 1)], tag="ld_lnbf")
        P.add("vector", lambda e: e.tensor_copy(lngb[:], wst[:, 0:D]),
              reads=[("wst", 0)], writes=["lngb"], tag="c_lngb")
        P.add("vector", lambda e: e.tensor_copy(lnbb[:], wst[:, D:2 * D]),
              reads=[("wst", 1)], writes=["lnbb"], tag="c_lnbb")
        P.add("vector", lambda e: e.memset(gacc[:], 0.0), writes=["gacc"], tag="ms_g")
        P.add("vector", lambda e: e.memset(kvl[:], 0.0), writes=["kvl"], tag="ms_kvl")
        P.add("vector", lambda e: e.memset(vext[:], 1.0),
              writes=[("vext", c, vg) for c in range(TCH) for vg in range(2)],
              tag="ms_vx")

        _ldn = [0]

        def ldw(dkey, dsl, src, eng="vector"):
            q = _ldn[0] % 2
            _ldn[0] += 1
            sl = rstg[:, q * D:q * D + src.shape[1]]
            P.dma("scalar", lambda e, src=src, sl=sl: e.dma_start(out=sl, in_=src),
                  writes=[("rstg", q)], tag="ld" + str(dkey))
            P.add(eng, lambda e, dsl=dsl, sl=sl: e.tensor_copy(dsl, sl),
                  reads=[("rstg", q)], writes=[dkey], tag="cast" + str(dkey))

        for kc in range(NKC):
            ldw(("kT", kc), dch(kT, kc, D), wqkvT_d[kc * CH:(kc + 1) * CH, D:2 * D])
        for kc in range(NKC):
            ldw(("vT", kc), dch(vT, kc, D), wqkvT_d[kc * CH:(kc + 1) * CH, 2 * D:3 * D])
        for pr in range(H // 2):
            P.dma("sync", lambda e, pr=pr: e.dma_start(
                out=wst[:, 0:2 * FF], in_=projP_d[pr]),
                writes=[("wst", 0)], tag=f"ldpj_{pr}")
            P.add("vector", lambda e, pr=pr: e.tensor_copy(
                dch(projBD, pr, 2 * FF), wst[:, 0:2 * FF]),
                reads=[("wst", 0)], writes=[("projBD", pr)], tag=f"cpj_{pr}")

        def prep_gen():
            fw_r = fw_d.rearrange("(a p) u -> p a u", p=CH)
            for hf in range(2):
                P.dma("scalar", lambda e, hf=hf: e.dma_start(
                    out=wst[:].rearrange("p (a u) -> p a u", u=D),
                    in_=fw_r[:, hf * 4:(hf + 1) * 4, :]),
                    writes=[("wst", q) for q in range(4)], tag=f"ld_fw{hf}")
                P.add("vector", lambda e, hf=hf: e.reduce_sum(
                    sred[:, hf:hf + 1], wst[:], axis=AX, apply_absolute_value=True),
                    reads=[("wst", q) for q in range(4)], writes=[("sred", hf)],
                    tag=f"asum{hf}", late=True)
            P.add("vector", lambda e: e.tensor_add(sred[:, 2:3], sred[:, 0:1], sred[:, 1:2]),
                  reads=[("sred", 0), ("sred", 1)], writes=[("sred", 2)], tag="asum2")
            P.dma("gpsimd", lambda e: e.dma_start(out=srow[:], in_=sred[:, 2:3], allow_slow_non_contiguous=True),
                  reads=[("sred", 2)], writes=["srow"], tag="fold_b")
            P.add("vector", lambda e: e.reduce_sum(scal[:, 0:1], srow[:], axis=AX),
                  reads=["srow"], writes=[("scal", 0)], tag="bsum", late=True)
            P.add("vector", lambda e: e.tensor_scalar(
                scal[:, 1:2], scal[:, 0:1], 0.5 / (D * D), 0.5 * EPS_FFN, ALU.mult, ALU.add),
                reads=[("scal", 0)], writes=[("scal", 1)], tag="hb")
            P.add("vector", lambda e: e.tensor_scalar_mul(scal[:, 2:3], scal[:, 1:2], -1.0),
                  reads=[("scal", 1)], writes=[("scal", 2)], tag="nhb")
            P.add("tensor", lambda e: e.matmul(PF[0][:, 0:2], onesr[:], scal[:, 1:3],
                                               start=True, stop=True),
                  reads=[("scal", 1), ("scal", 2), "onesr"], writes=["PF0"], tag="bc_hb")
            P.add("vector", lambda e: e.tensor_copy(gsc[:, 2:4], PF[0][:, 0:2]),
                  reads=["PF0"], writes=[("gsc", 2)], tag="ev_hb")
            for hf in range(2):
                P.dma("scalar", lambda e, hf=hf: e.dma_start(
                    out=wst[:].rearrange("p (a u) -> p a u", u=D),
                    in_=fw_r[:, hf * 4:(hf + 1) * 4, :]),
                    writes=[("wst", q) for q in range(4)], tag=f"ld_fwb{hf}")
                P.add("vector", lambda e: e.tensor_single_scalar(
                    wst2[:], wst[:], gsc[:, 2:3], ALU.is_gt),
                    reads=[("wst", q) for q in range(4)] + [("gsc", 2)],
                    writes=[("wst2", q) for q in range(4)], tag=f"gt{hf}")
                P.add("vector", lambda e, hf=hf: e.scalar_tensor_tensor(
                    mwq[:, hf * DFF:(hf + 1) * DFF], wst[:], gsc[:, 3:4], wst2[:],
                    ALU.is_lt, ALU.subtract),
                    reads=[("wst", q) for q in range(4)]
                    + [("wst2", q) for q in range(4)] + [("gsc", 2)],
                    writes=[("mwq", hf)], tag=f"mwq{hf}")
                yield

            for kc in range(NMF):
                q = kc % 4
                ssl = wst[:, q * D:(q + 1) * D]
                s2 = wst2[:, q * D:(q + 1) * D]
                P.dma("scalar", lambda e, kc=kc, ssl=ssl: e.dma_start(
                    out=ssl, in_=w2T_d[kc * CH:(kc + 1) * CH, :]),
                    writes=[("wst", q)], tag=f"ldw2_{kc}")
                P.add("gpsimd", lambda e, ssl=ssl, s2=s2: e.tensor_copy(s2, ssl),
                      reads=[("wst", q)], writes=[("wst2", q)], tag=f"cw2_{kc}")
                P.dma("gpsimd", lambda e, kc=kc, s2=s2: e.dma_start(
                    out=w2s_s[kc * CH:(kc + 1) * CH, :], in_=s2),
                    reads=[("wst2", q)], writes=["w2s"], tag=f"sw2_{kc}")
                yield
            for kc in range(NKC):
                P.dma("scalar", lambda e, kc=kc: e.dma_start(
                    out=wst[:], in_=w1T_d[kc * CH:(kc + 1) * CH, :]),
                    writes=[("wst", q) for q in range(4)], tag=f"ldw1_{kc}")
                P.add("gpsimd", lambda e: e.tensor_copy(wst2[:], wst[:]),
                      reads=[("wst", q) for q in range(4)],
                      writes=[("wst2", q) for q in range(4)], tag=f"cw1_{kc}")
                P.dma("gpsimd", lambda e, kc=kc: e.dma_start(
                    out=w1s_s[:, :, kc * CH:(kc + 1) * CH].rearrange("m p c -> p m c"),
                    in_=wst2[:].rearrange("p (m c) -> p m c", c=CH)),
                    reads=[("wst2", q) for q in range(4)], writes=["w1s"], tag=f"sw1_{kc}")
                yield


        _pg = prep_gen()
        pend_kv = []
        for t in range(NTL):
            if t >= 1:
                for _ in range(4):
                    next(_pg, None)
            tc0 = t * TCH
            for c in range(TCH):
                cc = t * TCH + c
                pk = c % 2
                xs = dch(xin, pk, D)
                P.dma("sync", lambda e, cc=cc, xs=xs: e.dma_start(
                    out=xs, in_=x_d[cc * CH:(cc + 1) * CH, :]),
                    writes=[("xin", pk)], tag=f"ldx_{cc}")
                P.add("vector", lambda e, xs=xs, c=c: e.reduce_sum(
                    sred[:, c:c + 1], xs, axis=AX),
                    reads=[("xin", pk)], writes=[("sred", c)], tag=f"sum_{cc}",
                    late=True)
                P.add("vector", lambda e, xs=xs, c=c: e.scalar_tensor_tensor(
                    sqd[:], xs, 1.0, xs, ALU.mult, ALU.mult,
                    accum_out=sred[:, 4 + c:5 + c]),
                    reads=[("xin", pk)], writes=[("sred", 4 + c), "sqd"],
                    tag=f"sq_{cc}", late=True)
            SR = lambda a, b: stS[:, a:b]
            P.add("vector", lambda e: e.tensor_scalar_mul(
                SR(0, 4), sred[:, 0:4], 1.0 / D),
                reads=[("sred", c2) for c2 in range(4)], writes=[("stS", 0)],
                tag=f"bmu_{t}")
            P.add("vector", lambda e: e.tensor_scalar_mul(
                SR(4, 8), sred[:, 4:8], 1.0 / D),
                reads=[("sred", 4 + c2) for c2 in range(4)], writes=[("stS", 1)],
                tag=f"be2_{t}")
            P.add("vector", lambda e: e.tensor_mul(SR(8, 12), SR(0, 4), SR(0, 4)),
                  reads=[("stS", 0)], writes=[("stS", 2)], tag=f"bm2_{t}")
            P.add("vector", lambda e: e.tensor_sub(SR(12, 16), SR(4, 8), SR(8, 12)),
                  reads=[("stS", 1), ("stS", 2)], writes=[("stS", 3)], tag=f"bvr_{t}")
            P.add("scalar", lambda e: e.activation(
                SR(8, 12), SR(12, 16), AF.Ln, bias=EPS1),
                reads=[("stS", 3)], writes=[("stS", 2)], tag=f"bln_{t}")
            P.add("scalar", lambda e, tc0=tc0: e.activation(
                statsR[:, tc0:tc0 + 4], SR(8, 12), AF.Exp, scale=-0.5),
                reads=[("stS", 2)],
                writes=[("stR", tc0 + c2) for c2 in range(4)], tag=f"br_{t}")
            P.add("vector", lambda e, tc0=tc0: e.scalar_tensor_tensor(
                statsMUR[:, tc0:tc0 + 4], SR(0, 4), -1.0, statsR[:, tc0:tc0 + 4],
                ALU.mult, ALU.mult),
                reads=[("stS", 0)] + [("stR", tc0 + c2) for c2 in range(4)],
                writes=[("stM", tc0 + c2) for c2 in range(4)], tag=f"bmr_{t}")
            P.add("vector", lambda e, tc0=tc0: e.tensor_mul(
                SR(0, 4), statsR[:, tc0:tc0 + 4], statsR[:, tc0:tc0 + 4]),
                reads=[("stR", tc0 + c2) for c2 in range(4)], writes=[("stS", 0)],
                tag=f"brr_{t}")
            P.add("vector", lambda e: e.tensor_mul(SR(4, 8), SR(0, 4), SR(12, 16)),
                  reads=[("stS", 0), ("stS", 3)], writes=[("stS", 1)], tag=f"bv2_{t}")
            P.add("scalar", lambda e: e.activation(
                SR(8, 12), SR(4, 8), AF.Ln, bias=EPSF),
                reads=[("stS", 1)], writes=[("stS", 2)], tag=f"bl2_{t}")
            P.add("scalar", lambda e, tc0=tc0: e.activation(
                statsH2[:, tc0:tc0 + 4], SR(8, 12), AF.Exp, scale=-0.5),
                reads=[("stS", 2)],
                writes=[("stH", tc0 + c2) for c2 in range(4)], tag=f"bh2_{t}")
            for c in range(TCH):
                cc = t * TCH + c
                pk = c % 2
                xs = dch(xin, pk, D)
                P.dma("sync", lambda e, cc=cc, xs=xs: e.dma_start(
                    out=xs, in_=x_d[cc * CH:(cc + 1) * CH, :]),
                    writes=[("xin", pk)], tag=f"ldx2_{cc}")
                P.add("scalar", lambda e, xs=xs, pk=pk, cc=cc: e.activation(
                    dch(xntm, pk, D), xs, AF.Identity,
                    bias=statsMUR[:, cc:cc + 1], scale=statsR[:, cc:cc + 1]),
                    reads=[("xin", pk), ("stM", cc), ("stR", cc)],
                    writes=[("xntm", pk)], tag=f"xn_{cc}")
                P.add("vector", lambda e, pk=pk, cc=cc: e.scalar_tensor_tensor(
                    gscr[:], dch(xntm, pk, D), statsH2[:, cc:cc + 1], lngb[:],
                    ALU.mult, ALU.mult),
                    reads=[("xntm", pk), ("stH", cc), "lngb"], writes=["gscr"],
                    tag=f"g1_{cc}")
                P.add("gpsimd", lambda e: e.tensor_add(gscr2[:], gscr[:], lnbb[:]),
                      reads=["gscr", "lnbb"], writes=["gscr2"], tag=f"g2_{cc}")
                P.add("vector", lambda e: e.reduce_max(
                    sred[:, 0:1], gscr2[:], axis=AX, apply_absolute_value=True),
                    reads=["gscr2"], writes=[("sred", 0)], tag=f"g3_{cc}",
                    late=True)
                P.add("vector", lambda e: e.tensor_max(gacc[:], gacc[:], sred[:, 0:1]),
                      reads=[("sred", 0), "gacc"], writes=["gacc"], tag=f"g4_{cc}")
                for kc in range(NKC):
                    P.add("tensor", lambda e, kc=kc, pk=pk: e.transpose(
                        PT[:, pk * D + kc * CH:pk * D + (kc + 1) * CH],
                        xntm[:, pk * D + kc * CH:pk * D + (kc + 1) * CH], iden[:]),
                        reads=[("xntm", pk), "iden"], writes=[("PT", pk)],
                        tag=f"xT_{cc}_{kc}")
                P.add("vector", lambda e, c=c, pk=pk: e.tensor_copy(
                    xnfm[:].rearrange("p (k tt) -> p k tt", tt=TT)
                    [:, :, c * CH:(c + 1) * CH],
                    PT[:, pk * D:(pk + 1) * D].rearrange("p (k w) -> p k w", w=CH)),
                    reads=[("PT", pk)], writes=[("xnfm", c)], tag=f"xTe_{cc}")
            P.dma("gpsimd", lambda e, t=t: e.dma_start(
                out=h2row_s[:, t * TT:(t + 1) * TT].rearrange(
                    "o (c p) -> (o p) c", p=CH),
                in_=statsH2[:, t * TCH:(t + 1) * TCH],
                allow_slow_non_contiguous=True),
                reads=[("stH", t * TCH + c2) for c2 in range(TCH)],
                writes=["h2row"], tag=f"h2r_{t}")

            for fn in pend_kv:
                fn()
            pend_kv = []
            for mc in range(NKC):
                bank = mc % 2
                def mm_k(e, mc=mc, bank=bank):
                    last = None
                    for kc in range(NKC):
                        last = e.matmul(PF[bank][:],
                                        kT[:, kc * D + mc * CH:kc * D + (mc + 1) * CH],
                                        dch(xnfm, kc, TT),
                                        start=(kc == 0), stop=(kc == NKC - 1))
                    return last
                P.add("tensor", mm_k,
                      reads=[("kT", kc2) for kc2 in range(NKC)]
                      + [("xnfm", c2) for c2 in range(TCH)],
                      writes=[f"PF{bank}"], tag=f"k_{t}_{mc}")
                P.add("scalar", lambda e, mc=mc, bank=bank: e.activation(
                    dch(kfm, mc, TT), PF[bank][:], AF.Copy),
                    reads=[f"PF{bank}"], writes=[("kfm", mc)], tag=f"ke_{t}_{mc}")

            for c in range(TCH):
                for vg in range(2):
                    bank = 2 + vg
                    def mm_v(e, c=c, vg=vg, bank=bank):
                        last = None
                        for kc in range(NKC):
                            last = e.matmul(
                                PF[bank][:],
                                xnfm[:, kc * TT + c * CH:kc * TT + (c + 1) * CH],
                                vT[:, kc * D + vg * 512:kc * D + (vg + 1) * 512],
                                start=(kc == 0), stop=(kc == NKC - 1))
                        return last
                    P.add("tensor", mm_v,
                          reads=[("vT", kc2) for kc2 in range(NKC)] + [("xnfm", c)],
                          writes=[f"PF{bank}"], tag=f"v_{t}_{c}_{vg}")
                    def ev_v(e, c=c, vg=vg, bank=bank):
                        dst = vext[:].rearrange("p (cc h s) -> p cc h s", h=H, s=66)[
                            :, c, vg * 8:(vg + 1) * 8, 0:64]
                        return e.tensor_copy(dst, PF[bank][:].rearrange(
                            "p (h w) -> p h w", w=64))
                    P.add("vector", ev_v, reads=[f"PF{bank}"],
                          writes=[("vext", c, vg)], tag=f"ve_{t}_{c}_{vg}")
            for c in range(TCH):
                for pr in range(H // 2):
                    bank = (c * (H // 2) + pr) % 4
                    pkk = pr % 2
                    P.add("tensor", lambda e, c=c, pr=pr, bank=bank: e.matmul(
                        PF[bank][:],
                        kfm[:, pr * TT + c * CH:pr * TT + (c + 1) * CH],
                        dch(projBD, pr, 2 * FF),
                        start=True, stop=True),
                        reads=[("kfm", pr), ("projBD", pr)],
                        writes=[f"PF{bank}"], tag=f"kf_{t}_{c}_{pr}")
                    eb = dch(ebuf, pkk, 2 * FF)
                    tb = dch(tbuf, pkk, 2 * FF)
                    P.add("scalar", lambda e, bank=bank, eb=eb: e.activation(
                        eb, PF[bank][:], AF.Exp),
                        reads=[f"PF{bank}"], writes=[("eb", pkk)],
                        tag=f"kfE_{t}_{c}_{pr}")
                    P.add("vector", lambda e, bank=bank, tb=tb: e.tensor_scalar(
                        tb, PF[bank][:], 1.0, 1.0, ALU.add, ALU.max),
                        reads=[f"PF{bank}"], writes=[("tb", pkk)],
                        tag=f"kfT_{t}_{c}_{pr}")
                    P.add("vector", lambda e, c=c, pr=pr, eb=eb, tb=tb: e.tensor_tensor(
                        kftm[:, (c * H + pr * 2) * FF:(c * H + pr * 2 + 2) * FF],
                        eb, tb, ALU.min),
                        reads=[("eb", pkk), ("tb", pkk)],
                        writes=[("kftm", c, pr)], tag=f"kfM_{t}_{c}_{pr}")
            for pr in range(H // 2):
                bank = 4 + (pr % 2)
                def mm_kv(e, pr=pr, bank=bank):
                    last = None
                    for c in range(TCH):
                        for hh in range(2):
                            h0 = pr * 2 + hh
                            last = e.matmul(
                                PF[bank][0:65, hh * FF:(hh + 1) * FF],
                                vext[:, (c * H + h0) * 66:(c * H + h0) * 66 + 65],
                                kftm[:, (c * H + h0) * FF:(c * H + h0 + 1) * FF],
                                start=(c == 0 and hh == 0),
                                stop=(c == TCH - 1 and hh == 1))
                    return last
                P.add("tensor", mm_kv,
                      reads=[("vext", c2, vg) for c2 in range(TCH) for vg in range(2)]
                      + [("kftm", c2, pr) for c2 in range(TCH)],
                      writes=[f"PF{bank}"], tag=f"kv_{t}_{pr}")
                if t == 0:
                    P.add("vector", lambda e, pr=pr, bank=bank: e.tensor_copy(
                        kvacc[:, pr * 2 * FF:(pr * 2 + 2) * FF], PF[bank][0:65, :]),
                        reads=[f"PF{bank}"], writes=[("kva", pr)], tag=f"kve_{t}_{pr}")
                else:
                    P.add("vector", lambda e, pr=pr, bank=bank: e.tensor_add(
                        kvacc[:, pr * 2 * FF:(pr * 2 + 2) * FF],
                        kvacc[:, pr * 2 * FF:(pr * 2 + 2) * FF], PF[bank][0:65, :]),
                        reads=[f"PF{bank}", ("kva", pr)], writes=[("kva", pr)],
                        tag=f"kve_{t}_{pr}")

        for fn in pend_kv:
            fn()
        pend_kv = []
        for _ in _pg:
            pass
        def ptoff(s):
            return (s % 2) * 1024 + (s // 2) * 66
        P.add("vector", lambda e: e.tensor_copy(wst2[0:65, 0:H * FF], kvacc[:]),
              reads=[("kva", pr) for pr in range(H // 2)],
              writes=[("wst2", q) for q in range(4)], tag="kv_cast")
        for h0 in range(H):
            for fc in range(2):
                sl = h0 * 2 + fc
                hf = sl % 2
                P.add("tensor", lambda e, h0=h0, fc=fc, sl=sl: e.transpose(
                    PT[:, ptoff(sl % 16):ptoff(sl % 16) + 65],
                    wst2[0:65, h0 * FF + fc * CH:h0 * FF + (fc + 1) * CH],
                    iden[0:65, 0:65]),
                    reads=[("wst2", q) for q in range(4)] + ["iden"],
                    writes=[("PT", hf)], tag=f"kvT_{h0}_{fc}")
                P.add("vector", lambda e, sl=sl: e.tensor_copy(
                    kvl[:, sl * 66:sl * 66 + 65],
                    PT[:, ptoff(sl % 16):ptoff(sl % 16) + 65]),
                    reads=[("PT", hf)], writes=["kvl"], tag=f"kvle_{sl}")
        build_nc._stash = dict(sred=sred, wst=wst)
        P.emit(nc, sems)
      except _Skip:
        pass

    with contextlib.ExitStack() as p2:
      try:
        _guard(2)
        sb2 = lambda n, s, d_: p2.enter_context(nc.sbuf_tensor(n, s, d_))
        ps2 = lambda n, s, d_: p2.enter_context(nc.psum_tensor(n, s, d_))
        qT = sb2("qT", [CH, NKC * D], BF16)
        projQ = sb2("pjq", [CH, 2 * H * CH], BF16)
        woT = sb2("woT", [CH, NKC * D], BF16)
        wst = sb2("wsb", [CH, 2 * D], F32)
        xin = sb2("xnb", [CH, 2 * D], F32)
        xpin = xin
        xntm = sb2("xtb", [CH, 2 * D], BF16)
        xnfm = sb2("xfb", [CH, 2 * NKC * TT], BF16)
        qfm = sb2("qfm", [CH, NKC * TT], BF16)
        shA = qfm
        qf = sb2("qf", [CH, 2 * H * TT], BF16)
        ebf2 = sb2("eb2", [CH, 2 * TT], BF16)
        tbf2 = sb2("tb2", [CH, 2 * TT], BF16)
        attnr = sb2("atr", [CH, 2 * D], BF16)
        z16 = sb2("z16", [CH, 16], F32)
        zr16 = sb2("zrs", [CH, 16], F32)
        attnfm = sb2("atf", [CH, NKC * TT], BF16)
        x1b = sb2("x1b", [CH, NKC * TT], BF16)
        x1sq = shA
        x1n = sb2("x1m", [CH, NKC * TT], BF16)
        rows = sb2("rws", [1, 4 * TT], F32)
        bc1 = sb2("bc1", [CH, TT], F32)
        bc2 = sb2("bc2", [CH, TT], F32)
        xq1 = sb2("xq1", [CH, TT], F32)
        xq2 = sb2("xq2", [CH, TT], F32)
        xqb = sb2("xqb", [CH, NKC * TT], BF16)
        ybf = shA
        xpart = sb2("xpt", [CH, 2 * D], F32)
        h2r = sb2("h2r", [1, TT], F32)
        PT2 = ps2("PT2", [CH, 2 * D], BF16)
        PT2b = ps2("PT2b", [CH, D], BF16)
        PF = [ps2(f"QF{i}", [CH, 512], F32) for i in range(5)]
        P = Plan(state, drop,
                 psum_keys=[f"QF{i}" for i in range(5)]
                 + [("PT2", 0), ("PT2", 1), "PT2b"])

        def row(i):
            return rows[:, i * TT:(i + 1) * TT]

        for pq in range(4):
            q = pq % 2
            ssl = wst[:, q * D:(q + 1) * D]
            P.dma("scalar", lambda e, pq=pq, ssl=ssl: e.dma_start(
                out=ssl, in_=projQ_d[:, pq * D:(pq + 1) * D]),
                writes=[("wst", q)], tag=f"ldpq_{pq}")
            P.add("vector", lambda e, pq=pq, ssl=ssl: e.tensor_copy(
                projQ[:, pq * D:(pq + 1) * D], ssl),
                reads=[("wst", q)], writes=["projQ"], tag=f"cpq_{pq}")
        for kc in range(NKC):
            q = kc % 2
            ssl = wst[:, q * D:(q + 1) * D]
            P.dma("scalar", lambda e, kc=kc, ssl=ssl: e.dma_start(
                out=ssl, in_=wqkvT_d[kc * CH:(kc + 1) * CH, 0:D]),
                writes=[("wst", q)], tag=f"ldq_{kc}")
            P.add("vector", lambda e, kc=kc, ssl=ssl: e.tensor_copy(
                dch(qT, kc, D), ssl),
                reads=[("wst", q)], writes=[("qT", kc)], tag=f"cq_{kc}")
        for kc in range(NKC):
            q = kc % 2
            ssl = wst[:, q * D:(q + 1) * D]
            P.dma("scalar", lambda e, kc=kc, ssl=ssl: e.dma_start(
                out=ssl, in_=woutT_d[kc * CH:(kc + 1) * CH, :]),
                writes=[("wst", q)], tag=f"ldo_{kc}")
            P.add("vector", lambda e, kc=kc, ssl=ssl: e.tensor_copy(
                dch(woT, kc, D), ssl),
                reads=[("wst", q)], writes=[("woT", kc)], tag=f"co_{kc}")

        P.dma("gpsimd", lambda e: e.dma_start(out=srow[:], in_=gacc[:], allow_slow_non_contiguous=True),
              reads=["gacc"], writes=["srow"], tag="fold_g")
        P.add("vector", lambda e: e.reduce_max(scal[:, 3:4], srow[:], axis=AX),
              reads=["srow"], writes=[("scal", 3)], tag="gloc", late=True)
        P.dma("gpsimd", lambda e: e.dma_start(out=cc_in, in_=scal[:, 3:4]),
              reads=[("scal", 3)], writes=["cc_in"], tag="st_cc")
        P.add("gpsimd", lambda e: e.collective_compute(
            "AllReduce", ALU.max, replica_groups=[list(range(NC))],
            ins=[cc_in], outs=[cc_out]),
            reads=["cc_in"], writes=["cc_out"], kind="cc", tag="ar_gamma")
        P.dma("gpsimd", lambda e: e.dma_start(out=scal[:, 4:5], in_=cc_out),
              reads=["cc_out"], writes=[("scal", 4)], tag="ld_cc")
        P.add("vector", lambda e: e.reciprocal(scal[:, 5:6], scal[:, 4:5]),
              reads=[("scal", 4)], writes=[("scal", 5)], tag="rg0", late=True)
        P.add("vector", lambda e: e.tensor_scalar_mul(scal[:, 6:7], scal[:, 5:6], QB),
              reads=[("scal", 5)], writes=[("scal", 6)], tag="rgam")
        P.add("vector", lambda e: e.tensor_mul(scal[:, 7:8], scal[:, 4:5], scal[:, 0:1]),
              reads=[("scal", 4), ("scal", 0)], writes=[("scal", 7)], tag="gb")
        P.add("vector", lambda e: e.tensor_scalar_mul(
            scal[:, 8:9], scal[:, 7:8], -1.0 / (QB * D * D)),
            reads=[("scal", 7)], writes=[("scal", 8)], tag="nys")
        P.add("tensor", lambda e: e.matmul(PF[0][:, 0:4], onesr[:], scal[:, 5:9],
                                           start=True, stop=True),
              reads=[("scal", 6), ("scal", 8), "onesr"], writes=["QF0"], tag="bc_g")
        P.add("vector", lambda e: e.tensor_copy(gsc[:, 0:1], PF[0][:, 1:2]),
              reads=["QF0"], writes=[("gsc", 0)], tag="ev_rg")
        P.add("vector", lambda e: e.tensor_copy(gsc[:, 1:2], PF[0][:, 3:4]),
              reads=["QF0"], writes=[("gsc", 1)], tag="ev_ny")
        P.add("vector", lambda e: e.tensor_single_scalar(
            grow[:], onesr[:], scal[:, 6:7], ALU.mult),
            reads=[("scal", 6), "onesr"], writes=["grow"], tag="grow")
        P.add("vector", lambda e: e.tensor_copy(lngyc[:], LNG),
              reads=["vecs"], writes=["lngyc"], tag="lngyc")
        P.add("vector", lambda e: e.tensor_single_scalar(
            lnbyc[:], LNB, gsc[:, 0:1], ALU.mult),
            reads=["vecs", ("gsc", 0)], writes=["lnbyc"], tag="lnbyc")

        def ln1_tail(t):
                def mm_s1(e):
                    last = None
                    for kc in range(NKC):
                        last = e.matmul(PF[4][0:1, :], onesc[:], dch(x1b, kc, TT),
                                        start=(kc == 0), stop=(kc == NKC - 1))
                    return last
                P.add("tensor", mm_s1,
                      reads=[("x1b", kc2) for kc2 in range(NKC)],
                      writes=["QF4"], tag=f"s1_{t}")
                P.add("vector", lambda e: e.tensor_scalar_mul(
                    row(0), PF[4][0:1, :], 1.0 / D),
                    reads=["QF4"], writes=[("rw", 0)], tag=f"s1e_{t}")
                def mm_s2(e):
                    last = None
                    for kc in range(NKC):
                        last = e.matmul(PF[4][0:1, :], onesc[:], dch(x1sq, kc, TT),
                                        start=(kc == 0), stop=(kc == NKC - 1))
                    return last
                P.add("tensor", mm_s2,
                      reads=[("shA", kc2) for kc2 in range(NKC)],
                      writes=["QF4"], tag=f"s2_{t}")
                P.add("vector", lambda e: e.tensor_scalar_mul(
                    row(1), PF[4][0:1, :], 1.0 / D),
                    reads=["QF4"], writes=[("rw", 1)], tag=f"s2e_{t}")
                P.add("vector", lambda e: e.tensor_mul(row(3), row(0), row(0)),
                      reads=[("rw", 0)], writes=[("rw", 3)], tag=f"v2_{t}")
                P.add("vector", lambda e: e.tensor_sub(row(2), row(1), row(3)),
                      reads=[("rw", 1), ("rw", 3)], writes=[("rw", 2)], tag=f"v3_{t}")
                P.add("scalar", lambda e: e.activation(
                    row(3), row(2), AF.Ln, bias=EPS1r),
                    reads=[("rw", 2)], writes=[("rw", 3)], tag=f"v4_{t}")
                P.add("scalar", lambda e: e.activation(
                    row(2), row(3), AF.Exp, scale=-0.5),
                    reads=[("rw", 3)], writes=[("rw", 2)], tag=f"r1_{t}")
                P.add("vector", lambda e: e.scalar_tensor_tensor(
                    row(3), row(0), -1.0, row(2), ALU.mult, ALU.mult),
                    reads=[("rw", 0), ("rw", 2)], writes=[("rw", 3)], tag=f"q1_{t}")
                P.add("tensor", lambda e: e.matmul(PF[4][:], onesr[:], row(2),
                                                   start=True, stop=True),
                      reads=[("rw", 2)], writes=["QF4"], tag=f"b1_{t}")
                P.add("vector", lambda e: e.tensor_copy(bc1[:], PF[4][:]),
                      reads=["QF4"], writes=["bc1"], tag=f"b1e_{t}")
                P.add("tensor", lambda e: e.matmul(PF[4][:], onesr[:], row(3),
                                                   start=True, stop=True),
                      reads=[("rw", 3)], writes=["QF4"], tag=f"b2_{t}")
                P.add("vector", lambda e: e.tensor_copy(bc2[:], PF[4][:]),
                      reads=["QF4"], writes=["bc2"], tag=f"b2e_{t}")
                for mc in range(NKC):
                    P.add("vector", lambda e, mc=mc: e.tensor_mul(
                        xq2[:], dch(x1b, mc, TT), bc1[:]),
                        reads=[("x1b", mc), "bc1"], writes=["xq2"], tag=f"n1_{t}_{mc}")
                    P.add("vector", lambda e: e.tensor_add(xq2[:], xq2[:], bc2[:]),
                          reads=["xq2", "bc2"], writes=["xq2"], tag=f"n2_{t}_{mc}")
                    P.add("vector", lambda e, mc=mc: e.tensor_scalar(
                        dch(x1n, mc, TT), xq2[:], GAM[:, mc:mc + 1], BET[:, mc:mc + 1],
                        ALU.mult, ALU.add),
                        reads=["xq2"], writes=[("x1n", mc)], tag=f"n3_{t}_{mc}")
                    P.dma("gpsimd", lambda e, mc=mc, t=t: e.dma_start(
                        out=x1n_s[mc * CH:(mc + 1) * CH, t * TT:(t + 1) * TT],
                        in_=dch(x1n, mc, TT)),
                        reads=[("x1n", mc)], writes=[("x1ns", t, mc)], tag=f"nS_{t}_{mc}")

        for t in range(NTL):
            xo = (t % 2) * NKC * TT
            xnk = t % 2

            def xch(kc, xo=None, t=t):
                o = (t % 2) * NKC * TT
                return xnfm[:, o + kc * TT:o + (kc + 1) * TT]
            for c in range(TCH):
                cc = t * TCH + c
                pk = c % 2
                xs = dch(xin, pk, D)
                P.dma("sync", lambda e, cc=cc, xs=xs: e.dma_start(
                    out=xs, in_=x_d[cc * CH:(cc + 1) * CH, :]),
                    writes=[("xin", pk)], tag=f"p2x_{cc}")
                P.add("scalar", lambda e, xs=xs, pk=pk, cc=cc: e.activation(
                    dch(xntm, pk, D), xs, AF.Identity,
                    bias=statsMUR[:, cc:cc + 1], scale=statsR[:, cc:cc + 1]),
                    reads=[("xin", pk)], writes=[("xntm", pk)], tag=f"p2n_{cc}")
                for kc in range(NKC):
                    P.add("tensor", lambda e, kc=kc, pk=pk: e.transpose(
                        PT2b[:, kc * CH:(kc + 1) * CH],
                        xntm[:, pk * D + kc * CH:pk * D + (kc + 1) * CH], iden[:]),
                        reads=[("xntm", pk)], writes=["PT2b"],
                        tag=f"p2T_{cc}_{kc}")
                P.add("vector", lambda e, c=c, pk=pk, xo=xo: e.tensor_copy(
                    xnfm[:, xo:xo + NKC * TT].rearrange("p (k tt) -> p k tt", tt=TT)
                    [:, :, c * CH:(c + 1) * CH],
                    PT2b[:].rearrange("p (k w) -> p k w", w=CH)),
                    reads=["PT2b"], writes=[("xnfm", xnk, c)], tag=f"p2e_{cc}")
            if t > 0:
                ln1_tail(t - 1)
            P.dma("sync", lambda e, t=t: e.dma_start(
                out=h2r[:], in_=h2row_s[:, t * TT:(t + 1) * TT]),
                writes=["h2r"], tag=f"fh_{t}")
            P.add("tensor", lambda e: e.matmul(PF[4][:], grow[:], h2r[:],
                                               start=True, stop=True),
                  reads=["h2r"], writes=["QF4"], tag=f"fb_{t}")
            for mc in range(NKC):
                P.add("vector", lambda e, mc=mc, xch=xch: e.scalar_tensor_tensor(
                    xq1[:], xch(mc), lngyc[:, mc:mc + 1], PF[4][:],
                    ALU.mult, ALU.mult),
                    reads=[("xnfm", xnk, c2) for c2 in range(TCH)] + ["QF4"],
                    writes=["xq1"], tag=f"f1_{t}_{mc}")
                P.add("vector", lambda e, mc=mc: e.tensor_scalar(
                    xq1[:], xq1[:], lnbyc[:, mc:mc + 1], CLIP, ALU.add, ALU.min),
                    reads=["xq1"], writes=["xq1"], tag=f"f2_{t}_{mc}")
                P.add("vector", lambda e, mc=mc: e.tensor_scalar_max(
                    dch(xqb, mc, TT), xq1[:], -CLIP),
                    reads=["xq1"], writes=[("xqb", mc)], tag=f"f3_{t}_{mc}")
            for mc in range(NKC):
                bank = mc % 2
                def mm_q(e, mc=mc, bank=bank, xch=xch):
                    last = None
                    for kc in range(NKC):
                        last = e.matmul(PF[bank][:],
                                        qT[:, kc * D + mc * CH:kc * D + (mc + 1) * CH],
                                        xch(kc),
                                        start=(kc == 0), stop=(kc == NKC - 1))
                    return last
                P.add("tensor", mm_q,
                      reads=[("qT", kc2) for kc2 in range(NKC)]
                      + [("xnfm", xnk, c2) for c2 in range(TCH)],
                      writes=[f"QF{bank}"], tag=f"q_{t}_{mc}")
                P.add("scalar", lambda e, mc=mc, bank=bank: e.activation(
                    dch(qfm, mc, TT), PF[bank][:], AF.Copy),
                    reads=[f"QF{bank}"], writes=[("shA", mc)], tag=f"qe_{t}_{mc}")
            for fc in range(2):
                for h0 in range(H):
                    pr, hh = h0 // 2, h0 % 2
                    base = hh * 64
                    sl = h0 * 2 + fc
                    bank = (fc * H + h0) % 4
                    P.add("tensor",
                          lambda e, pr=pr, sl=sl, bank=bank: e.matmul(
                              PF[bank][:],
                              dch(projQ, sl, CH),
                              dch(qfm, pr, TT),
                              start=True, stop=True),
                          reads=[("shA", pr), "projQ"], writes=[f"QF{bank}"],
                          tag=f"qf_{t}_{h0}_{fc}")
                    pe2 = sl % 2
                    P.add("scalar", lambda e, bank=bank, pe2=pe2: e.activation(
                        dch(ebf2, pe2, TT), PF[bank][:], AF.Exp),
                        reads=[f"QF{bank}"], writes=[("ebf2", pe2)],
                        tag=f"qfE_{t}_{h0}_{fc}")
                    P.add("vector", lambda e, bank=bank, pe2=pe2: e.tensor_scalar(
                        dch(tbf2, pe2, TT), PF[bank][:], 1.0, 1.0, ALU.add, ALU.max),
                        reads=[f"QF{bank}"], writes=[("tbf2", pe2)],
                        tag=f"qfT_{t}_{h0}_{fc}")
                    P.add("vector", lambda e, sl=sl, pe2=pe2: e.tensor_tensor(
                        qf[:, sl * TT:(sl + 1) * TT], dch(ebf2, pe2, TT),
                        dch(tbf2, pe2, TT), ALU.min),
                        reads=[("ebf2", pe2), ("tbf2", pe2)], writes=[("qf", sl)],
                        tag=f"qfM_{t}_{h0}_{fc}")
            for c in range(TCH):
                pk = c % 2
                for bl in range(4):
                    bank = 2 + bl % 2
                    def mm_a(e, c=c, bl=bl, bank=bank):
                        last = None
                        for s in range(4):
                            h0 = bl * 4 + s
                            for fc in range(2):
                                sl = h0 * 2 + fc
                                last = e.matmul(
                                    PF[bank][:, s * CH:s * CH + 65],
                                    qf[:, sl * TT + c * CH:sl * TT + (c + 1) * CH],
                                    kvl[:, sl * 66:sl * 66 + 65],
                                    start=(fc == 0), stop=(fc == 1))
                        return last
                    P.add("tensor", mm_a,
                          reads=[("qf", (bl * 4 + s) * 2 + fc)
                                 for s in range(4) for fc in range(2)],
                          writes=[f"QF{bank}"], tag=f"a_{t}_{c}_{bl}")
                    P.add("scalar", lambda e, bl=bl, bank=bank, pk=pk: e.activation(
                        attnr[:, pk * D + bl * 4 * 64:pk * D + (bl + 1) * 4 * 64]
                        .rearrange("p (s w) -> p s w", w=64),
                        PF[bank][:].rearrange("p (s w) -> p s w", w=CH)[:, :, 0:64],
                        AF.Copy),
                        reads=[f"QF{bank}"], writes=[("atr", pk)],
                        tag=f"ae_{t}_{c}_{bl}")
                    P.add("vector", lambda e, bl=bl, bank=bank: e.tensor_scalar_add(
                        z16[:, bl * 4:(bl + 1) * 4].rearrange(
                            "p (s o) -> p s o", o=1),
                        PF[bank][:].rearrange("p (s w) -> p s w", w=CH)[:, :, 64:65],
                        EPS_ATTN),
                        reads=[f"QF{bank}"], writes=[("z16", bl)],
                        tag=f"az_{t}_{c}_{bl}")
                P.add("vector", lambda e: e.reciprocal(zr16[:], z16[:]),
                      reads=[("z16", bl2) for bl2 in range(4)], writes=["zr16"],
                      tag=f"ar_{t}_{c}", late=True)
                P.add("vector", lambda e, pk=pk: e.tensor_tensor(
                    dch(attnr, pk, D).rearrange("p (h w) -> p h w", w=64),
                    dch(attnr, pk, D).rearrange("p (h w) -> p h w", w=64),
                    zr16[:].rearrange("p (h o) -> p h o", o=1).broadcast_to(
                        (CH, H, 64)), ALU.mult),
                    reads=["zr16", ("atr", pk)], writes=[("atr", pk)],
                    tag=f"as_{t}_{c}")
                for kc in range(NKC):
                    P.add("tensor", lambda e, kc=kc, pk=pk: e.transpose(
                        PT2[:, pk * D + kc * CH:pk * D + (kc + 1) * CH],
                        attnr[:, pk * D + kc * CH:pk * D + (kc + 1) * CH], iden[:]),
                        reads=[("atr", pk)], writes=[("PT2", pk)],
                        tag=f"aT_{t}_{c}_{kc}")
                P.add("vector", lambda e, c=c, pk=pk: e.tensor_copy(
                    attnfm[:].rearrange("p (k tt) -> p k tt", tt=TT)
                    [:, :, c * CH:(c + 1) * CH],
                    PT2[:, pk * D:(pk + 1) * D].rearrange("p (k w) -> p k w", w=CH)),
                    reads=[("PT2", pk)], writes=[("atf", c)], tag=f"aTe_{t}_{c}")
            for mc in range(NKC):
                bank = mc % 2
                def mm_y(e, mc=mc, bank=bank):
                    last = None
                    for kc in range(NKC):
                        last = e.matmul(PF[bank][:],
                                        mwq[:, kc * D + mc * CH:kc * D + (mc + 1) * CH],
                                        dch(xqb, kc, TT),
                                        start=(kc == 0), stop=(kc == NKC - 1))
                    return last
                P.add("tensor", mm_y,
                      reads=[("xqb", kc2) for kc2 in range(NKC)],
                      writes=[f"QF{bank}"], tag=f"y_{t}_{mc}")
                P.add("scalar", lambda e, mc=mc, bank=bank: e.activation(
                    dch(ybf, mc, TT), PF[bank][:], AF.Copy),
                    reads=[f"QF{bank}"], writes=[("shA", mc)], tag=f"ye_{t}_{mc}")
            for c in range(TCH):
                cc = t * TCH + c
                pk = c % 2
                for kc in range(NKC):
                    P.add("tensor", lambda e, kc=kc, pk=pk, c=c: e.transpose(
                        PT2[:, pk * D + kc * CH:pk * D + (kc + 1) * CH],
                        ybf[:, kc * TT + c * CH:kc * TT + (c + 1) * CH], iden[:]),
                        reads=[("shA", kc)], writes=[("PT2", pk)],
                        tag=f"yT_{t}_{c}_{kc}")
                xs = dch(xpin, pk, D)
                P.dma("sync", lambda e, cc=cc, xs=xs: e.dma_start(
                    out=xs, in_=x_d[cc * CH:(cc + 1) * CH, :]),
                    writes=[("xin", pk)], tag=f"px_{cc}")
                P.add("vector", lambda e, pk=pk: e.scalar_tensor_tensor(
                    dch(xpart, pk, D), PT2[:, pk * D:(pk + 1) * D], gsc[:, 1:2],
                    dch(xpin, pk, D), ALU.mult, ALU.add),
                    reads=[("PT2", pk), ("xin", pk)],
                    writes=[("xpart", pk)], tag=f"pp_{cc}")
                P.dma("scalar", lambda e, cc=cc, pk=pk: e.dma_start(
                    out=part_s[cc * CH:(cc + 1) * CH, :], in_=dch(xpart, pk, D)),
                    reads=[("xpart", pk)], writes=[("parts", cc)], tag=f"ps_{cc}")
            for mc in range(NKC):
                bank = mc % 2
                def mm_o(e, mc=mc, bank=bank):
                    last = None
                    for kc in range(NKC):
                        last = e.matmul(PF[bank][:],
                                        woT[:, kc * D + mc * CH:kc * D + (mc + 1) * CH],
                                        dch(attnfm, kc, TT),
                                        start=(kc == 0), stop=(kc == NKC - 1))
                    return last
                P.add("tensor", mm_o,
                      reads=[("woT", kc2) for kc2 in range(NKC)]
                      + [("atf", c2) for c2 in range(TCH)],
                      writes=[f"QF{bank}"], tag=f"o_{t}_{mc}")
                P.add("vector", lambda e, mc=mc, bank=bank, xch=xch: e.scalar_tensor_tensor(
                    dch(x1b, mc, TT), PF[bank][:], B_OUT[:, mc:mc + 1],
                    xch(mc), ALU.add, ALU.add),
                    reads=[f"QF{bank}"] + [("xnfm", xnk, c2) for c2 in range(TCH)],
                    writes=[("x1b", mc)], tag=f"x1_{t}_{mc}")
                P.add("scalar", lambda e, mc=mc: e.activation(
                    dch(x1sq, mc, TT), dch(x1b, mc, TT), AF.Square),
                    reads=[("x1b", mc)], writes=[("shA", mc)], tag=f"x1q_{t}_{mc}")
        ln1_tail(NTL - 1)
        P.emit(nc, sems)
      except _Skip:
        pass

    with contextlib.ExitStack() as p3:
      try:
        _guard(3)
        sb3 = lambda n, s, d_: p3.enter_context(nc.sbuf_tensor(n, s, d_))
        ps3 = lambda n, s, d_: p3.enter_context(nc.psum_tensor(n, s, d_))
        w2r = sb3("w2r", [CH, NMF * D], BF16)
        w1b = sb3("w1b", [CH, 2 * D], BF16)
        x1nb = sb3("x1v", [CH, 2 * NKC * TT], BF16)
        hbuf = sb3("hbf", [CH, NMF * TT], BF16)
        x2b = sb3("x2b", [CH, NKC * TT], BF16)
        x2sq = sb3("x2q", [CH, NKC * TT], BF16)
        aob = sb3("aob", [CH, NKC * TT], BF16)
        rows = sb3("rw3", [1, 4 * TT], F32)
        bc1 = sb3("bc3", [CH, TT], F32)
        bc2 = sb3("bc4", [CH, TT], F32)
        tsc = sb3("tsc", [CH, TT], F32)
        xpart = sb3("xp3", [CH, 2 * D], F32)
        outb = sb3("oub", [CH, 2 * D], F32)
        PT3 = ps3("PT3", [CH, 2 * D], BF16)
        PF = [ps3(f"RF{i}", [CH, 512], F32) for i in range(5)]
        P = Plan(state, drop,
                 psum_keys=[f"RF{i}" for i in range(5)] + [("PT3", 0), ("PT3", 1)])

        def row(i):
            return rows[:, i * TT:(i + 1) * TT]

        for kc in range(NMF):
            P.dma("sync", lambda e, kc=kc: e.dma_start(
                out=dch(w2r, kc, D), in_=w2s_s[kc * CH:(kc + 1) * CH, :]),
                writes=[("w2r", kc)], tag=f"lw2_{kc}")

        def ln2_tail(t):
                def mm_t1(e):
                    last = None
                    for kc in range(NKC):
                        last = e.matmul(PF[4][0:1, :], onesc[:], dch(x2b, kc, TT),
                                        start=(kc == 0), stop=(kc == NKC - 1))
                    return last
                P.add("tensor", mm_t1, reads=[("x2b", kc2) for kc2 in range(NKC)],
                      writes=["RF4"], tag=f"t1_{t}")
                P.add("vector", lambda e: e.tensor_scalar_mul(
                    row(0), PF[4][0:1, :], 1.0 / D),
                    reads=["RF4"], writes=[("rw", 0)], tag=f"t1e_{t}")
                def mm_t2(e):
                    last = None
                    for kc in range(NKC):
                        last = e.matmul(PF[4][0:1, :], onesc[:], dch(x2sq, kc, TT),
                                        start=(kc == 0), stop=(kc == NKC - 1))
                    return last
                P.add("tensor", mm_t2, reads=[("x2sq", kc2) for kc2 in range(NKC)],
                      writes=["RF4"], tag=f"t2_{t}")
                P.add("vector", lambda e: e.tensor_scalar_mul(
                    row(1), PF[4][0:1, :], 1.0 / D),
                    reads=["RF4"], writes=[("rw", 1)], tag=f"t2e_{t}")
                P.add("vector", lambda e: e.tensor_mul(row(3), row(0), row(0)),
                      reads=[("rw", 0)], writes=[("rw", 3)], tag=f"u2_{t}")
                P.add("vector", lambda e: e.tensor_sub(row(2), row(1), row(3)),
                      reads=[("rw", 1), ("rw", 3)], writes=[("rw", 2)], tag=f"u3_{t}")
                P.add("scalar", lambda e: e.activation(
                    row(3), row(2), AF.Ln, bias=EPS1r),
                    reads=[("rw", 2)], writes=[("rw", 3)], tag=f"u4_{t}")
                P.add("scalar", lambda e: e.activation(
                    row(2), row(3), AF.Exp, scale=-0.5),
                    reads=[("rw", 3)], writes=[("rw", 2)], tag=f"r2_{t}")
                P.add("vector", lambda e: e.scalar_tensor_tensor(
                    row(3), row(0), -1.0, row(2), ALU.mult, ALU.mult),
                    reads=[("rw", 0), ("rw", 2)], writes=[("rw", 3)], tag=f"q2_{t}")
                P.add("tensor", lambda e: e.matmul(PF[4][:], onesr[:], row(2),
                                                   start=True, stop=True),
                      reads=[("rw", 2)], writes=["RF4"], tag=f"w1b_{t}")
                P.add("vector", lambda e: e.tensor_copy(bc1[:], PF[4][:]),
                      reads=["RF4"], writes=["bc1"], tag=f"w1e_{t}")
                P.add("tensor", lambda e: e.matmul(PF[4][:], onesr[:], row(3),
                                                   start=True, stop=True),
                      reads=[("rw", 3)], writes=["RF4"], tag=f"w2b_{t}")
                P.add("vector", lambda e: e.tensor_copy(bc2[:], PF[4][:]),
                      reads=["RF4"], writes=["bc2"], tag=f"w2e_{t}")
                for mc in range(NKC):
                    P.add("vector", lambda e, mc=mc: e.tensor_mul(
                        tsc[:], dch(x2b, mc, TT), bc1[:]),
                        reads=[("x2b", mc), "bc1"], writes=["tsc"], tag=f"m1_{t}_{mc}")
                    P.add("vector", lambda e: e.tensor_add(tsc[:], tsc[:], bc2[:]),
                          reads=["tsc", "bc2"], writes=["tsc"], tag=f"m2_{t}_{mc}")
                    P.add("vector", lambda e, mc=mc: e.tensor_scalar(
                        dch(aob, mc, TT), tsc[:], GAM[:, mc:mc + 1], BET[:, mc:mc + 1],
                        ALU.mult, ALU.add),
                        reads=["tsc"], writes=[("aob", mc)], tag=f"m3_{t}_{mc}")
                for c in range(TCH):
                    cc = t * TCH + c
                    pk = c % 2
                    for kc in range(NKC):
                        P.add("tensor", lambda e, kc=kc, pk=pk, c=c: e.transpose(
                            PT3[:, pk * D + kc * CH:pk * D + (kc + 1) * CH],
                            aob[:, kc * TT + c * CH:kc * TT + (c + 1) * CH], iden[:]),
                            reads=[("aob", kc)], writes=[("PT3", pk)],
                            tag=f"oT_{t}_{c}_{kc}")
                    xs = dch(xpart, pk, D)
                    P.dma("gpsimd", lambda e, cc=cc, xs=xs: e.dma_start(
                        out=xs, in_=part_s[cc * CH:(cc + 1) * CH, :]),
                        writes=[("xp3", pk)], tag=f"lp_{cc}")
                    P.add("vector", lambda e, pk=pk: e.tensor_add(
                        dch(outb, pk, D), PT3[:, pk * D:(pk + 1) * D], dch(xpart, pk, D)),
                        reads=[("PT3", pk), ("xp3", pk)], writes=[("oub", pk)],
                        tag=f"os_{cc}")
                    P.dma("gpsimd", lambda e, cc=cc, pk=pk: e.dma_start(
                        out=out_d[cc * CH:(cc + 1) * CH, :], in_=dch(outb, pk, D)),
                        reads=[("oub", pk)], writes=[("outd", cc)], tag=f"ow_{cc}")
        for t in range(NTL):
            tb = t % 2
            for mc in range(NKC):
                P.dma("sync", lambda e, t=t, mc=mc, tb=tb: e.dma_start(
                    out=x1nb[:, (tb * NKC + mc) * TT:(tb * NKC + mc + 1) * TT],
                    in_=x1n_s[mc * CH:(mc + 1) * CH, t * TT:(t + 1) * TT]),
                    writes=[("x1v", tb, mc)], tag=f"lx1_{t}_{mc}")
            if t > 0:
                ln2_tail(t - 1)
            for m in range(NMF):
                wb = m % 2
                bank = m % 2
                P.dma("sync", lambda e, m=m, wb=wb: e.dma_start(
                    out=dch(w1b, wb, D), in_=w1s_s[m]),
                    writes=[("w1b", wb)], tag=f"lw1_{t}_{m}")
                def mm_h(e, m=m, wb=wb, bank=bank, tb=tb):
                    last = None
                    for kc in range(NKC):
                        last = e.matmul(
                            PF[bank][:],
                            w1b[:, wb * D + kc * CH:wb * D + (kc + 1) * CH],
                            x1nb[:, (tb * NKC + kc) * TT:(tb * NKC + kc + 1) * TT],
                            start=(kc == 0), stop=(kc == NKC - 1))
                    return last
                P.add("tensor", mm_h,
                      reads=[("w1b", wb)] + [("x1v", tb, kc2) for kc2 in range(NKC)],
                      writes=[f"RF{bank}"], tag=f"h_{t}_{m}")
                P.add("scalar", lambda e, m=m, bank=bank: e.activation(
                    dch(hbuf, m, TT), PF[bank][:], AF.Gelu, bias=b1c[:, m:m + 1]),
                    reads=[f"RF{bank}"], writes=[("h", m)], tag=f"he_{t}_{m}")
            for mc in range(NKC):
                bank = 2 + mc % 2
                def mm_x2(e, mc=mc, bank=bank):
                    last = None
                    for kc in range(NMF):
                        last = e.matmul(
                            PF[bank][:],
                            w2r[:, kc * D + mc * CH:kc * D + (mc + 1) * CH],
                            dch(hbuf, kc, TT),
                            start=(kc == 0), stop=(kc == NMF - 1))
                    return last
                P.add("tensor", mm_x2,
                      reads=[("w2r", kc2) for kc2 in range(NMF)]
                      + [("h", kc2) for kc2 in range(NMF)],
                      writes=[f"RF{bank}"], tag=f"x2_{t}_{mc}")
                P.add("vector",
                      lambda e, mc=mc, bank=bank, tb=tb: e.scalar_tensor_tensor(
                          dch(x2b, mc, TT), PF[bank][:], B2[:, mc:mc + 1],
                          x1nb[:, (tb * NKC + mc) * TT:(tb * NKC + mc + 1) * TT],
                          ALU.add, ALU.add),
                      reads=[f"RF{bank}", ("x1v", tb, mc)],
                      writes=[("x2b", mc)], tag=f"x2e_{t}_{mc}")
                P.add("scalar", lambda e, mc=mc: e.activation(
                    dch(x2sq, mc, TT), dch(x2b, mc, TT), AF.Square),
                    reads=[("x2b", mc)], writes=[("x2sq", mc)], tag=f"x2q_{t}_{mc}")
        ln2_tail(NTL - 1)
        P.emit(nc, sems)
      except _Skip:
        pass

    if debug:
        P = Plan(state)
        P.dma("sync", lambda e: e.dma_start(out=dbg["scal"], in_=scal[:]), tag="dbg_scal")
        P.dma("sync", lambda e: e.dma_start(
            out=dbg["gsc"], in_=build_nc._stash["sred"][:]), tag="dbg_gsc")
        P.dma("sync", lambda e: e.dma_start(out=dbg["kvl"], in_=kvl[:]), tag="dbg_kvl")
        P.dma("sync", lambda e: e.dma_start(out=dbg["x1n"], in_=x1n_s), tag="dbg_x1n")
        P.dma("sync", lambda e: e.dma_start(out=dbg["part"], in_=part_s), tag="dbg_part")
        P.dma("sync", lambda e: e.dma_start(out=dbg["h2row"], in_=h2row_s), tag="dbg_h2row")
        P.dma("sync", lambda e: e.dma_start(out=dbg["stH"], in_=statsH2[:]), tag="dbg_stH")
        P.dma("sync", lambda e: e.dma_start(out=dbg["stR"], in_=statsR[:]), tag="dbg_stR")
        P.dma("sync", lambda e: e.dma_start(out=dbg["stM"], in_=statsMUR[:]), tag="dbg_stM")
        P.dma("sync", lambda e: e.dma_start(out=dbg["mwq"], in_=mwq[:]), tag="dbg_mwq")
        P.emit(nc, sems)

    es.close()
    return nc


def _shard_inputs(L0_attn_x, L0_attn_proj, L0_attn_w_qkv, L0_attn_w_out,
                  L0_attn_b_out, L0_attn_gamma, L0_attn_beta, L0_attn_w1,
                  L0_attn_b1, L0_attn_w2, L0_attn_b2, L0_ffn_weight,
                  L0_ffn_ln_gamma, L0_ffn_ln_beta, NT):
    import ml_dtypes
    f32 = lambda a: np.ascontiguousarray(np.asarray(a, dtype=np.float32))
    x = f32(L0_attn_x)
    wqkvT = np.ascontiguousarray(f32(L0_attn_w_qkv).T)
    pj = f32(L0_attn_proj)
    projP = np.zeros((H // 2, CH, 2 * FF), np.float32)
    for pr in range(H // 2):
        projP[pr, 0:64, 0:FF] = pj[2 * pr]
        projP[pr, 64:128, FF:2 * FF] = pj[2 * pr + 1]
    projQ = np.zeros((CH, 2 * H * CH), np.float32)
    for h0 in range(H):
        for fc in range(2):
            sl = h0 * 2 + fc
            projQ[(h0 % 2) * 64:(h0 % 2) * 64 + 64, sl * CH:(sl + 1) * CH] = \
                pj[h0][:, fc * CH:(fc + 1) * CH]
    woutT = np.ascontiguousarray(f32(L0_attn_w_out).T)
    w1T = np.ascontiguousarray(f32(L0_attn_w1).T)
    w2T = np.ascontiguousarray(f32(L0_attn_w2).T)
    fw = f32(L0_ffn_weight)
    vecs = np.zeros((CH, 64), np.float32)
    for i, v in enumerate((L0_attn_b_out, L0_attn_gamma, L0_attn_beta,
                           L0_attn_b2, L0_ffn_ln_gamma, L0_ffn_ln_beta)):
        vecs[:, i * 8:(i + 1) * 8] = f32(v).reshape(NKC, CH).T
    b1c = np.ascontiguousarray(f32(L0_attn_b1).reshape(NMF, CH).T)
    vecs[:, 48] = EPS_LN
    vecs[:, 49] = LN_EPS_FFN
    lngbf = np.ascontiguousarray(np.broadcast_to(f32(L0_ffn_ln_gamma), (CH, D)))
    lnbbf = np.ascontiguousarray(np.broadcast_to(f32(L0_ffn_ln_beta), (CH, D)))
    iden = np.eye(CH, dtype=ml_dtypes.bfloat16)
    onesr = np.ones((1, CH), np.float32)
    onescb = np.ones((CH, 1), ml_dtypes.bfloat16)
    shared = dict(wqkvT=wqkvT, projP=projP, projQ=projQ, woutT=woutT, w1T=w1T, w2T=w2T,
                  fw=fw, vecs=vecs, b1c=b1c, lngbf=lngbf, lnbbf=lnbbf,
                  iden=iden, onesr=onesr, onescb=onescb)
    return [dict(shared, x=np.ascontiguousarray(x[i, :NT])) for i in range(NC)]


def kernel(**inputs):
    NT = 8192
    in_maps = _shard_inputs(NT=NT, **inputs)
    nc = build_nc(NT=NT)
    res = run_bass_kernel_spmd(nc, in_maps, core_ids=list(range(NC)))
    return np.stack([np.asarray(r["out"]) for r in res.results]).astype(np.float32)
